# version 02
# baseline (speedup 1.0000x reference)
import numpy as np
import ml_dtypes
import concourse.bass as bass
import concourse.mybir as mybir
from concourse.bass_utils import run_bass_kernel_spmd
from concourse.alu_op_type import AluOpType as ALU
from contextlib import ExitStack

F32 = mybir.dt.float32
BF16 = mybir.dt.bfloat16
AF = mybir.ActivationFunctionType
BF = ml_dtypes.bfloat16
ENGINES = ("sync", "scalar", "vector", "tensor", "gpsimd")

D = 1024
B = 2
L = 8192
T = B * L
NC = 8
TC = T // NC


SBUF_WORDS = 52992
PSUM_COLS = 4096


class SP:
    def __init__(self, fused=False):
        self.nc = bass.Bass("TRN2", target_bir_lowering=False, debug=False)
        self.steps = []
        self.stack = ExitStack()
        self._uid = 0
        self.fused = fused
        if fused:
            self._backing = self.stack.enter_context(
                self.nc.sbuf_tensor("GBUF", [128, SBUF_WORDS], F32))
            self._pbacking = self.stack.enter_context(
                self.nc.psum_tensor("GPSUM", [128, PSUM_COLS], F32))
            self._sb_base = 0
            self._ps_base = 0

    def sub_begin(self):
        if self.fused:
            self._sb_base = 0
            self._ps_base = 0

    def uid(self):
        self._uid += 1
        return self._uid

    def sbuf(self, name, shape, dtype):
        if self.fused:
            parts = shape[0]
            n = int(np.prod(shape[1:]))
            if dtype == BF16:
                words = (n + 1) // 2
            elif dtype in (F32, mybir.dt.int32):
                words = n
            else:
                raise ValueError(dtype)
            off = self._sb_base
            assert off + words <= SBUF_WORDS, (name, off, words)
            self._sb_base = off + words
            ap = self._backing[0:parts, off:off + words]
            if dtype != F32:
                ap = ap.bitcast(dtype)
                if dtype == BF16 and ap.shape[1] != n:
                    ap = ap[:, 0:n]
            if len(shape) > 1:
                dims = " ".join(f"d{i}" for i in range(1, len(shape)))
                ap = ap.rearrange(f"p ({dims}) -> p {dims}",
                                  **{f"d{i}": shape[i] for i in range(1, len(shape))})
            return ap
        return self.stack.enter_context(self.nc.sbuf_tensor(f"{name}{self.uid()}", shape, dtype))

    def psum(self, name, shape, dtype=F32):
        if self.fused:
            assert dtype == F32, name
            parts = shape[0]
            n = int(np.prod(shape[1:]))
            cols = ceil_div(n, 512) * 512
            off = self._ps_base
            assert off + cols <= PSUM_COLS, (name, off, cols)
            self._ps_base = off + cols
            ap = self._pbacking[0:parts, off:off + n]
            if len(shape) > 2:
                dims = " ".join(f"d{i}" for i in range(1, len(shape)))
                ap = ap.rearrange(f"p ({dims}) -> p {dims}",
                                  **{f"d{i}": shape[i] for i in range(1, len(shape))})
            return ap
        return self.stack.enter_context(self.nc.psum_tensor(f"{name}{self.uid()}", shape, dtype))

    def din(self, name, shape, dtype=F32):
        return self.nc.dram_tensor(name, list(shape), dtype, kind="ExternalInput").ap()

    def dout(self, name, shape, dtype=F32):
        return self.nc.dram_tensor(name, list(shape), dtype, kind="ExternalOutput").ap()

    def dtmp(self, name, shape, dtype=F32):
        return self.nc.dram_tensor(f"{name}{self.uid()}", list(shape), dtype, kind="Internal").ap()

    def step(self, engine, inc, fn):
        self.steps.append((engine, fn, inc))

    def op(self, engine, fn):
        self.step(engine, 1, lambda eng, sem: fn(eng))

    def ops(self, engine, fn, n):
        self.step(engine, 1, fn)

    def dma(self, out_ap, in_ap, engine="sync", **kw):
        def f(eng, sem):
            return eng.dma_start(out=out_ap, in_=in_ap, **kw)
        self.step(engine, 16, f)

    def dmas(self, pairs, engine="sync", **kw):
        def f(eng, sem):
            for o, i in pairs:
                eng.dma_start(out=o, in_=i, **kw).then_inc(sem, 16)
            return None
        self.step(engine, 16 * len(pairs), f)

    def mms(self, fn):
        self.step("tensor", 1, lambda eng, sem: fn(eng))

    def tt(self, out, a, b, op, engine="vector"):
        self.op(engine, lambda e: e.tensor_tensor(out=out, in0=a, in1=b, op=op))

    def ts(self, out, a, s1, op, engine="vector"):
        self.op(engine, lambda e: e.tensor_scalar(out=out, in0=a, scalar1=s1, scalar2=None, op0=op))

    def stt(self, out, in0, scalar, in1, op0, op1, engine="vector"):
        self.op(engine, lambda e: e.scalar_tensor_tensor(
            out=out, in0=in0, scalar=scalar, in1=in1, op0=op0, op1=op1))

    def act(self, out, in_, func, bias=0.0, scale=1.0, accum_out=None):
        self.op("scalar", lambda e: e.activation(
            out=out, in_=in_, func=func, bias=bias, scale=scale, accum_out=accum_out))

    def copy(self, out, in_, engine="vector"):
        if engine == "scalar":
            self.op(engine, lambda e: e.copy(out=out, in_=in_))
        else:
            self.op(engine, lambda e: e.tensor_copy(out=out, in_=in_))

    def memset(self, ap, v, engine="vector"):
        self.op(engine, lambda e: e.memset(ap, v))

    def scan(self, out, d0, d1, initial, op0=ALU.mult, op1=ALU.add):
        self.op("vector", lambda e: e.tensor_tensor_scan(
            out=out, data0=d0, data1=d1, initial=initial, op0=op0, op1=op1))

    def recip(self, out, in_):
        self.op("vector", lambda e: e.reciprocal(out=out, in_=in_))

    def transpose(self, out_psum, in_, ident):
        self.op("tensor", lambda e: e.transpose(out_psum, in_, ident))

    def reduce(self, out, in_, op):
        self.op("vector", lambda e: e.tensor_reduce(
            out=out, in_=in_, axis=mybir.AxisListType.X, op=op))

    def run(self, in_maps, core_ids, **kw):
        nc = self.nc
        waits = []
        k = 0
        for (en, fn, inc) in self.steps:
            waits.append(k)
            k += inc
        total = k
        with nc.semaphore("SS") as sem, nc.Block() as blk:
            for en_name in ENGINES:
                def mk(en_name):
                    def body(eng):
                        for (en, fn, inc), kb in zip(self.steps, waits):
                            if en != en_name:
                                continue
                            if kb:
                                eng.wait_ge(sem, kb)
                            inst = fn(eng, sem)
                            if inst is not None and inc:
                                inst.then_inc(sem, inc)
                    return body
                getattr(blk, en_name)(mk(en_name))

            def fin(eng):
                eng.wait_ge(sem, total)
            blk.gpsimd(fin)
        try:
            res = run_bass_kernel_spmd(nc, in_maps, core_ids=list(core_ids), **kw)
        finally:
            self.stack.close()
        return res


def ceil_div(a, b):
    return (a + b - 1) // b


def chunks(total, size):
    out = []
    o = 0
    while o < total:
        out.append((o, min(size, total - o)))
        o += size
    return out


def make_ident(sp, name, n, dtype):
    t = sp.sbuf(name, [n, n], dtype)
    sp.op("gpsimd", lambda e: e.memset(t[:], 0.0))
    sp.op("gpsimd", lambda e: e.affine_select(
        out=t[:], in_=t[:], compare_op=ALU.not_equal, fill=1.0,
        base=0, pattern=[[-1, n]], channel_multiplier=1))
    return t


def emit_linear(sp, out_psum_fn, w_tiles_fn, act_fn, nfi, N, start=True, stop=True):
    def f(eng):
        last = None
        for fi in range(nfi):
            last = eng.matmul(out_psum_fn(), w_tiles_fn(fi), act_fn(fi),
                              start=(start and fi == 0), stop=(stop and fi == nfi - 1))
        return last
    sp.mms(f)


def build_prep(sp=None, io=None):
    if sp is None:
        sp = SP()
    io = io or {}
    nc = sp.nc
    ins = {}
    outs = {}

    def inp(name, shape, dtype=F32):
        ins[name] = nc.dram_tensor(name, list(shape), dtype, kind="ExternalInput").ap()
        return ins[name]

    def out(name, shape, dtype=BF16):
        if name in io:
            outs[name] = io[name]
        else:
            outs[name] = nc.dram_tensor(name, list(shape), dtype, kind="ExternalOutput").ap()
        return outs[name]

    bi = sp.sbuf("ci", [128, 4096], F32)
    bo = sp.sbuf("co", [128, 4096], BF16)
    ident_f32 = make_ident(sp, "idf0", 128, F32)
    pstf0 = sp.psum("pstf0", [128, 128], F32)
    tbuf = sp.sbuf("tb", [128, 128], BF16)

    def cast_tensor(src_ap, dst_ap, R, C):
        assert R % 128 == 0 or R < 128
        for r0, rn in chunks(R, 128):
            for c0, cn in chunks(C, 4096):
                sp.dma(bi[:rn, :cn], src_ap[r0:r0 + rn, c0:c0 + cn])
                sp.copy(bo[:rn, :cn], bi[:rn, :cn])
                sp.dma(dst_ap[r0:r0 + rn, c0:c0 + cn], bo[:rn, :cn])

    casts = [
        ("L0_attn_Wq1", (1024, 1024)), ("L0_attn_Wq2", (1024, 1024)),
        ("L0_attn_Wk1", (1024, 128)), ("L0_attn_Wk2", (1024, 128)),
        ("L0_attn_Wv", (1024, 128)), ("L0_attn_Wout", (1024, 1024)),
        ("L1_ffn_shared_gate_w", (1024, 1024)), ("L1_ffn_shared_up_w", (1024, 1024)),
        ("L1_ffn_shared_down_w", (1024, 1024)),
        ("L3_attn_Wq", (1024, 1024)), ("L3_attn_Wk", (1024, 512)),
        ("L3_attn_Wv", (1024, 512)), ("L3_attn_memory", (512, 512)),
    ]
    for name, shp in casts:
        src = inp(name, shp)
        dst = out(name + "_b", shp)
        cast_tensor(src, dst, shp[0], shp[1])

    casts3 = [
        ("L0_ffn_W1", (8, 1024, 4096)), ("L0_ffn_W2", (8, 2048, 1024)),
        ("L1_ffn_gate_w", (16, 1024, 512)), ("L1_ffn_up_w", (16, 1024, 512)),
        ("L1_ffn_down_w", (16, 512, 1024)),
    ]
    for name, shp in casts3:
        src = inp(name, shp)
        dst = out(name + "_b", shp)
        e, r, c = shp
        cast_tensor(src.rearrange("e r c -> (e r) c"), dst.rearrange("e r c -> (e r) c"),
                    e * r, c)

    def transpose_tensor(name, R, C, src_ap=None, outname=None):
        src = src_ap if src_ap is not None else inp(name, (R, C))
        dst = out(outname or (name + "_tb"), (C, R))
        nR, nCc = ceil_div(R, 128), ceil_div(C, 128)
        for j in range(nCc):
            c0, cn = j * 128, min(128, C - j * 128)
            for i in range(nR):
                r0, rn = i * 128, min(128, R - i * 128)
                sp.dma(bi[:rn, :cn], src[r0:r0 + rn, c0:c0 + cn])
                sp.transpose(pstf0[:cn, :rn], bi[:rn, :cn], ident_f32[:rn, :rn])
                sp.copy(tbuf[:cn, :rn], pstf0[:cn, :rn], engine="scalar")
                sp.dma(dst[c0:c0 + cn, r0:r0 + rn], tbuf[:cn, :rn])

    transpose_tensor("L1_attn_w_in", 4096, 1024)
    transpose_tensor("L1_attn_w_x", 96, 2048)
    transpose_tensor("L1_attn_w_dt", 2048, 64)
    transpose_tensor("L1_attn_w_out", 1024, 2048)
    transpose_tensor("L2_attn_in_proj_w", 4096, 1024)
    transpose_tensor("L2_attn_B_w", 64, 2048)
    transpose_tensor("L2_attn_C_w", 64, 2048)
    transpose_tensor("L2_attn_dt_w", 2048, 2048)
    transpose_tensor("L2_attn_out_proj_w", 1024, 2048)

    ps1 = sp.psum("p1", [1, 8], F32)
    rowacc = sp.sbuf("racc", [128, 32], F32)
    onescol = sp.sbuf("ones", [128, 1], F32)
    sp.memset(onescol[:], 1.0)
    throw = sp.sbuf("thr", [1, 1], F32)
    thd = sp.dtmp("thd", [1, 1])
    thcol = sp.sbuf("thc", [128, 1], F32)
    absb = sp.sbuf("absb", [128, 4096], F32)
    sgnb = sp.sbuf("sgnb", [128, 4096], F32)

    sumcol = sp.sbuf("sumc", [128, 1], F32)

    def ternary_transpose(name, R, C):
        src = inp(name, (R, C))
        dst = out(name + "_qtb", (C, R))
        nR = R // 128
        for i in range(nR):
            sp.dma(bi[:, :C], src[i * 128:(i + 1) * 128, :])
            sp.op("scalar", lambda e, _i=i: e.activation(
                out=absb[:, :C], in_=bi[:, :C], func=AF.Abs, accum_out=rowacc[:, _i:_i + 1]))
        sp.reduce(sumcol[:], rowacc[:, 0:nR], ALU.add)
        def f(eng):
            return eng.matmul(ps1[0:1, 0:1], onescol[:], sumcol[:], start=True, stop=True)
        sp.mms(f)
        sp.act(throw[:], ps1[0:1, 0:1], AF.Copy, scale=0.5 / (R * C))
        sp.dma(thd, throw[:])
        sp.dma(thcol[:], thd.to_broadcast([128, 1]))
        for i in range(nR):
            sp.dma(bi[:, :C], src[i * 128:(i + 1) * 128, :])
            sp.op("scalar", lambda e: e.activation(out=absb[:, :C], in_=bi[:, :C], func=AF.Abs))
            sp.op("scalar", lambda e: e.sign(out=sgnb[:, :C], in_=bi[:, :C]))
            sp.ts(absb[:, :C], absb[:, :C], thcol[:], ALU.is_gt)
            sp.tt(bi[:, :C], sgnb[:, :C], absb[:, :C], ALU.mult)
            for j in range(C // 128):
                sp.transpose(pstf0[:], bi[:, j * 128:(j + 1) * 128], ident_f32[:])
                sp.copy(tbuf[:], pstf0[:], engine="scalar")
                sp.dma(dst[j * 128:(j + 1) * 128, i * 128:(i + 1) * 128], tbuf[:])

    ternary_transpose("L3_ffn_w_up", 4096, 1024)
    ternary_transpose("L3_ffn_w_down", 1024, 4096)

    upw = [inp(f"upw{k}", (12, 2048)) for k in range(4)]
    dnw = [inp(f"dnw{k}", (12, 2048)) for k in range(4)]
    W_exp = inp("L2_ffn_W_exp", (1024, 4096))
    b_exp = inp("L2_ffn_b_exp", (1, 4096))
    up_bias = inp("L2_ffn_up_bias", (1, 4096))
    W_con = inp("L2_ffn_W_con", (4096, 1024))
    b_con = inp("L2_ffn_b_con", (1, 1024))
    down_bias = inp("L2_ffn_down_bias", (1, 1024))

    wrow = sp.sbuf("wrow", [128, 4096], F32)
    wc = [sp.sbuf(f"wc{k}", [128, 2048], F32) for k in range(4)]
    tna = sp.sbuf("tna", [128, 2048], F32)
    tnb = sp.sbuf("tnb", [128, 2048], F32)
    t2 = sp.sbuf("t2b", [128, 2048], F32)
    ident_f = make_ident(sp, "idf", 128, F32)
    pstf = sp.psum("pstf", [128, 128], F32)
    tbf = sp.sbuf("tbf", [128, 128], F32)

    def fold_rowtile(weights4, nrow, reverse):
        stages = list(range(12))
        order = [0, 1, 2, 3] if not reverse else [0, 2, 1, 3]
        if reverse:
            stages = stages[::-1]
        for st in stages:
            s = 1 << st
            for k in range(4):
                sp.dma(wc[k][:nrow, :], weights4[order[k]][st:st + 1, :].to_broadcast([nrow, 2048]))
            y = wrow[:nrow, :].rearrange("p (nb two s) -> p nb two s", two=2, s=s)
            w4 = [wc[k][:nrow, :].rearrange("p (nb s) -> p nb s", s=s) for k in range(4)]
            a = y[:, :, 0, :]
            b_ = y[:, :, 1, :]
            nav = tna[:nrow, :].rearrange("p (nb s) -> p nb s", s=s)
            nbv = tnb[:nrow, :].rearrange("p (nb s) -> p nb s", s=s)
            tv = t2[:nrow, :].rearrange("p (nb s) -> p nb s", s=s)
            sp.tt(nav, a, w4[0], ALU.mult)
            sp.tt(tv, b_, w4[1], ALU.mult)
            sp.tt(nav, nav, tv, ALU.add)
            sp.tt(nbv, a, w4[2], ALU.mult)
            sp.tt(tv, b_, w4[3], ALU.mult)
            sp.tt(nbv, nbv, tv, ALU.add)
            sp.copy(a, nav)
            sp.copy(b_, nbv)

    W_exp_b = out("W_exp_fb", (1024, 4096))
    bexp_f = out("bexp_f", (1, 4096), F32)
    for rt in range(8):
        sp.dma(wrow[:], W_exp[rt * 128:(rt + 1) * 128, :])
        fold_rowtile(upw, 128, reverse=False)
        sp.copy(bo[:], wrow[:])
        sp.dma(W_exp_b[rt * 128:(rt + 1) * 128, :], bo[:])
    sp.dma(wrow[0:1, :], b_exp)
    fold_rowtile(upw, 1, reverse=False)
    sp.dma(bi[0:1, :4096], up_bias)
    sp.tt(wrow[0:1, :], wrow[0:1, :], bi[0:1, :4096], ALU.add)
    sp.dma(bexp_f, wrow[0:1, :])

    W_con_b = out("W_con_fb", (4096, 1024))
    for rt in range(8):
        for i in range(32):
            sp.dma(bi[:, :128], W_con[i * 128:(i + 1) * 128, rt * 128:(rt + 1) * 128])
            sp.transpose(pstf[:], bi[:, :128], ident_f[:])
            sp.copy(tbf[:], pstf[:], engine="scalar")
            sp.copy(wrow[:, i * 128:(i + 1) * 128], tbf[:])
        fold_rowtile(dnw, 128, reverse=True)
        for j in range(32):
            sp.transpose(pstf[:], wrow[:, j * 128:(j + 1) * 128], ident_f[:])
            sp.copy(tbuf[:], pstf[:], engine="scalar")
            sp.dma(W_con_b[j * 128:(j + 1) * 128, rt * 128:(rt + 1) * 128], tbuf[:])
    bcon_f = out("bcon_f", (1, 1024), F32)
    sp.dma(bi[0:1, :1024], b_con)
    sp.dma(tna[0:1, :1024], down_bias)
    sp.tt(bi[0:1, :1024], bi[0:1, :1024], tna[0:1, :1024], ALU.add)
    sp.dma(bcon_f, bi[0:1, :1024])

    lq1 = inp("L0_attn_lq1", (1, 64))
    lk1 = inp("L0_attn_lk1", (1, 64))
    lq2 = inp("L0_attn_lq2", (1, 64))
    lk2 = inp("L0_attn_lk2", (1, 64))
    lam_o = out("lam", (1, 1), F32)
    r1 = sp.sbuf("r1", [1, 64], F32)
    r2 = sp.sbuf("r2", [1, 64], F32)
    d1 = sp.sbuf("d1", [1, 1], F32)
    d2 = sp.sbuf("d2", [1, 1], F32)
    sp.dma(r1[:], lq1)
    sp.dma(r2[:], lk1)
    sp.tt(r1[:], r1[:], r2[:], ALU.mult)
    sp.op("scalar", lambda e: e.activation(out=r1[:], in_=r1[:], func=AF.Copy, accum_out=d1[:]))
    sp.dma(r1[:], lq2)
    sp.dma(r2[:], lk2)
    sp.tt(r1[:], r1[:], r2[:], ALU.mult)
    sp.op("scalar", lambda e: e.activation(out=r1[:], in_=r1[:], func=AF.Copy, accum_out=d2[:]))
    sp.act(d1[:], d1[:], AF.Exp)
    sp.act(d2[:], d2[:], AF.Exp)
    sp.tt(d1[:], d1[:], d2[:], ALU.subtract)
    sp.ts(d1[:], d1[:], 0.8, ALU.add)
    sp.ts(d1[:], d1[:], 0.9, ALU.min)
    sp.ts(d1[:], d1[:], 0.1, ALU.max)
    sp.dma(lam_o, d1[:])

    return sp, ins, outs


_PREP_KEYS3 = None


def run_prep(inp):
    sp, ins, outs = build_prep()
    feed = {}
    for name in ins:
        if name.startswith("upw") or name.startswith("dnw"):
            k = int(name[3])
            src = inp["L2_ffn_up_weights" if name[0] == "u" else "L2_ffn_down_weights"]
            feed[name] = np.ascontiguousarray(src[:, :, k // 2, k % 2])
        elif name in ("L2_ffn_b_exp", "L2_ffn_up_bias", "L2_ffn_b_con", "L2_ffn_down_bias"):
            feed[name] = np.ascontiguousarray(inp[name].reshape(1, -1))
        elif name.startswith("L0_attn_l"):
            feed[name] = np.ascontiguousarray(inp[name].reshape(1, -1))
        else:
            feed[name] = np.ascontiguousarray(inp[name])
    res = sp.run([feed], core_ids=[0])
    return dict(res.results[0])


class Arena:
    def __init__(self, sp, words=50176):
        if sp.fused:
            assert sp._sb_base == 0, "allocate Arena before smalls in fused mode"
            self.t = sp._backing[:, 0:words]
            sp._sb_base = words
        else:
            self.t = sp.sbuf("AR", [128, words], F32)
        self.words = words

    def f32(self, off, shape):
        n = int(np.prod(shape))
        assert off + n <= self.words, (off, shape)
        ap = self.t[:, off:off + n]
        if len(shape) > 1:
            dims = " ".join(f"d{i}" for i in range(len(shape)))
            ap = ap.rearrange(f"p ({dims}) -> p {dims}",
                              **{f"d{i}": shape[i] for i in range(len(shape))})
        return ap

    def row(self, off, n):
        assert off + n <= self.words
        return self.t[0:1, off:off + n]

    def bf16(self, off, shape):
        n = int(np.prod(shape))
        assert n % 2 == 0
        assert off + n // 2 <= self.words, (off, shape)
        ap = self.t[:, off:off + n // 2].bitcast(BF16)
        if len(shape) > 1:
            dims = " ".join(f"d{i}" for i in range(len(shape)))
            ap = ap.rearrange(f"p ({dims}) -> p {dims}",
                              **{f"d{i}": shape[i] for i in range(len(shape))})
        return ap


def emit_ln(sp, ar, x_sb, xn_f32, xn_bf, rows_off, ps_row, ps_b0, ps_b1, ones_col, ones_row,
            ntok=TC, nf=8, eps=1e-5, sq_off=None):
    NTC = ceil_div(ntok, 512)
    mrow = ar.f32(rows_off, (1, ntok))
    srow = ar.f32(rows_off + ntok, (1, ntok))
    sinv = ar.f32(rows_off + 2 * ntok, (1, ntok))
    msr = ar.f32(rows_off + 3 * ntok, (1, ntok))
    sq = ar.f32(sq_off, (nf, 512))
    for tc in range(NTC):
        t0 = tc * 512
        tn = min(512, ntok - t0)
        tsl = slice(t0, t0 + tn)
        for a in range(nf):
            sp.tt(sq[:, a, :tn].rearrange("p a t -> p (a t)") if False else sq[:, a, :tn],
                  x_sb[:, a, tsl], x_sb[:, a, tsl], ALU.mult)
        def fmean(eng, _tsl=tsl, _tn=tn):
            last = None
            for a in range(nf):
                last = eng.matmul(ps_row[0:1, :_tn], ones_col[:], x_sb[:, a, _tsl],
                                  start=(a == 0), stop=(a == nf - 1))
            return last
        sp.mms(fmean)
        sp.act(mrow[:, tsl], ps_row[0:1, :tn], AF.Copy, scale=1.0 / (nf * 128))
        def fsq(eng, _tn=tn):
            last = None
            for a in range(nf):
                last = eng.matmul(ps_row[0:1, :_tn], ones_col[:], sq[:, a, :_tn],
                                  start=(a == 0), stop=(a == nf - 1))
            return last
        sp.mms(fsq)
        sp.act(srow[:, tsl], ps_row[0:1, :tn], AF.Copy, scale=1.0 / (nf * 128))
    sp.tt(msr[:], mrow[:], mrow[:], ALU.mult)
    sp.tt(srow[:], srow[:], msr[:], ALU.subtract)
    sp.act(srow[:], srow[:], AF.Sqrt, bias=eps)
    sp.recip(sinv[:], srow[:])
    sp.tt(msr[:], mrow[:], sinv[:], ALU.mult)
    for tc in range(NTC):
        t0 = tc * 512
        tn = min(512, ntok - t0)
        tsl = slice(t0, t0 + tn)
        def fb(eng, _tsl=tsl, _tn=tn):
            eng.matmul(ps_b0[:, :_tn], ones_row[:], sinv[:, _tsl], start=True, stop=True)
            return eng.matmul(ps_b1[:, :_tn], ones_row[:], msr[:, _tsl], start=True, stop=True)
        sp.mms(fb)
        for a in range(nf):
            if xn_f32 is not None:
                sp.tt(xn_f32[:, a, tsl], x_sb[:, a, tsl], ps_b0[:, :tn], ALU.mult)
                sp.tt(xn_f32[:, a, tsl], xn_f32[:, a, tsl], ps_b1[:, :tn], ALU.subtract)
                sp.copy(xn_bf[:, a, tsl], xn_f32[:, a, tsl])
            else:
                tmp = sq[:, 0, :tn]
                sp.tt(tmp, x_sb[:, a, tsl], ps_b0[:, :tn], ALU.mult)
                sp.tt(xn_bf[:, a, tsl], tmp, ps_b1[:, :tn], ALU.subtract)


def emit_gating_top2(sp, ar, goff, lsb8, c_dram, nexp, renorm, scale_out, goff_mk=None, goff_r=None):
    l_dram = sp.dtmp("gl", (nexp, TC))
    sp.dma(l_dram, lsb8)
    if goff_mk is None:
        goff_mk = goff + nexp * 512
    if goff_r is None:
        goff_r = goff_mk + nexp * 512
    le = [ar.row(goff + e * 512, 512) for e in range(nexp)]
    mk = [ar.row(goff_mk + e * 512, 512) for e in range(nexp)]
    r0 = goff_r
    m1 = ar.row(r0, 512)
    m2 = ar.row(r0 + 512, 512)
    t1 = ar.row(r0 + 1024, 512)
    t2 = ar.row(r0 + 1536, 512)
    for tc in range(TC // 512):
        tsl = slice(tc * 512, tc * 512 + 512)
        sp.dmas([(le[e], l_dram[e:e + 1, tsl]) for e in range(nexp)])
        sp.tt(m1, le[0], le[1], ALU.max)
        for e in range(2, nexp):
            sp.tt(m1, m1, le[e], ALU.max)
        if renorm:
            for e in range(nexp):
                sp.tt(mk[e], le[e], m1, ALU.is_equal)
                sp.stt(mk[e], mk[e], -1e30, le[e], ALU.mult, ALU.add)
            sp.tt(m2, mk[0], mk[1], ALU.max)
            for e in range(2, nexp):
                sp.tt(m2, m2, mk[e], ALU.max)
            for e in range(nexp):
                sp.tt(mk[e], le[e], m2, ALU.is_ge)
            sp.tt(t1, m2, m1, ALU.subtract)
            sp.act(t1, t1, AF.Exp)
            sp.ts(t1, t1, 1.0, ALU.add)
            sp.recip(t1, t1)
            sp.ts(t1, t1, scale_out, ALU.mult)
            for e in range(nexp):
                sp.tt(le[e], le[e], m1, ALU.subtract)
                sp.act(le[e], le[e], AF.Exp)
                sp.tt(le[e], le[e], mk[e], ALU.mult)
                sp.tt(le[e], le[e], t1, ALU.mult)
        else:
            for e in range(nexp):
                sp.tt(le[e], le[e], m1, ALU.subtract)
                sp.act(le[e], le[e], AF.Exp)
            sp.tt(t1, le[0], le[1], ALU.add)
            for e in range(2, nexp):
                sp.tt(t1, t1, le[e], ALU.add)
            sp.recip(t1, t1)
            for e in range(nexp):
                sp.tt(le[e], le[e], t1, ALU.mult)
            sp.tt(m1, le[0], le[1], ALU.max)
            for e in range(2, nexp):
                sp.tt(m1, m1, le[e], ALU.max)
            for e in range(nexp):
                sp.tt(mk[e], le[e], m1, ALU.is_equal)
                sp.stt(mk[e], mk[e], -1e30, le[e], ALU.mult, ALU.add)
            sp.tt(m2, mk[0], mk[1], ALU.max)
            for e in range(2, nexp):
                sp.tt(m2, m2, mk[e], ALU.max)
            for e in range(nexp):
                sp.tt(mk[e], le[e], m2, ALU.is_ge)
                sp.tt(le[e], le[e], mk[e], ALU.mult)
                sp.ts(le[e], le[e], scale_out, ALU.mult)
        sp.dmas([(c_dram[e:e + 1, tsl], le[e]) for e in range(nexp)])


def build_moe0(sp=None, io=None):
    if sp is None:
        sp = SP()
    io = io or {}
    def din(n, s, d=F32):
        return io[n] if n in io else sp.din(n, s, d)
    def dout(n, s, d=F32):
        return io[n] if n in io else sp.dout(n, s, d)
    xT = din("xT", (1024, TC))
    W1b = din("W1b", (8, 1024, 4096), BF16)
    W2b = din("W2b", (8, 2048, 1024), BF16)
    WrL = din("WrL", (128, 8, 8))
    brC = din("brC", (8, 1))
    b1L = din("b1L", (128, 32, 8))
    b2L = din("b2L", (128, 8, 8))
    yT = dout("yT", (1024, TC))
    c_dram = sp.dtmp("cdram", (8, TC))

    ar = Arena(sp, 40960)
    x_sb = ar.f32(0, (8, TC))
    xn_bf = ar.bf16(16384, (8, TC))
    sq = ar.f32(24576, (1, 512))
    rows_off = 25088
    goff = 25088
    act = ar.bf16(24576, (16, TC))
    acc = ar.f32(0, (8, TC))

    ones_col = sp.sbuf("onc", [128, 1], F32)
    ones_r8 = sp.sbuf("o8", [1, 8], F32)
    ones_r128 = sp.sbuf("o128", [1, 128], F32)
    wr_sb = sp.sbuf("wr", [128, 8, 8], F32)
    br_sb = sp.sbuf("brs", [8, 1], F32)
    b1_sb = sp.sbuf("b1s", [128, 32, 8], F32)
    b2_sb = sp.sbuf("b2s", [128, 8, 8], F32)
    ncol = sp.sbuf("ncl", [8, 1], F32)
    xr_sb = sp.sbuf("xrs", [8, 512], F32)
    lsb8 = sp.sbuf("l8", [8, TC], F32)
    w1a = sp.sbuf("w1a", [128, 8, 128], BF16)
    w1g = sp.sbuf("w1g", [128, 8, 128], BF16)
    w2f = sp.sbuf("w2f", [128, 16, 128], BF16)
    c_rep = sp.sbuf("crp", [128, TC], F32)
    silu_a = sp.sbuf("sla", [128, 512], F32)
    tmp_f = sp.sbuf("tmf", [128, 512], F32)
    xload = sp.sbuf("xld", [128, TC], F32)

    ps_r8 = sp.psum("psr", [8, 512])
    ps8a = sp.psum("p8a", [8, 512])
    ps8b = sp.psum("p8b", [8, 512])
    psB0 = sp.psum("pb0", [128, 512])
    psB1 = sp.psum("pb1", [128, 512])
    psA = sp.psum("psa", [128, 512])
    psG = sp.psum("psg", [128, 512])
    psO = sp.psum("pso", [128, 512])

    sp.memset(ones_col[:], 1.0, engine="gpsimd")
    sp.memset(ones_r8[:], 1.0, engine="gpsimd")
    sp.memset(ones_r128[:], 1.0, engine="gpsimd")
    sp.dmas([(x_sb, xT.rearrange("(a p) t -> p a t", p=128)),
             (wr_sb[:], WrL), (br_sb[:], brC), (b1_sb[:], b1L), (b2_sb[:], b2L)])

    mrow = ar.row(rows_off, TC)
    srow = ar.row(rows_off + TC, TC)
    sinv = ar.row(rows_off + 2 * TC, TC)
    msr = ar.row(rows_off + 3 * TC, TC)
    NF = 8
    for tc in range(4):
        tsl = slice(tc * 512, tc * 512 + 512)
        def fmean(eng, _tsl=tsl):
            last = None
            for a in range(NF):
                last = eng.matmul(ps8a[0:1, :], ones_col[:], x_sb[:, a, _tsl],
                                  start=(a == 0), stop=(a == NF - 1))
            return last
        sp.mms(fmean)
        sp.act(mrow[:, tsl], ps8a[0:1, :], AF.Copy, scale=1.0 / 1024)
        for a in range(NF):
            sp.tt(sq[:, 0, :], x_sb[:, a, tsl], x_sb[:, a, tsl], ALU.mult)
            def fsq(eng, _a=a):
                return eng.matmul(ps8b[0:1, :], ones_col[:], sq[:, 0, :],
                                  start=(_a == 0), stop=(_a == NF - 1))
            sp.mms(fsq)
        sp.act(srow[:, tsl], ps8b[0:1, :], AF.Copy, scale=1.0 / 1024)
    sp.tt(msr[:], mrow[:], mrow[:], ALU.mult)
    sp.tt(srow[:], srow[:], msr[:], ALU.subtract)
    sp.ts(srow[:], srow[:], 1e-5, ALU.add)
    sp.act(srow[:], srow[:], AF.Sqrt)
    sp.recip(sinv[:], srow[:])
    sp.tt(msr[:], mrow[:], sinv[:], ALU.mult)
    def fcs(eng):
        last = None
        for a in range(NF):
            last = eng.matmul(ps8a[0:8, 0:1], wr_sb[:, a, :], ones_col[:],
                              start=(a == 0), stop=(a == NF - 1))
        return last
    sp.mms(fcs)
    sp.act(ncol[:], ps8a[0:8, 0:1], AF.Copy, scale=-1.0)
    for tc in range(4):
        tsl = slice(tc * 512, tc * 512 + 512)
        def fb(eng, _tsl=tsl):
            eng.matmul(psB0[:], ones_r128[:], sinv[:, _tsl], start=True, stop=True)
            return eng.matmul(psB1[:], ones_r128[:], msr[:, _tsl], start=True, stop=True)
        sp.mms(fb)
        for a in range(NF):
            sp.tt(tmp_f[:], x_sb[:, a, tsl], psB0[:], ALU.mult)
            sp.tt(xn_bf[:, a, tsl], tmp_f[:], psB1[:], ALU.subtract)
        def fxr(eng, _tsl=tsl):
            last = None
            for a in range(NF):
                last = eng.matmul(ps_r8[:8, :], wr_sb[:, a, :], x_sb[:, a, _tsl],
                                  start=(a == 0), stop=(a == NF - 1))
            return last
        sp.mms(fxr)
        sp.copy(xr_sb[:], ps_r8[:8, :], engine="scalar")
        def fb8(eng, _tsl=tsl):
            eng.matmul(ps8a[:8, :], ones_r8[:], sinv[:, _tsl], start=True, stop=True)
            return eng.matmul(ps8b[:8, :], ones_r8[:], msr[:, _tsl], start=True, stop=True)
        sp.mms(fb8)
        sp.tt(xr_sb[:], xr_sb[:], ps8a[:8, :], ALU.mult)
        sp.stt(lsb8[:, tsl], ps8b[:8, :], ncol[:], xr_sb[:], ALU.mult, ALU.add)
        sp.ts(lsb8[:, tsl], lsb8[:, tsl], br_sb[:], ALU.add)

    emit_gating_top2(sp, ar, goff, lsb8[:], c_dram, 8, renorm=True, scale_out=0.5)

    sp.memset(acc.rearrange("p a t -> p (a t)"), 0.0, engine="gpsimd")
    for e in range(8):
        sp.dmas([(c_rep[:, tc * 512:(tc + 1) * 512],
                  c_dram[e:e + 1, tc * 512:(tc + 1) * 512].to_broadcast([128, 512]))
                 for tc in range(4)])
        for h in range(16):
            sp.dmas([(w1a[:], W1b[e, :, h * 128:(h + 1) * 128].rearrange("(a p) f -> p a f", p=128)),
                     (w1g[:], W1b[e, :, 2048 + h * 128:2048 + (h + 1) * 128].rearrange("(a p) f -> p a f", p=128))])
            for tc in range(4):
                tsl = slice(tc * 512, tc * 512 + 512)
                def fa(eng, _tsl=tsl):
                    last = None
                    for a in range(NF):
                        last = eng.matmul(psA[:], w1a[:, a, :], xn_bf[:, a, _tsl],
                                          start=(a == 0), stop=(a == NF - 1))
                    return last
                sp.mms(fa)
                def fg(eng, _tsl=tsl):
                    last = None
                    for a in range(NF):
                        last = eng.matmul(psG[:], w1g[:, a, :], xn_bf[:, a, _tsl],
                                          start=(a == 0), stop=(a == NF - 1))
                    return last
                sp.mms(fg)
                sp.act(silu_a[:], psA[:], AF.Silu, bias=b1_sb[:, h, e:e + 1])
                sp.stt(act[:, h, tsl], psG[:], b1_sb[:, 16 + h, e:e + 1], silu_a[:],
                       ALU.add, ALU.mult)
        for fo in range(8):
            sp.dma(w2f[:], W2b[e, :, fo * 128:(fo + 1) * 128].rearrange("(hh p) f -> p hh f", p=128))
            for tc in range(4):
                tsl = slice(tc * 512, tc * 512 + 512)
                def fo_(eng, _tsl=tsl):
                    last = None
                    for hh in range(16):
                        last = eng.matmul(psO[:], w2f[:, hh, :], act[:, hh, _tsl],
                                          start=(hh == 0), stop=(hh == 15))
                    return last
                sp.mms(fo_)
                sp.stt(tmp_f[:], psO[:], b2_sb[:, fo, e:e + 1], c_rep[:, tsl],
                       ALU.add, ALU.mult)
                sp.tt(acc[:, fo, tsl], acc[:, fo, tsl], tmp_f[:], ALU.add)

    for fo in range(8):
        sp.dma(xload[:], xT[fo * 128:(fo + 1) * 128, :])
        sp.tt(xload[:], xload[:], acc[:, fo, :], ALU.add)
        sp.dma(yT[fo * 128:(fo + 1) * 128, :], xload[:])
    return sp


def _stage_moe0(inp, Wp):
    Wr = inp["L0_ffn_Wr"]
    b1 = inp["L0_ffn_b1"]
    b2 = inp["L0_ffn_b2"]
    return {
        "W1b": np.ascontiguousarray(Wp["L0_ffn_W1_b"]),
        "W2b": np.ascontiguousarray(Wp["L0_ffn_W2_b"]),
        "WrL": np.ascontiguousarray(Wr.reshape(8, 128, 8).transpose(1, 0, 2)),
        "brC": np.ascontiguousarray(inp["L0_ffn_br"].reshape(8, 1)),
        "b1L": np.ascontiguousarray(b1.T.reshape(32, 128, 8).transpose(1, 0, 2)),
        "b2L": np.ascontiguousarray(b2.T.reshape(8, 128, 8).transpose(1, 0, 2)),
    }


def run_moe0(X, inp, Wp):
    sp = build_moe0()
    shared = _stage_moe0(inp, Wp)
    in_maps = []
    for c in range(NC):
        m = dict(shared)
        m["xT"] = np.ascontiguousarray(X[c * TC:(c + 1) * TC].T)
        in_maps.append(m)
    res = sp.run(in_maps, core_ids=range(NC))
    out = np.empty_like(X)
    for c in range(NC):
        out[c * TC:(c + 1) * TC] = res.results[c]["yT"].T
    return out


def build_attn0(sp=None, io=None):
    NW = 1152
    if sp is None:
        sp = SP()
    io = io or {}
    def din(n, s, d=F32):
        return io[n] if n in io else sp.din(n, s, d)
    def dout(n, s, d=F32):
        return io[n] if n in io else sp.dout(n, s, d)
    xwT = None if "xw_load" in (io or {}) else din("xwT", (1024, NW))
    xrT = din("xrT", (1024, TC))
    Wq1b = din("Wq1b", (1024, 1024), BF16)
    Wq2b = din("Wq2b", (1024, 1024), BF16)
    Wk1b = din("Wk1b", (1024, 128), BF16)
    Wk2b = din("Wk2b", (1024, 128), BF16)
    Wvb = din("Wvb", (1024, 128), BF16)
    Woutb = din("Woutb", (1024, 1024), BF16)
    boutL = din("boutL", (128, 8))
    lamC = din("lamC", (128, 1))
    posrow = din("posrow", (1, NW))
    divrow = din("divrow", (1, 32))
    yT = dout("yT", (1024, TC))

    ar = Arena(sp, 30720)
    x_sb = ar.f32(0, (8, NW))
    xn_bf = ar.bf16(9216, (8, NW))
    rows_off = 13824
    def v64(off, nh):
        return ar.t[0:64, off:off + nh * NW // 2].bitcast(BF16).rearrange(
            "p (h t) -> p h t", h=nh)
    qrot1_v = v64(0, 16)
    qrot2_v = v64(18432, 16)
    krot1_v = v64(27648, 2)
    krot2_v = v64(28800, 2)

    ones_col = sp.sbuf("onc", [128, 1], F32)
    ones_col_bf = sp.sbuf("onb", [128, 1], BF16)
    ones_r128 = sp.sbuf("o12", [1, 128], F32)
    ones_r64 = sp.sbuf("o64", [1, 64], F32)
    sq = sp.sbuf("sq", [128, 512], F32)
    tmp = sp.sbuf("tmp", [128, 512], F32)
    v_sb = sp.sbuf("vsb", [128, NW], F32)
    wq_sb = sp.sbuf("wqs", [128, 8, 128], BF16)
    wout_sb = sp.sbuf("wos", [128, 8, 1024], BF16)
    bout_sb = sp.sbuf("bos", [128, 8], F32)
    lam_sb = sp.sbuf("lam", [128, 1], F32)
    cos_sb = sp.sbuf("cos", [32, NW], F32)
    sin_sb = sp.sbuf("sin", [32, NW], F32)
    th_sb = sp.sbuf("ths", [32, NW], F32)
    msk_sb = sp.sbuf("msk", [32, NW], F32)
    vt_sb = sp.sbuf("vts", [128, 2, 2, 64], BF16)
    expT = sp.sbuf("exT", [128, 256], BF16)
    expT2 = sp.sbuf("ex2", [128, 256], BF16)
    rrow = sp.sbuf("rrw", [1, 256], F32)
    rrep = sp.sbuf("rrp", [128, 256], F32)
    a_b1 = sp.sbuf("ab1", [128, 256], F32)
    a_b2 = sp.sbuf("ab2", [128, 256], F32)
    aM = sp.sbuf("aM", [128, 8, 256], BF16)
    xr_load = sp.sbuf("xrl", [128, 256], F32)
    t32a = sp.sbuf("t3a", [32, 512], F32)
    t32b = sp.sbuf("t3b", [32, 512], F32)

    psE = sp.psum("psE", [32, 512])
    psO32 = sp.psum("psO", [32, 512])
    psS = sp.psum("psS", [128, 512])
    psW = sp.psum("psW", [128, 512])
    psPV = sp.psum("psP", [128, 256])
    psR = sp.psum("psR", [1, 256])
    psB = sp.psum("psB", [128, 256])
    psT = sp.psum("psT", [128, 64], F32)
    identb = make_ident(sp, "idb", 128, F32)

    sp.memset(ones_col[:], 1.0, engine="gpsimd")
    sp.memset(ones_col_bf[:], 1.0, engine="gpsimd")
    sp.memset(ones_r128[:], 1.0, engine="gpsimd")
    sp.memset(ones_r64[:], 1.0, engine="gpsimd")
    if "xw_load" in io:
        io["xw_load"](sp, x_sb)
    else:
        sp.dmas([(x_sb, xwT.rearrange("(a p) t -> p a t", p=128))])
    sp.dmas([(wout_sb[:], Woutb.rearrange("(a p) f -> p a f", p=128)),
             (bout_sb[:], boutL), (lam_sb[:], lamC)])
    sp.ts(lam_sb[:], lam_sb[:], -1.0, ALU.mult)

    mrow = ar.row(rows_off, NW)
    srow = ar.row(rows_off + NW, NW)
    sinv = ar.row(rows_off + 2 * NW, NW)
    msr = ar.row(rows_off + 3 * NW, NW)
    tch = chunks(NW, 512)
    for (t0, tn) in tch:
        tsl = slice(t0, t0 + tn)
        def fmean(eng, _tsl=tsl, _tn=tn):
            last = None
            for a in range(8):
                last = eng.matmul(psE[0:1, :_tn], ones_col[:], x_sb[:, a, _tsl],
                                  start=(a == 0), stop=(a == 7))
            return last
        sp.mms(fmean)
        sp.act(mrow[:, tsl], psE[0:1, :tn], AF.Copy, scale=1.0 / 1024)
        for a in range(8):
            sp.tt(sq[:, :tn], x_sb[:, a, tsl], x_sb[:, a, tsl], ALU.mult)
            def fsq(eng, _a=a, _tn=tn):
                return eng.matmul(psO32[0:1, :_tn], ones_col[:], sq[:, :_tn],
                                  start=(_a == 0), stop=(_a == 7))
            sp.mms(fsq)
        sp.act(srow[:, tsl], psO32[0:1, :tn], AF.Copy, scale=1.0 / 1024)
    sp.tt(msr[:], mrow[:], mrow[:], ALU.mult)
    sp.tt(srow[:], srow[:], msr[:], ALU.subtract)
    sp.ts(srow[:], srow[:], 1e-5, ALU.add)
    sp.act(srow[:], srow[:], AF.Sqrt)
    sp.recip(sinv[:], srow[:])
    sp.tt(msr[:], mrow[:], sinv[:], ALU.mult)
    for (t0, tn) in tch:
        tsl = slice(t0, t0 + tn)
        def fb(eng, _tsl=tsl, _tn=tn):
            eng.matmul(psW[:, :_tn], ones_r128[:], sinv[:, _tsl], start=True, stop=True)
            return eng.matmul(psS[:, :_tn], ones_r128[:], msr[:, _tsl], start=True, stop=True)
        sp.mms(fb)
        for a in range(8):
            sp.tt(tmp[:, :tn], x_sb[:, a, tsl], psW[:, :tn], ALU.mult)
            sp.tt(xn_bf[:, a, tsl], tmp[:, :tn], psS[:, :tn], ALU.subtract)

    div_sb = sp.sbuf("div", [1, 32], F32)
    pos_sb = sp.sbuf("pos", [1, NW], F32)
    sp.dmas([(div_sb[:], divrow), (pos_sb[:], posrow)])
    for (t0, tn) in tch:
        tsl = slice(t0, t0 + tn)
        def fth(eng, _tsl=tsl, _tn=tn):
            return eng.matmul(psE[0:32, :_tn], div_sb[:], pos_sb[0:1, _tsl],
                              start=True, stop=True)
        sp.mms(fth)
        sp.copy(th_sb[:, tsl], psE[0:32, :tn], engine="scalar")
    TWO_PI = float(2 * np.pi)
    i32t = sp.sbuf("i32", [32, NW], mybir.dt.int32)
    sp.ts(th_sb[:], th_sb[:], float(1.0 / TWO_PI), ALU.mult)
    sp.copy(i32t[:], th_sb[:])
    sp.copy(msk_sb[:], i32t[:])
    sp.stt(th_sb[:], msk_sb[:], -1.0, th_sb[:], ALU.mult, ALU.add)
    sp.ts(th_sb[:], th_sb[:], TWO_PI, ALU.mult)
    sp.ts(msk_sb[:], th_sb[:], float(np.pi), ALU.is_gt)
    sp.stt(th_sb[:], msk_sb[:], -TWO_PI, th_sb[:], ALU.mult, ALU.add)
    sp.ts(msk_sb[:], th_sb[:], float(-np.pi), ALU.is_lt)
    sp.stt(th_sb[:], msk_sb[:], TWO_PI, th_sb[:], ALU.mult, ALU.add)
    sp.act(sin_sb[:], th_sb[:], AF.Sin)
    sp.ts(th_sb[:], th_sb[:], float(np.pi / 2), ALU.add)
    sp.ts(msk_sb[:], th_sb[:], float(np.pi), ALU.is_gt)
    sp.stt(th_sb[:], msk_sb[:], -TWO_PI, th_sb[:], ALU.mult, ALU.add)
    sp.act(cos_sb[:], th_sb[:], AF.Sin)

    def project_rope(Wsrc, nfo, rot_dst):
        for fo in range(nfo):
            sp.dma(wq_sb[:], Wsrc[:, fo * 128:(fo + 1) * 128].rearrange("(a p) f -> p a f", p=128))
            wv = wq_sb[:].rearrange("p a (hh k par) -> p a hh k par", hh=2, par=2)
            for hh in range(2):
                h = fo * 2 + hh
                for (t0, tn) in tch:
                    tsl = slice(t0, t0 + tn)
                    def fe(eng, _hh=hh, _tsl=tsl, _tn=tn):
                        last = None
                        for a in range(8):
                            last = eng.matmul(psE[:, :_tn], wv[:, a, _hh, :, 0], xn_bf[:, a, _tsl],
                                              start=(a == 0), stop=(a == 7))
                        return last
                    sp.mms(fe)
                    def fodd(eng, _hh=hh, _tsl=tsl, _tn=tn):
                        last = None
                        for a in range(8):
                            last = eng.matmul(psO32[:, :_tn], wv[:, a, _hh, :, 1], xn_bf[:, a, _tsl],
                                              start=(a == 0), stop=(a == 7))
                        return last
                    sp.mms(fodd)
                    sp.tt(t32a[:, :tn], psE[0:32, :tn], cos_sb[:, tsl], ALU.mult)
                    sp.tt(t32b[:, :tn], psO32[0:32, :tn], sin_sb[:, tsl], ALU.mult)
                    sp.tt(rot_dst[0:32, h, tsl], t32a[:, :tn], t32b[:, :tn], ALU.subtract)
                    sp.tt(t32a[:, :tn], psE[0:32, :tn], sin_sb[:, tsl], ALU.mult)
                    sp.tt(t32b[:, :tn], psO32[0:32, :tn], cos_sb[:, tsl], ALU.mult)
                    sp.tt(rot_dst[32:64, h, tsl], t32a[:, :tn], t32b[:, :tn], ALU.add)

    project_rope(Wq1b, 8, qrot1_v)
    project_rope(Wq2b, 8, qrot2_v)
    project_rope(Wk1b, 1, krot1_v)
    project_rope(Wk2b, 1, krot2_v)
    sp.dma(wq_sb[:], Wvb.rearrange("(a p) f -> p a f", p=128))
    for (t0, tn) in tch:
        tsl = slice(t0, t0 + tn)
        def fv(eng, _tsl=tsl, _tn=tn):
            last = None
            for a in range(8):
                last = eng.matmul(psS[:, :_tn], wq_sb[:, a, :], xn_bf[:, a, _tsl],
                                  start=(a == 0), stop=(a == 7))
            return last
        sp.mms(fv)
        sp.copy(v_sb[:, tsl], psS[:, :tn], engine="scalar")

    for n in range(8):
        wsl = slice(n * 128, n * 128 + 256)
        for g in range(2):
            for half in range(2):
                ksl = slice(n * 128 + half * 128, n * 128 + half * 128 + 128)
                sp.transpose(psT[:, :], v_sb[g * 64:(g + 1) * 64, ksl],
                             identb[g * 64:(g + 1) * 64, g * 64:(g + 1) * 64])
                sp.copy(vt_sb[:, g, half, :], psT[:, :], engine="scalar")
        for h in range(16):
            g = h % 2
            for br, (qro, kro, adst) in enumerate([(qrot1_v, krot1_v, a_b1), (qrot2_v, krot2_v, a_b2)]):
                ob = (h % 2) * 64
                def fsT2(eng, _qro=qro, _kro=kro, _g=g, _h=h, _n=n):
                    last = None
                    for half in range(2):
                        ksl = slice(_n * 128 + half * 128, _n * 128 + half * 128 + 128)
                        last = eng.matmul(psS[:, half * 256:half * 256 + 256],
                                          _kro[0:64, _g, ksl],
                                          _qro[0:64, _h, _n * 128:_n * 128 + 256],
                                          start=True, stop=True)
                    return last
                sp.mms(fsT2)
                sp.act(expT[:, :], psS[:, 0:256], AF.Exp, scale=0.125)
                sp.act(expT2[:, :], psS[:, 256:512], AF.Exp, scale=0.125)
                def fpv(eng, _g=g, _ob=ob):
                    eng.matmul(psPV[_ob:_ob + 64, :], vt_sb[:, _g, 0, :], expT[:, :],
                               start=True, stop=False)
                    return eng.matmul(psPV[_ob:_ob + 64, :], vt_sb[:, _g, 1, :], expT2[:, :],
                                      start=False, stop=True)
                sp.mms(fpv)
                def fsum(eng):
                    eng.matmul(psR[:, :], ones_col_bf[:], expT[:, :], start=True, stop=False)
                    return eng.matmul(psR[:, :], ones_col_bf[:], expT2[:, :], start=False, stop=True)
                sp.mms(fsum)
                sp.recip(rrow[:], psR[0:1, :])
                def fbc(eng, _ob=ob):
                    return eng.matmul(psB[_ob:_ob + 64, :], ones_r64[:], rrow[:],
                                      start=True, stop=True)
                sp.mms(fbc)
                sp.copy(rrep[ob:ob + 64, :], psB[ob:ob + 64, :], engine="scalar")
                sp.tt(adst[ob:ob + 64, :], psPV[ob:ob + 64, :], rrep[ob:ob + 64, :], ALU.mult)
            ob = (h % 2) * 64
            sp.ts(a_b2[ob:ob + 64, :], a_b2[ob:ob + 64, :], lam_sb[ob:ob + 64, :], ALU.mult)
            sp.tt(aM[ob:ob + 64, h // 2, :], a_b1[ob:ob + 64, :], a_b2[ob:ob + 64, :], ALU.add)
        osl = slice(n * 256, n * 256 + 256)
        for fo in range(8):
            sp.dma(xr_load[:], xrT[fo * 128:(fo + 1) * 128, osl])
            def fw(eng, _fo=fo):
                last = None
                for a in range(8):
                    last = eng.matmul(psW[:, :256], wout_sb[:, a, _fo * 128:(_fo + 1) * 128],
                                      aM[:, a, :], start=(a == 0), stop=(a == 7))
                return last
            sp.mms(fw)
            sp.stt(xr_load[:], psW[:, :256], bout_sb[:, fo:fo + 1], xr_load[:],
                   ALU.add, ALU.add)
            sp.dma(yT[fo * 128:(fo + 1) * 128, osl], xr_load[:])
    return sp


def _stage_attn0(inp, Wp):
    lam = np.asarray(Wp["lam"], dtype=np.float32).reshape(1)
    div = np.exp(np.arange(0, 64, 2, dtype=np.float32) * (-np.log(10000.0) / 64))
    return {
        "Wq1b": np.ascontiguousarray(Wp["L0_attn_Wq1_b"]),
        "Wq2b": np.ascontiguousarray(Wp["L0_attn_Wq2_b"]),
        "Wk1b": np.ascontiguousarray(Wp["L0_attn_Wk1_b"]),
        "Wk2b": np.ascontiguousarray(Wp["L0_attn_Wk2_b"]),
        "Wvb": np.ascontiguousarray(Wp["L0_attn_Wv_b"]),
        "Woutb": np.ascontiguousarray(Wp["L0_attn_Wout_b"]),
        "boutL": np.ascontiguousarray(inp["L0_attn_bout"].reshape(8, 128).T),
        "lamC": np.broadcast_to(lam.reshape(1, 1), (128, 1)).copy(),
        "divrow": div.reshape(1, 32),
    }


def run_attn0(X, inp, Wp):
    sp = build_attn0()
    shared = _stage_attn0(inp, Wp)
    in_maps = []
    for c in range(NC):
        b, q = c // 4, c % 4
        m = dict(shared)
        xb = X[b * 8192:(b + 1) * 8192]
        m["xwT"] = np.ascontiguousarray(xb[1024 * q:1024 * q + 1152].T)
        m["xrT"] = np.ascontiguousarray(xb[2048 * q:2048 * q + 2048].T)
        m["posrow"] = (1024.0 * q + np.arange(1152, dtype=np.float32)).reshape(1, 1152)
        in_maps.append(m)
    res = sp.run(in_maps, core_ids=range(NC))
    out = np.empty_like(X)
    for c in range(NC):
        out[c * TC:(c + 1) * TC] = res.results[c]["yT"].T
    return out


def build_moe1(skip_experts=False, skip_shared=False, sp=None, io=None):
    if sp is None:
        sp = SP()
    io = io or {}
    def din(n, s, d=F32):
        return io[n] if n in io else sp.din(n, s, d)
    def dout(n, s, d=F32):
        return io[n] if n in io else sp.dout(n, s, d)
    xT = din("xT", (1024, TC))
    Gb = din("Gb", (16, 1024, 512), BF16)
    Ub = din("Ub", (16, 1024, 512), BF16)
    Db = din("Db", (16, 512, 1024), BF16)
    SGb = din("SGb", (1024, 1024), BF16)
    SUb = din("SUb", (1024, 1024), BF16)
    SDb = din("SDb", (1024, 1024), BF16)
    RwL = din("RwL", (128, 8, 16))
    yT = dout("yT", (1024, TC))
    c_dram = sp.dtmp("cdram", (16, TC))

    ar = Arena(sp, 40960)
    x_sb = ar.f32(0, (8, TC))
    xn_bf = ar.bf16(16384, (8, TC))
    goff = 0
    act_sh = ar.bf16(0, (8, TC))
    act_ex = ar.bf16(8192, (4, TC))
    acc = ar.f32(24576, (8, TC))
    rows_off = 12288
    rows_off = 24576

    ones_col = sp.sbuf("onc", [128, 1], F32)
    ones_r16 = sp.sbuf("o16", [1, 16], F32)
    ones_r128 = sp.sbuf("o128", [1, 128], F32)
    rw_sb = sp.sbuf("rw", [128, 8, 16], F32)
    ncol = sp.sbuf("ncl", [16, 1], F32)
    xr_sb = sp.sbuf("xrs", [16, 512], F32)
    lsb = sp.sbuf("l16", [16, TC], F32)
    wg = sp.sbuf("wg", [128, 8, 128], BF16)
    wu = sp.sbuf("wu", [128, 8, 128], BF16)
    wd = sp.sbuf("wd", [128, 8, 128], BF16)
    c_rep = sp.sbuf("crp", [128, TC], F32)
    silu_a = sp.sbuf("sla", [128, 512], F32)
    tmp_f = sp.sbuf("tmf", [128, 512], F32)
    sq = sp.sbuf("sqq", [128, 512], F32)
    xload = sp.sbuf("xld", [128, TC], F32)

    ps_r16 = sp.psum("psr", [16, 512])
    ps16a = sp.psum("p6a", [16, 512])
    ps16b = sp.psum("p6b", [16, 512])
    psB0 = sp.psum("pb0", [128, 512])
    psB1 = sp.psum("pb1", [128, 512])
    psA = sp.psum("psa", [128, 512])
    psG = sp.psum("psg", [128, 512])
    psO = sp.psum("pso", [128, 512])

    sp.memset(ones_col[:], 1.0, engine="gpsimd")
    sp.memset(ones_r16[:], 1.0, engine="gpsimd")
    sp.memset(ones_r128[:], 1.0, engine="gpsimd")
    sp.dmas([(x_sb, xT.rearrange("(a p) t -> p a t", p=128)), (rw_sb[:], RwL)])

    mrow = ar.row(rows_off, TC)
    srow = ar.row(rows_off + TC, TC)
    sinv = ar.row(rows_off + 2 * TC, TC)
    msr = ar.row(rows_off + 3 * TC, TC)
    NF = 8
    for tc in range(4):
        tsl = slice(tc * 512, tc * 512 + 512)
        def fmean(eng, _tsl=tsl):
            last = None
            for a in range(NF):
                last = eng.matmul(ps16a[0:1, :], ones_col[:], x_sb[:, a, _tsl],
                                  start=(a == 0), stop=(a == NF - 1))
            return last
        sp.mms(fmean)
        sp.act(mrow[:, tsl], ps16a[0:1, :], AF.Copy, scale=1.0 / 1024)
        for a in range(NF):
            sp.tt(sq[:], x_sb[:, a, tsl], x_sb[:, a, tsl], ALU.mult)
            def fsq(eng, _a=a):
                return eng.matmul(ps16b[0:1, :], ones_col[:], sq[:],
                                  start=(_a == 0), stop=(_a == NF - 1))
            sp.mms(fsq)
        sp.act(srow[:, tsl], ps16b[0:1, :], AF.Copy, scale=1.0 / 1024)
    sp.tt(msr[:], mrow[:], mrow[:], ALU.mult)
    sp.tt(srow[:], srow[:], msr[:], ALU.subtract)
    sp.ts(srow[:], srow[:], 1e-5, ALU.add)
    sp.act(srow[:], srow[:], AF.Sqrt)
    sp.recip(sinv[:], srow[:])
    sp.tt(msr[:], mrow[:], sinv[:], ALU.mult)
    def fcs(eng):
        last = None
        for a in range(NF):
            last = eng.matmul(ps16a[0:16, 0:1], rw_sb[:, a, :], ones_col[:],
                              start=(a == 0), stop=(a == NF - 1))
        return last
    sp.mms(fcs)
    sp.act(ncol[:], ps16a[0:16, 0:1], AF.Copy, scale=-1.0)
    for tc in range(4):
        tsl = slice(tc * 512, tc * 512 + 512)
        def fb(eng, _tsl=tsl):
            eng.matmul(psB0[:], ones_r128[:], sinv[:, _tsl], start=True, stop=True)
            return eng.matmul(psB1[:], ones_r128[:], msr[:, _tsl], start=True, stop=True)
        sp.mms(fb)
        for a in range(NF):
            sp.tt(tmp_f[:], x_sb[:, a, tsl], psB0[:], ALU.mult)
            sp.tt(xn_bf[:, a, tsl], tmp_f[:], psB1[:], ALU.subtract)
        def fxr(eng, _tsl=tsl):
            last = None
            for a in range(NF):
                last = eng.matmul(ps_r16[:16, :], rw_sb[:, a, :], x_sb[:, a, _tsl],
                                  start=(a == 0), stop=(a == NF - 1))
            return last
        sp.mms(fxr)
        sp.copy(xr_sb[:], ps_r16[:16, :], engine="scalar")
        def fb16(eng, _tsl=tsl):
            eng.matmul(ps16a[:16, :], ones_r16[:], sinv[:, _tsl], start=True, stop=True)
            return eng.matmul(ps16b[:16, :], ones_r16[:], msr[:, _tsl], start=True, stop=True)
        sp.mms(fb16)
        sp.tt(xr_sb[:], xr_sb[:], ps16a[:16, :], ALU.mult)
        sp.stt(lsb[:, tsl], ps16b[:16, :], ncol[:], xr_sb[:], ALU.mult, ALU.add)

    emit_gating_top2(sp, ar, goff, lsb[:], c_dram, 16, renorm=False, scale_out=0.5,
                     goff_mk=24576, goff_r=8192)

    if skip_shared:
        sp.memset(acc.rearrange("p a t -> p (a t)"), 0.0, engine="gpsimd")
    for hf in range(0 if skip_shared else 8):
        sp.dmas([(wg[:], SGb[:, hf * 128:(hf + 1) * 128].rearrange("(a p) f -> p a f", p=128)),
                 (wu[:], SUb[:, hf * 128:(hf + 1) * 128].rearrange("(a p) f -> p a f", p=128))])
        for tc in range(4):
            tsl = slice(tc * 512, tc * 512 + 512)
            def fa(eng, _tsl=tsl):
                last = None
                for a in range(NF):
                    last = eng.matmul(psA[:], wg[:, a, :], xn_bf[:, a, _tsl],
                                      start=(a == 0), stop=(a == NF - 1))
                return last
            sp.mms(fa)
            def fg(eng, _tsl=tsl):
                last = None
                for a in range(NF):
                    last = eng.matmul(psG[:], wu[:, a, :], xn_bf[:, a, _tsl],
                                      start=(a == 0), stop=(a == NF - 1))
                return last
            sp.mms(fg)
            sp.act(silu_a[:], psA[:], AF.Silu)
            sp.tt(act_sh[:, hf, tsl], psG[:], silu_a[:], ALU.mult)
    for fo in range(8):
        sp.dma(wd[:], SDb[:, fo * 128:(fo + 1) * 128].rearrange("(a p) f -> p a f", p=128))
        for tc in range(4):
            tsl = slice(tc * 512, tc * 512 + 512)
            def fd(eng, _tsl=tsl):
                last = None
                for a in range(NF):
                    last = eng.matmul(psO[:], wd[:, a, :], act_sh[:, a, _tsl],
                                      start=(a == 0), stop=(a == NF - 1))
                return last
            sp.mms(fd)
            sp.ts(acc[:, fo, tsl], psO[:], 0.5, ALU.mult)

    for e in range(0 if skip_experts else 16):
        sp.dmas([(c_rep[:, tc * 512:(tc + 1) * 512],
                  c_dram[e:e + 1, tc * 512:(tc + 1) * 512].to_broadcast([128, 512]))
                 for tc in range(4)])
        for hf in range(4):
            sp.dmas([(wg[:], Gb[e, :, hf * 128:(hf + 1) * 128].rearrange("(a p) f -> p a f", p=128)),
                     (wu[:], Ub[e, :, hf * 128:(hf + 1) * 128].rearrange("(a p) f -> p a f", p=128))])
            for tc in range(4):
                tsl = slice(tc * 512, tc * 512 + 512)
                def fa(eng, _tsl=tsl):
                    last = None
                    for a in range(NF):
                        last = eng.matmul(psA[:], wg[:, a, :], xn_bf[:, a, _tsl],
                                          start=(a == 0), stop=(a == NF - 1))
                    return last
                sp.mms(fa)
                def fg(eng, _tsl=tsl):
                    last = None
                    for a in range(NF):
                        last = eng.matmul(psG[:], wu[:, a, :], xn_bf[:, a, _tsl],
                                          start=(a == 0), stop=(a == NF - 1))
                    return last
                sp.mms(fg)
                sp.act(silu_a[:], psA[:], AF.Silu)
                sp.tt(act_ex[:, hf, tsl], psG[:], silu_a[:], ALU.mult)
        for fo in range(8):
            sp.dma(wd[:, 0:4, :], Db[e, :, fo * 128:(fo + 1) * 128].rearrange("(a p) f -> p a f", p=128))
            for tc in range(4):
                tsl = slice(tc * 512, tc * 512 + 512)
                def fd(eng, _tsl=tsl):
                    last = None
                    for a in range(4):
                        last = eng.matmul(psO[:], wd[:, a, :], act_ex[:, a, _tsl],
                                          start=(a == 0), stop=(a == 3))
                    return last
                sp.mms(fd)
                sp.tt(tmp_f[:], psO[:], c_rep[:, tsl], ALU.mult)
                sp.tt(acc[:, fo, tsl], acc[:, fo, tsl], tmp_f[:], ALU.add)

    for fo in range(8):
        sp.dma(xload[:], xT[fo * 128:(fo + 1) * 128, :])
        sp.tt(xload[:], xload[:], acc[:, fo, :], ALU.add)
        sp.dma(yT[fo * 128:(fo + 1) * 128, :], xload[:])
    return sp


def run_moe1(X, inp, Wp):
    sp = build_moe1()
    Rw = inp["L1_ffn_router_w"]
    shared = {
        "Gb": np.ascontiguousarray(Wp["L1_ffn_gate_w_b"]),
        "Ub": np.ascontiguousarray(Wp["L1_ffn_up_w_b"]),
        "Db": np.ascontiguousarray(Wp["L1_ffn_down_w_b"]),
        "SGb": np.ascontiguousarray(Wp["L1_ffn_shared_gate_w_b"]),
        "SUb": np.ascontiguousarray(Wp["L1_ffn_shared_up_w_b"]),
        "SDb": np.ascontiguousarray(Wp["L1_ffn_shared_down_w_b"]),
        "RwL": np.ascontiguousarray(Rw.reshape(8, 128, 16).transpose(1, 0, 2)),
    }
    in_maps = []
    for c in range(NC):
        m = dict(shared)
        m["xT"] = np.ascontiguousarray(X[c * TC:(c + 1) * TC].T)
        in_maps.append(m)
    res = sp.run(in_maps, core_ids=range(NC))
    out = np.empty_like(X)
    for c in range(NC):
        out[c * TC:(c + 1) * TC] = res.results[c]["yT"].T
    return out


def build_l1a(sp=None, io=None):
    NT = 2051
    if sp is None:
        sp = SP()
    io = io or {}
    def din(n, s, d=F32):
        return io[n] if n in io else sp.din(n, s, d)
    def dout(n, s, d=F32):
        return io[n] if n in io else sp.dout(n, s, d)
    xhT = None if "xh_load" in (io or {}) else din("xhT", (1024, NT))
    winT = din("winT", (1024, 4096), BF16)
    wxT = din("wxT", (2048, 96), BF16)
    wdtT = din("wdtT", (64, 2048), BF16)
    gamL = din("gamL", (128, 8))
    betL = din("betL", (128, 8))
    cwL = din("cwL", (128, 16, 4))
    cbL = din("cbL", (128, 16))
    bdtL = din("bdtL", (128, 16))
    delta_o = dout("delta_o", (2048, TC))
    u_o = dout("u_o", (2048, TC))
    xs_o = dout("xs_o", (2048, TC))
    z_o = dout("z_o", (2048, TC))
    B_o = dout("B_o", (16, TC))
    C_o = dout("C_o", (16, TC))

    ar = Arena(sp, 42048)
    x_sb = ar.f32(0, (8, NT))
    xn_bf = ar.bf16(16416, (8, NT))
    rows_off = 24624
    xs_bf = ar.bf16(24624, (16, TC))
    dtl_off = 41008

    ones_col = sp.sbuf("onc", [128, 1], F32)
    one_col = sp.sbuf("one", [128, 1], F32)
    ones_r128 = sp.sbuf("o128", [1, 128], F32)
    gam = sp.sbuf("gam", [128, 8], F32)
    bet = sp.sbuf("bet", [128, 8], F32)
    cw = sp.sbuf("cw", [128, 16, 4], F32)
    cb = sp.sbuf("cb", [128, 16], F32)
    bdt = sp.sbuf("bdt", [128, 16], F32)
    w_sb = sp.sbuf("wsb", [128, 8, 128], BF16)
    wx_sb = sp.sbuf("wxs", [128, 16, 96], BF16)
    wdt_sb = sp.sbuf("wds", [64, 2048], BF16)
    xp_g = sp.sbuf("xpg", [128, NT], BF16)
    accv = sp.sbuf("acv", [128, TC], F32)
    xs_g = sp.sbuf("xsg", [128, TC], F32)
    brow = sp.sbuf("brw", [16, 512], F32)
    crow = sp.sbuf("crw", [16, 512], F32)
    t1b = sp.sbuf("t1b", [128, 512], F32)
    t2b = sp.sbuf("t2b", [128, 512], F32)
    t3b = sp.sbuf("t3b", [128, 512], F32)

    psA = sp.psum("psa", [128, 512])
    psDt = sp.psum("pdt", [64, 512])
    psB = sp.psum("psb", [16, 512])
    psC = sp.psum("psc", [16, 512])
    psB0 = sp.psum("pb0", [128, 512])
    psB1 = sp.psum("pb1", [128, 512])

    dtlow = ar.t[0:64, dtl_off:dtl_off + 1024].bitcast(BF16)

    sp.memset(ones_col[:], 1.0, engine="gpsimd")
    sp.memset(one_col[:], 1.0, engine="gpsimd")
    sp.memset(ones_r128[:], 1.0, engine="gpsimd")
    if "xh_load" in io:
        io["xh_load"](sp, x_sb)
    else:
        sp.dmas([(x_sb, xhT.rearrange("(a p) t -> p a t", p=128))])
    sp.dmas([
             (gam[:], gamL), (bet[:], betL), (cw[:], cwL), (cb[:], cbL), (bdt[:], bdtL),
             (wx_sb[:], wxT.rearrange("(g p) r -> p g r", p=128)),
             (wdt_sb[:], wdtT)])

    mrow = ar.row(rows_off, NT)
    srow = ar.row(rows_off + NT, NT)
    sinv = ar.row(rows_off + 2 * NT, NT)
    msr = ar.row(rows_off + 3 * NT, NT)
    tch = chunks(NT, 512)
    def ln_stats():
        for (t0, tn) in tch:
            tsl = slice(t0, t0 + tn)
            def fmean(eng, _tsl=tsl, _tn=tn):
                last = None
                for a in range(8):
                    last = eng.matmul(psB[0:1, :_tn], ones_col[:], x_sb[:, a, _tsl],
                                      start=(a == 0), stop=(a == 7))
                return last
            sp.mms(fmean)
            sp.act(mrow[:, tsl], psB[0:1, :tn], AF.Copy, scale=1.0 / 1024)
            for a in range(8):
                sp.tt(t1b[:, :tn], x_sb[:, a, tsl], x_sb[:, a, tsl], ALU.mult)
                def fsq(eng, _a=a, _tn=tn):
                    return eng.matmul(psC[0:1, :_tn], ones_col[:], t1b[:, :_tn],
                                      start=(_a == 0), stop=(_a == 7))
                sp.mms(fsq)
            sp.act(srow[:, tsl], psC[0:1, :tn], AF.Copy, scale=1.0 / 1024)
        sp.tt(msr[:], mrow[:], mrow[:], ALU.mult)
        sp.tt(srow[:], srow[:], msr[:], ALU.subtract)
        sp.ts(srow[:], srow[:], 1e-5, ALU.add)
        sp.act(srow[:], srow[:], AF.Sqrt)
        sp.recip(sinv[:], srow[:])
        sp.tt(msr[:], mrow[:], sinv[:], ALU.mult)
    ln_stats()
    for (t0, tn) in tch:
        tsl = slice(t0, t0 + tn)
        def fb(eng, _tsl=tsl, _tn=tn):
            eng.matmul(psB0[:, :_tn], ones_r128[:], sinv[:, _tsl], start=True, stop=True)
            return eng.matmul(psB1[:, :_tn], ones_r128[:], msr[:, _tsl], start=True, stop=True)
        sp.mms(fb)
        for a in range(8):
            sp.tt(t1b[:, :tn], x_sb[:, a, tsl], psB0[:, :tn], ALU.mult)
            sp.tt(x_sb[:, a, tsl], t1b[:, :tn], psB1[:, :tn], ALU.subtract)
    ln_stats()
    for (t0, tn) in tch:
        tsl = slice(t0, t0 + tn)
        def fb(eng, _tsl=tsl, _tn=tn):
            eng.matmul(psB0[:, :_tn], ones_r128[:], sinv[:, _tsl], start=True, stop=True)
            return eng.matmul(psB1[:, :_tn], ones_r128[:], msr[:, _tsl], start=True, stop=True)
        sp.mms(fb)
        for a in range(8):
            sp.tt(t1b[:, :tn], x_sb[:, a, tsl], psB0[:, :tn], ALU.mult)
            sp.tt(t2b[:, :tn], t1b[:, :tn], psB1[:, :tn], ALU.subtract)
            def fts(eng, _a=a, _tsl=tsl, _tn=tn):
                return eng.tensor_scalar(out=xn_bf[:, _a, _tsl], in0=t2b[:, :_tn],
                                         scalar1=gam[:, _a:_a + 1], scalar2=bet[:, _a:_a + 1],
                                         op0=ALU.mult, op1=ALU.add)
            sp.op("vector", fts)

    for g in range(16):
        sp.dma(w_sb[:], winT[:, g * 128:(g + 1) * 128].rearrange("(a p) f -> p a f", p=128))
        for (t0, tn) in tch:
            tsl = slice(t0, t0 + tn)
            def fxp(eng, _tsl=tsl, _tn=tn):
                last = None
                for a in range(8):
                    last = eng.matmul(psA[:, :_tn], w_sb[:, a, :], xn_bf[:, a, _tsl],
                                      start=(a == 0), stop=(a == 7))
                return last
            sp.mms(fxp)
            sp.copy(xp_g[:, tsl], psA[:, :tn], engine="scalar")
        def fc0(eng, _g=g):
            return eng.tensor_scalar(out=accv[:], in0=xp_g[:, 0:TC], scalar1=cw[:, _g, 0:1],
                                     scalar2=None, op0=ALU.mult)
        sp.op("vector", fc0)
        for k in range(1, 4):
            def fck(eng, _g=g, _k=k):
                return eng.scalar_tensor_tensor(out=accv[:], in0=xp_g[:, _k:TC + _k],
                                                scalar=cw[:, _g, _k:_k + 1], in1=accv[:],
                                                op0=ALU.mult, op1=ALU.add)
            sp.op("vector", fck)
        def fsilu(eng, _g=g):
            return eng.activation(out=xs_g[:], in_=accv[:], func=AF.Silu, bias=cb[:, _g:_g + 1])
        sp.op("scalar", fsilu)
        sp.dma(xs_o[g * 128:(g + 1) * 128, :], xs_g[:])
        sp.copy(xs_bf[:, g, :], xs_g[:])
    for g in range(16):
        sp.dma(w_sb[:], winT[:, 2048 + g * 128:2048 + (g + 1) * 128].rearrange("(a p) f -> p a f", p=128))
        for tc in range(4):
            tsl = slice(3 + tc * 512, 3 + tc * 512 + 512)
            def fz(eng, _tsl=tsl):
                last = None
                for a in range(8):
                    last = eng.matmul(psA[:], w_sb[:, a, :], xn_bf[:, a, _tsl],
                                      start=(a == 0), stop=(a == 7))
                return last
            sp.mms(fz)
            sp.copy(xs_g[:, tc * 512:(tc + 1) * 512], psA[:], engine="scalar")
        sp.dma(z_o[g * 128:(g + 1) * 128, :], xs_g[:])

    for tc in range(4):
        tsl = slice(tc * 512, tc * 512 + 512)
        def fdt(eng, _tsl=tsl):
            last = None
            for g in range(16):
                last = eng.matmul(psDt[:, :], wx_sb[:, g, 0:64], xs_bf[:, g, _tsl],
                                  start=(g == 0), stop=(g == 15))
            return last
        sp.mms(fdt)
        sp.copy(dtlow[:, tsl], psDt[:, :], engine="scalar")
        def fB(eng, _tsl=tsl):
            last = None
            for g in range(16):
                last = eng.matmul(psB[:, :], wx_sb[:, g, 64:80], xs_bf[:, g, _tsl],
                                  start=(g == 0), stop=(g == 15))
            return last
        sp.mms(fB)
        sp.copy(brow[:], psB[:, :], engine="scalar")
        sp.dma(B_o[:, tsl], brow[:])
        def fC(eng, _tsl=tsl):
            last = None
            for g in range(16):
                last = eng.matmul(psC[:, :], wx_sb[:, g, 80:96], xs_bf[:, g, _tsl],
                                  start=(g == 0), stop=(g == 15))
            return last
        sp.mms(fC)
        sp.copy(crow[:], psC[:, :], engine="scalar")
        sp.dma(C_o[:, tsl], crow[:])

    for g in range(16):
        sp.dma(xs_g[:], xs_o[g * 128:(g + 1) * 128, :])
        for tc in range(4):
            tsl = slice(tc * 512, tc * 512 + 512)
            def fdl(eng, _g=g, _tsl=tsl):
                return eng.matmul(psA[:], wdt_sb[:, _g * 128:(_g + 1) * 128],
                                  dtlow[:, _tsl], start=True, stop=True)
            sp.mms(fdl)
            sp.ts(t1b[:], psA[:], bdt[:, g:g + 1], ALU.add)
            sp.op("scalar", lambda e: e.activation(out=t2b[:], in_=t1b[:], func=AF.Abs))
            sp.op("scalar", lambda e: e.activation(out=t2b[:], in_=t2b[:], func=AF.Exp, scale=-1.0))
            def fln(eng):
                return eng.activation(out=t2b[:], in_=t2b[:], func=AF.Ln, bias=one_col[:])
            sp.op("scalar", fln)
            sp.ts(t3b[:], t1b[:], 0.0, ALU.max)
            sp.tt(t3b[:], t3b[:], t2b[:], ALU.add)
            sp.dma(delta_o[g * 128:(g + 1) * 128, tsl], t3b[:])
            sp.tt(t3b[:], t3b[:], xs_g[:, tsl], ALU.mult)
            sp.dma(u_o[g * 128:(g + 1) * 128, tsl], t3b[:])
    return sp


def run_l1a(X, inp, Wp):
    sp = build_l1a()
    cw = inp["L1_attn_conv_w"].reshape(2048, 4)
    shared = {
        "winT": np.ascontiguousarray(Wp["L1_attn_w_in_tb"]),
        "wxT": np.ascontiguousarray(Wp["L1_attn_w_x_tb"]),
        "wdtT": np.ascontiguousarray(Wp["L1_attn_w_dt_tb"]),
        "gamL": np.ascontiguousarray(inp["L1_attn_gamma"].reshape(8, 128).T),
        "betL": np.ascontiguousarray(inp["L1_attn_beta"].reshape(8, 128).T),
        "cwL": np.ascontiguousarray(cw.reshape(16, 128, 4).transpose(1, 0, 2)),
        "cbL": np.ascontiguousarray(inp["L1_attn_conv_b"].reshape(16, 128).T),
        "bdtL": np.ascontiguousarray(inp["L1_attn_b_dt"].reshape(16, 128).T),
    }
    outs = []
    in_maps = []
    for c in range(NC):
        b, q = c // 4, c % 4
        xb = X[b * 8192:(b + 1) * 8192]
        t0 = 2048 * q
        xh = np.zeros((2051, 1024), dtype=np.float32)
        lo = max(0, t0 - 3)
        xh[3 - (t0 - lo):3] = xb[lo:t0]
        xh[3:] = xb[t0:t0 + 2048]
        m = dict(shared)
        m["xhT"] = np.ascontiguousarray(xh.T)
        in_maps.append(m)
    res = sp.run(in_maps, core_ids=range(NC))
    return [dict(r) for r in res.results]


def build_scan(nstate, nchunk=4096, sp=None, io=None):
    if sp is None:
        sp = SP()
    io = io or {}
    def din(n, s, d=F32):
        return io[n] if n in io else sp.din(n, s, d)
    def dout(n, s, d=F32):
        return io[n] if n in io else sp.dout(n, s, d)
    dS = None if "load_du" in (io or {}) else din("dS", (256, T))
    uS = None if "load_du" in (io or {}) else din("uS", (256, T))
    Bin = None if "load_bc" in (io or {}) else din("Bin", (nstate, T))
    Cin = None if "load_bc" in (io or {}) else din("Cin", (nstate, T))
    AlogS = din("AlogS", (256, nstate))
    yS = dout("yS", (256, T))

    A_sb = sp.sbuf("asb", [128, 2, nstate], F32)
    carry = sp.sbuf("car", [128, nstate], F32)
    dblk = sp.sbuf("dbk", [128, nchunk], F32)
    ublk = sp.sbuf("ubk", [128, nchunk], F32)
    accb = sp.sbuf("acb", [128, nchunk], F32)
    dA = sp.sbuf("dab", [128, nchunk], F32)
    ut = sp.sbuf("utb", [128, nchunk], F32)
    hb = sp.sbuf("hbb", [128, nchunk], F32)
    Brep = sp.sbuf("brp", [128, nchunk], F32)
    Crep = sp.sbuf("crp", [128, nchunk], F32)

    sp.dma(A_sb[:], AlogS.rearrange("(a p) n -> p a n", p=128))
    sp.op("scalar", lambda e: e.activation(out=A_sb[:].rearrange("p a n -> p (a n)"),
                                           in_=A_sb[:].rearrange("p a n -> p (a n)"),
                                           func=AF.Exp, scale=1.0))
    sp.ts(A_sb[:].rearrange("p a n -> p (a n)"), A_sb[:].rearrange("p a n -> p (a n)"),
          -1.0, ALU.mult)

    NCH = L // nchunk
    for dt in range(2):
        for b in range(B):
            sp.memset(carry[:], 0.0, engine="gpsimd")
            for ch in range(NCH):
                c0 = b * L + ch * nchunk
                csl = slice(c0, c0 + nchunk)
                if "load_du" in io:
                    io["load_du"](sp, dt, b, ch, dblk[:], ublk[:])
                else:
                    sp.dmas([(dblk[:], dS[dt * 128:(dt + 1) * 128, csl]),
                             (ublk[:], uS[dt * 128:(dt + 1) * 128, csl])])
                first = True
                for n in range(nstate):
                    if "load_bc" in io:
                        io["load_bc"](sp, n, b, ch, Brep[:], Crep[:])
                    else:
                        sp.dmas([(Brep[:], Bin[n:n + 1, csl].to_broadcast([128, nchunk])),
                                 (Crep[:], Cin[n:n + 1, csl].to_broadcast([128, nchunk]))])
                    def fdA(eng, _dt=dt, _n=n):
                        return eng.activation(out=dA[:], in_=dblk[:], func=AF.Exp,
                                              scale=A_sb[:, _dt, _n:_n + 1])
                    sp.op("scalar", fdA)
                    sp.tt(ut[:], ublk[:], Brep[:], ALU.mult)
                    sp.scan(hb[:], dA[:], ut[:], carry[:, n:n + 1])
                    sp.copy(carry[:, n:n + 1], hb[:, nchunk - 1:nchunk])
                    if first:
                        sp.tt(accb[:], hb[:], Crep[:], ALU.mult)
                        first = False
                    else:
                        sp.tt(hb[:], hb[:], Crep[:], ALU.mult)
                        sp.tt(accb[:], accb[:], hb[:], ALU.add)
                sp.dma(yS[dt * 128:(dt + 1) * 128, csl], accb[:])
    return sp


def run_scan(parts_list, keyd, keyu, keyB, keyC, Alog, nstate):
    sp = build_scan(nstate)
    dall = np.concatenate([np.asarray(p[keyd]) for p in parts_list], axis=1)
    uall = np.concatenate([np.asarray(p[keyu]) for p in parts_list], axis=1)
    Ball = np.concatenate([np.asarray(p[keyB]) for p in parts_list], axis=1)
    Call = np.concatenate([np.asarray(p[keyC]) for p in parts_list], axis=1)
    in_maps = []
    for c in range(NC):
        m = {
            "dS": np.ascontiguousarray(dall[256 * c:256 * (c + 1)]),
            "uS": np.ascontiguousarray(uall[256 * c:256 * (c + 1)]),
            "Bin": Ball, "Cin": Call,
            "AlogS": np.ascontiguousarray(Alog[256 * c:256 * (c + 1)]),
        }
        in_maps.append(m)
    res = sp.run(in_maps, core_ids=range(NC))
    return np.concatenate([res.results[c]["yS"] for c in range(NC)], axis=0)


def build_l1c(rms=False, sp=None, io=None):
    if sp is None:
        sp = SP()
    io = io or {}
    def din(n, s, d=F32):
        return io[n] if n in io else sp.din(n, s, d)
    def dout(n, s, d=F32):
        return io[n] if n in io else sp.dout(n, s, d)
    xT = din("xT", (1024, TC))
    ysT = None if "ys_load" in (io or {}) else din("ysT", (2048, TC))
    xsT = din("xsT", (2048, TC))
    zT = din("zT", (2048, TC))
    DpL = din("DpL", (128, 16))
    woTb = din("woTb", (2048, 1024), BF16)
    yT = dout("yT", (1024, TC))

    ybf = sp.sbuf("ybf", [128, 16, TC], BF16)
    x_sb = sp.sbuf("xsb", [128, 8, TC], F32)
    g1 = sp.sbuf("g1", [128, TC], F32)
    g2 = sp.sbuf("g2", [128, TC], F32)
    g3 = sp.sbuf("g3", [128, TC], F32)
    dp = sp.sbuf("dp", [128, 16], F32)
    wo = sp.sbuf("wo", [128, 16, 128], BF16)
    psO = sp.psum("pso", [128, 512])
    psR = sp.psum("psr", [1, 512])
    psB0 = sp.psum("pb0", [128, 512])
    psB1 = sp.psum("pb1", [128, 512])
    ones_col = sp.sbuf("onc", [128, 1], F32)
    ones_r128 = sp.sbuf("o128", [1, 128], F32)
    mrow = sp.sbuf("mr", [1, TC], F32)
    srow = sp.sbuf("sr", [1, TC], F32)
    sinv = sp.sbuf("si", [1, TC], F32)
    msr = sp.sbuf("ms", [1, TC], F32)

    sp.memset(ones_col[:], 1.0, engine="gpsimd")
    sp.memset(ones_r128[:], 1.0, engine="gpsimd")
    sp.dma(dp[:], DpL)
    sp.dma(x_sb[:], xT.rearrange("(a p) t -> p a t", p=128))
    for tc in range(4):
        tsl = slice(tc * 512, tc * 512 + 512)
        if not rms:
            def fmean(eng, _tsl=tsl):
                last = None
                for a in range(8):
                    last = eng.matmul(psR[0:1, :], ones_col[:], x_sb[:, a, _tsl],
                                      start=(a == 0), stop=(a == 7))
                return last
            sp.mms(fmean)
            sp.act(mrow[0:1, tsl], psR[0:1, :], AF.Copy, scale=1.0 / 1024)
        for a in range(8):
            sp.tt(g1[:, :512], x_sb[:, a, tsl], x_sb[:, a, tsl], ALU.mult)
            def fsq(eng, _a=a):
                return eng.matmul(psB1[0:1, :], ones_col[:], g1[:, :512],
                                  start=(_a == 0), stop=(_a == 7))
            sp.mms(fsq)
        sp.act(srow[0:1, tsl], psB1[0:1, :], AF.Copy, scale=1.0 / 1024)
    if not rms:
        sp.tt(msr[:], mrow[:], mrow[:], ALU.mult)
        sp.tt(srow[:], srow[:], msr[:], ALU.subtract)
        sp.ts(srow[:], srow[:], 1e-5, ALU.add)
    else:
        sp.ts(srow[:], srow[:], 1e-6, ALU.add)
    sp.act(srow[:], srow[:], AF.Sqrt)
    sp.recip(sinv[:], srow[:])
    if not rms:
        sp.tt(msr[:], mrow[:], sinv[:], ALU.mult)
    for tc in range(4):
        tsl = slice(tc * 512, tc * 512 + 512)
        def fb(eng, _tsl=tsl):
            eng.matmul(psB0[:], ones_r128[:], sinv[0:1, _tsl], start=True, stop=True)
            if not rms:
                return eng.matmul(psB1[:], ones_r128[:], msr[0:1, _tsl], start=True, stop=True)
            return None
        if rms:
            def fb(eng, _tsl=tsl):
                return eng.matmul(psB0[:], ones_r128[:], sinv[0:1, _tsl], start=True, stop=True)
        sp.mms(fb)
        for a in range(8):
            sp.tt(g1[:, :512], x_sb[:, a, tsl], psB0[:], ALU.mult)
            if not rms:
                sp.tt(g1[:, :512], g1[:, :512], psB1[:], ALU.subtract)
            sp.tt(x_sb[:, a, tsl], x_sb[:, a, tsl], g1[:, :512], ALU.add)
    for g in range(16):
        if "ys_load" in io:
            io["ys_load"](sp, g, g1[:])
            sp.dmas([(g2[:], xsT[g * 128:(g + 1) * 128, :]),
                     (g3[:], zT[g * 128:(g + 1) * 128, :])])
        else:
            sp.dmas([(g1[:], ysT[g * 128:(g + 1) * 128, :]),
                     (g2[:], xsT[g * 128:(g + 1) * 128, :]),
                     (g3[:], zT[g * 128:(g + 1) * 128, :])])
        def fd(eng, _g=g):
            return eng.scalar_tensor_tensor(out=g1[:], in0=g2[:], scalar=dp[:, _g:_g + 1],
                                            in1=g1[:], op0=ALU.mult, op1=ALU.add)
        sp.op("vector", fd)
        sp.op("scalar", lambda e: e.activation(out=g3[:], in_=g3[:], func=AF.Silu))
        sp.tt(ybf[:, g, :], g1[:], g3[:], ALU.mult)
    for fo in range(8):
        sp.dma(wo[:], woTb[:, fo * 128:(fo + 1) * 128].rearrange("(g p) f -> p g f", p=128))
        for tc in range(4):
            tsl = slice(tc * 512, tc * 512 + 512)
            def fw(eng, _tsl=tsl):
                last = None
                for g in range(16):
                    last = eng.matmul(psO[:], wo[:, g, :], ybf[:, g, _tsl],
                                      start=(g == 0), stop=(g == 15))
                return last
            sp.mms(fw)
            sp.tt(x_sb[:, fo, tsl], x_sb[:, fo, tsl], psO[:], ALU.add)
        sp.dma(yT[fo * 128:(fo + 1) * 128, :], x_sb[:, fo, :])
    return sp


def run_l1bc(X, inp, Wp, parts_list):
    y = run_scan(parts_list, "delta_o", "u_o", "B_o", "C_o", inp["L1_attn_A_log"], 16)
    sp = build_l1c()
    shared = {
        "DpL": np.ascontiguousarray(inp["L1_attn_Dp"].reshape(16, 128).T),
        "woTb": np.ascontiguousarray(Wp["L1_attn_w_out_tb"]),
    }
    in_maps = []
    for c in range(NC):
        m = dict(shared)
        m["xT"] = np.ascontiguousarray(X[c * TC:(c + 1) * TC].T)
        m["ysT"] = np.ascontiguousarray(y[:, c * TC:(c + 1) * TC])
        m["xsT"] = np.asarray(parts_list[c]["xs_o"])
        m["zT"] = np.asarray(parts_list[c]["z_o"])
        in_maps.append(m)
    res = sp.run(in_maps, core_ids=range(NC))
    out = np.empty_like(X)
    for c in range(NC):
        out[c * TC:(c + 1) * TC] = res.results[c]["yT"].T
    return out


def run_l1(X, inp, Wp):
    parts_list = run_l1a(X, inp, Wp)
    return run_l1bc(X, inp, Wp, parts_list)


def build_l2a(sp=None, io=None):
    NT = 2051
    if sp is None:
        sp = SP()
    io = io or {}
    def din(n, s, d=F32):
        return io[n] if n in io else sp.din(n, s, d)
    def dout(n, s, d=F32):
        return io[n] if n in io else sp.dout(n, s, d)
    xhT = None if "xh_load" in (io or {}) else din("xhT", (1024, NT))
    winT = din("winT", (1024, 4096), BF16)
    bwT = din("bwT", (2048, 64), BF16)
    cwT = din("cwT", (2048, 64), BF16)
    dtwT = din("dtwT", (2048, 2048), BF16)
    nwL = din("nwL", (128, 8))
    cwL = din("cwL", (128, 16, 4))
    cbL = din("cbL", (128, 16))
    bdtL = din("bdtL", (128, 16))
    dt_o = dout("dt_o", (2048, TC))
    u_o = dout("u_o", (2048, TC))
    xc_o = dout("xc_o", (2048, TC))
    z_o = dout("z_o", (2048, TC))
    B_o = dout("B_o", (64, TC))
    C_o = dout("C_o", (64, TC))

    ar = Arena(sp, 42048)
    x_sb = ar.f32(0, (8, NT))
    xn_bf = ar.bf16(16416, (8, NT))
    rows_off = 24624
    xs_bf = ar.bf16(24624, (16, TC))

    ones_col = sp.sbuf("onc", [128, 1], F32)
    one_col = sp.sbuf("one", [128, 1], F32)
    ones_r128 = sp.sbuf("o128", [1, 128], F32)
    nw = sp.sbuf("nw", [128, 8], F32)
    cw = sp.sbuf("cw", [128, 16, 4], F32)
    cb = sp.sbuf("cb", [128, 16], F32)
    bdt = sp.sbuf("bdt", [128, 16], F32)
    w_sb = sp.sbuf("wsb", [128, 8, 128], BF16)
    bwt_sb = sp.sbuf("bws", [128, 16, 64], BF16)
    cwt_sb = sp.sbuf("cws", [128, 16, 64], BF16)
    dtw_g = sp.sbuf("dtg", [128, 16, 128], BF16)
    xp_g = sp.sbuf("xpg", [128, NT], BF16)
    accv = sp.sbuf("acv", [128, TC], F32)
    xs_g = sp.sbuf("xsg", [128, TC], F32)
    brow = sp.sbuf("brw", [64, 512], F32)
    t1b = sp.sbuf("t1b", [128, 512], F32)
    t2b = sp.sbuf("t2b", [128, 512], F32)
    t3b = sp.sbuf("t3b", [128, 512], F32)

    psA = sp.psum("psa", [128, 512])
    psB = sp.psum("psb", [64, 512])
    psC = sp.psum("psc", [64, 512])
    psB0 = sp.psum("pb0", [128, 512])
    psB1 = sp.psum("pb1", [128, 512])

    sp.memset(ones_col[:], 1.0, engine="gpsimd")
    sp.memset(one_col[:], 1.0, engine="gpsimd")
    sp.memset(ones_r128[:], 1.0, engine="gpsimd")
    if "xh_load" in io:
        io["xh_load"](sp, x_sb)
    else:
        sp.dmas([(x_sb, xhT.rearrange("(a p) t -> p a t", p=128))])
    sp.dmas([
             (nw[:], nwL), (cw[:], cwL), (cb[:], cbL), (bdt[:], bdtL),
             (bwt_sb[:], bwT.rearrange("(g p) n -> p g n", p=128)),
             (cwt_sb[:], cwT.rearrange("(g p) n -> p g n", p=128))])

    mrow = ar.row(rows_off, NT)
    srow = ar.row(rows_off + NT, NT)
    sinv = ar.row(rows_off + 2 * NT, NT)
    msr = ar.row(rows_off + 3 * NT, NT)
    tch = chunks(NT, 512)

    def ln_stats(with_mean=True, eps=1e-5):
        for (t0, tn) in tch:
            tsl = slice(t0, t0 + tn)
            if with_mean:
                def fmean(eng, _tsl=tsl, _tn=tn):
                    last = None
                    for a in range(8):
                        last = eng.matmul(psB[0:1, :_tn], ones_col[:], x_sb[:, a, _tsl],
                                          start=(a == 0), stop=(a == 7))
                    return last
                sp.mms(fmean)
                sp.act(mrow[:, tsl], psB[0:1, :tn], AF.Copy, scale=1.0 / 1024)
            for a in range(8):
                sp.tt(t1b[:, :tn], x_sb[:, a, tsl], x_sb[:, a, tsl], ALU.mult)
                def fsq(eng, _a=a, _tn=tn):
                    return eng.matmul(psC[0:1, :_tn], ones_col[:], t1b[:, :_tn],
                                      start=(_a == 0), stop=(_a == 7))
                sp.mms(fsq)
            sp.act(srow[:, tsl], psC[0:1, :tn], AF.Copy, scale=1.0 / 1024)
        if with_mean:
            sp.tt(msr[:], mrow[:], mrow[:], ALU.mult)
            sp.tt(srow[:], srow[:], msr[:], ALU.subtract)
        sp.ts(srow[:], srow[:], eps, ALU.add)
        sp.act(srow[:], srow[:], AF.Sqrt)
        sp.recip(sinv[:], srow[:])
        if with_mean:
            sp.tt(msr[:], mrow[:], sinv[:], ALU.mult)

    ln_stats(True, 1e-5)
    for (t0, tn) in tch:
        tsl = slice(t0, t0 + tn)
        def fb(eng, _tsl=tsl, _tn=tn):
            eng.matmul(psB0[:, :_tn], ones_r128[:], sinv[:, _tsl], start=True, stop=True)
            return eng.matmul(psB1[:, :_tn], ones_r128[:], msr[:, _tsl], start=True, stop=True)
        sp.mms(fb)
        for a in range(8):
            sp.tt(t1b[:, :tn], x_sb[:, a, tsl], psB0[:, :tn], ALU.mult)
            sp.tt(x_sb[:, a, tsl], t1b[:, :tn], psB1[:, :tn], ALU.subtract)
    ln_stats(False, 1e-6)
    for (t0, tn) in tch:
        tsl = slice(t0, t0 + tn)
        def fb2(eng, _tsl=tsl, _tn=tn):
            return eng.matmul(psB0[:, :_tn], ones_r128[:], sinv[:, _tsl], start=True, stop=True)
        sp.mms(fb2)
        for a in range(8):
            def fts(eng, _a=a, _tsl=tsl, _tn=tn):
                return eng.scalar_tensor_tensor(out=xn_bf[:, _a, _tsl], in0=x_sb[:, _a, _tsl],
                                                scalar=nw[:, _a:_a + 1], in1=psB0[:, :_tn],
                                                op0=ALU.mult, op1=ALU.mult)
            sp.op("vector", fts)

    for g in range(16):
        sp.dma(w_sb[:], winT[:, g * 128:(g + 1) * 128].rearrange("(a p) f -> p a f", p=128))
        for (t0, tn) in tch:
            tsl = slice(t0, t0 + tn)
            def fxp(eng, _tsl=tsl, _tn=tn):
                last = None
                for a in range(8):
                    last = eng.matmul(psA[:, :_tn], w_sb[:, a, :], xn_bf[:, a, _tsl],
                                      start=(a == 0), stop=(a == 7))
                return last
            sp.mms(fxp)
            sp.copy(xp_g[:, tsl], psA[:, :tn], engine="scalar")
        def fc0(eng, _g=g):
            return eng.tensor_scalar(out=accv[:], in0=xp_g[:, 0:TC], scalar1=cw[:, _g, 0:1],
                                     scalar2=None, op0=ALU.mult)
        sp.op("vector", fc0)
        for k in range(1, 4):
            def fck(eng, _g=g, _k=k):
                return eng.scalar_tensor_tensor(out=accv[:], in0=xp_g[:, _k:TC + _k],
                                                scalar=cw[:, _g, _k:_k + 1], in1=accv[:],
                                                op0=ALU.mult, op1=ALU.add)
            sp.op("vector", fck)
        def fsilu(eng, _g=g):
            return eng.activation(out=xs_g[:], in_=accv[:], func=AF.Silu, bias=cb[:, _g:_g + 1])
        sp.op("scalar", fsilu)
        sp.dma(xc_o[g * 128:(g + 1) * 128, :], xs_g[:])
        sp.copy(xs_bf[:, g, :], xs_g[:])
    for g in range(16):
        sp.dma(w_sb[:], winT[:, 2048 + g * 128:2048 + (g + 1) * 128].rearrange("(a p) f -> p a f", p=128))
        for tc in range(4):
            tsl = slice(3 + tc * 512, 3 + tc * 512 + 512)
            def fz(eng, _tsl=tsl):
                last = None
                for a in range(8):
                    last = eng.matmul(psA[:], w_sb[:, a, :], xn_bf[:, a, _tsl],
                                      start=(a == 0), stop=(a == 7))
                return last
            sp.mms(fz)
            sp.copy(xs_g[:, tc * 512:(tc + 1) * 512], psA[:], engine="scalar")
        sp.dma(z_o[g * 128:(g + 1) * 128, :], xs_g[:])

    for tc in range(4):
        tsl = slice(tc * 512, tc * 512 + 512)
        def fB(eng, _tsl=tsl):
            last = None
            for g in range(16):
                last = eng.matmul(psB[:, :], bwt_sb[:, g, :], xs_bf[:, g, _tsl],
                                  start=(g == 0), stop=(g == 15))
            return last
        sp.mms(fB)
        sp.copy(brow[:], psB[:, :], engine="scalar")
        sp.dma(B_o[:, tsl], brow[:])
        def fC(eng, _tsl=tsl):
            last = None
            for g in range(16):
                last = eng.matmul(psC[:, :], cwt_sb[:, g, :], xs_bf[:, g, _tsl],
                                  start=(g == 0), stop=(g == 15))
            return last
        sp.mms(fC)
        sp.copy(brow[:], psC[:, :], engine="scalar")
        sp.dma(C_o[:, tsl], brow[:])

    for g in range(16):
        sp.dma(dtw_g[:], dtwT[:, g * 128:(g + 1) * 128].rearrange("(f p) c -> p f c", p=128))
        sp.dma(xs_g[:], xc_o[g * 128:(g + 1) * 128, :])
        for tc in range(4):
            tsl = slice(tc * 512, tc * 512 + 512)
            def fdl(eng, _tsl=tsl):
                last = None
                for f in range(16):
                    last = eng.matmul(psA[:], dtw_g[:, f, :], xs_bf[:, f, _tsl],
                                      start=(f == 0), stop=(f == 15))
                return last
            sp.mms(fdl)
            sp.ts(t1b[:], psA[:], bdt[:, g:g + 1], ALU.add)
            sp.op("scalar", lambda e: e.activation(out=t2b[:], in_=t1b[:], func=AF.Abs))
            sp.op("scalar", lambda e: e.activation(out=t2b[:], in_=t2b[:], func=AF.Exp, scale=-1.0))
            def fln(eng):
                return eng.activation(out=t2b[:], in_=t2b[:], func=AF.Ln, bias=one_col[:])
            sp.op("scalar", fln)
            sp.ts(t3b[:], t1b[:], 0.0, ALU.max)
            sp.tt(t3b[:], t3b[:], t2b[:], ALU.add)
            sp.ts(t3b[:], t3b[:], 1e-4, ALU.add)
            sp.dma(dt_o[g * 128:(g + 1) * 128, tsl], t3b[:])
            sp.tt(t3b[:], t3b[:], xs_g[:, tsl], ALU.mult)
            sp.dma(u_o[g * 128:(g + 1) * 128, tsl], t3b[:])
    return sp


def run_l2(X, inp, Wp):
    sp = build_l2a()
    cw = inp["L2_attn_conv_w"].reshape(2048, 4)
    shared = {
        "winT": np.ascontiguousarray(Wp["L2_attn_in_proj_w_tb"]),
        "bwT": np.ascontiguousarray(Wp["L2_attn_B_w_tb"]),
        "cwT": np.ascontiguousarray(Wp["L2_attn_C_w_tb"]),
        "dtwT": np.ascontiguousarray(Wp["L2_attn_dt_w_tb"]),
        "nwL": np.ascontiguousarray(inp["L2_attn_norm_w"].reshape(8, 128).T),
        "cwL": np.ascontiguousarray(cw.reshape(16, 128, 4).transpose(1, 0, 2)),
        "cbL": np.ascontiguousarray(inp["L2_attn_conv_b"].reshape(16, 128).T),
        "bdtL": np.ascontiguousarray(inp["L2_attn_dt_b"].reshape(16, 128).T),
    }
    in_maps = []
    for c in range(NC):
        b, q = c // 4, c % 4
        xb = X[b * 8192:(b + 1) * 8192]
        t0 = 2048 * q
        xh = np.zeros((2051, 1024), dtype=np.float32)
        lo = max(0, t0 - 3)
        xh[3 - (t0 - lo):3] = xb[lo:t0]
        xh[3:] = xb[t0:t0 + 2048]
        m = dict(shared)
        m["xhT"] = np.ascontiguousarray(xh.T)
        in_maps.append(m)
    res = sp.run(in_maps, core_ids=range(NC))
    parts_list = [dict(r) for r in res.results]
    y = run_scan(parts_list, "dt_o", "u_o", "B_o", "C_o", inp["L2_attn_A_log"], 64)
    spc = build_l1c()
    shared_c = {
        "DpL": np.ascontiguousarray(inp["L2_attn_D_param"].reshape(16, 128).T),
        "woTb": np.ascontiguousarray(Wp["L2_attn_out_proj_w_tb"]),
    }
    in_maps = []
    for c in range(NC):
        m = dict(shared_c)
        m["xT"] = np.ascontiguousarray(X[c * TC:(c + 1) * TC].T)
        m["ysT"] = np.ascontiguousarray(y[:, c * TC:(c + 1) * TC])
        m["xsT"] = np.asarray(parts_list[c]["xc_o"])
        m["zT"] = np.asarray(parts_list[c]["z_o"])
        in_maps.append(m)
    res = spc.run(in_maps, core_ids=range(NC))
    out = np.empty_like(X)
    for c in range(NC):
        out[c * TC:(c + 1) * TC] = res.results[c]["yT"].T
    return out


def build_mlp2(sp=None, io=None):
    if sp is None:
        sp = SP()
    io = io or {}
    def din(n, s, d=F32):
        return io[n] if n in io else sp.din(n, s, d)
    def dout(n, s, d=F32):
        return io[n] if n in io else sp.dout(n, s, d)
    xT = din("xT", (1024, TC))
    Wub = din("Wub", (1024, 4096), BF16)
    Wdb = din("Wdb", (4096, 1024), BF16)
    bupL = None if "bup_row" in (io or {}) else din("bupL", (128, 32))
    bdnL = None if "bup_row" in (io or {}) else din("bdnL", (128, 8))
    yT = dout("yT", (1024, TC))

    ar = Arena(sp, 40960)
    x_sb = ar.f32(0, (8, TC))
    xn_bf = ar.bf16(16384, (8, TC))
    rows_off = 24576
    h_bf = ar.bf16(0, (16, TC))
    acc = ar.f32(24576, (8, TC))

    ones_col = sp.sbuf("onc", [128, 1], F32)
    ones_r128 = sp.sbuf("o128", [1, 128], F32)
    bup = sp.sbuf("bup", [128, 32], F32)
    bdn = sp.sbuf("bdn", [128, 8], F32)
    brow_t = sp.sbuf("brt", [1, 4096], F32)
    ident1 = make_ident(sp, "id1", 1, F32)
    psT1 = sp.psum("pT1", [128, 1])
    w_sb = sp.sbuf("wsb", [128, 8, 128], BF16)
    wdn = sp.sbuf("wdn", [128, 16, 128], BF16)
    sq = sp.sbuf("sqq", [128, 512], F32)
    tmp_f = sp.sbuf("tmf", [128, 512], F32)
    xload = sp.sbuf("xld", [128, TC], F32)
    psA = sp.psum("psa", [128, 512])
    psO = sp.psum("pso", [128, 512])
    psB0 = sp.psum("pb0", [128, 512])
    psB1 = sp.psum("pb1", [128, 512])
    psR = sp.psum("psr", [1, 512])

    sp.memset(ones_col[:], 1.0, engine="gpsimd")
    sp.memset(ones_r128[:], 1.0, engine="gpsimd")
    if "bup_row" in io:
        sp.dmas([(x_sb, xT.rearrange("(a p) t -> p a t", p=128)),
                 (brow_t[:], io["bup_row"])])
        for j in range(32):
            sp.transpose(psT1[:, :], brow_t[0:1, j * 128:(j + 1) * 128], ident1[:])
            sp.copy(bup[:, j:j + 1], psT1[:, :], engine="scalar")
        sp.dma(brow_t[0:1, 0:1024], io["bdn_row"])
        for j in range(8):
            sp.transpose(psT1[:, :], brow_t[0:1, j * 128:(j + 1) * 128], ident1[:])
            sp.copy(bdn[:, j:j + 1], psT1[:, :], engine="scalar")
    else:
        sp.dmas([(x_sb, xT.rearrange("(a p) t -> p a t", p=128)),
                 (bup[:], bupL), (bdn[:], bdnL)])
    sp.ts(bdn[:], bdn[:], 0.5, ALU.mult)

    mrow = ar.row(rows_off, TC)
    srow = ar.row(rows_off + TC, TC)
    sinv = ar.row(rows_off + 2 * TC, TC)
    msr = ar.row(rows_off + 3 * TC, TC)
    for tc in range(4):
        tsl = slice(tc * 512, tc * 512 + 512)
        def fmean(eng, _tsl=tsl):
            last = None
            for a in range(8):
                last = eng.matmul(psR[0:1, :], ones_col[:], x_sb[:, a, _tsl],
                                  start=(a == 0), stop=(a == 7))
            return last
        sp.mms(fmean)
        sp.act(mrow[:, tsl], psR[0:1, :], AF.Copy, scale=1.0 / 1024)
        for a in range(8):
            sp.tt(sq[:], x_sb[:, a, tsl], x_sb[:, a, tsl], ALU.mult)
            def fsq(eng, _a=a):
                return eng.matmul(psR[0:1, :], ones_col[:], sq[:],
                                  start=(_a == 0), stop=(_a == 7))
            sp.mms(fsq)
        sp.act(srow[:, tsl], psR[0:1, :], AF.Copy, scale=1.0 / 1024)
    sp.tt(msr[:], mrow[:], mrow[:], ALU.mult)
    sp.tt(srow[:], srow[:], msr[:], ALU.subtract)
    sp.ts(srow[:], srow[:], 1e-5, ALU.add)
    sp.act(srow[:], srow[:], AF.Sqrt)
    sp.recip(sinv[:], srow[:])
    sp.tt(msr[:], mrow[:], sinv[:], ALU.mult)
    for tc in range(4):
        tsl = slice(tc * 512, tc * 512 + 512)
        def fb(eng, _tsl=tsl):
            eng.matmul(psB0[:], ones_r128[:], sinv[:, _tsl], start=True, stop=True)
            return eng.matmul(psB1[:], ones_r128[:], msr[:, _tsl], start=True, stop=True)
        sp.mms(fb)
        for a in range(8):
            sp.tt(tmp_f[:], x_sb[:, a, tsl], psB0[:], ALU.mult)
            sp.tt(xn_bf[:, a, tsl], tmp_f[:], psB1[:], ALU.subtract)

    for half in range(2):
        for hf in range(16):
            j = half * 16 + hf
            sp.dma(w_sb[:], Wub[:, j * 128:(j + 1) * 128].rearrange("(a p) f -> p a f", p=128))
            for tc in range(4):
                tsl = slice(tc * 512, tc * 512 + 512)
                def fa(eng, _tsl=tsl):
                    last = None
                    for a in range(8):
                        last = eng.matmul(psA[:], w_sb[:, a, :], xn_bf[:, a, _tsl],
                                          start=(a == 0), stop=(a == 7))
                    return last
                sp.mms(fa)
                def fgel(eng, _hf=hf, _j=j, _tsl=tsl):
                    return eng.activation(out=h_bf[:, _hf, _tsl], in_=psA[:], func=AF.Gelu,
                                          bias=bup[:, _j:_j + 1])
                sp.op("scalar", fgel)
        for fo in range(8):
            sp.dma(wdn[:], Wdb[half * 2048:(half + 1) * 2048, fo * 128:(fo + 1) * 128]
                   .rearrange("(g p) f -> p g f", p=128))
            for tc in range(4):
                tsl = slice(tc * 512, tc * 512 + 512)
                def fd(eng, _tsl=tsl):
                    last = None
                    for g in range(16):
                        last = eng.matmul(psO[:], wdn[:, g, :], h_bf[:, g, _tsl],
                                          start=(g == 0), stop=(g == 15))
                    return last
                sp.mms(fd)
                if half == 0:
                    sp.ts(acc[:, fo, tsl], psO[:], 0.5, ALU.mult)
                else:
                    def facc(eng, _fo=fo, _tsl=tsl):
                        return eng.scalar_tensor_tensor(out=acc[:, _fo, _tsl], in0=psO[:],
                                                        scalar=0.5, in1=acc[:, _fo, _tsl],
                                                        op0=ALU.mult, op1=ALU.add)
                    sp.op("vector", facc)
    for fo in range(8):
        sp.dma(xload[:], xT[fo * 128:(fo + 1) * 128, :])
        def ffin(eng, _fo=fo):
            return eng.scalar_tensor_tensor(out=xload[:], in0=acc[:, _fo, :],
                                            scalar=bdn[:, _fo:_fo + 1], in1=xload[:],
                                            op0=ALU.add, op1=ALU.add)
        sp.op("vector", ffin)
        sp.dma(yT[fo * 128:(fo + 1) * 128, :], xload[:])
    return sp


def run_mlp2(X, inp, Wp):
    sp = build_mlp2()
    shared = {
        "Wub": np.ascontiguousarray(Wp["W_exp_fb"]),
        "Wdb": np.ascontiguousarray(Wp["W_con_fb"]),
        "bupL": np.ascontiguousarray(np.asarray(Wp["bexp_f"]).reshape(32, 128).T),
        "bdnL": np.ascontiguousarray(np.asarray(Wp["bcon_f"]).reshape(8, 128).T),
    }
    in_maps = []
    for c in range(NC):
        m = dict(shared)
        m["xT"] = np.ascontiguousarray(X[c * TC:(c + 1) * TC].T)
        in_maps.append(m)
    res = sp.run(in_maps, core_ids=range(NC))
    out = np.empty_like(X)
    for c in range(NC):
        out[c * TC:(c + 1) * TC] = res.results[c]["yT"].T
    return out


def build_mlp3(sp=None, io=None):
    if sp is None:
        sp = SP()
    io = io or {}
    def din(n, s, d=F32):
        return io[n] if n in io else sp.din(n, s, d)
    def dout(n, s, d=F32):
        return io[n] if n in io else sp.dout(n, s, d)
    xT = din("xT", (1024, TC))
    Wub = din("Wub", (1024, 4096), BF16)
    Wdb = din("Wdb", (4096, 1024), BF16)
    yT = dout("yT", (1024, TC))

    ar = Arena(sp, 40960)
    x_sb = ar.f32(0, (8, TC))
    xn_bf = ar.bf16(16384, (8, TC))
    rows_off = 24576
    h_lo = ar.bf16(0, (16, TC))
    h_hi = ar.bf16(24576, (16, TC))

    def hv(j):
        return (h_lo if j < 16 else h_hi)[:, j % 16, :]

    ones_col = sp.sbuf("onc", [128, 1], F32)
    ones_r128 = sp.sbuf("o128", [1, 128], F32)
    w_sb = sp.sbuf("wsb", [128, 8, 128], BF16)
    wdn = sp.sbuf("wdn", [128, 32, 128], BF16)
    sq = sp.sbuf("sqq", [128, 512], F32)
    tmp_f = sp.sbuf("tmf", [128, 512], F32)
    s1rep = sp.sbuf("s1r", [128, TC], F32)
    xload = sp.sbuf("xld", [128, TC], F32)
    psA = sp.psum("psa", [128, 512])
    psO = sp.psum("pso", [128, 512])
    psB0 = sp.psum("pb0", [128, 512])
    psB1 = sp.psum("pb1", [128, 512])
    psR2 = sp.psum("ps2", [1, TC])
    psR = psR2[0:1, 0:512]
    tmp2 = sp.sbuf("tm2", [128, 512], F32)

    sp.memset(ones_col[:], 1.0, engine="gpsimd")
    sp.memset(ones_r128[:], 1.0, engine="gpsimd")
    sp.dma(x_sb, xT.rearrange("(a p) t -> p a t", p=128))

    mrow = ar.row(rows_off, TC)
    srow = ar.row(rows_off + TC, TC)
    sinv = ar.row(rows_off + 2 * TC, TC)
    msr = ar.row(rows_off + 3 * TC, TC)
    for tc in range(4):
        tsl = slice(tc * 512, tc * 512 + 512)
        def fmean(eng, _tsl=tsl):
            last = None
            for a in range(8):
                last = eng.matmul(psR[0:1, :], ones_col[:], x_sb[:, a, _tsl],
                                  start=(a == 0), stop=(a == 7))
            return last
        sp.mms(fmean)
        sp.act(mrow[:, tsl], psR[0:1, :], AF.Copy, scale=1.0 / 1024)
        for a in range(8):
            sp.tt(sq[:], x_sb[:, a, tsl], x_sb[:, a, tsl], ALU.mult)
            def fsq(eng, _a=a):
                return eng.matmul(psR[0:1, :], ones_col[:], sq[:],
                                  start=(_a == 0), stop=(_a == 7))
            sp.mms(fsq)
        sp.act(srow[:, tsl], psR[0:1, :], AF.Copy, scale=1.0 / 1024)
    sp.tt(msr[:], mrow[:], mrow[:], ALU.mult)
    sp.tt(srow[:], srow[:], msr[:], ALU.subtract)
    sp.ts(srow[:], srow[:], 1e-5, ALU.add)
    sp.act(srow[:], srow[:], AF.Sqrt)
    sp.recip(sinv[:], srow[:])
    sp.tt(msr[:], mrow[:], sinv[:], ALU.mult)
    for tc in range(4):
        tsl = slice(tc * 512, tc * 512 + 512)
        def fb(eng, _tsl=tsl):
            eng.matmul(psB0[:], ones_r128[:], sinv[:, _tsl], start=True, stop=True)
            return eng.matmul(psB1[:], ones_r128[:], msr[:, _tsl], start=True, stop=True)
        sp.mms(fb)
        for a in range(8):
            sp.tt(tmp_f[:], x_sb[:, a, tsl], psB0[:], ALU.mult)
            sp.tt(x_sb[:, a, tsl], tmp_f[:], psB1[:], ALU.subtract)
    for tc in range(4):
        tsl = slice(tc * 512, tc * 512 + 512)
        for a in range(8):
            sp.tt(sq[:], x_sb[:, a, tsl], x_sb[:, a, tsl], ALU.mult)
            def fsq(eng, _a=a):
                return eng.matmul(psR[0:1, :], ones_col[:], sq[:],
                                  start=(_a == 0), stop=(_a == 7))
            sp.mms(fsq)
        sp.act(srow[:, tsl], psR[0:1, :], AF.Copy, scale=1.0 / 1024)
    sp.act(srow[:], srow[:], AF.Sqrt)
    sp.ts(srow[:], srow[:], 1e-8, ALU.add)
    sp.recip(sinv[:], srow[:])
    for tc in range(4):
        tsl = slice(tc * 512, tc * 512 + 512)
        def fb1(eng, _tsl=tsl):
            return eng.matmul(psB0[:], ones_r128[:], sinv[:, _tsl], start=True, stop=True)
        sp.mms(fb1)
        sp.copy(s1rep[:, tsl], psB0[:])
        for a in range(8):
            sp.copy(xn_bf[:, a, tsl], x_sb[:, a, tsl])

    for j in range(32):
        sp.dma(w_sb[:], Wub[:, j * 128:(j + 1) * 128].rearrange("(a p) f -> p a f", p=128))
        for tc in range(4):
            tsl = slice(tc * 512, tc * 512 + 512)
            def fa(eng, _tsl=tsl):
                last = None
                for a in range(8):
                    last = eng.matmul(psA[:], w_sb[:, a, :], xn_bf[:, a, _tsl],
                                      start=(a == 0), stop=(a == 7))
                return last
            sp.mms(fa)
            sp.tt(tmp_f[:], psA[:], s1rep[:, tsl], ALU.mult)
            sp.op("scalar", lambda e: e.activation(out=tmp2[:], in_=tmp_f[:], func=AF.Gelu))
            sp.copy(hv(j)[:, tsl], tmp2[:])
            sp.tt(sq[:], tmp2[:], tmp2[:], ALU.mult)
            def fh2(eng, _j=j, _tc=tc):
                return eng.matmul(psR2[0:1, _tc * 512:(_tc + 1) * 512],
                                  ones_col[:], sq[:], start=(_j == 0), stop=(_j == 31))
            sp.mms(fh2)
    srow2 = ar.row(16384, TC)
    sinv2 = ar.row(16384 + TC, TC)
    s2rep = ar.f32(16384 + 2 * TC, (TC,))
    sp.act(srow2[:], psR2[0:1, :], AF.Copy, scale=1.0 / 4096)
    sp.act(srow2[:], srow2[:], AF.Sqrt)
    sp.ts(srow2[:], srow2[:], 1e-8, ALU.add)
    sp.recip(sinv2[:], srow2[:])
    for tc in range(4):
        tsl = slice(tc * 512, tc * 512 + 512)
        def fb2(eng, _tsl=tsl):
            return eng.matmul(psB0[:], ones_r128[:], sinv2[:, _tsl], start=True, stop=True)
        sp.mms(fb2)
        sp.copy(s2rep[:, tsl], psB0[:])
    for fo in range(8):
        sp.dma(wdn[:], Wdb[:, fo * 128:(fo + 1) * 128].rearrange("(g p) f -> p g f", p=128))
        sp.dma(xload[:], xT[fo * 128:(fo + 1) * 128, :])
        for tc in range(4):
            tsl = slice(tc * 512, tc * 512 + 512)
            def fd(eng, _tsl=tsl):
                last = None
                for g in range(32):
                    last = eng.matmul(psO[:], wdn[:, g, :], hv(g)[:, _tsl],
                                      start=(g == 0), stop=(g == 31))
                return last
            sp.mms(fd)
            sp.tt(tmp_f[:], psO[:], s2rep[:, tsl], ALU.mult)
            def ffin(eng, _tsl=tsl):
                return eng.scalar_tensor_tensor(out=xload[:, _tsl], in0=tmp_f[:],
                                                scalar=0.5, in1=xload[:, _tsl],
                                                op0=ALU.mult, op1=ALU.add)
            sp.op("vector", ffin)
        sp.dma(yT[fo * 128:(fo + 1) * 128, :], xload[:])
    return sp


def run_mlp3(X, inp, Wp):
    sp = build_mlp3()
    shared = {
        "Wub": np.ascontiguousarray(Wp["L3_ffn_w_up_qtb"]),
        "Wdb": np.ascontiguousarray(Wp["L3_ffn_w_down_qtb"]),
    }
    in_maps = []
    for c in range(NC):
        m = dict(shared)
        m["xT"] = np.ascontiguousarray(X[c * TC:(c + 1) * TC].T)
        in_maps.append(m)
    res = sp.run(in_maps, core_ids=range(NC))
    out = np.empty_like(X)
    for c in range(NC):
        out[c * TC:(c + 1) * TC] = res.results[c]["yT"].T
    return out


def build_attn3(sp=None, io=None):
    if sp is None:
        sp = SP()
    io = io or {}
    def din(n, s, d=F32):
        return io[n] if n in io else sp.din(n, s, d)
    def dout(n, s, d=F32):
        return io[n] if n in io else sp.dout(n, s, d)
    xfT = None if "xf_chunk" in (io or {}) else din("xfT", (1024, L))
    xqT = din("xqT", (1024, TC))
    xrT = din("xrT", (1024, TC))
    Wqb = din("Wqb", (1024, 1024), BF16)
    Wkb = din("Wkb", (1024, 512), BF16)
    Wvb = din("Wvb", (1024, 512), BF16)
    memb = din("memb", (512, 512), BF16)
    mnL = din("mnL", (128, 4))
    mwC = din("mwC", (1, 1))
    yT = dout("yT", (1024, TC))

    KH = 4096
    ar = Arena(sp, 40960)
    k_sb = ar.bf16(0, (4, KH))
    vt = ar.bf16(8192, (8, 32, 65))
    q_sb = ar.bf16(16512, (8, TC))
    xch = ar.f32(24704, (8, 512))
    xnch = ar.bf16(28800, (8, 512))
    probsT = ar.bf16(24704, (TC,))
    qal = ar.bf16(29824, (TC,))
    pvb = ar.f32(30848, (TC,))[0:65, :]
    sig = ar.bf16(0, (4, TC))
    mem_sb = ar.bf16(8192, (4, 512))
    e1s = ar.f32(24704, (512,))
    mks = ar.f32(25216, (512,))
    LN_off = 32896
    sums_off = 34944

    ones_col = sp.sbuf("onc", [128, 1], F32)
    ones_r128 = sp.sbuf("o12", [1, 128], F32)
    ones_r64 = sp.sbuf("o64", [1, 64], F32)
    identf = make_ident(sp, "idf", 128, F32)
    wk_sb = sp.sbuf("wks", [128, 8, 512], BF16)
    wv_sb = sp.sbuf("wvs", [128, 8, 512], BF16)
    wq_h = sp.sbuf("wqh", [128, 8, 64], BF16)
    v_tile = sp.sbuf("vtl", [128, 512], F32)
    rrep = sp.sbuf("rrp", [64, 512], F32)
    xl64 = sp.sbuf("xl", [64, 512], F32)
    t64 = sp.sbuf("t64", [64, 512], F32)
    wrow = sp.sbuf("wrw", [1, 1], F32)
    wcol = sp.sbuf("wcl", [128, 1], F32)
    omw = sp.sbuf("omw", [128, 1], F32)
    mn_sb = sp.sbuf("mns", [128, 4], F32)
    mn_bf = sp.sbuf("mnb", [128, 4], BF16)
    wdram = sp.dtmp("wd", (1, 1))

    pv2 = sp.sbuf("pv2", [65, TC], F32)
    pv_dram = [sp.dtmp(f"pvd{h}", (65, TC)) for h in range(16)]
    sums_dram = sp.dtmp("smd", (1, 512))
    psS = sp.psum("psS", [128, 2048])
    psPV = sp.psum("psP", [128, 2048])

    sp.memset(ones_col[:], 1.0, engine="gpsimd")
    sp.memset(ones_r128[:], 1.0, engine="gpsimd")
    sp.memset(ones_r64[:], 1.0, engine="gpsimd")
    sp.memset(ar.t[:, 8192:16512].bitcast(BF16), 1.0, engine="gpsimd")
    sp.dmas([(wk_sb[:], Wkb.rearrange("(a p) f -> p a f", p=128)),
             (wv_sb[:], Wvb.rearrange("(a p) f -> p a f", p=128)),
             (mn_sb[:], mnL), (wrow[:], mwC)])
    sp.copy(mn_bf[:], mn_sb[:])
    sp.act(wrow[:], wrow[:], AF.Sigmoid)
    sp.dma(wdram, wrow[:])
    sp.dma(wcol[:], wdram.to_broadcast([128, 1]))
    sp.op("vector", lambda e: e.tensor_scalar(out=omw[:], in0=wcol[:], scalar1=-1.0,
                                              scalar2=1.0, op0=ALU.mult, op1=ALU.add))

    mrow = ar.row(LN_off, 512)
    srow = ar.row(LN_off + 512, 512)
    sinv = ar.row(LN_off + 1024, 512)
    msr = ar.row(LN_off + 1536, 512)
    sums = ar.row(sums_off, 512)
    rcp = ar.row(sums_off + 512, 512)

    def emit_ln_chunk(src_ap):
        def fmean(eng):
            last = None
            for a in range(8):
                last = eng.matmul(psS[0:1, 0:512], ones_col[:], xch[:, a, :],
                                  start=(a == 0), stop=(a == 7))
            return last
        sp.mms(fmean)
        sp.act(mrow[:], psS[0:1, 0:512], AF.Copy, scale=1.0 / 1024)
        for a in range(8):
            sp.tt(v_tile[:], xch[:, a, :], xch[:, a, :], ALU.mult)
            def fsq(eng, _a=a):
                return eng.matmul(psS[0:1, 512:1024], ones_col[:], v_tile[:],
                                  start=(_a == 0), stop=(_a == 7))
            sp.mms(fsq)
        sp.act(srow[:], psS[0:1, 512:1024], AF.Copy, scale=1.0 / 1024)
        sp.tt(msr[:], mrow[:], mrow[:], ALU.mult)
        sp.tt(srow[:], srow[:], msr[:], ALU.subtract)
        sp.ts(srow[:], srow[:], 1e-5, ALU.add)
        sp.act(srow[:], srow[:], AF.Sqrt)
        sp.recip(sinv[:], srow[:])
        sp.tt(msr[:], mrow[:], sinv[:], ALU.mult)
        def fb(eng):
            eng.matmul(psS[:, 512:1024], ones_r128[:], sinv[:], start=True, stop=True)
            return eng.matmul(psS[:, 1024:1536], ones_r128[:], msr[:], start=True, stop=True)
        sp.mms(fb)
        for a in range(8):
            sp.tt(v_tile[:], xch[:, a, :], psS[:, 512:1024], ALU.mult)
            sp.tt(xnch[:, a, :], v_tile[:], psS[:, 1024:1536], ALU.subtract)

    for tcc in range(4):
        tsl = slice(tcc * 512, tcc * 512 + 512)
        sp.dma(xch, xqT.rearrange("(a p) t -> p a t", p=128)[:, :, tsl])
        emit_ln_chunk(None)
        for h in range(16):
            ob = (h % 2) * 64
            sp.dma(wq_h[:], Wqb[:, h * 64:(h + 1) * 64].rearrange("(a p) f -> p a f", p=128))
            def fq(eng, _ob=ob):
                last = None
                for a in range(8):
                    last = eng.matmul(psS[_ob:_ob + 64, 1536:2048], wq_h[:, a, :],
                                      xnch[:, a, :], start=(a == 0), stop=(a == 7))
                return last
            sp.mms(fq)
            def fevq(eng, _h=h, _ob=ob, _tsl=tsl):
                return eng.copy(out=q_sb[_ob:_ob + 64, _h // 2, _tsl],
                                in_=psS[_ob:_ob + 64, 1536:2048])
            sp.op("scalar", fevq)

    for kh in range(2):
        for tcc in range(8):
            t0 = kh * KH + tcc * 512
            lsl = slice(tcc * 512, tcc * 512 + 512)
            if "xf_chunk" in io:
                sp.dma(xch, io["xf_chunk"](t0))
            else:
                sp.dma(xch, xfT.rearrange("(a p) t -> p a t", p=128)[:, :, t0:t0 + 512])
            emit_ln_chunk(None)
            for g in range(8):
                ob = (g % 2) * 64
                def fk(eng, _g=g, _ob=ob):
                    last = None
                    for a in range(8):
                        last = eng.matmul(psS[_ob:_ob + 64, 1536:2048],
                                          wk_sb[:, a, _g * 64:(_g + 1) * 64],
                                          xnch[:, a, :], start=(a == 0), stop=(a == 7))
                    return last
                sp.mms(fk)
                def fevk(eng, _g=g, _ob=ob, _lsl=lsl):
                    return eng.copy(out=k_sb[_ob:_ob + 64, _g // 2, _lsl],
                                    in_=psS[_ob:_ob + 64, 1536:2048])
                sp.op("scalar", fevk)
            for fo in range(4):
                def fv(eng, _fo=fo):
                    last = None
                    for a in range(8):
                        last = eng.matmul(psS[:, 1536:2048],
                                          wv_sb[:, a, _fo * 128:(_fo + 1) * 128],
                                          xnch[:, a, :], start=(a == 0), stop=(a == 7))
                    return last
                sp.mms(fv)
                sp.copy(v_tile[:], psS[:, 1536:2048], engine="scalar")
                for hh in range(2):
                    g = fo * 2 + hh
                    for kb in range(4):
                        kbg = tcc * 4 + kb
                        def ftr(eng, _hh=hh, _kb=kb):
                            return eng.transpose(psPV[:, 0:64],
                                                 v_tile[_hh * 64:(_hh + 1) * 64,
                                                        _kb * 128:(_kb + 1) * 128],
                                                 identf[_hh * 64:(_hh + 1) * 64,
                                                        _hh * 64:(_hh + 1) * 64])
                        sp.op("tensor", ftr)
                        def fevt(eng, _g=g, _kbg=kbg):
                            return eng.copy(out=vt[:, _g, _kbg, 0:64], in_=psPV[:, 0:64])
                        sp.op("scalar", fevt)
        for h in range(16):
            g = h // 2
            obq = (h % 2) * 64
            obk = (g % 2) * 64
            sp.dma(qal.rearrange("p t -> p t")[obk:obk + 64, :] if False else
                   ar.t[obk:obk + 64, 29824:29824 + 1024].bitcast(BF16),
                   ar.t[obq:obq + 64, 16512 + (h // 2) * 1024:16512 + (h // 2) * 1024 + 1024]
                   .bitcast(BF16), engine="sync")
            for kbg in range(32):
                ksl = slice(kbg * 128, kbg * 128 + 128)
                def fsT(eng, _g=g, _obk=obk, _ksl=ksl):
                    last = None
                    for qc in range(4):
                        last = eng.matmul(psS[:, qc * 512:(qc + 1) * 512],
                                          k_sb[_obk:_obk + 64, _g // 2, _ksl],
                                          ar.t[_obk:_obk + 64, 29824 + qc * 256:29824 + (qc + 1) * 256]
                                          .bitcast(BF16),
                                          start=True, stop=True)
                    return last
                sp.mms(fsT)
                sp.act(probsT, psS[:, :], AF.Exp, scale=0.125)
                def fpv(eng, _g=g, _kbg=kbg):
                    last = None
                    pr = probsT.rearrange("p (qc t) -> p qc t", qc=4)
                    for qc in range(4):
                        last = eng.matmul(psPV[0:65, qc * 512:(qc + 1) * 512],
                                          vt[:, _g, _kbg, :], pr[:, qc, :],
                                          start=(_kbg == 0), stop=(_kbg == 31))
                    return last
                sp.mms(fpv)
            sp.copy(pvb, psPV[0:65, :], engine="scalar")
            if kh == 0:
                sp.dma(pv_dram[h], pvb)
            else:
                sp.dma(pv2[:], pv_dram[h])
                sp.tt(pvb, pvb, pv2[:], ALU.add)
                for tc in range(4):
                    tsl = slice(tc * 512, tc * 512 + 512)
                    sp.dma(sums_dram, pvb[64:65, tsl])
                    sp.dma(sums, sums_dram)
                    sp.recip(rcp[:], sums[:])
                    def fbc(eng):
                        return eng.matmul(psS[0:64, 0:512], ones_r64[:], rcp[:],
                                          start=True, stop=True)
                    sp.mms(fbc)
                    sp.copy(rrep[:], psS[0:64, 0:512], engine="scalar")
                    sp.tt(t64[:], pvb[0:64, tsl], rrep[:], ALU.mult)
                    sp.ts(t64[:], t64[:], omw[0:64, :], ALU.mult)
                    sp.dma(xl64[:], xrT[h * 64:(h + 1) * 64, tsl])
                    sp.tt(xl64[:], xl64[:], t64[:], ALU.add)
                    sp.dma(yT[h * 64:(h + 1) * 64, tsl], xl64[:])

    sp.dma(mem_sb[:], memb.rearrange("(j p) f -> p j f", p=128))
    e1v = ar.f32(24704, (512,))
    mkv = ar.f32(25216, (512,))
    t128 = sp.sbuf("t128", [128, 512], F32)
    xl128 = sp.sbuf("xl2", [128, 512], F32)
    rrep128 = sp.sbuf("rr2", [128, 512], F32)
    for g2 in range(2):
        for jj in range(4):
            j = g2 * 4 + jj
            for tc in range(4):
                tsl = slice(tc * 512, tc * 512 + 512)
                sp.ts(mkv, q_sb[:, j, tsl], 0.0, ALU.min)
                sp.op("scalar", lambda e: e.activation(out=e1v, in_=mkv, func=AF.Exp))
                sp.ts(e1v, e1v, 1.0, ALU.subtract)
                sp.ts(mkv, q_sb[:, j, tsl], 0.0, ALU.max)
                sp.tt(sig[:, jj, tsl], mkv, e1v, ALU.add)
        for tc in range(4):
            tsl = slice(tc * 512, tc * 512 + 512)
            def fden(eng, _tsl=tsl):
                last = None
                for jj in range(4):
                    last = eng.matmul(psS[0:1, 0:512], mn_bf[:, jj:jj + 1],
                                      sig[:, jj, _tsl], start=(jj == 0), stop=(jj == 3))
                return last
            sp.mms(fden)
            sp.act(sums[:], psS[0:1, 0:512], AF.Copy)
            sp.recip(rcp[:], sums[:])
            def fbc2(eng):
                eng.matmul(psS[0:64, 512:1024], ones_r64[:], rcp[:], start=True, stop=True)
                return eng.matmul(psS[64:128, 512:1024], ones_r64[:], rcp[:], start=True, stop=True)
            sp.mms(fbc2)
            sp.copy(rrep128[:], psS[:, 512:1024], engine="scalar")
            for fo in range(4):
                def fnum(eng, _fo=fo, _tsl=tsl):
                    last = None
                    for jj in range(4):
                        last = eng.matmul(psPV[0:128, 0:512],
                                          mem_sb[:, jj, _fo * 128:(_fo + 1) * 128],
                                          sig[:, jj, _tsl], start=(jj == 0), stop=(jj == 3))
                    return last
                sp.mms(fnum)
                sp.tt(t128[:], psPV[0:128, 0:512], rrep128[:], ALU.mult)
                sp.ts(t128[:], t128[:], wcol[:], ALU.mult)
                r0 = g2 * 512 + fo * 128
                sp.dma(xl128[:], yT[r0:r0 + 128, tsl])
                sp.tt(xl128[:], xl128[:], t128[:], ALU.add)
                sp.dma(yT[r0:r0 + 128, tsl], xl128[:])
    return sp


def run_attn3(X, inp, Wp):
    sp = build_attn3()
    shared = {
        "Wqb": np.ascontiguousarray(Wp["L3_attn_Wq_b"]),
        "Wkb": np.ascontiguousarray(Wp["L3_attn_Wk_b"]),
        "Wvb": np.ascontiguousarray(Wp["L3_attn_Wv_b"]),
        "memb": np.ascontiguousarray(Wp["L3_attn_memory_b"]),
        "mnL": np.ascontiguousarray(inp["L3_attn_memory_norm"].reshape(4, 128).T),
        "mwC": np.asarray(inp["L3_attn_memory_weight"], np.float32).reshape(1, 1),
    }
    in_maps = []
    xfTs = [np.ascontiguousarray(X[b * 8192:(b + 1) * 8192].T) for b in range(2)]
    for c in range(NC):
        b, q = c // 4, c % 4
        m = dict(shared)
        m["xfT"] = xfTs[b]
        xo = np.ascontiguousarray(X[b * 8192 + 2048 * q: b * 8192 + 2048 * (q + 1)].T)
        m["xqT"] = xo
        m["xrT"] = xo
        in_maps.append(m)
    res = sp.run(in_maps, core_ids=range(NC))
    out = np.empty_like(X)
    for c in range(NC):
        out[c * TC:(c + 1) * TC] = res.results[c]["yT"].T
    return out


def kernel(**inp):
    inp = {k: np.asarray(v) for k, v in inp.items()}
    X = run_fused(inp)
    return X.reshape(2, 8192, 1024)


def kernel_multilaunch(**inp):
    inp = {k: np.asarray(v) for k, v in inp.items()}
    Wp = run_prep(inp)
    X = np.ascontiguousarray(np.asarray(inp["L0_attn_x"], np.float32).reshape(T, D))
    X = run_moe0(X, inp, Wp)
    X = run_attn0(X, inp, Wp)
    X = run_moe0(X, inp, Wp)
    X = run_moe1(X, inp, Wp)
    X = run_l1(X, inp, Wp)
    X = run_moe1(X, inp, Wp)
    X = run_mlp2(X, inp, Wp)
    X = run_l2(X, inp, Wp)
    X = run_mlp2(X, inp, Wp)
    X = run_mlp3(X, inp, Wp)
    X = run_attn3(X, inp, Wp)
    X = run_mlp3(X, inp, Wp)
    return X.reshape(2, 8192, 1024)


GROUPS8 = [list(range(8))]
GROUPSB = [[0, 1, 2, 3], [4, 5, 6, 7]]

_PREP_OUT_SHAPES = {
    "L0_attn_Wq1_b": ((1024, 1024), BF16), "L0_attn_Wq2_b": ((1024, 1024), BF16),
    "L0_attn_Wk1_b": ((1024, 128), BF16), "L0_attn_Wk2_b": ((1024, 128), BF16),
    "L0_attn_Wv_b": ((1024, 128), BF16), "L0_attn_Wout_b": ((1024, 1024), BF16),
    "L1_ffn_shared_gate_w_b": ((1024, 1024), BF16), "L1_ffn_shared_up_w_b": ((1024, 1024), BF16),
    "L1_ffn_shared_down_w_b": ((1024, 1024), BF16),
    "L3_attn_Wq_b": ((1024, 1024), BF16), "L3_attn_Wk_b": ((1024, 512), BF16),
    "L3_attn_Wv_b": ((1024, 512), BF16), "L3_attn_memory_b": ((512, 512), BF16),
    "L0_ffn_W1_b": ((8, 1024, 4096), BF16), "L0_ffn_W2_b": ((8, 2048, 1024), BF16),
    "L1_ffn_gate_w_b": ((16, 1024, 512), BF16), "L1_ffn_up_w_b": ((16, 1024, 512), BF16),
    "L1_ffn_down_w_b": ((16, 512, 1024), BF16),
    "L1_attn_w_in_tb": ((1024, 4096), BF16), "L1_attn_w_x_tb": ((2048, 96), BF16),
    "L1_attn_w_dt_tb": ((64, 2048), BF16), "L1_attn_w_out_tb": ((2048, 1024), BF16),
    "L2_attn_in_proj_w_tb": ((1024, 4096), BF16), "L2_attn_B_w_tb": ((2048, 64), BF16),
    "L2_attn_C_w_tb": ((2048, 64), BF16), "L2_attn_dt_w_tb": ((2048, 2048), BF16),
    "L2_attn_out_proj_w_tb": ((2048, 1024), BF16),
    "L3_ffn_w_up_qtb": ((1024, 4096), BF16), "L3_ffn_w_down_qtb": ((4096, 1024), BF16),
    "W_exp_fb": ((1024, 4096), BF16), "bexp_f": ((1, 4096), F32),
    "W_con_fb": ((4096, 1024), BF16), "bcon_f": ((1, 1024), F32),
    "lam": ((1, 1), F32),
}


def build_fused(n_sub=13):
    sp = SP(fused=True)
    nc = sp.nc
    xin = sp.din("xin", (1024, TC))
    out = sp.dout("out", (1024, TC))
    offs_in = sp.din("offs", (1, 32), mybir.dt.int32)
    hmask_in = sp.din("hmask", (128, 1))
    posrow = sp.din("posrow", (1, 1152))
    divrow = sp.din("divrow", (1, 32))
    alog1 = sp.din("AlogS1", (256, 16))
    alog2 = sp.din("AlogS2", (256, 64))
    raw = {}
    for nm, shp in [("WrL", (128, 8, 8)), ("brC", (8, 1)), ("b1L", (128, 32, 8)),
                    ("b2L", (128, 8, 8)), ("RwL", (128, 8, 16)),
                    ("gamL1", (128, 8)), ("betL1", (128, 8)), ("cwL1", (128, 16, 4)),
                    ("cbL1", (128, 16)), ("bdtL1", (128, 16)), ("DpL1", (128, 16)),
                    ("nwL2", (128, 8)), ("cwL2", (128, 16, 4)), ("cbL2", (128, 16)),
                    ("bdtL2", (128, 16)), ("DpL2", (128, 16)),
                    ("boutL", (128, 8)), ("mnL", (128, 4)), ("mwC", (1, 1))]:
        raw[nm] = sp.din(nm, shp)

    Wp = {nm: sp.dtmp(nm + "_i", shp, dt) for nm, (shp, dt) in _PREP_OUT_SHAPES.items()}
    Xa = sp.dtmp("Xa", (1024, TC))
    Xb = sp.dtmp("Xb", (1024, TC))
    XG = sp.dtmp("XG", (4096, TC))
    dG = sp.dtmp("dG", (T, TC))
    uG = sp.dtmp("uG", (T, TC))
    BG1 = sp.dtmp("BG1", (128, TC))
    CG1 = sp.dtmp("CG1", (128, TC))
    BG2 = sp.dtmp("BG2", (512, TC))
    CG2 = sp.dtmp("CG2", (512, TC))
    ys_own = sp.dtmp("ysown", (256, T))
    yG = sp.dtmp("yG", (2048, T))
    dS_loc = sp.dtmp("dSloc", (256, T))
    uS_loc = sp.dtmp("uSloc", (256, T))
    ys_loc = sp.dtmp("ysloc", (2048, TC))
    d_o = sp.dtmp("d_o", (2048, TC))
    u_o = sp.dtmp("u_o", (2048, TC))
    xs_o = sp.dtmp("xs_o", (2048, TC))
    z_o = sp.dtmp("z_o", (2048, TC))
    B_o1 = sp.dtmp("B_o1", (16, TC))
    C_o1 = sp.dtmp("C_o1", (16, TC))
    B_o2 = sp.dtmp("B_o2", (64, TC))
    C_o2 = sp.dtmp("C_o2", (64, TC))

    regs = {}

    def glue_regs(names_cols):
        t = sp.sbuf("OFF", [1, 32], mybir.dt.int32)
        sp.dma(t[:], offs_in)
        def f(eng, sem):
            for nm, col in names_cols:
                regs[nm] = eng.value_load(t[0:1, col:col + 1])
            return None
        sp.step("gpsimd", 0, f)

    def ag(ins, outs, groups):
        def f(eng, sem):
            return eng.collective_compute(
                "AllGather", ALU.bypass, replica_groups=groups,
                ins=[a.opt() for a in ins], outs=[a.opt() for a in outs])
        sp.step("gpsimd", 1, f)

    def mk_xw_load():
        def f(sp_, dst):
            glue_regs([("ra", 0), ("ca", 1), ("rb", 2), ("cb", 3)])
            def g(eng, sem):
                eng.dma_start(out=dst[:, :, 0:1024],
                              in_=XG[bass.ds(regs["ra"], 1024), bass.ds(regs["ca"], 1024)]
                              .rearrange("(a p) t -> p a t", p=128)).then_inc(sem, 16)
                eng.dma_start(out=dst[:, :, 1024:1152],
                              in_=XG[bass.ds(regs["rb"], 1024), bass.ds(regs["cb"], 128)]
                              .rearrange("(a p) t -> p a t", p=128)).then_inc(sem, 16)
                return None
            sp_.step("gpsimd", 32, g)
        return f

    def mk_xh_load(Xcur):
        def f(sp_, dst):
            glue_regs([("hr", 4)])
            hm = sp_.sbuf("HM", [128, 1], F32)
            sp_.dma(hm[:], hmask_in)
            def g(eng, sem):
                eng.dma_start(out=dst[:, :, 0:3],
                              in_=XG[bass.ds(regs["hr"], 1024), 2045:2048]
                              .rearrange("(a p) t -> p a t", p=128)).then_inc(sem, 16)
                eng.dma_start(out=dst[:, :, 3:2051],
                              in_=Xcur.rearrange("(a p) t -> p a t", p=128)).then_inc(sem, 16)
                return None
            sp_.step("gpsimd", 32, g)
            hv = dst[:, :, 0:3]
            sp_.ts(hv, hv, hm[:], ALU.mult)
        return f

    def reshard_du():
        glue_regs([("o256a", 5), ("o256b", 7)])
        buf = sp.sbuf("RSB", [128, 8, TC], F32)
        for dt, rg in ((0, "o256a"), (1, "o256b")):
            for srcg, dstl in ((dG, dS_loc), (uG, uS_loc)):
                def g(eng, sem, _rg=rg, _srcg=srcg):
                    return eng.dma_start(
                        out=buf,
                        in_=_srcg.rearrange("(r dd) t -> dd r t", dd=TC)[bass.ds(regs[_rg], 128), :, :])
                sp.step("gpsimd", 16, g)
                sp.dma(dstl[dt * 128:(dt + 1) * 128, :].rearrange("p (r t) -> p r t", r=8), buf)

    def reshard_ys():
        glue_regs([("o2048", 6)])
        buf = sp.sbuf("RSB2", [128, 8, TC], F32)
        for gh in range(2):
            def g(eng, sem, _gh=gh):
                return eng.dma_start(
                    out=buf,
                    in_=yG.rearrange("(g p) t -> p g t", p=128)[:, _gh * 8:(_gh + 1) * 8,
                                                               bass.ds(regs["o2048"], TC)])
            sp.step("gpsimd", 16, g)
            sp.dma(ys_loc.rearrange("(g p) t -> p g t", p=128)[:, gh * 8:(gh + 1) * 8, :], buf)

    def mk_load_bc(BGx, CGx, nst):
        def f(sp_, n, b, ch, Brep, Crep):
            pairs = []
            for ri, r in enumerate((b * 4 + ch * 2, b * 4 + ch * 2 + 1)):
                row = r * nst + n
                pairs.append((Brep[:, ri * 2048:(ri + 1) * 2048],
                              BGx[row:row + 1, :].to_broadcast([128, 2048])))
                pairs.append((Crep[:, ri * 2048:(ri + 1) * 2048],
                              CGx[row:row + 1, :].to_broadcast([128, 2048])))
            sp_.dmas(pairs)
        return f

    def mk_ys_load():
        def f(sp_, g_, dst):
            def g(eng, sem):
                return eng.dma_start(out=dst,
                                     in_=yG[g_ * 128:(g_ + 1) * 128, bass.ds(regs["o2048"], 2048)])
            sp_.step("gpsimd", 16, g)
        return f

    def xf_chunk(t0):
        cb = t0 // 2048
        c0 = t0 % 2048
        return XG[cb * 1024:(cb + 1) * 1024, c0:c0 + 512].rearrange("(a p) t -> p a t", p=128)

    sp.sub_begin()
    build_prep(sp=sp, io=Wp)

    cur, nxt = xin, Xa
    step_i = 0

    def bump(written):
        nonlocal cur, nxt, step_i
        cur = written
        nxt = Xb if written is Xa else Xa
        step_i += 1

    io_moe0 = dict(W1b=Wp["L0_ffn_W1_b"], W2b=Wp["L0_ffn_W2_b"], WrL=raw["WrL"],
                   brC=raw["brC"], b1L=raw["b1L"], b2L=raw["b2L"])
    io_moe1 = dict(Gb=Wp["L1_ffn_gate_w_b"], Ub=Wp["L1_ffn_up_w_b"], Db=Wp["L1_ffn_down_w_b"],
                   SGb=Wp["L1_ffn_shared_gate_w_b"], SUb=Wp["L1_ffn_shared_up_w_b"],
                   SDb=Wp["L1_ffn_shared_down_w_b"], RwL=raw["RwL"])
    io_mlp2 = dict(Wub=Wp["W_exp_fb"], Wdb=Wp["W_con_fb"],
                   bup_row=Wp["bexp_f"], bdn_row=Wp["bcon_f"])
    io_mlp3 = dict(Wub=Wp["L3_ffn_w_up_qtb"], Wdb=Wp["L3_ffn_w_down_qtb"])

    seq = []
    seq.append(("moe0a", lambda: build_moe0(sp=sp, io=dict(io_moe0, xT=cur, yT=nxt))))
    def attn0_step():
        ag([cur], [XG], GROUPSB)
        build_attn0(sp=sp, io=dict(
            xrT=cur, yT=nxt, xw_load=mk_xw_load(),
            Wq1b=Wp["L0_attn_Wq1_b"], Wq2b=Wp["L0_attn_Wq2_b"], Wk1b=Wp["L0_attn_Wk1_b"],
            Wk2b=Wp["L0_attn_Wk2_b"], Wvb=Wp["L0_attn_Wv_b"], Woutb=Wp["L0_attn_Wout_b"],
            boutL=raw["boutL"], lamC=Wp["lam"].to_broadcast([128, 1]),
            posrow=posrow, divrow=divrow))
    seq.append(("attn0", attn0_step))
    seq.append(("moe0b", lambda: build_moe0(sp=sp, io=dict(io_moe0, xT=cur, yT=nxt))))
    seq.append(("moe1a", lambda: build_moe1(sp=sp, io=dict(io_moe1, xT=cur, yT=nxt))))

    def l1_step():
        ag([cur], [XG], GROUPSB)
        build_l1a(sp=sp, io=dict(
            xh_load=mk_xh_load(cur), winT=Wp["L1_attn_w_in_tb"], wxT=Wp["L1_attn_w_x_tb"],
            wdtT=Wp["L1_attn_w_dt_tb"], gamL=raw["gamL1"], betL=raw["betL1"],
            cwL=raw["cwL1"], cbL=raw["cbL1"], bdtL=raw["bdtL1"],
            delta_o=d_o, u_o=u_o, xs_o=xs_o, z_o=z_o, B_o=B_o1, C_o=C_o1))
        ag([d_o], [dG], GROUPS8)
        ag([u_o], [uG], GROUPS8)
        ag([B_o1], [BG1], GROUPS8)
        ag([C_o1], [CG1], GROUPS8)
        sp.sub_begin()
        reshard_du()
        sp.sub_begin()
        build_scan(16, sp=sp, io=dict(dS=dS_loc, uS=uS_loc, load_bc=mk_load_bc(BG1, CG1, 16),
                                      AlogS=alog1, yS=ys_own))
        ag([ys_own], [yG], GROUPS8)
        sp.sub_begin()
        reshard_ys()
        sp.sub_begin()
        build_l1c(sp=sp, io=dict(ysT=ys_loc, xsT=xs_o, zT=z_o, xT=cur, yT=nxt,
                                 DpL=raw["DpL1"], woTb=Wp["L1_attn_w_out_tb"]))
    seq.append(("l1", l1_step))
    seq.append(("moe1b", lambda: build_moe1(sp=sp, io=dict(io_moe1, xT=cur, yT=nxt))))
    seq.append(("mlp2a", lambda: build_mlp2(sp=sp, io=dict(io_mlp2, xT=cur, yT=nxt))))

    def l2_step():
        ag([cur], [XG], GROUPSB)
        build_l2a(sp=sp, io=dict(
            xh_load=mk_xh_load(cur), winT=Wp["L2_attn_in_proj_w_tb"], bwT=Wp["L2_attn_B_w_tb"],
            cwT=Wp["L2_attn_C_w_tb"], dtwT=Wp["L2_attn_dt_w_tb"], nwL=raw["nwL2"],
            cwL=raw["cwL2"], cbL=raw["cbL2"], bdtL=raw["bdtL2"],
            dt_o=d_o, u_o=u_o, xc_o=xs_o, z_o=z_o, B_o=B_o2, C_o=C_o2))
        ag([d_o], [dG], GROUPS8)
        ag([u_o], [uG], GROUPS8)
        ag([B_o2], [BG2], GROUPS8)
        ag([C_o2], [CG2], GROUPS8)
        sp.sub_begin()
        reshard_du()
        sp.sub_begin()
        build_scan(64, sp=sp, io=dict(dS=dS_loc, uS=uS_loc, load_bc=mk_load_bc(BG2, CG2, 64),
                                      AlogS=alog2, yS=ys_own))
        ag([ys_own], [yG], GROUPS8)
        sp.sub_begin()
        reshard_ys()
        sp.sub_begin()
        build_l1c(sp=sp, io=dict(ysT=ys_loc, xsT=xs_o, zT=z_o, xT=cur, yT=nxt,
                                 DpL=raw["DpL2"], woTb=Wp["L2_attn_out_proj_w_tb"]))
    seq.append(("l2", l2_step))
    seq.append(("mlp2b", lambda: build_mlp2(sp=sp, io=dict(io_mlp2, xT=cur, yT=nxt))))
    seq.append(("mlp3a", lambda: build_mlp3(sp=sp, io=dict(io_mlp3, xT=cur, yT=nxt))))

    def attn3_step():
        ag([cur], [XG], GROUPSB)
        build_attn3(sp=sp, io=dict(
            xf_chunk=xf_chunk, xqT=cur, xrT=cur, yT=nxt,
            Wqb=Wp["L3_attn_Wq_b"], Wkb=Wp["L3_attn_Wk_b"], Wvb=Wp["L3_attn_Wv_b"],
            memb=Wp["L3_attn_memory_b"], mnL=raw["mnL"], mwC=raw["mwC"]))
    seq.append(("attn3", attn3_step))
    seq.append(("mlp3b", lambda: build_mlp3(sp=sp, io=dict(io_mlp3, xT=cur, yT=nxt))))

    for i, (nm, fn) in enumerate(seq):
        if i >= n_sub:
            break
        if i == n_sub - 1 or i == len(seq) - 1:
            nxt = out
        sp.sub_begin()
        fn()
        bump(nxt)
    return sp


def stage_fused(inp, c):
    b, q = c // 4, c % 4
    m = {}
    for name in ["L0_attn_Wq1", "L0_attn_Wq2", "L0_attn_Wk1", "L0_attn_Wk2", "L0_attn_Wv",
                 "L0_attn_Wout", "L1_ffn_shared_gate_w", "L1_ffn_shared_up_w",
                 "L1_ffn_shared_down_w", "L3_attn_Wq", "L3_attn_Wk", "L3_attn_Wv",
                 "L3_attn_memory", "L0_ffn_W1", "L0_ffn_W2", "L1_ffn_gate_w", "L1_ffn_up_w",
                 "L1_ffn_down_w", "L1_attn_w_in", "L1_attn_w_x", "L1_attn_w_dt",
                 "L1_attn_w_out", "L2_attn_in_proj_w", "L2_attn_B_w", "L2_attn_C_w",
                 "L2_attn_dt_w", "L2_attn_out_proj_w", "L3_ffn_w_up", "L3_ffn_w_down",
                 "L2_ffn_W_exp", "L2_ffn_W_con"]:
        m[name] = np.ascontiguousarray(inp[name])
    for k in range(4):
        m[f"upw{k}"] = np.ascontiguousarray(inp["L2_ffn_up_weights"][:, :, k // 2, k % 2])
        m[f"dnw{k}"] = np.ascontiguousarray(inp["L2_ffn_down_weights"][:, :, k // 2, k % 2])
    for name in ["L2_ffn_b_exp", "L2_ffn_up_bias", "L2_ffn_b_con", "L2_ffn_down_bias"]:
        m[name] = np.ascontiguousarray(inp[name].reshape(1, -1))
    for name in ["L0_attn_lq1", "L0_attn_lk1", "L0_attn_lq2", "L0_attn_lk2"]:
        m[name] = np.ascontiguousarray(inp[name].reshape(1, -1))
    m["WrL"] = np.ascontiguousarray(inp["L0_ffn_Wr"].reshape(8, 128, 8).transpose(1, 0, 2))
    m["brC"] = np.ascontiguousarray(inp["L0_ffn_br"].reshape(8, 1))
    m["b1L"] = np.ascontiguousarray(inp["L0_ffn_b1"].T.reshape(32, 128, 8).transpose(1, 0, 2))
    m["b2L"] = np.ascontiguousarray(inp["L0_ffn_b2"].T.reshape(8, 128, 8).transpose(1, 0, 2))
    m["RwL"] = np.ascontiguousarray(inp["L1_ffn_router_w"].reshape(8, 128, 16).transpose(1, 0, 2))
    m["gamL1"] = np.ascontiguousarray(inp["L1_attn_gamma"].reshape(8, 128).T)
    m["betL1"] = np.ascontiguousarray(inp["L1_attn_beta"].reshape(8, 128).T)
    m["cwL1"] = np.ascontiguousarray(inp["L1_attn_conv_w"].reshape(2048, 4).reshape(16, 128, 4).transpose(1, 0, 2))
    m["cbL1"] = np.ascontiguousarray(inp["L1_attn_conv_b"].reshape(16, 128).T)
    m["bdtL1"] = np.ascontiguousarray(inp["L1_attn_b_dt"].reshape(16, 128).T)
    m["DpL1"] = np.ascontiguousarray(inp["L1_attn_Dp"].reshape(16, 128).T)
    m["nwL2"] = np.ascontiguousarray(inp["L2_attn_norm_w"].reshape(8, 128).T)
    m["cwL2"] = np.ascontiguousarray(inp["L2_attn_conv_w"].reshape(2048, 4).reshape(16, 128, 4).transpose(1, 0, 2))
    m["cbL2"] = np.ascontiguousarray(inp["L2_attn_conv_b"].reshape(16, 128).T)
    m["bdtL2"] = np.ascontiguousarray(inp["L2_attn_dt_b"].reshape(16, 128).T)
    m["DpL2"] = np.ascontiguousarray(inp["L2_attn_D_param"].reshape(16, 128).T)
    m["boutL"] = np.ascontiguousarray(inp["L0_attn_bout"].reshape(8, 128).T)
    m["mnL"] = np.ascontiguousarray(inp["L3_attn_memory_norm"].reshape(4, 128).T)
    m["mwC"] = np.asarray(inp["L3_attn_memory_weight"], np.float32).reshape(1, 1)
    m["posrow"] = (1024.0 * q + np.arange(1152, dtype=np.float32)).reshape(1, 1152)
    m["divrow"] = np.exp(np.arange(0, 64, 2, dtype=np.float32) * (-np.log(10000.0) / 64)).reshape(1, 32)
    m["AlogS1"] = np.ascontiguousarray(inp["L1_attn_A_log"][256 * c:256 * (c + 1)])
    m["AlogS2"] = np.ascontiguousarray(inp["L2_attn_A_log"][256 * c:256 * (c + 1)])
    s1 = 1024 * q
    s2 = s1 + 1024
    offs = np.zeros((1, 32), np.int32)
    offs[0, 0] = (s1 // 2048) * 1024
    offs[0, 1] = s1 % 2048
    offs[0, 2] = (s2 // 2048) * 1024
    offs[0, 3] = s2 % 2048
    offs[0, 4] = (q - 1) * 1024 if q > 0 else 0
    offs[0, 5] = 256 * c
    offs[0, 6] = 2048 * c
    offs[0, 7] = 256 * c + 128
    for r in range(8):
        for dt in range(2):
            offs[0, 8 + r * 2 + dt] = 256 * c + r * 2048 + dt * 128
    m["offs"] = offs
    m["hmask"] = np.full((128, 1), 1.0 if q > 0 else 0.0, np.float32)
    X = np.asarray(inp["L0_attn_x"], np.float32).reshape(T, D)
    m["xin"] = np.ascontiguousarray(X[c * TC:(c + 1) * TC].T)
    return m


def run_fused(inp, n_sub=13):
    sp = build_fused(n_sub)
    in_maps = [stage_fused(inp, c) for c in range(NC)]
    res = sp.run(in_maps, core_ids=range(NC))
    out = np.empty((T, D), np.float32)
    for c in range(NC):
        out[c * TC:(c + 1) * TC] = res.results[c]["out"].T
    return out


# version 03
# speedup vs baseline: 1.0341x; 1.0341x over previous
import numpy as np
import ml_dtypes
import concourse.bass as bass
import concourse.mybir as mybir
from concourse.bass_utils import run_bass_kernel_spmd
from concourse.alu_op_type import AluOpType as ALU
from contextlib import ExitStack

F32 = mybir.dt.float32
BF16 = mybir.dt.bfloat16
AF = mybir.ActivationFunctionType
BF = ml_dtypes.bfloat16
ENGINES = ("sync", "scalar", "vector", "tensor", "gpsimd")

D = 1024
B = 2
L = 8192
T = B * L
NC = 8
TC = T // NC


SBUF_WORDS = 52992
PSUM_COLS = 4096


class SP:
    def __init__(self, fused=False):
        self.nc = bass.Bass("TRN2", target_bir_lowering=False, debug=False)
        self.steps = []
        self.stack = ExitStack()
        self._uid = 0
        self.fused = fused
        if fused:
            self._backing = self.stack.enter_context(
                self.nc.sbuf_tensor("GBUF", [128, SBUF_WORDS], F32))
            self._pbacking = self.stack.enter_context(
                self.nc.psum_tensor("GPSUM", [128, PSUM_COLS], F32))
            self._sb_base = 0
            self._ps_base = 0

    def sub_begin(self):
        if self.fused:
            self._sb_base = 0
            self._ps_base = 0

    def uid(self):
        self._uid += 1
        return self._uid

    def sbuf(self, name, shape, dtype):
        if self.fused:
            parts = shape[0]
            n = int(np.prod(shape[1:]))
            if dtype == BF16:
                words = (n + 1) // 2
            elif dtype in (F32, mybir.dt.int32):
                words = n
            else:
                raise ValueError(dtype)
            off = self._sb_base
            assert off + words <= SBUF_WORDS, (name, off, words)
            self._sb_base = off + words
            ap = self._backing[0:parts, off:off + words]
            if dtype != F32:
                ap = ap.bitcast(dtype)
                if dtype == BF16 and ap.shape[1] != n:
                    ap = ap[:, 0:n]
            if len(shape) > 1:
                dims = " ".join(f"d{i}" for i in range(1, len(shape)))
                ap = ap.rearrange(f"p ({dims}) -> p {dims}",
                                  **{f"d{i}": shape[i] for i in range(1, len(shape))})
            return ap
        return self.stack.enter_context(self.nc.sbuf_tensor(f"{name}{self.uid()}", shape, dtype))

    def psum_at(self, unit_off, name, shape, dtype=F32):
        assert self.fused and dtype == F32
        parts = shape[0]
        n = int(np.prod(shape[1:]))
        ap = self._pbacking[0:parts, unit_off * 512:unit_off * 512 + n]
        if len(shape) > 2:
            dims = " ".join(f"d{i}" for i in range(1, len(shape)))
            ap = ap.rearrange(f"p ({dims}) -> p {dims}",
                              **{f"d{i}": shape[i] for i in range(1, len(shape))})
        return ap

    def multi(self, engine, fns):
        def f(eng, sem):
            last = None
            for fn in fns:
                last = fn(eng)
            return last
        self.step(engine, 1, f)

    def psum(self, name, shape, dtype=F32):
        if self.fused:
            assert dtype == F32, name
            parts = shape[0]
            n = int(np.prod(shape[1:]))
            cols = ceil_div(n, 512) * 512
            off = self._ps_base
            assert off + cols <= PSUM_COLS, (name, off, cols)
            self._ps_base = off + cols
            ap = self._pbacking[0:parts, off:off + n]
            if len(shape) > 2:
                dims = " ".join(f"d{i}" for i in range(1, len(shape)))
                ap = ap.rearrange(f"p ({dims}) -> p {dims}",
                                  **{f"d{i}": shape[i] for i in range(1, len(shape))})
            return ap
        return self.stack.enter_context(self.nc.psum_tensor(f"{name}{self.uid()}", shape, dtype))

    def din(self, name, shape, dtype=F32):
        return self.nc.dram_tensor(name, list(shape), dtype, kind="ExternalInput").ap()

    def dout(self, name, shape, dtype=F32):
        return self.nc.dram_tensor(name, list(shape), dtype, kind="ExternalOutput").ap()

    def dtmp(self, name, shape, dtype=F32):
        return self.nc.dram_tensor(f"{name}{self.uid()}", list(shape), dtype, kind="Internal").ap()

    def step(self, engine, inc, fn):
        self.steps.append((engine, fn, inc))

    def op(self, engine, fn):
        self.step(engine, 1, lambda eng, sem: fn(eng))

    def ops(self, engine, fn, n):
        self.step(engine, 1, fn)

    def dma(self, out_ap, in_ap, engine="sync", **kw):
        def f(eng, sem):
            return eng.dma_start(out=out_ap, in_=in_ap, **kw)
        self.step(engine, 16, f)

    def dmas(self, pairs, engine="sync", **kw):
        def f(eng, sem):
            for o, i in pairs:
                eng.dma_start(out=o, in_=i, **kw).then_inc(sem, 16)
            return None
        self.step(engine, 16 * len(pairs), f)

    def mms(self, fn):
        self.step("tensor", 1, lambda eng, sem: fn(eng))

    def tt(self, out, a, b, op, engine="vector"):
        self.op(engine, lambda e: e.tensor_tensor(out=out, in0=a, in1=b, op=op))

    def ts(self, out, a, s1, op, engine="vector"):
        self.op(engine, lambda e: e.tensor_scalar(out=out, in0=a, scalar1=s1, scalar2=None, op0=op))

    def stt(self, out, in0, scalar, in1, op0, op1, engine="vector"):
        self.op(engine, lambda e: e.scalar_tensor_tensor(
            out=out, in0=in0, scalar=scalar, in1=in1, op0=op0, op1=op1))

    def act(self, out, in_, func, bias=0.0, scale=1.0, accum_out=None):
        self.op("scalar", lambda e: e.activation(
            out=out, in_=in_, func=func, bias=bias, scale=scale, accum_out=accum_out))

    def copy(self, out, in_, engine="vector"):
        if engine == "scalar":
            self.op(engine, lambda e: e.copy(out=out, in_=in_))
        else:
            self.op(engine, lambda e: e.tensor_copy(out=out, in_=in_))

    def memset(self, ap, v, engine="vector"):
        self.op(engine, lambda e: e.memset(ap, v))

    def scan(self, out, d0, d1, initial, op0=ALU.mult, op1=ALU.add):
        self.op("vector", lambda e: e.tensor_tensor_scan(
            out=out, data0=d0, data1=d1, initial=initial, op0=op0, op1=op1))

    def recip(self, out, in_):
        self.op("vector", lambda e: e.reciprocal(out=out, in_=in_))

    def transpose(self, out_psum, in_, ident):
        self.op("tensor", lambda e: e.transpose(out_psum, in_, ident))

    def reduce(self, out, in_, op):
        self.op("vector", lambda e: e.tensor_reduce(
            out=out, in_=in_, axis=mybir.AxisListType.X, op=op))

    def run(self, in_maps, core_ids, **kw):
        nc = self.nc
        waits = []
        k = 0
        for (en, fn, inc) in self.steps:
            waits.append(k)
            k += inc
        total = k
        with nc.semaphore("SS") as sem, nc.Block() as blk:
            for en_name in ENGINES:
                def mk(en_name):
                    def body(eng):
                        for (en, fn, inc), kb in zip(self.steps, waits):
                            if en != en_name:
                                continue
                            if kb:
                                eng.wait_ge(sem, kb)
                            inst = fn(eng, sem)
                            if inst is not None and inc:
                                inst.then_inc(sem, inc)
                    return body
                getattr(blk, en_name)(mk(en_name))

            def fin(eng):
                eng.wait_ge(sem, total)
            blk.gpsimd(fin)
        try:
            res = run_bass_kernel_spmd(nc, in_maps, core_ids=list(core_ids), **kw)
        finally:
            self.stack.close()
        return res


def ceil_div(a, b):
    return (a + b - 1) // b


def chunks(total, size):
    out = []
    o = 0
    while o < total:
        out.append((o, min(size, total - o)))
        o += size
    return out


def make_ident(sp, name, n, dtype):
    t = sp.sbuf(name, [n, n], dtype)
    sp.op("gpsimd", lambda e: e.memset(t[:], 0.0))
    sp.op("gpsimd", lambda e: e.affine_select(
        out=t[:], in_=t[:], compare_op=ALU.not_equal, fill=1.0,
        base=0, pattern=[[-1, n]], channel_multiplier=1))
    return t


def emit_linear(sp, out_psum_fn, w_tiles_fn, act_fn, nfi, N, start=True, stop=True):
    def f(eng):
        last = None
        for fi in range(nfi):
            last = eng.matmul(out_psum_fn(), w_tiles_fn(fi), act_fn(fi),
                              start=(start and fi == 0), stop=(stop and fi == nfi - 1))
        return last
    sp.mms(f)


def build_prep(sp=None, io=None):
    if sp is None:
        sp = SP()
    io = io or {}
    nc = sp.nc
    ins = {}
    outs = {}

    def inp(name, shape, dtype=F32):
        ins[name] = nc.dram_tensor(name, list(shape), dtype, kind="ExternalInput").ap()
        return ins[name]

    def out(name, shape, dtype=BF16):
        if name in io:
            outs[name] = io[name]
        else:
            outs[name] = nc.dram_tensor(name, list(shape), dtype, kind="ExternalOutput").ap()
        return outs[name]

    bi = sp.sbuf("ci", [128, 4096], F32)
    bo = sp.sbuf("co", [128, 4096], BF16)
    ident_f32 = make_ident(sp, "idf0", 128, F32)
    pstf0 = sp.psum("pstf0", [128, 128], F32)
    tbuf = sp.sbuf("tb", [128, 128], BF16)

    def cast_tensor(src_ap, dst_ap, R, C):
        assert R % 128 == 0 or R < 128
        for r0, rn in chunks(R, 128):
            for c0, cn in chunks(C, 4096):
                sp.dma(bi[:rn, :cn], src_ap[r0:r0 + rn, c0:c0 + cn])
                sp.copy(bo[:rn, :cn], bi[:rn, :cn])
                sp.dma(dst_ap[r0:r0 + rn, c0:c0 + cn], bo[:rn, :cn])

    casts = [
        ("L0_attn_Wq1", (1024, 1024)), ("L0_attn_Wq2", (1024, 1024)),
        ("L0_attn_Wk1", (1024, 128)), ("L0_attn_Wk2", (1024, 128)),
        ("L0_attn_Wv", (1024, 128)), ("L0_attn_Wout", (1024, 1024)),
        ("L1_ffn_shared_gate_w", (1024, 1024)), ("L1_ffn_shared_up_w", (1024, 1024)),
        ("L1_ffn_shared_down_w", (1024, 1024)),
        ("L3_attn_Wq", (1024, 1024)), ("L3_attn_Wk", (1024, 512)),
        ("L3_attn_Wv", (1024, 512)), ("L3_attn_memory", (512, 512)),
    ]
    for name, shp in casts:
        src = inp(name, shp)
        dst = out(name + "_b", shp)
        cast_tensor(src, dst, shp[0], shp[1])

    casts3 = [
        ("L0_ffn_W1", (8, 1024, 4096)), ("L0_ffn_W2", (8, 2048, 1024)),
        ("L1_ffn_gate_w", (16, 1024, 512)), ("L1_ffn_up_w", (16, 1024, 512)),
        ("L1_ffn_down_w", (16, 512, 1024)),
    ]
    for name, shp in casts3:
        src = inp(name, shp)
        dst = out(name + "_b", shp)
        e, r, c = shp
        cast_tensor(src.rearrange("e r c -> (e r) c"), dst.rearrange("e r c -> (e r) c"),
                    e * r, c)

    def transpose_tensor(name, R, C, src_ap=None, outname=None):
        src = src_ap if src_ap is not None else inp(name, (R, C))
        dst = out(outname or (name + "_tb"), (C, R))
        nR, nCc = ceil_div(R, 128), ceil_div(C, 128)
        for j in range(nCc):
            c0, cn = j * 128, min(128, C - j * 128)
            for i in range(nR):
                r0, rn = i * 128, min(128, R - i * 128)
                sp.dma(bi[:rn, :cn], src[r0:r0 + rn, c0:c0 + cn])
                sp.transpose(pstf0[:cn, :rn], bi[:rn, :cn], ident_f32[:rn, :rn])
                sp.copy(tbuf[:cn, :rn], pstf0[:cn, :rn], engine="scalar")
                sp.dma(dst[c0:c0 + cn, r0:r0 + rn], tbuf[:cn, :rn])

    transpose_tensor("L1_attn_w_in", 4096, 1024)
    transpose_tensor("L1_attn_w_x", 96, 2048)
    transpose_tensor("L1_attn_w_dt", 2048, 64)
    transpose_tensor("L1_attn_w_out", 1024, 2048)
    transpose_tensor("L2_attn_in_proj_w", 4096, 1024)
    transpose_tensor("L2_attn_B_w", 64, 2048)
    transpose_tensor("L2_attn_C_w", 64, 2048)
    transpose_tensor("L2_attn_dt_w", 2048, 2048)
    transpose_tensor("L2_attn_out_proj_w", 1024, 2048)

    ps1 = sp.psum("p1", [1, 8], F32)
    rowacc = sp.sbuf("racc", [128, 32], F32)
    onescol = sp.sbuf("ones", [128, 1], F32)
    sp.memset(onescol[:], 1.0)
    throw = sp.sbuf("thr", [1, 1], F32)
    thd = sp.dtmp("thd", [1, 1])
    thcol = sp.sbuf("thc", [128, 1], F32)
    absb = sp.sbuf("absb", [128, 4096], F32)
    sgnb = sp.sbuf("sgnb", [128, 4096], F32)

    sumcol = sp.sbuf("sumc", [128, 1], F32)

    def ternary_transpose(name, R, C):
        src = inp(name, (R, C))
        dst = out(name + "_qtb", (C, R))
        nR = R // 128
        for i in range(nR):
            sp.dma(bi[:, :C], src[i * 128:(i + 1) * 128, :])
            sp.op("scalar", lambda e, _i=i: e.activation(
                out=absb[:, :C], in_=bi[:, :C], func=AF.Abs, accum_out=rowacc[:, _i:_i + 1]))
        sp.reduce(sumcol[:], rowacc[:, 0:nR], ALU.add)
        def f(eng):
            return eng.matmul(ps1[0:1, 0:1], onescol[:], sumcol[:], start=True, stop=True)
        sp.mms(f)
        sp.act(throw[:], ps1[0:1, 0:1], AF.Copy, scale=0.5 / (R * C))
        sp.dma(thd, throw[:])
        sp.dma(thcol[:], thd.to_broadcast([128, 1]))
        for i in range(nR):
            sp.dma(bi[:, :C], src[i * 128:(i + 1) * 128, :])
            sp.op("scalar", lambda e: e.activation(out=absb[:, :C], in_=bi[:, :C], func=AF.Abs))
            sp.op("scalar", lambda e: e.sign(out=sgnb[:, :C], in_=bi[:, :C]))
            sp.ts(absb[:, :C], absb[:, :C], thcol[:], ALU.is_gt)
            sp.tt(bi[:, :C], sgnb[:, :C], absb[:, :C], ALU.mult)
            for j in range(C // 128):
                sp.transpose(pstf0[:], bi[:, j * 128:(j + 1) * 128], ident_f32[:])
                sp.copy(tbuf[:], pstf0[:], engine="scalar")
                sp.dma(dst[j * 128:(j + 1) * 128, i * 128:(i + 1) * 128], tbuf[:])

    ternary_transpose("L3_ffn_w_up", 4096, 1024)
    ternary_transpose("L3_ffn_w_down", 1024, 4096)

    upw = [inp(f"upw{k}", (12, 2048)) for k in range(4)]
    dnw = [inp(f"dnw{k}", (12, 2048)) for k in range(4)]
    W_exp = inp("L2_ffn_W_exp", (1024, 4096))
    b_exp = inp("L2_ffn_b_exp", (1, 4096))
    up_bias = inp("L2_ffn_up_bias", (1, 4096))
    W_con = inp("L2_ffn_W_con", (4096, 1024))
    b_con = inp("L2_ffn_b_con", (1, 1024))
    down_bias = inp("L2_ffn_down_bias", (1, 1024))

    wrow = sp.sbuf("wrow", [128, 4096], F32)
    wc = [sp.sbuf(f"wc{k}", [128, 2048], F32) for k in range(4)]
    tna = sp.sbuf("tna", [128, 2048], F32)
    tnb = sp.sbuf("tnb", [128, 2048], F32)
    t2 = sp.sbuf("t2b", [128, 2048], F32)
    ident_f = make_ident(sp, "idf", 128, F32)
    pstf = sp.psum("pstf", [128, 128], F32)
    tbf = sp.sbuf("tbf", [128, 128], F32)

    def fold_rowtile(weights4, nrow, reverse):
        stages = list(range(12))
        order = [0, 1, 2, 3] if not reverse else [0, 2, 1, 3]
        if reverse:
            stages = stages[::-1]
        for st in stages:
            s = 1 << st
            for k in range(4):
                sp.dma(wc[k][:nrow, :], weights4[order[k]][st:st + 1, :].to_broadcast([nrow, 2048]))
            y = wrow[:nrow, :].rearrange("p (nb two s) -> p nb two s", two=2, s=s)
            w4 = [wc[k][:nrow, :].rearrange("p (nb s) -> p nb s", s=s) for k in range(4)]
            a = y[:, :, 0, :]
            b_ = y[:, :, 1, :]
            nav = tna[:nrow, :].rearrange("p (nb s) -> p nb s", s=s)
            nbv = tnb[:nrow, :].rearrange("p (nb s) -> p nb s", s=s)
            tv = t2[:nrow, :].rearrange("p (nb s) -> p nb s", s=s)
            sp.tt(nav, a, w4[0], ALU.mult)
            sp.tt(tv, b_, w4[1], ALU.mult)
            sp.tt(nav, nav, tv, ALU.add)
            sp.tt(nbv, a, w4[2], ALU.mult)
            sp.tt(tv, b_, w4[3], ALU.mult)
            sp.tt(nbv, nbv, tv, ALU.add)
            sp.copy(a, nav)
            sp.copy(b_, nbv)

    W_exp_b = out("W_exp_fb", (1024, 4096))
    bexp_f = out("bexp_f", (1, 4096), F32)
    for rt in range(8):
        sp.dma(wrow[:], W_exp[rt * 128:(rt + 1) * 128, :])
        fold_rowtile(upw, 128, reverse=False)
        sp.copy(bo[:], wrow[:])
        sp.dma(W_exp_b[rt * 128:(rt + 1) * 128, :], bo[:])
    sp.dma(wrow[0:1, :], b_exp)
    fold_rowtile(upw, 1, reverse=False)
    sp.dma(bi[0:1, :4096], up_bias)
    sp.tt(wrow[0:1, :], wrow[0:1, :], bi[0:1, :4096], ALU.add)
    sp.dma(bexp_f, wrow[0:1, :])

    W_con_b = out("W_con_fb", (4096, 1024))
    for rt in range(8):
        for i in range(32):
            sp.dma(bi[:, :128], W_con[i * 128:(i + 1) * 128, rt * 128:(rt + 1) * 128])
            sp.transpose(pstf[:], bi[:, :128], ident_f[:])
            sp.copy(tbf[:], pstf[:], engine="scalar")
            sp.copy(wrow[:, i * 128:(i + 1) * 128], tbf[:])
        fold_rowtile(dnw, 128, reverse=True)
        for j in range(32):
            sp.transpose(pstf[:], wrow[:, j * 128:(j + 1) * 128], ident_f[:])
            sp.copy(tbuf[:], pstf[:], engine="scalar")
            sp.dma(W_con_b[j * 128:(j + 1) * 128, rt * 128:(rt + 1) * 128], tbuf[:])
    bcon_f = out("bcon_f", (1, 1024), F32)
    sp.dma(bi[0:1, :1024], b_con)
    sp.dma(tna[0:1, :1024], down_bias)
    sp.tt(bi[0:1, :1024], bi[0:1, :1024], tna[0:1, :1024], ALU.add)
    sp.dma(bcon_f, bi[0:1, :1024])

    lq1 = inp("L0_attn_lq1", (1, 64))
    lk1 = inp("L0_attn_lk1", (1, 64))
    lq2 = inp("L0_attn_lq2", (1, 64))
    lk2 = inp("L0_attn_lk2", (1, 64))
    lam_o = out("lam", (1, 1), F32)
    r1 = sp.sbuf("r1", [1, 64], F32)
    r2 = sp.sbuf("r2", [1, 64], F32)
    d1 = sp.sbuf("d1", [1, 1], F32)
    d2 = sp.sbuf("d2", [1, 1], F32)
    sp.dma(r1[:], lq1)
    sp.dma(r2[:], lk1)
    sp.tt(r1[:], r1[:], r2[:], ALU.mult)
    sp.op("scalar", lambda e: e.activation(out=r1[:], in_=r1[:], func=AF.Copy, accum_out=d1[:]))
    sp.dma(r1[:], lq2)
    sp.dma(r2[:], lk2)
    sp.tt(r1[:], r1[:], r2[:], ALU.mult)
    sp.op("scalar", lambda e: e.activation(out=r1[:], in_=r1[:], func=AF.Copy, accum_out=d2[:]))
    sp.act(d1[:], d1[:], AF.Exp)
    sp.act(d2[:], d2[:], AF.Exp)
    sp.tt(d1[:], d1[:], d2[:], ALU.subtract)
    sp.ts(d1[:], d1[:], 0.8, ALU.add)
    sp.ts(d1[:], d1[:], 0.9, ALU.min)
    sp.ts(d1[:], d1[:], 0.1, ALU.max)
    sp.dma(lam_o, d1[:])

    return sp, ins, outs


_PREP_KEYS3 = None


def run_prep(inp):
    sp, ins, outs = build_prep()
    feed = {}
    for name in ins:
        if name.startswith("upw") or name.startswith("dnw"):
            k = int(name[3])
            src = inp["L2_ffn_up_weights" if name[0] == "u" else "L2_ffn_down_weights"]
            feed[name] = np.ascontiguousarray(src[:, :, k // 2, k % 2])
        elif name in ("L2_ffn_b_exp", "L2_ffn_up_bias", "L2_ffn_b_con", "L2_ffn_down_bias"):
            feed[name] = np.ascontiguousarray(inp[name].reshape(1, -1))
        elif name.startswith("L0_attn_l"):
            feed[name] = np.ascontiguousarray(inp[name].reshape(1, -1))
        else:
            feed[name] = np.ascontiguousarray(inp[name])
    res = sp.run([feed], core_ids=[0])
    return dict(res.results[0])


class Arena:
    def __init__(self, sp, words=50176):
        if sp.fused:
            assert sp._sb_base == 0, "allocate Arena before smalls in fused mode"
            self.t = sp._backing[:, 0:words]
            sp._sb_base = words
        else:
            self.t = sp.sbuf("AR", [128, words], F32)
        self.words = words

    def f32(self, off, shape):
        n = int(np.prod(shape))
        assert off + n <= self.words, (off, shape)
        ap = self.t[:, off:off + n]
        if len(shape) > 1:
            dims = " ".join(f"d{i}" for i in range(len(shape)))
            ap = ap.rearrange(f"p ({dims}) -> p {dims}",
                              **{f"d{i}": shape[i] for i in range(len(shape))})
        return ap

    def row(self, off, n):
        assert off + n <= self.words
        return self.t[0:1, off:off + n]

    def bf16(self, off, shape):
        n = int(np.prod(shape))
        assert n % 2 == 0
        assert off + n // 2 <= self.words, (off, shape)
        ap = self.t[:, off:off + n // 2].bitcast(BF16)
        if len(shape) > 1:
            dims = " ".join(f"d{i}" for i in range(len(shape)))
            ap = ap.rearrange(f"p ({dims}) -> p {dims}",
                              **{f"d{i}": shape[i] for i in range(len(shape))})
        return ap


def emit_ln(sp, ar, x_sb, xn_f32, xn_bf, rows_off, ps_row, ps_b0, ps_b1, ones_col, ones_row,
            ntok=TC, nf=8, eps=1e-5, sq_off=None):
    NTC = ceil_div(ntok, 512)
    mrow = ar.f32(rows_off, (1, ntok))
    srow = ar.f32(rows_off + ntok, (1, ntok))
    sinv = ar.f32(rows_off + 2 * ntok, (1, ntok))
    msr = ar.f32(rows_off + 3 * ntok, (1, ntok))
    sq = ar.f32(sq_off, (nf, 512))
    for tc in range(NTC):
        t0 = tc * 512
        tn = min(512, ntok - t0)
        tsl = slice(t0, t0 + tn)
        for a in range(nf):
            sp.tt(sq[:, a, :tn].rearrange("p a t -> p (a t)") if False else sq[:, a, :tn],
                  x_sb[:, a, tsl], x_sb[:, a, tsl], ALU.mult)
        def fmean(eng, _tsl=tsl, _tn=tn):
            last = None
            for a in range(nf):
                last = eng.matmul(ps_row[0:1, :_tn], ones_col[:], x_sb[:, a, _tsl],
                                  start=(a == 0), stop=(a == nf - 1))
            return last
        sp.mms(fmean)
        sp.act(mrow[:, tsl], ps_row[0:1, :tn], AF.Copy, scale=1.0 / (nf * 128))
        def fsq(eng, _tn=tn):
            last = None
            for a in range(nf):
                last = eng.matmul(ps_row[0:1, :_tn], ones_col[:], sq[:, a, :_tn],
                                  start=(a == 0), stop=(a == nf - 1))
            return last
        sp.mms(fsq)
        sp.act(srow[:, tsl], ps_row[0:1, :tn], AF.Copy, scale=1.0 / (nf * 128))
    sp.tt(msr[:], mrow[:], mrow[:], ALU.mult)
    sp.tt(srow[:], srow[:], msr[:], ALU.subtract)
    sp.act(srow[:], srow[:], AF.Sqrt, bias=eps)
    sp.recip(sinv[:], srow[:])
    sp.tt(msr[:], mrow[:], sinv[:], ALU.mult)
    for tc in range(NTC):
        t0 = tc * 512
        tn = min(512, ntok - t0)
        tsl = slice(t0, t0 + tn)
        def fb(eng, _tsl=tsl, _tn=tn):
            eng.matmul(ps_b0[:, :_tn], ones_row[:], sinv[:, _tsl], start=True, stop=True)
            return eng.matmul(ps_b1[:, :_tn], ones_row[:], msr[:, _tsl], start=True, stop=True)
        sp.mms(fb)
        for a in range(nf):
            if xn_f32 is not None:
                sp.tt(xn_f32[:, a, tsl], x_sb[:, a, tsl], ps_b0[:, :tn], ALU.mult)
                sp.tt(xn_f32[:, a, tsl], xn_f32[:, a, tsl], ps_b1[:, :tn], ALU.subtract)
                sp.copy(xn_bf[:, a, tsl], xn_f32[:, a, tsl])
            else:
                tmp = sq[:, 0, :tn]
                sp.tt(tmp, x_sb[:, a, tsl], ps_b0[:, :tn], ALU.mult)
                sp.tt(xn_bf[:, a, tsl], tmp, ps_b1[:, :tn], ALU.subtract)


def emit_gating_top2(sp, ar, goff, lsb8, c_dram, nexp, renorm, scale_out, goff_mk=None, goff_r=None):
    l_dram = sp.dtmp("gl", (nexp, TC))
    sp.dma(l_dram, lsb8)
    if goff_mk is None:
        goff_mk = goff + nexp * 512
    if goff_r is None:
        goff_r = goff_mk + nexp * 512
    le = [ar.row(goff + e * 512, 512) for e in range(nexp)]
    mk = [ar.row(goff_mk + e * 512, 512) for e in range(nexp)]
    r0 = goff_r
    m1 = ar.row(r0, 512)
    m2 = ar.row(r0 + 512, 512)
    t1 = ar.row(r0 + 1024, 512)
    t2 = ar.row(r0 + 1536, 512)
    for tc in range(TC // 512):
        tsl = slice(tc * 512, tc * 512 + 512)
        sp.dmas([(le[e], l_dram[e:e + 1, tsl]) for e in range(nexp)])
        sp.tt(m1, le[0], le[1], ALU.max)
        for e in range(2, nexp):
            sp.tt(m1, m1, le[e], ALU.max)
        if renorm:
            for e in range(nexp):
                sp.tt(mk[e], le[e], m1, ALU.is_equal)
                sp.stt(mk[e], mk[e], -1e30, le[e], ALU.mult, ALU.add)
            sp.tt(m2, mk[0], mk[1], ALU.max)
            for e in range(2, nexp):
                sp.tt(m2, m2, mk[e], ALU.max)
            for e in range(nexp):
                sp.tt(mk[e], le[e], m2, ALU.is_ge)
            sp.tt(t1, m2, m1, ALU.subtract)
            sp.act(t1, t1, AF.Exp)
            sp.ts(t1, t1, 1.0, ALU.add)
            sp.recip(t1, t1)
            sp.ts(t1, t1, scale_out, ALU.mult)
            for e in range(nexp):
                sp.tt(le[e], le[e], m1, ALU.subtract)
                sp.act(le[e], le[e], AF.Exp)
                sp.tt(le[e], le[e], mk[e], ALU.mult)
                sp.tt(le[e], le[e], t1, ALU.mult)
        else:
            for e in range(nexp):
                sp.tt(le[e], le[e], m1, ALU.subtract)
                sp.act(le[e], le[e], AF.Exp)
            sp.tt(t1, le[0], le[1], ALU.add)
            for e in range(2, nexp):
                sp.tt(t1, t1, le[e], ALU.add)
            sp.recip(t1, t1)
            for e in range(nexp):
                sp.tt(le[e], le[e], t1, ALU.mult)
            sp.tt(m1, le[0], le[1], ALU.max)
            for e in range(2, nexp):
                sp.tt(m1, m1, le[e], ALU.max)
            for e in range(nexp):
                sp.tt(mk[e], le[e], m1, ALU.is_equal)
                sp.stt(mk[e], mk[e], -1e30, le[e], ALU.mult, ALU.add)
            sp.tt(m2, mk[0], mk[1], ALU.max)
            for e in range(2, nexp):
                sp.tt(m2, m2, mk[e], ALU.max)
            for e in range(nexp):
                sp.tt(mk[e], le[e], m2, ALU.is_ge)
                sp.tt(le[e], le[e], mk[e], ALU.mult)
                sp.ts(le[e], le[e], scale_out, ALU.mult)
        sp.dmas([(c_dram[e:e + 1, tsl], le[e]) for e in range(nexp)])


def build_moe0(sp=None, io=None):
    if sp is None:
        sp = SP(fused=True)
    io = io or {}
    def din(n, s, d=F32):
        return io[n] if n in io else sp.din(n, s, d)
    def dout(n, s, d=F32):
        return io[n] if n in io else sp.dout(n, s, d)
    xT = din("xT", (1024, TC))
    W1b = din("W1b", (8, 1024, 4096), BF16)
    W2b = din("W2b", (8, 2048, 1024), BF16)
    WrL = din("WrL", (128, 8, 8))
    brC = din("brC", (8, 1))
    b1L = din("b1L", (128, 32, 8))
    b2L = din("b2L", (128, 8, 8))
    yT = dout("yT", (1024, TC))
    c_dram = sp.dtmp("cdram", (8, TC))

    ar = Arena(sp, 40960)
    x_sb = ar.f32(0, (8, TC))
    xn_bf = ar.bf16(16384, (8, TC))
    sq = ar.f32(24576, (8, 512))
    rows_off = 28672
    goff = 25088
    act = ar.bf16(24576, (16, TC))
    acc = ar.f32(0, (8, TC))

    ones_col = sp.sbuf("onc", [128, 1], F32)
    ones_r8 = sp.sbuf("o8", [1, 8], F32)
    ones_r128 = sp.sbuf("o128", [1, 128], F32)
    wr_sb = sp.sbuf("wr", [128, 8, 8], F32)
    br_sb = sp.sbuf("brs", [8, 1], F32)
    b1_sb = sp.sbuf("b1s", [128, 32, 8], F32)
    b2_sb = sp.sbuf("b2s", [128, 8, 8], F32)
    ncol = sp.sbuf("ncl", [8, 1], F32)
    xr_sb = ar.t[0:8, 38912:40960]
    lsb8 = ar.t[0:8, 36864:38912]
    w1a = sp.sbuf("w1a", [128, 8, 128], BF16)
    w1g = sp.sbuf("w1g", [128, 8, 128], BF16)
    w2f = sp.sbuf("w2f", [128, 16, 128], BF16)
    c_rep = sp.sbuf("crp", [128, TC], F32)
    silu_a = sp.sbuf("sla", [128, TC], F32)
    tmp_w = sp.sbuf("tmw", [128, TC], F32)
    xload = tmp_w

    psSt = sp.psum_at(0, "pst", (1, 1024))
    psB0 = sp.psum_at(0, "pb0", (128, TC))
    psB1 = sp.psum_at(4, "pb1", (128, TC))
    ps_r8 = sp.psum_at(0, "pr8", (8, TC))
    ps8a = sp.psum_at(4, "p8a", (8, TC))
    ps8b = sp.psum_at(0, "p8b", (8, TC))
    psA = sp.psum_at(0, "psa", (128, TC))
    psG = sp.psum_at(4, "psg", (128, TC))
    psO = sp.psum_at(0, "pso", (128, TC))

    sp.multi("gpsimd", [lambda e: e.memset(ones_col[:], 1.0),
                        lambda e: e.memset(ones_r8[:], 1.0),
                        lambda e: e.memset(ones_r128[:], 1.0)])
    sp.dmas([(x_sb, xT.rearrange("(a p) t -> p a t", p=128)),
             (wr_sb[:], WrL), (br_sb[:], brC), (b1_sb[:], b1L), (b2_sb[:], b2L)])

    mrow = ar.row(rows_off, TC)
    srow = ar.row(rows_off + TC, TC)
    sinv = ar.row(rows_off + 2 * TC, TC)
    msr = ar.row(rows_off + 3 * TC, TC)
    NF = 8
    for tc in range(4):
        tsl = slice(tc * 512, tc * 512 + 512)
        sp.multi("vector", [
            (lambda e, _a=a, _tsl=tsl: e.tensor_tensor(out=sq[:, _a, :], in0=x_sb[:, _a, _tsl],
                                                       in1=x_sb[:, _a, _tsl], op=ALU.mult))
            for a in range(NF)])
        def fstat(eng, _tsl=tsl):
            last = None
            for a in range(NF):
                eng.matmul(psSt[0:1, 0:512], ones_col[:], x_sb[:, a, _tsl],
                           start=(a == 0), stop=(a == NF - 1))
            for a in range(NF):
                last = eng.matmul(psSt[0:1, 512:1024], ones_col[:], sq[:, a, :],
                                  start=(a == 0), stop=(a == NF - 1))
            return last
        sp.mms(fstat)
        sp.multi("scalar", [
            lambda e, _tsl=tsl: e.activation(out=mrow[:, _tsl], in_=psSt[0:1, 0:512],
                                             func=AF.Copy, scale=1.0 / 1024),
            lambda e, _tsl=tsl: e.activation(out=srow[:, _tsl], in_=psSt[0:1, 512:1024],
                                             func=AF.Copy, scale=1.0 / 1024)])
    sp.tt(msr[:], mrow[:], mrow[:], ALU.mult)
    sp.tt(srow[:], srow[:], msr[:], ALU.subtract)
    sp.ts(srow[:], srow[:], 1e-5, ALU.add)
    sp.act(srow[:], srow[:], AF.Sqrt)
    sp.recip(sinv[:], srow[:])
    sp.tt(msr[:], mrow[:], sinv[:], ALU.mult)
    def fb(eng):
        last = None
        for tc in range(4):
            tsl = slice(tc * 512, tc * 512 + 512)
            eng.matmul(psB0[:, tsl], ones_r128[:], sinv[:, tsl], start=True, stop=True)
            last = eng.matmul(psB1[:, tsl], ones_r128[:], msr[:, tsl], start=True, stop=True)
        return last
    sp.mms(fb)
    for a in range(NF):
        sp.tt(tmp_w[:], x_sb[:, a, :], psB0[:, :], ALU.mult)
        sp.tt(xn_bf[:, a, :], tmp_w[:], psB1[:, :], ALU.subtract)
    def fcs(eng):
        last = None
        for a in range(NF):
            last = eng.matmul(ps8b[0:8, 0:1], wr_sb[:, a, :], ones_col[:],
                              start=(a == 0), stop=(a == NF - 1))
        return last
    sp.mms(fcs)
    sp.act(ncol[:], ps8b[0:8, 0:1], AF.Copy, scale=-1.0)
    def fxr(eng):
        last = None
        for tc in range(4):
            tsl = slice(tc * 512, tc * 512 + 512)
            for a in range(NF):
                last = eng.matmul(ps_r8[:8, tsl], wr_sb[:, a, :], x_sb[:, a, tsl],
                                  start=(a == 0), stop=(a == NF - 1))
        return last
    sp.mms(fxr)
    sp.copy(xr_sb, ps_r8[:8, :], engine="scalar")
    def fb8(eng):
        last = None
        for tc in range(4):
            tsl = slice(tc * 512, tc * 512 + 512)
            eng.matmul(ps8a[:8, tsl], ones_r8[:], sinv[:, tsl], start=True, stop=True)
            last = eng.matmul(ps8b[:8, tsl], ones_r8[:], msr[:, tsl], start=True, stop=True)
        return last
    sp.mms(fb8)
    sp.tt(xr_sb, xr_sb, ps8a[:8, :], ALU.mult)
    sp.stt(lsb8, ps8b[:8, :], ncol[:], xr_sb, ALU.mult, ALU.add)
    sp.ts(lsb8, lsb8, br_sb[:], ALU.add)

    emit_gating_top2(sp, ar, goff, lsb8, c_dram, 8, renorm=True, scale_out=0.5)

    sp.memset(acc.rearrange("p a t -> p (a t)"), 0.0, engine="gpsimd")
    for e in range(8):
        sp.dmas([(c_rep[:, tc * 512:(tc + 1) * 512],
                  c_dram[e:e + 1, tc * 512:(tc + 1) * 512].to_broadcast([128, 512]))
                 for tc in range(4)])
        for h in range(16):
            sp.dmas([(w1a[:], W1b[e, :, h * 128:(h + 1) * 128].rearrange("(a p) f -> p a f", p=128)),
                     (w1g[:], W1b[e, :, 2048 + h * 128:2048 + (h + 1) * 128].rearrange("(a p) f -> p a f", p=128))])
            def fup(eng):
                last = None
                for tc in range(4):
                    tsl = slice(tc * 512, tc * 512 + 512)
                    for a in range(NF):
                        eng.matmul(psA[:, tsl], w1a[:, a, :], xn_bf[:, a, tsl],
                                   start=(a == 0), stop=(a == NF - 1))
                    for a in range(NF):
                        last = eng.matmul(psG[:, tsl], w1g[:, a, :], xn_bf[:, a, tsl],
                                          start=(a == 0), stop=(a == NF - 1))
                return last
            sp.mms(fup)
            def fsilu(eng, _h=h, _e=e):
                return eng.activation(out=silu_a[:], in_=psA[:, :], func=AF.Silu,
                                      bias=b1_sb[:, _h, _e:_e + 1])
            sp.op("scalar", fsilu)
            def fstt(eng, _h=h, _e=e):
                return eng.scalar_tensor_tensor(out=act[:, _h, :], in0=psG[:, :],
                                                scalar=b1_sb[:, 16 + _h, _e:_e + 1],
                                                in1=silu_a[:], op0=ALU.add, op1=ALU.mult)
            sp.op("vector", fstt)
        for fo in range(8):
            sp.dma(w2f[:], W2b[e, :, fo * 128:(fo + 1) * 128].rearrange("(hh p) f -> p hh f", p=128))
            def fdn(eng):
                last = None
                for tc in range(4):
                    tsl = slice(tc * 512, tc * 512 + 512)
                    for hh in range(16):
                        last = eng.matmul(psO[:, tsl], w2f[:, hh, :], act[:, hh, tsl],
                                          start=(hh == 0), stop=(hh == 15))
                return last
            sp.mms(fdn)
            def fev(eng, _fo=fo, _e=e):
                return eng.scalar_tensor_tensor(out=tmp_w[:], in0=psO[:, :],
                                                scalar=b2_sb[:, _fo, _e:_e + 1],
                                                in1=c_rep[:], op0=ALU.add, op1=ALU.mult)
            sp.op("vector", fev)
            sp.tt(acc[:, fo, :], acc[:, fo, :], tmp_w[:], ALU.add)

    for fo in range(8):
        sp.dma(xload[:], xT[fo * 128:(fo + 1) * 128, :])
        sp.tt(xload[:], xload[:], acc[:, fo, :], ALU.add)
        sp.dma(yT[fo * 128:(fo + 1) * 128, :], xload[:])
    return sp

def _stage_moe0(inp, Wp):
    Wr = inp["L0_ffn_Wr"]
    b1 = inp["L0_ffn_b1"]
    b2 = inp["L0_ffn_b2"]
    return {
        "W1b": np.ascontiguousarray(Wp["L0_ffn_W1_b"]),
        "W2b": np.ascontiguousarray(Wp["L0_ffn_W2_b"]),
        "WrL": np.ascontiguousarray(Wr.reshape(8, 128, 8).transpose(1, 0, 2)),
        "brC": np.ascontiguousarray(inp["L0_ffn_br"].reshape(8, 1)),
        "b1L": np.ascontiguousarray(b1.T.reshape(32, 128, 8).transpose(1, 0, 2)),
        "b2L": np.ascontiguousarray(b2.T.reshape(8, 128, 8).transpose(1, 0, 2)),
    }


def run_moe0(X, inp, Wp):
    sp = build_moe0()
    shared = _stage_moe0(inp, Wp)
    in_maps = []
    for c in range(NC):
        m = dict(shared)
        m["xT"] = np.ascontiguousarray(X[c * TC:(c + 1) * TC].T)
        in_maps.append(m)
    res = sp.run(in_maps, core_ids=range(NC))
    out = np.empty_like(X)
    for c in range(NC):
        out[c * TC:(c + 1) * TC] = res.results[c]["yT"].T
    return out


def build_attn0(sp=None, io=None):
    NW = 1152
    if sp is None:
        sp = SP()
    io = io or {}
    def din(n, s, d=F32):
        return io[n] if n in io else sp.din(n, s, d)
    def dout(n, s, d=F32):
        return io[n] if n in io else sp.dout(n, s, d)
    xwT = None if "xw_load" in (io or {}) else din("xwT", (1024, NW))
    xrT = din("xrT", (1024, TC))
    Wq1b = din("Wq1b", (1024, 1024), BF16)
    Wq2b = din("Wq2b", (1024, 1024), BF16)
    Wk1b = din("Wk1b", (1024, 128), BF16)
    Wk2b = din("Wk2b", (1024, 128), BF16)
    Wvb = din("Wvb", (1024, 128), BF16)
    Woutb = din("Woutb", (1024, 1024), BF16)
    boutL = din("boutL", (128, 8))
    lamC = din("lamC", (128, 1))
    posrow = din("posrow", (1, NW))
    divrow = din("divrow", (1, 32))
    yT = dout("yT", (1024, TC))

    ar = Arena(sp, 30720)
    x_sb = ar.f32(0, (8, NW))
    xn_bf = ar.bf16(9216, (8, NW))
    rows_off = 13824
    def v64(off, nh):
        return ar.t[0:64, off:off + nh * NW // 2].bitcast(BF16).rearrange(
            "p (h t) -> p h t", h=nh)
    qrot1_v = v64(0, 16)
    qrot2_v = v64(18432, 16)
    krot1_v = v64(27648, 2)
    krot2_v = v64(28800, 2)

    ones_col = sp.sbuf("onc", [128, 1], F32)
    ones_col_bf = sp.sbuf("onb", [128, 1], BF16)
    ones_r128 = sp.sbuf("o12", [1, 128], F32)
    ones_r64 = sp.sbuf("o64", [1, 64], F32)
    sq = sp.sbuf("sq", [128, 512], F32)
    tmp = sp.sbuf("tmp", [128, 512], F32)
    v_sb = sp.sbuf("vsb", [128, NW], F32)
    wq_sb = sp.sbuf("wqs", [128, 8, 128], BF16)
    wout_sb = sp.sbuf("wos", [128, 8, 1024], BF16)
    bout_sb = sp.sbuf("bos", [128, 8], F32)
    lam_sb = sp.sbuf("lam", [128, 1], F32)
    cos_sb = sp.sbuf("cos", [32, NW], F32)
    sin_sb = sp.sbuf("sin", [32, NW], F32)
    th_sb = sp.sbuf("ths", [32, NW], F32)
    msk_sb = sp.sbuf("msk", [32, NW], F32)
    vt_sb = sp.sbuf("vts", [128, 2, 2, 64], BF16)
    expT = sp.sbuf("exT", [128, 256], BF16)
    expT2 = sp.sbuf("ex2", [128, 256], BF16)
    rrow = sp.sbuf("rrw", [1, 256], F32)
    rrep = sp.sbuf("rrp", [128, 256], F32)
    a_b1 = sp.sbuf("ab1", [128, 256], F32)
    a_b2 = sp.sbuf("ab2", [128, 256], F32)
    aM = sp.sbuf("aM", [128, 8, 256], BF16)
    xr_load = sp.sbuf("xrl", [128, 256], F32)
    t32a = sp.sbuf("t3a", [32, 512], F32)
    t32b = sp.sbuf("t3b", [32, 512], F32)

    psE = sp.psum("psE", [32, 512])
    psO32 = sp.psum("psO", [32, 512])
    psS = sp.psum("psS", [128, 512])
    psW = sp.psum("psW", [128, 512])
    psPV = sp.psum("psP", [128, 256])
    psR = sp.psum("psR", [1, 256])
    psB = sp.psum("psB", [128, 256])
    psT = sp.psum("psT", [128, 64], F32)
    identb = make_ident(sp, "idb", 128, F32)

    sp.memset(ones_col[:], 1.0, engine="gpsimd")
    sp.memset(ones_col_bf[:], 1.0, engine="gpsimd")
    sp.memset(ones_r128[:], 1.0, engine="gpsimd")
    sp.memset(ones_r64[:], 1.0, engine="gpsimd")
    if "xw_load" in io:
        io["xw_load"](sp, x_sb)
    else:
        sp.dmas([(x_sb, xwT.rearrange("(a p) t -> p a t", p=128))])
    sp.dmas([(wout_sb[:], Woutb.rearrange("(a p) f -> p a f", p=128)),
             (bout_sb[:], boutL), (lam_sb[:], lamC)])
    sp.ts(lam_sb[:], lam_sb[:], -1.0, ALU.mult)

    mrow = ar.row(rows_off, NW)
    srow = ar.row(rows_off + NW, NW)
    sinv = ar.row(rows_off + 2 * NW, NW)
    msr = ar.row(rows_off + 3 * NW, NW)
    tch = chunks(NW, 512)
    for (t0, tn) in tch:
        tsl = slice(t0, t0 + tn)
        def fmean(eng, _tsl=tsl, _tn=tn):
            last = None
            for a in range(8):
                last = eng.matmul(psE[0:1, :_tn], ones_col[:], x_sb[:, a, _tsl],
                                  start=(a == 0), stop=(a == 7))
            return last
        sp.mms(fmean)
        sp.act(mrow[:, tsl], psE[0:1, :tn], AF.Copy, scale=1.0 / 1024)
        for a in range(8):
            sp.tt(sq[:, :tn], x_sb[:, a, tsl], x_sb[:, a, tsl], ALU.mult)
            def fsq(eng, _a=a, _tn=tn):
                return eng.matmul(psO32[0:1, :_tn], ones_col[:], sq[:, :_tn],
                                  start=(_a == 0), stop=(_a == 7))
            sp.mms(fsq)
        sp.act(srow[:, tsl], psO32[0:1, :tn], AF.Copy, scale=1.0 / 1024)
    sp.tt(msr[:], mrow[:], mrow[:], ALU.mult)
    sp.tt(srow[:], srow[:], msr[:], ALU.subtract)
    sp.ts(srow[:], srow[:], 1e-5, ALU.add)
    sp.act(srow[:], srow[:], AF.Sqrt)
    sp.recip(sinv[:], srow[:])
    sp.tt(msr[:], mrow[:], sinv[:], ALU.mult)
    for (t0, tn) in tch:
        tsl = slice(t0, t0 + tn)
        def fb(eng, _tsl=tsl, _tn=tn):
            eng.matmul(psW[:, :_tn], ones_r128[:], sinv[:, _tsl], start=True, stop=True)
            return eng.matmul(psS[:, :_tn], ones_r128[:], msr[:, _tsl], start=True, stop=True)
        sp.mms(fb)
        for a in range(8):
            sp.tt(tmp[:, :tn], x_sb[:, a, tsl], psW[:, :tn], ALU.mult)
            sp.tt(xn_bf[:, a, tsl], tmp[:, :tn], psS[:, :tn], ALU.subtract)

    div_sb = sp.sbuf("div", [1, 32], F32)
    pos_sb = sp.sbuf("pos", [1, NW], F32)
    sp.dmas([(div_sb[:], divrow), (pos_sb[:], posrow)])
    for (t0, tn) in tch:
        tsl = slice(t0, t0 + tn)
        def fth(eng, _tsl=tsl, _tn=tn):
            return eng.matmul(psE[0:32, :_tn], div_sb[:], pos_sb[0:1, _tsl],
                              start=True, stop=True)
        sp.mms(fth)
        sp.copy(th_sb[:, tsl], psE[0:32, :tn], engine="scalar")
    TWO_PI = float(2 * np.pi)
    i32t = sp.sbuf("i32", [32, NW], mybir.dt.int32)
    sp.ts(th_sb[:], th_sb[:], float(1.0 / TWO_PI), ALU.mult)
    sp.copy(i32t[:], th_sb[:])
    sp.copy(msk_sb[:], i32t[:])
    sp.stt(th_sb[:], msk_sb[:], -1.0, th_sb[:], ALU.mult, ALU.add)
    sp.ts(th_sb[:], th_sb[:], TWO_PI, ALU.mult)
    sp.ts(msk_sb[:], th_sb[:], float(np.pi), ALU.is_gt)
    sp.stt(th_sb[:], msk_sb[:], -TWO_PI, th_sb[:], ALU.mult, ALU.add)
    sp.ts(msk_sb[:], th_sb[:], float(-np.pi), ALU.is_lt)
    sp.stt(th_sb[:], msk_sb[:], TWO_PI, th_sb[:], ALU.mult, ALU.add)
    sp.act(sin_sb[:], th_sb[:], AF.Sin)
    sp.ts(th_sb[:], th_sb[:], float(np.pi / 2), ALU.add)
    sp.ts(msk_sb[:], th_sb[:], float(np.pi), ALU.is_gt)
    sp.stt(th_sb[:], msk_sb[:], -TWO_PI, th_sb[:], ALU.mult, ALU.add)
    sp.act(cos_sb[:], th_sb[:], AF.Sin)

    def project_rope(Wsrc, nfo, rot_dst):
        for fo in range(nfo):
            sp.dma(wq_sb[:], Wsrc[:, fo * 128:(fo + 1) * 128].rearrange("(a p) f -> p a f", p=128))
            wv = wq_sb[:].rearrange("p a (hh k par) -> p a hh k par", hh=2, par=2)
            for hh in range(2):
                h = fo * 2 + hh
                for (t0, tn) in tch:
                    tsl = slice(t0, t0 + tn)
                    def fe(eng, _hh=hh, _tsl=tsl, _tn=tn):
                        last = None
                        for a in range(8):
                            last = eng.matmul(psE[:, :_tn], wv[:, a, _hh, :, 0], xn_bf[:, a, _tsl],
                                              start=(a == 0), stop=(a == 7))
                        return last
                    sp.mms(fe)
                    def fodd(eng, _hh=hh, _tsl=tsl, _tn=tn):
                        last = None
                        for a in range(8):
                            last = eng.matmul(psO32[:, :_tn], wv[:, a, _hh, :, 1], xn_bf[:, a, _tsl],
                                              start=(a == 0), stop=(a == 7))
                        return last
                    sp.mms(fodd)
                    sp.tt(t32a[:, :tn], psE[0:32, :tn], cos_sb[:, tsl], ALU.mult)
                    sp.tt(t32b[:, :tn], psO32[0:32, :tn], sin_sb[:, tsl], ALU.mult)
                    sp.tt(rot_dst[0:32, h, tsl], t32a[:, :tn], t32b[:, :tn], ALU.subtract)
                    sp.tt(t32a[:, :tn], psE[0:32, :tn], sin_sb[:, tsl], ALU.mult)
                    sp.tt(t32b[:, :tn], psO32[0:32, :tn], cos_sb[:, tsl], ALU.mult)
                    sp.tt(rot_dst[32:64, h, tsl], t32a[:, :tn], t32b[:, :tn], ALU.add)

    project_rope(Wq1b, 8, qrot1_v)
    project_rope(Wq2b, 8, qrot2_v)
    project_rope(Wk1b, 1, krot1_v)
    project_rope(Wk2b, 1, krot2_v)
    sp.dma(wq_sb[:], Wvb.rearrange("(a p) f -> p a f", p=128))
    for (t0, tn) in tch:
        tsl = slice(t0, t0 + tn)
        def fv(eng, _tsl=tsl, _tn=tn):
            last = None
            for a in range(8):
                last = eng.matmul(psS[:, :_tn], wq_sb[:, a, :], xn_bf[:, a, _tsl],
                                  start=(a == 0), stop=(a == 7))
            return last
        sp.mms(fv)
        sp.copy(v_sb[:, tsl], psS[:, :tn], engine="scalar")

    for n in range(8):
        wsl = slice(n * 128, n * 128 + 256)
        for g in range(2):
            for half in range(2):
                ksl = slice(n * 128 + half * 128, n * 128 + half * 128 + 128)
                sp.transpose(psT[:, :], v_sb[g * 64:(g + 1) * 64, ksl],
                             identb[g * 64:(g + 1) * 64, g * 64:(g + 1) * 64])
                sp.copy(vt_sb[:, g, half, :], psT[:, :], engine="scalar")
        for h in range(16):
            g = h % 2
            for br, (qro, kro, adst) in enumerate([(qrot1_v, krot1_v, a_b1), (qrot2_v, krot2_v, a_b2)]):
                ob = (h % 2) * 64
                def fsT2(eng, _qro=qro, _kro=kro, _g=g, _h=h, _n=n):
                    last = None
                    for half in range(2):
                        ksl = slice(_n * 128 + half * 128, _n * 128 + half * 128 + 128)
                        last = eng.matmul(psS[:, half * 256:half * 256 + 256],
                                          _kro[0:64, _g, ksl],
                                          _qro[0:64, _h, _n * 128:_n * 128 + 256],
                                          start=True, stop=True)
                    return last
                sp.mms(fsT2)
                sp.act(expT[:, :], psS[:, 0:256], AF.Exp, scale=0.125)
                sp.act(expT2[:, :], psS[:, 256:512], AF.Exp, scale=0.125)
                def fpv(eng, _g=g, _ob=ob):
                    eng.matmul(psPV[_ob:_ob + 64, :], vt_sb[:, _g, 0, :], expT[:, :],
                               start=True, stop=False)
                    return eng.matmul(psPV[_ob:_ob + 64, :], vt_sb[:, _g, 1, :], expT2[:, :],
                                      start=False, stop=True)
                sp.mms(fpv)
                def fsum(eng):
                    eng.matmul(psR[:, :], ones_col_bf[:], expT[:, :], start=True, stop=False)
                    return eng.matmul(psR[:, :], ones_col_bf[:], expT2[:, :], start=False, stop=True)
                sp.mms(fsum)
                sp.recip(rrow[:], psR[0:1, :])
                def fbc(eng, _ob=ob):
                    return eng.matmul(psB[_ob:_ob + 64, :], ones_r64[:], rrow[:],
                                      start=True, stop=True)
                sp.mms(fbc)
                sp.copy(rrep[ob:ob + 64, :], psB[ob:ob + 64, :], engine="scalar")
                sp.tt(adst[ob:ob + 64, :], psPV[ob:ob + 64, :], rrep[ob:ob + 64, :], ALU.mult)
            ob = (h % 2) * 64
            sp.ts(a_b2[ob:ob + 64, :], a_b2[ob:ob + 64, :], lam_sb[ob:ob + 64, :], ALU.mult)
            sp.tt(aM[ob:ob + 64, h // 2, :], a_b1[ob:ob + 64, :], a_b2[ob:ob + 64, :], ALU.add)
        osl = slice(n * 256, n * 256 + 256)
        for fo in range(8):
            sp.dma(xr_load[:], xrT[fo * 128:(fo + 1) * 128, osl])
            def fw(eng, _fo=fo):
                last = None
                for a in range(8):
                    last = eng.matmul(psW[:, :256], wout_sb[:, a, _fo * 128:(_fo + 1) * 128],
                                      aM[:, a, :], start=(a == 0), stop=(a == 7))
                return last
            sp.mms(fw)
            sp.stt(xr_load[:], psW[:, :256], bout_sb[:, fo:fo + 1], xr_load[:],
                   ALU.add, ALU.add)
            sp.dma(yT[fo * 128:(fo + 1) * 128, osl], xr_load[:])
    return sp


def _stage_attn0(inp, Wp):
    lam = np.asarray(Wp["lam"], dtype=np.float32).reshape(1)
    div = np.exp(np.arange(0, 64, 2, dtype=np.float32) * (-np.log(10000.0) / 64))
    return {
        "Wq1b": np.ascontiguousarray(Wp["L0_attn_Wq1_b"]),
        "Wq2b": np.ascontiguousarray(Wp["L0_attn_Wq2_b"]),
        "Wk1b": np.ascontiguousarray(Wp["L0_attn_Wk1_b"]),
        "Wk2b": np.ascontiguousarray(Wp["L0_attn_Wk2_b"]),
        "Wvb": np.ascontiguousarray(Wp["L0_attn_Wv_b"]),
        "Woutb": np.ascontiguousarray(Wp["L0_attn_Wout_b"]),
        "boutL": np.ascontiguousarray(inp["L0_attn_bout"].reshape(8, 128).T),
        "lamC": np.broadcast_to(lam.reshape(1, 1), (128, 1)).copy(),
        "divrow": div.reshape(1, 32),
    }


def run_attn0(X, inp, Wp):
    sp = build_attn0()
    shared = _stage_attn0(inp, Wp)
    in_maps = []
    for c in range(NC):
        b, q = c // 4, c % 4
        m = dict(shared)
        xb = X[b * 8192:(b + 1) * 8192]
        m["xwT"] = np.ascontiguousarray(xb[1024 * q:1024 * q + 1152].T)
        m["xrT"] = np.ascontiguousarray(xb[2048 * q:2048 * q + 2048].T)
        m["posrow"] = (1024.0 * q + np.arange(1152, dtype=np.float32)).reshape(1, 1152)
        in_maps.append(m)
    res = sp.run(in_maps, core_ids=range(NC))
    out = np.empty_like(X)
    for c in range(NC):
        out[c * TC:(c + 1) * TC] = res.results[c]["yT"].T
    return out


def build_moe1(skip_experts=False, skip_shared=False, sp=None, io=None):
    if sp is None:
        sp = SP()
    io = io or {}
    def din(n, s, d=F32):
        return io[n] if n in io else sp.din(n, s, d)
    def dout(n, s, d=F32):
        return io[n] if n in io else sp.dout(n, s, d)
    xT = din("xT", (1024, TC))
    Gb = din("Gb", (16, 1024, 512), BF16)
    Ub = din("Ub", (16, 1024, 512), BF16)
    Db = din("Db", (16, 512, 1024), BF16)
    SGb = din("SGb", (1024, 1024), BF16)
    SUb = din("SUb", (1024, 1024), BF16)
    SDb = din("SDb", (1024, 1024), BF16)
    RwL = din("RwL", (128, 8, 16))
    yT = dout("yT", (1024, TC))
    c_dram = sp.dtmp("cdram", (16, TC))

    ar = Arena(sp, 40960)
    x_sb = ar.f32(0, (8, TC))
    xn_bf = ar.bf16(16384, (8, TC))
    goff = 0
    act_sh = ar.bf16(0, (8, TC))
    act_ex = ar.bf16(8192, (4, TC))
    acc = ar.f32(24576, (8, TC))
    rows_off = 12288
    rows_off = 24576

    ones_col = sp.sbuf("onc", [128, 1], F32)
    ones_r16 = sp.sbuf("o16", [1, 16], F32)
    ones_r128 = sp.sbuf("o128", [1, 128], F32)
    rw_sb = sp.sbuf("rw", [128, 8, 16], F32)
    ncol = sp.sbuf("ncl", [16, 1], F32)
    xr_sb = sp.sbuf("xrs", [16, 512], F32)
    lsb = sp.sbuf("l16", [16, TC], F32)
    wg = sp.sbuf("wg", [128, 8, 128], BF16)
    wu = sp.sbuf("wu", [128, 8, 128], BF16)
    wd = sp.sbuf("wd", [128, 8, 128], BF16)
    c_rep = sp.sbuf("crp", [128, TC], F32)
    tmp_f = sp.sbuf("tmf", [128, 512], F32)
    silu_a2 = sp.sbuf("sl2", [128, TC], F32)
    tmp_w2 = sp.sbuf("tw2", [128, TC], F32)
    sq = sp.sbuf("sqq", [128, 512], F32)
    xload = tmp_w2

    ps_r16 = sp.psum("psr", [16, 512])
    ps16a = sp.psum("p6a", [16, 512])
    ps16b = sp.psum("p6b", [16, 512])
    psB0 = sp.psum("pb0", [128, 512])
    psB1 = sp.psum("pb1", [128, 512])
    psA = sp.psum_at(0, "psa", (128, TC))
    psG = sp.psum_at(4, "psg", (128, TC))
    psO = sp.psum_at(0, "pso", (128, TC))

    sp.memset(ones_col[:], 1.0, engine="gpsimd")
    sp.memset(ones_r16[:], 1.0, engine="gpsimd")
    sp.memset(ones_r128[:], 1.0, engine="gpsimd")
    sp.dmas([(x_sb, xT.rearrange("(a p) t -> p a t", p=128)), (rw_sb[:], RwL)])

    mrow = ar.row(rows_off, TC)
    srow = ar.row(rows_off + TC, TC)
    sinv = ar.row(rows_off + 2 * TC, TC)
    msr = ar.row(rows_off + 3 * TC, TC)
    NF = 8
    for tc in range(4):
        tsl = slice(tc * 512, tc * 512 + 512)
        def fmean(eng, _tsl=tsl):
            last = None
            for a in range(NF):
                last = eng.matmul(ps16a[0:1, :], ones_col[:], x_sb[:, a, _tsl],
                                  start=(a == 0), stop=(a == NF - 1))
            return last
        sp.mms(fmean)
        sp.act(mrow[:, tsl], ps16a[0:1, :], AF.Copy, scale=1.0 / 1024)
        for a in range(NF):
            sp.tt(sq[:], x_sb[:, a, tsl], x_sb[:, a, tsl], ALU.mult)
            def fsq(eng, _a=a):
                return eng.matmul(ps16b[0:1, :], ones_col[:], sq[:],
                                  start=(_a == 0), stop=(_a == NF - 1))
            sp.mms(fsq)
        sp.act(srow[:, tsl], ps16b[0:1, :], AF.Copy, scale=1.0 / 1024)
    sp.tt(msr[:], mrow[:], mrow[:], ALU.mult)
    sp.tt(srow[:], srow[:], msr[:], ALU.subtract)
    sp.ts(srow[:], srow[:], 1e-5, ALU.add)
    sp.act(srow[:], srow[:], AF.Sqrt)
    sp.recip(sinv[:], srow[:])
    sp.tt(msr[:], mrow[:], sinv[:], ALU.mult)
    def fcs(eng):
        last = None
        for a in range(NF):
            last = eng.matmul(ps16a[0:16, 0:1], rw_sb[:, a, :], ones_col[:],
                              start=(a == 0), stop=(a == NF - 1))
        return last
    sp.mms(fcs)
    sp.act(ncol[:], ps16a[0:16, 0:1], AF.Copy, scale=-1.0)
    for tc in range(4):
        tsl = slice(tc * 512, tc * 512 + 512)
        def fb(eng, _tsl=tsl):
            eng.matmul(psB0[:], ones_r128[:], sinv[:, _tsl], start=True, stop=True)
            return eng.matmul(psB1[:], ones_r128[:], msr[:, _tsl], start=True, stop=True)
        sp.mms(fb)
        for a in range(NF):
            sp.tt(tmp_f[:], x_sb[:, a, tsl], psB0[:], ALU.mult)
            sp.tt(xn_bf[:, a, tsl], tmp_f[:], psB1[:], ALU.subtract)
        def fxr(eng, _tsl=tsl):
            last = None
            for a in range(NF):
                last = eng.matmul(ps_r16[:16, :], rw_sb[:, a, :], x_sb[:, a, _tsl],
                                  start=(a == 0), stop=(a == NF - 1))
            return last
        sp.mms(fxr)
        sp.copy(xr_sb[:], ps_r16[:16, :], engine="scalar")
        def fb16(eng, _tsl=tsl):
            eng.matmul(ps16a[:16, :], ones_r16[:], sinv[:, _tsl], start=True, stop=True)
            return eng.matmul(ps16b[:16, :], ones_r16[:], msr[:, _tsl], start=True, stop=True)
        sp.mms(fb16)
        sp.tt(xr_sb[:], xr_sb[:], ps16a[:16, :], ALU.mult)
        sp.stt(lsb[:, tsl], ps16b[:16, :], ncol[:], xr_sb[:], ALU.mult, ALU.add)

    emit_gating_top2(sp, ar, goff, lsb[:], c_dram, 16, renorm=False, scale_out=0.5,
                     goff_mk=24576, goff_r=8192)

    if skip_shared:
        sp.memset(acc.rearrange("p a t -> p (a t)"), 0.0, engine="gpsimd")
    for hf in range(0 if skip_shared else 8):
        sp.dmas([(wg[:], SGb[:, hf * 128:(hf + 1) * 128].rearrange("(a p) f -> p a f", p=128)),
                 (wu[:], SUb[:, hf * 128:(hf + 1) * 128].rearrange("(a p) f -> p a f", p=128))])
        def fup(eng):
            last = None
            for tc in range(4):
                tsl = slice(tc * 512, tc * 512 + 512)
                for a in range(NF):
                    eng.matmul(psA[:, tsl], wg[:, a, :], xn_bf[:, a, tsl],
                               start=(a == 0), stop=(a == NF - 1))
                for a in range(NF):
                    last = eng.matmul(psG[:, tsl], wu[:, a, :], xn_bf[:, a, tsl],
                                      start=(a == 0), stop=(a == NF - 1))
            return last
        sp.mms(fup)
        sp.act(silu_a2[:], psA[:, :], AF.Silu)
        def fmul(eng, _hf=hf):
            return eng.tensor_tensor(out=act_sh[:, _hf, :], in0=psG[:, :], in1=silu_a2[:],
                                     op=ALU.mult)
        sp.op("vector", fmul)
    for fo in range(8):
        sp.dma(wd[:], SDb[:, fo * 128:(fo + 1) * 128].rearrange("(a p) f -> p a f", p=128))
        def fd(eng):
            last = None
            for tc in range(4):
                tsl = slice(tc * 512, tc * 512 + 512)
                for a in range(NF):
                    last = eng.matmul(psO[:, tsl], wd[:, a, :], act_sh[:, a, tsl],
                                      start=(a == 0), stop=(a == NF - 1))
            return last
        sp.mms(fd)
        sp.ts(acc[:, fo, :], psO[:, :], 0.5, ALU.mult)

    for e in range(0 if skip_experts else 16):
        sp.dmas([(c_rep[:, tc * 512:(tc + 1) * 512],
                  c_dram[e:e + 1, tc * 512:(tc + 1) * 512].to_broadcast([128, 512]))
                 for tc in range(4)])
        for hf in range(4):
            sp.dmas([(wg[:], Gb[e, :, hf * 128:(hf + 1) * 128].rearrange("(a p) f -> p a f", p=128)),
                     (wu[:], Ub[e, :, hf * 128:(hf + 1) * 128].rearrange("(a p) f -> p a f", p=128))])
            def fup2(eng):
                last = None
                for tc in range(4):
                    tsl = slice(tc * 512, tc * 512 + 512)
                    for a in range(NF):
                        eng.matmul(psA[:, tsl], wg[:, a, :], xn_bf[:, a, tsl],
                                   start=(a == 0), stop=(a == NF - 1))
                    for a in range(NF):
                        last = eng.matmul(psG[:, tsl], wu[:, a, :], xn_bf[:, a, tsl],
                                          start=(a == 0), stop=(a == NF - 1))
                return last
            sp.mms(fup2)
            sp.act(silu_a2[:], psA[:, :], AF.Silu)
            def fmul2(eng, _hf=hf):
                return eng.tensor_tensor(out=act_ex[:, _hf, :], in0=psG[:, :], in1=silu_a2[:],
                                         op=ALU.mult)
            sp.op("vector", fmul2)
        for fo in range(8):
            sp.dma(wd[:, 0:4, :], Db[e, :, fo * 128:(fo + 1) * 128].rearrange("(a p) f -> p a f", p=128))
            def fd2(eng):
                last = None
                for tc in range(4):
                    tsl = slice(tc * 512, tc * 512 + 512)
                    for a in range(4):
                        last = eng.matmul(psO[:, tsl], wd[:, a, :], act_ex[:, a, tsl],
                                          start=(a == 0), stop=(a == 3))
                return last
            sp.mms(fd2)
            sp.tt(tmp_w2[:], psO[:, :], c_rep[:, :], ALU.mult)
            sp.tt(acc[:, fo, :], acc[:, fo, :], tmp_w2[:], ALU.add)

    for fo in range(8):
        sp.dma(xload[:], xT[fo * 128:(fo + 1) * 128, :])
        sp.tt(xload[:], xload[:], acc[:, fo, :], ALU.add)
        sp.dma(yT[fo * 128:(fo + 1) * 128, :], xload[:])
    return sp


def run_moe1(X, inp, Wp):
    sp = build_moe1()
    Rw = inp["L1_ffn_router_w"]
    shared = {
        "Gb": np.ascontiguousarray(Wp["L1_ffn_gate_w_b"]),
        "Ub": np.ascontiguousarray(Wp["L1_ffn_up_w_b"]),
        "Db": np.ascontiguousarray(Wp["L1_ffn_down_w_b"]),
        "SGb": np.ascontiguousarray(Wp["L1_ffn_shared_gate_w_b"]),
        "SUb": np.ascontiguousarray(Wp["L1_ffn_shared_up_w_b"]),
        "SDb": np.ascontiguousarray(Wp["L1_ffn_shared_down_w_b"]),
        "RwL": np.ascontiguousarray(Rw.reshape(8, 128, 16).transpose(1, 0, 2)),
    }
    in_maps = []
    for c in range(NC):
        m = dict(shared)
        m["xT"] = np.ascontiguousarray(X[c * TC:(c + 1) * TC].T)
        in_maps.append(m)
    res = sp.run(in_maps, core_ids=range(NC))
    out = np.empty_like(X)
    for c in range(NC):
        out[c * TC:(c + 1) * TC] = res.results[c]["yT"].T
    return out


def build_l1a(sp=None, io=None):
    NT = 2051
    if sp is None:
        sp = SP()
    io = io or {}
    def din(n, s, d=F32):
        return io[n] if n in io else sp.din(n, s, d)
    def dout(n, s, d=F32):
        return io[n] if n in io else sp.dout(n, s, d)
    xhT = None if "xh_load" in (io or {}) else din("xhT", (1024, NT))
    winT = din("winT", (1024, 4096), BF16)
    wxT = din("wxT", (2048, 96), BF16)
    wdtT = din("wdtT", (64, 2048), BF16)
    gamL = din("gamL", (128, 8))
    betL = din("betL", (128, 8))
    cwL = din("cwL", (128, 16, 4))
    cbL = din("cbL", (128, 16))
    bdtL = din("bdtL", (128, 16))
    delta_o = dout("delta_o", (2048, TC))
    u_o = dout("u_o", (2048, TC))
    xs_o = dout("xs_o", (2048, TC))
    z_o = dout("z_o", (2048, TC))
    B_o = dout("B_o", (16, TC))
    C_o = dout("C_o", (16, TC))

    ar = Arena(sp, 42048)
    x_sb = ar.f32(0, (8, NT))
    xn_bf = ar.bf16(16416, (8, NT))
    rows_off = 24624
    xs_bf = ar.bf16(24624, (16, TC))
    dtl_off = 41008

    ones_col = sp.sbuf("onc", [128, 1], F32)
    one_col = sp.sbuf("one", [128, 1], F32)
    ones_r128 = sp.sbuf("o128", [1, 128], F32)
    gam = sp.sbuf("gam", [128, 8], F32)
    bet = sp.sbuf("bet", [128, 8], F32)
    cw = sp.sbuf("cw", [128, 16, 4], F32)
    cb = sp.sbuf("cb", [128, 16], F32)
    bdt = sp.sbuf("bdt", [128, 16], F32)
    w_sb = sp.sbuf("wsb", [128, 8, 128], BF16)
    wx_sb = sp.sbuf("wxs", [128, 16, 96], BF16)
    wdt_sb = sp.sbuf("wds", [64, 2048], BF16)
    xp_g = sp.sbuf("xpg", [128, NT], BF16)
    accv = sp.sbuf("acv", [128, TC], F32)
    xs_g = sp.sbuf("xsg", [128, TC], F32)
    brow = sp.sbuf("brw", [16, 512], F32)
    crow = sp.sbuf("crw", [16, 512], F32)
    t1b = sp.sbuf("t1b", [128, 512], F32)
    t2b = sp.sbuf("t2b", [128, 512], F32)
    t3b = sp.sbuf("t3b", [128, 512], F32)

    psA = sp.psum("psa", [128, 512])
    psDt = sp.psum("pdt", [64, 512])
    psB = sp.psum("psb", [16, 512])
    psC = sp.psum("psc", [16, 512])
    psB0 = sp.psum("pb0", [128, 512])
    psB1 = sp.psum("pb1", [128, 512])

    dtlow = ar.t[0:64, dtl_off:dtl_off + 1024].bitcast(BF16)

    sp.memset(ones_col[:], 1.0, engine="gpsimd")
    sp.memset(one_col[:], 1.0, engine="gpsimd")
    sp.memset(ones_r128[:], 1.0, engine="gpsimd")
    if "xh_load" in io:
        io["xh_load"](sp, x_sb)
    else:
        sp.dmas([(x_sb, xhT.rearrange("(a p) t -> p a t", p=128))])
    sp.dmas([
             (gam[:], gamL), (bet[:], betL), (cw[:], cwL), (cb[:], cbL), (bdt[:], bdtL),
             (wx_sb[:], wxT.rearrange("(g p) r -> p g r", p=128)),
             (wdt_sb[:], wdtT)])

    mrow = ar.row(rows_off, NT)
    srow = ar.row(rows_off + NT, NT)
    sinv = ar.row(rows_off + 2 * NT, NT)
    msr = ar.row(rows_off + 3 * NT, NT)
    tch = chunks(NT, 512)
    def ln_stats():
        for (t0, tn) in tch:
            tsl = slice(t0, t0 + tn)
            def fmean(eng, _tsl=tsl, _tn=tn):
                last = None
                for a in range(8):
                    last = eng.matmul(psB[0:1, :_tn], ones_col[:], x_sb[:, a, _tsl],
                                      start=(a == 0), stop=(a == 7))
                return last
            sp.mms(fmean)
            sp.act(mrow[:, tsl], psB[0:1, :tn], AF.Copy, scale=1.0 / 1024)
            for a in range(8):
                sp.tt(t1b[:, :tn], x_sb[:, a, tsl], x_sb[:, a, tsl], ALU.mult)
                def fsq(eng, _a=a, _tn=tn):
                    return eng.matmul(psC[0:1, :_tn], ones_col[:], t1b[:, :_tn],
                                      start=(_a == 0), stop=(_a == 7))
                sp.mms(fsq)
            sp.act(srow[:, tsl], psC[0:1, :tn], AF.Copy, scale=1.0 / 1024)
        sp.tt(msr[:], mrow[:], mrow[:], ALU.mult)
        sp.tt(srow[:], srow[:], msr[:], ALU.subtract)
        sp.ts(srow[:], srow[:], 1e-5, ALU.add)
        sp.act(srow[:], srow[:], AF.Sqrt)
        sp.recip(sinv[:], srow[:])
        sp.tt(msr[:], mrow[:], sinv[:], ALU.mult)
    ln_stats()
    for (t0, tn) in tch:
        tsl = slice(t0, t0 + tn)
        def fb(eng, _tsl=tsl, _tn=tn):
            eng.matmul(psB0[:, :_tn], ones_r128[:], sinv[:, _tsl], start=True, stop=True)
            return eng.matmul(psB1[:, :_tn], ones_r128[:], msr[:, _tsl], start=True, stop=True)
        sp.mms(fb)
        for a in range(8):
            sp.tt(t1b[:, :tn], x_sb[:, a, tsl], psB0[:, :tn], ALU.mult)
            sp.tt(x_sb[:, a, tsl], t1b[:, :tn], psB1[:, :tn], ALU.subtract)
    ln_stats()
    for (t0, tn) in tch:
        tsl = slice(t0, t0 + tn)
        def fb(eng, _tsl=tsl, _tn=tn):
            eng.matmul(psB0[:, :_tn], ones_r128[:], sinv[:, _tsl], start=True, stop=True)
            return eng.matmul(psB1[:, :_tn], ones_r128[:], msr[:, _tsl], start=True, stop=True)
        sp.mms(fb)
        for a in range(8):
            sp.tt(t1b[:, :tn], x_sb[:, a, tsl], psB0[:, :tn], ALU.mult)
            sp.tt(t2b[:, :tn], t1b[:, :tn], psB1[:, :tn], ALU.subtract)
            def fts(eng, _a=a, _tsl=tsl, _tn=tn):
                return eng.tensor_scalar(out=xn_bf[:, _a, _tsl], in0=t2b[:, :_tn],
                                         scalar1=gam[:, _a:_a + 1], scalar2=bet[:, _a:_a + 1],
                                         op0=ALU.mult, op1=ALU.add)
            sp.op("vector", fts)

    for g in range(16):
        sp.dma(w_sb[:], winT[:, g * 128:(g + 1) * 128].rearrange("(a p) f -> p a f", p=128))
        for (t0, tn) in tch:
            tsl = slice(t0, t0 + tn)
            def fxp(eng, _tsl=tsl, _tn=tn):
                last = None
                for a in range(8):
                    last = eng.matmul(psA[:, :_tn], w_sb[:, a, :], xn_bf[:, a, _tsl],
                                      start=(a == 0), stop=(a == 7))
                return last
            sp.mms(fxp)
            sp.copy(xp_g[:, tsl], psA[:, :tn], engine="scalar")
        def fc0(eng, _g=g):
            return eng.tensor_scalar(out=accv[:], in0=xp_g[:, 0:TC], scalar1=cw[:, _g, 0:1],
                                     scalar2=None, op0=ALU.mult)
        sp.op("vector", fc0)
        for k in range(1, 4):
            def fck(eng, _g=g, _k=k):
                return eng.scalar_tensor_tensor(out=accv[:], in0=xp_g[:, _k:TC + _k],
                                                scalar=cw[:, _g, _k:_k + 1], in1=accv[:],
                                                op0=ALU.mult, op1=ALU.add)
            sp.op("vector", fck)
        def fsilu(eng, _g=g):
            return eng.activation(out=xs_g[:], in_=accv[:], func=AF.Silu, bias=cb[:, _g:_g + 1])
        sp.op("scalar", fsilu)
        sp.dma(xs_o[g * 128:(g + 1) * 128, :], xs_g[:])
        sp.copy(xs_bf[:, g, :], xs_g[:])
    for g in range(16):
        sp.dma(w_sb[:], winT[:, 2048 + g * 128:2048 + (g + 1) * 128].rearrange("(a p) f -> p a f", p=128))
        for tc in range(4):
            tsl = slice(3 + tc * 512, 3 + tc * 512 + 512)
            def fz(eng, _tsl=tsl):
                last = None
                for a in range(8):
                    last = eng.matmul(psA[:], w_sb[:, a, :], xn_bf[:, a, _tsl],
                                      start=(a == 0), stop=(a == 7))
                return last
            sp.mms(fz)
            sp.copy(xs_g[:, tc * 512:(tc + 1) * 512], psA[:], engine="scalar")
        sp.dma(z_o[g * 128:(g + 1) * 128, :], xs_g[:])

    for tc in range(4):
        tsl = slice(tc * 512, tc * 512 + 512)
        def fdt(eng, _tsl=tsl):
            last = None
            for g in range(16):
                last = eng.matmul(psDt[:, :], wx_sb[:, g, 0:64], xs_bf[:, g, _tsl],
                                  start=(g == 0), stop=(g == 15))
            return last
        sp.mms(fdt)
        sp.copy(dtlow[:, tsl], psDt[:, :], engine="scalar")
        def fB(eng, _tsl=tsl):
            last = None
            for g in range(16):
                last = eng.matmul(psB[:, :], wx_sb[:, g, 64:80], xs_bf[:, g, _tsl],
                                  start=(g == 0), stop=(g == 15))
            return last
        sp.mms(fB)
        sp.copy(brow[:], psB[:, :], engine="scalar")
        sp.dma(B_o[:, tsl], brow[:])
        def fC(eng, _tsl=tsl):
            last = None
            for g in range(16):
                last = eng.matmul(psC[:, :], wx_sb[:, g, 80:96], xs_bf[:, g, _tsl],
                                  start=(g == 0), stop=(g == 15))
            return last
        sp.mms(fC)
        sp.copy(crow[:], psC[:, :], engine="scalar")
        sp.dma(C_o[:, tsl], crow[:])

    for g in range(16):
        sp.dma(xs_g[:], xs_o[g * 128:(g + 1) * 128, :])
        for tc in range(4):
            tsl = slice(tc * 512, tc * 512 + 512)
            def fdl(eng, _g=g, _tsl=tsl):
                return eng.matmul(psA[:], wdt_sb[:, _g * 128:(_g + 1) * 128],
                                  dtlow[:, _tsl], start=True, stop=True)
            sp.mms(fdl)
            sp.ts(t1b[:], psA[:], bdt[:, g:g + 1], ALU.add)
            sp.op("scalar", lambda e: e.activation(out=t2b[:], in_=t1b[:], func=AF.Abs))
            sp.op("scalar", lambda e: e.activation(out=t2b[:], in_=t2b[:], func=AF.Exp, scale=-1.0))
            def fln(eng):
                return eng.activation(out=t2b[:], in_=t2b[:], func=AF.Ln, bias=one_col[:])
            sp.op("scalar", fln)
            sp.ts(t3b[:], t1b[:], 0.0, ALU.max)
            sp.tt(t3b[:], t3b[:], t2b[:], ALU.add)
            sp.dma(delta_o[g * 128:(g + 1) * 128, tsl], t3b[:])
            sp.tt(t3b[:], t3b[:], xs_g[:, tsl], ALU.mult)
            sp.dma(u_o[g * 128:(g + 1) * 128, tsl], t3b[:])
    return sp


def run_l1a(X, inp, Wp):
    sp = build_l1a()
    cw = inp["L1_attn_conv_w"].reshape(2048, 4)
    shared = {
        "winT": np.ascontiguousarray(Wp["L1_attn_w_in_tb"]),
        "wxT": np.ascontiguousarray(Wp["L1_attn_w_x_tb"]),
        "wdtT": np.ascontiguousarray(Wp["L1_attn_w_dt_tb"]),
        "gamL": np.ascontiguousarray(inp["L1_attn_gamma"].reshape(8, 128).T),
        "betL": np.ascontiguousarray(inp["L1_attn_beta"].reshape(8, 128).T),
        "cwL": np.ascontiguousarray(cw.reshape(16, 128, 4).transpose(1, 0, 2)),
        "cbL": np.ascontiguousarray(inp["L1_attn_conv_b"].reshape(16, 128).T),
        "bdtL": np.ascontiguousarray(inp["L1_attn_b_dt"].reshape(16, 128).T),
    }
    outs = []
    in_maps = []
    for c in range(NC):
        b, q = c // 4, c % 4
        xb = X[b * 8192:(b + 1) * 8192]
        t0 = 2048 * q
        xh = np.zeros((2051, 1024), dtype=np.float32)
        lo = max(0, t0 - 3)
        xh[3 - (t0 - lo):3] = xb[lo:t0]
        xh[3:] = xb[t0:t0 + 2048]
        m = dict(shared)
        m["xhT"] = np.ascontiguousarray(xh.T)
        in_maps.append(m)
    res = sp.run(in_maps, core_ids=range(NC))
    return [dict(r) for r in res.results]


def build_scan(nstate, nchunk=4096, sp=None, io=None):
    if sp is None:
        sp = SP()
    io = io or {}
    def din(n, s, d=F32):
        return io[n] if n in io else sp.din(n, s, d)
    def dout(n, s, d=F32):
        return io[n] if n in io else sp.dout(n, s, d)
    dS = None if "load_du" in (io or {}) else din("dS", (256, T))
    uS = None if "load_du" in (io or {}) else din("uS", (256, T))
    Bin = None if "load_bc" in (io or {}) else din("Bin", (nstate, T))
    Cin = None if "load_bc" in (io or {}) else din("Cin", (nstate, T))
    AlogS = din("AlogS", (256, nstate))
    yS = dout("yS", (256, T))

    A_sb = sp.sbuf("asb", [128, 2, nstate], F32)
    carry = sp.sbuf("car", [128, nstate], F32)
    dblk = sp.sbuf("dbk", [128, nchunk], F32)
    ublk = sp.sbuf("ubk", [128, nchunk], F32)
    accb = sp.sbuf("acb", [128, nchunk], F32)
    dA = sp.sbuf("dab", [128, nchunk], F32)
    ut = sp.sbuf("utb", [128, nchunk], F32)
    hb = sp.sbuf("hbb", [128, nchunk], F32)
    Brep = sp.sbuf("brp", [128, nchunk], F32)
    Crep = sp.sbuf("crp", [128, nchunk], F32)

    sp.dma(A_sb[:], AlogS.rearrange("(a p) n -> p a n", p=128))
    sp.op("scalar", lambda e: e.activation(out=A_sb[:].rearrange("p a n -> p (a n)"),
                                           in_=A_sb[:].rearrange("p a n -> p (a n)"),
                                           func=AF.Exp, scale=1.0))
    sp.ts(A_sb[:].rearrange("p a n -> p (a n)"), A_sb[:].rearrange("p a n -> p (a n)"),
          -1.0, ALU.mult)

    NCH = L // nchunk
    for dt in range(2):
        for b in range(B):
            sp.memset(carry[:], 0.0, engine="gpsimd")
            for ch in range(NCH):
                c0 = b * L + ch * nchunk
                csl = slice(c0, c0 + nchunk)
                if "load_du" in io:
                    io["load_du"](sp, dt, b, ch, dblk[:], ublk[:])
                else:
                    sp.dmas([(dblk[:], dS[dt * 128:(dt + 1) * 128, csl]),
                             (ublk[:], uS[dt * 128:(dt + 1) * 128, csl])])
                first = True
                for n in range(nstate):
                    if "load_bc" in io:
                        io["load_bc"](sp, n, b, ch, Brep[:], Crep[:])
                    else:
                        sp.dmas([(Brep[:], Bin[n:n + 1, csl].to_broadcast([128, nchunk])),
                                 (Crep[:], Cin[n:n + 1, csl].to_broadcast([128, nchunk]))])
                    def fdA(eng, _dt=dt, _n=n):
                        return eng.activation(out=dA[:], in_=dblk[:], func=AF.Exp,
                                              scale=A_sb[:, _dt, _n:_n + 1])
                    sp.op("scalar", fdA)
                    sp.tt(ut[:], ublk[:], Brep[:], ALU.mult)
                    sp.scan(hb[:], dA[:], ut[:], carry[:, n:n + 1])
                    sp.copy(carry[:, n:n + 1], hb[:, nchunk - 1:nchunk])
                    if first:
                        sp.tt(accb[:], hb[:], Crep[:], ALU.mult)
                        first = False
                    else:
                        sp.tt(hb[:], hb[:], Crep[:], ALU.mult)
                        sp.tt(accb[:], accb[:], hb[:], ALU.add)
                sp.dma(yS[dt * 128:(dt + 1) * 128, csl], accb[:])
    return sp


def run_scan(parts_list, keyd, keyu, keyB, keyC, Alog, nstate):
    sp = build_scan(nstate)
    dall = np.concatenate([np.asarray(p[keyd]) for p in parts_list], axis=1)
    uall = np.concatenate([np.asarray(p[keyu]) for p in parts_list], axis=1)
    Ball = np.concatenate([np.asarray(p[keyB]) for p in parts_list], axis=1)
    Call = np.concatenate([np.asarray(p[keyC]) for p in parts_list], axis=1)
    in_maps = []
    for c in range(NC):
        m = {
            "dS": np.ascontiguousarray(dall[256 * c:256 * (c + 1)]),
            "uS": np.ascontiguousarray(uall[256 * c:256 * (c + 1)]),
            "Bin": Ball, "Cin": Call,
            "AlogS": np.ascontiguousarray(Alog[256 * c:256 * (c + 1)]),
        }
        in_maps.append(m)
    res = sp.run(in_maps, core_ids=range(NC))
    return np.concatenate([res.results[c]["yS"] for c in range(NC)], axis=0)


def build_l1c(rms=False, sp=None, io=None):
    if sp is None:
        sp = SP()
    io = io or {}
    def din(n, s, d=F32):
        return io[n] if n in io else sp.din(n, s, d)
    def dout(n, s, d=F32):
        return io[n] if n in io else sp.dout(n, s, d)
    xT = din("xT", (1024, TC))
    ysT = None if "ys_load" in (io or {}) else din("ysT", (2048, TC))
    xsT = din("xsT", (2048, TC))
    zT = din("zT", (2048, TC))
    DpL = din("DpL", (128, 16))
    woTb = din("woTb", (2048, 1024), BF16)
    yT = dout("yT", (1024, TC))

    ybf = sp.sbuf("ybf", [128, 16, TC], BF16)
    x_sb = sp.sbuf("xsb", [128, 8, TC], F32)
    g1 = sp.sbuf("g1", [128, TC], F32)
    g2 = sp.sbuf("g2", [128, TC], F32)
    g3 = sp.sbuf("g3", [128, TC], F32)
    dp = sp.sbuf("dp", [128, 16], F32)
    wo = sp.sbuf("wo", [128, 16, 128], BF16)
    psO = sp.psum("pso", [128, 512])
    psR = sp.psum("psr", [1, 512])
    psB0 = sp.psum("pb0", [128, 512])
    psB1 = sp.psum("pb1", [128, 512])
    ones_col = sp.sbuf("onc", [128, 1], F32)
    ones_r128 = sp.sbuf("o128", [1, 128], F32)
    mrow = sp.sbuf("mr", [1, TC], F32)
    srow = sp.sbuf("sr", [1, TC], F32)
    sinv = sp.sbuf("si", [1, TC], F32)
    msr = sp.sbuf("ms", [1, TC], F32)

    sp.memset(ones_col[:], 1.0, engine="gpsimd")
    sp.memset(ones_r128[:], 1.0, engine="gpsimd")
    sp.dma(dp[:], DpL)
    sp.dma(x_sb[:], xT.rearrange("(a p) t -> p a t", p=128))
    for tc in range(4):
        tsl = slice(tc * 512, tc * 512 + 512)
        if not rms:
            def fmean(eng, _tsl=tsl):
                last = None
                for a in range(8):
                    last = eng.matmul(psR[0:1, :], ones_col[:], x_sb[:, a, _tsl],
                                      start=(a == 0), stop=(a == 7))
                return last
            sp.mms(fmean)
            sp.act(mrow[0:1, tsl], psR[0:1, :], AF.Copy, scale=1.0 / 1024)
        for a in range(8):
            sp.tt(g1[:, :512], x_sb[:, a, tsl], x_sb[:, a, tsl], ALU.mult)
            def fsq(eng, _a=a):
                return eng.matmul(psB1[0:1, :], ones_col[:], g1[:, :512],
                                  start=(_a == 0), stop=(_a == 7))
            sp.mms(fsq)
        sp.act(srow[0:1, tsl], psB1[0:1, :], AF.Copy, scale=1.0 / 1024)
    if not rms:
        sp.tt(msr[:], mrow[:], mrow[:], ALU.mult)
        sp.tt(srow[:], srow[:], msr[:], ALU.subtract)
        sp.ts(srow[:], srow[:], 1e-5, ALU.add)
    else:
        sp.ts(srow[:], srow[:], 1e-6, ALU.add)
    sp.act(srow[:], srow[:], AF.Sqrt)
    sp.recip(sinv[:], srow[:])
    if not rms:
        sp.tt(msr[:], mrow[:], sinv[:], ALU.mult)
    for tc in range(4):
        tsl = slice(tc * 512, tc * 512 + 512)
        def fb(eng, _tsl=tsl):
            eng.matmul(psB0[:], ones_r128[:], sinv[0:1, _tsl], start=True, stop=True)
            if not rms:
                return eng.matmul(psB1[:], ones_r128[:], msr[0:1, _tsl], start=True, stop=True)
            return None
        if rms:
            def fb(eng, _tsl=tsl):
                return eng.matmul(psB0[:], ones_r128[:], sinv[0:1, _tsl], start=True, stop=True)
        sp.mms(fb)
        for a in range(8):
            sp.tt(g1[:, :512], x_sb[:, a, tsl], psB0[:], ALU.mult)
            if not rms:
                sp.tt(g1[:, :512], g1[:, :512], psB1[:], ALU.subtract)
            sp.tt(x_sb[:, a, tsl], x_sb[:, a, tsl], g1[:, :512], ALU.add)
    for g in range(16):
        if "ys_load" in io:
            io["ys_load"](sp, g, g1[:])
            sp.dmas([(g2[:], xsT[g * 128:(g + 1) * 128, :]),
                     (g3[:], zT[g * 128:(g + 1) * 128, :])])
        else:
            sp.dmas([(g1[:], ysT[g * 128:(g + 1) * 128, :]),
                     (g2[:], xsT[g * 128:(g + 1) * 128, :]),
                     (g3[:], zT[g * 128:(g + 1) * 128, :])])
        def fd(eng, _g=g):
            return eng.scalar_tensor_tensor(out=g1[:], in0=g2[:], scalar=dp[:, _g:_g + 1],
                                            in1=g1[:], op0=ALU.mult, op1=ALU.add)
        sp.op("vector", fd)
        sp.op("scalar", lambda e: e.activation(out=g3[:], in_=g3[:], func=AF.Silu))
        sp.tt(ybf[:, g, :], g1[:], g3[:], ALU.mult)
    for fo in range(8):
        sp.dma(wo[:], woTb[:, fo * 128:(fo + 1) * 128].rearrange("(g p) f -> p g f", p=128))
        for tc in range(4):
            tsl = slice(tc * 512, tc * 512 + 512)
            def fw(eng, _tsl=tsl):
                last = None
                for g in range(16):
                    last = eng.matmul(psO[:], wo[:, g, :], ybf[:, g, _tsl],
                                      start=(g == 0), stop=(g == 15))
                return last
            sp.mms(fw)
            sp.tt(x_sb[:, fo, tsl], x_sb[:, fo, tsl], psO[:], ALU.add)
        sp.dma(yT[fo * 128:(fo + 1) * 128, :], x_sb[:, fo, :])
    return sp


def run_l1bc(X, inp, Wp, parts_list):
    y = run_scan(parts_list, "delta_o", "u_o", "B_o", "C_o", inp["L1_attn_A_log"], 16)
    sp = build_l1c()
    shared = {
        "DpL": np.ascontiguousarray(inp["L1_attn_Dp"].reshape(16, 128).T),
        "woTb": np.ascontiguousarray(Wp["L1_attn_w_out_tb"]),
    }
    in_maps = []
    for c in range(NC):
        m = dict(shared)
        m["xT"] = np.ascontiguousarray(X[c * TC:(c + 1) * TC].T)
        m["ysT"] = np.ascontiguousarray(y[:, c * TC:(c + 1) * TC])
        m["xsT"] = np.asarray(parts_list[c]["xs_o"])
        m["zT"] = np.asarray(parts_list[c]["z_o"])
        in_maps.append(m)
    res = sp.run(in_maps, core_ids=range(NC))
    out = np.empty_like(X)
    for c in range(NC):
        out[c * TC:(c + 1) * TC] = res.results[c]["yT"].T
    return out


def run_l1(X, inp, Wp):
    parts_list = run_l1a(X, inp, Wp)
    return run_l1bc(X, inp, Wp, parts_list)


def build_l2a(sp=None, io=None):
    NT = 2051
    if sp is None:
        sp = SP()
    io = io or {}
    def din(n, s, d=F32):
        return io[n] if n in io else sp.din(n, s, d)
    def dout(n, s, d=F32):
        return io[n] if n in io else sp.dout(n, s, d)
    xhT = None if "xh_load" in (io or {}) else din("xhT", (1024, NT))
    winT = din("winT", (1024, 4096), BF16)
    bwT = din("bwT", (2048, 64), BF16)
    cwT = din("cwT", (2048, 64), BF16)
    dtwT = din("dtwT", (2048, 2048), BF16)
    nwL = din("nwL", (128, 8))
    cwL = din("cwL", (128, 16, 4))
    cbL = din("cbL", (128, 16))
    bdtL = din("bdtL", (128, 16))
    dt_o = dout("dt_o", (2048, TC))
    u_o = dout("u_o", (2048, TC))
    xc_o = dout("xc_o", (2048, TC))
    z_o = dout("z_o", (2048, TC))
    B_o = dout("B_o", (64, TC))
    C_o = dout("C_o", (64, TC))

    ar = Arena(sp, 42048)
    x_sb = ar.f32(0, (8, NT))
    xn_bf = ar.bf16(16416, (8, NT))
    rows_off = 24624
    xs_bf = ar.bf16(24624, (16, TC))

    ones_col = sp.sbuf("onc", [128, 1], F32)
    one_col = sp.sbuf("one", [128, 1], F32)
    ones_r128 = sp.sbuf("o128", [1, 128], F32)
    nw = sp.sbuf("nw", [128, 8], F32)
    cw = sp.sbuf("cw", [128, 16, 4], F32)
    cb = sp.sbuf("cb", [128, 16], F32)
    bdt = sp.sbuf("bdt", [128, 16], F32)
    w_sb = sp.sbuf("wsb", [128, 8, 128], BF16)
    bwt_sb = sp.sbuf("bws", [128, 16, 64], BF16)
    cwt_sb = sp.sbuf("cws", [128, 16, 64], BF16)
    dtw_g = sp.sbuf("dtg", [128, 16, 128], BF16)
    xp_g = sp.sbuf("xpg", [128, NT], BF16)
    accv = sp.sbuf("acv", [128, TC], F32)
    xs_g = sp.sbuf("xsg", [128, TC], F32)
    brow = sp.sbuf("brw", [64, 512], F32)
    t1b = sp.sbuf("t1b", [128, 512], F32)
    t2b = sp.sbuf("t2b", [128, 512], F32)
    t3b = sp.sbuf("t3b", [128, 512], F32)

    psA = sp.psum("psa", [128, 512])
    psB = sp.psum("psb", [64, 512])
    psC = sp.psum("psc", [64, 512])
    psB0 = sp.psum("pb0", [128, 512])
    psB1 = sp.psum("pb1", [128, 512])

    sp.memset(ones_col[:], 1.0, engine="gpsimd")
    sp.memset(one_col[:], 1.0, engine="gpsimd")
    sp.memset(ones_r128[:], 1.0, engine="gpsimd")
    if "xh_load" in io:
        io["xh_load"](sp, x_sb)
    else:
        sp.dmas([(x_sb, xhT.rearrange("(a p) t -> p a t", p=128))])
    sp.dmas([
             (nw[:], nwL), (cw[:], cwL), (cb[:], cbL), (bdt[:], bdtL),
             (bwt_sb[:], bwT.rearrange("(g p) n -> p g n", p=128)),
             (cwt_sb[:], cwT.rearrange("(g p) n -> p g n", p=128))])

    mrow = ar.row(rows_off, NT)
    srow = ar.row(rows_off + NT, NT)
    sinv = ar.row(rows_off + 2 * NT, NT)
    msr = ar.row(rows_off + 3 * NT, NT)
    tch = chunks(NT, 512)

    def ln_stats(with_mean=True, eps=1e-5):
        for (t0, tn) in tch:
            tsl = slice(t0, t0 + tn)
            if with_mean:
                def fmean(eng, _tsl=tsl, _tn=tn):
                    last = None
                    for a in range(8):
                        last = eng.matmul(psB[0:1, :_tn], ones_col[:], x_sb[:, a, _tsl],
                                          start=(a == 0), stop=(a == 7))
                    return last
                sp.mms(fmean)
                sp.act(mrow[:, tsl], psB[0:1, :tn], AF.Copy, scale=1.0 / 1024)
            for a in range(8):
                sp.tt(t1b[:, :tn], x_sb[:, a, tsl], x_sb[:, a, tsl], ALU.mult)
                def fsq(eng, _a=a, _tn=tn):
                    return eng.matmul(psC[0:1, :_tn], ones_col[:], t1b[:, :_tn],
                                      start=(_a == 0), stop=(_a == 7))
                sp.mms(fsq)
            sp.act(srow[:, tsl], psC[0:1, :tn], AF.Copy, scale=1.0 / 1024)
        if with_mean:
            sp.tt(msr[:], mrow[:], mrow[:], ALU.mult)
            sp.tt(srow[:], srow[:], msr[:], ALU.subtract)
        sp.ts(srow[:], srow[:], eps, ALU.add)
        sp.act(srow[:], srow[:], AF.Sqrt)
        sp.recip(sinv[:], srow[:])
        if with_mean:
            sp.tt(msr[:], mrow[:], sinv[:], ALU.mult)

    ln_stats(True, 1e-5)
    for (t0, tn) in tch:
        tsl = slice(t0, t0 + tn)
        def fb(eng, _tsl=tsl, _tn=tn):
            eng.matmul(psB0[:, :_tn], ones_r128[:], sinv[:, _tsl], start=True, stop=True)
            return eng.matmul(psB1[:, :_tn], ones_r128[:], msr[:, _tsl], start=True, stop=True)
        sp.mms(fb)
        for a in range(8):
            sp.tt(t1b[:, :tn], x_sb[:, a, tsl], psB0[:, :tn], ALU.mult)
            sp.tt(x_sb[:, a, tsl], t1b[:, :tn], psB1[:, :tn], ALU.subtract)
    ln_stats(False, 1e-6)
    for (t0, tn) in tch:
        tsl = slice(t0, t0 + tn)
        def fb2(eng, _tsl=tsl, _tn=tn):
            return eng.matmul(psB0[:, :_tn], ones_r128[:], sinv[:, _tsl], start=True, stop=True)
        sp.mms(fb2)
        for a in range(8):
            def fts(eng, _a=a, _tsl=tsl, _tn=tn):
                return eng.scalar_tensor_tensor(out=xn_bf[:, _a, _tsl], in0=x_sb[:, _a, _tsl],
                                                scalar=nw[:, _a:_a + 1], in1=psB0[:, :_tn],
                                                op0=ALU.mult, op1=ALU.mult)
            sp.op("vector", fts)

    for g in range(16):
        sp.dma(w_sb[:], winT[:, g * 128:(g + 1) * 128].rearrange("(a p) f -> p a f", p=128))
        for (t0, tn) in tch:
            tsl = slice(t0, t0 + tn)
            def fxp(eng, _tsl=tsl, _tn=tn):
                last = None
                for a in range(8):
                    last = eng.matmul(psA[:, :_tn], w_sb[:, a, :], xn_bf[:, a, _tsl],
                                      start=(a == 0), stop=(a == 7))
                return last
            sp.mms(fxp)
            sp.copy(xp_g[:, tsl], psA[:, :tn], engine="scalar")
        def fc0(eng, _g=g):
            return eng.tensor_scalar(out=accv[:], in0=xp_g[:, 0:TC], scalar1=cw[:, _g, 0:1],
                                     scalar2=None, op0=ALU.mult)
        sp.op("vector", fc0)
        for k in range(1, 4):
            def fck(eng, _g=g, _k=k):
                return eng.scalar_tensor_tensor(out=accv[:], in0=xp_g[:, _k:TC + _k],
                                                scalar=cw[:, _g, _k:_k + 1], in1=accv[:],
                                                op0=ALU.mult, op1=ALU.add)
            sp.op("vector", fck)
        def fsilu(eng, _g=g):
            return eng.activation(out=xs_g[:], in_=accv[:], func=AF.Silu, bias=cb[:, _g:_g + 1])
        sp.op("scalar", fsilu)
        sp.dma(xc_o[g * 128:(g + 1) * 128, :], xs_g[:])
        sp.copy(xs_bf[:, g, :], xs_g[:])
    for g in range(16):
        sp.dma(w_sb[:], winT[:, 2048 + g * 128:2048 + (g + 1) * 128].rearrange("(a p) f -> p a f", p=128))
        for tc in range(4):
            tsl = slice(3 + tc * 512, 3 + tc * 512 + 512)
            def fz(eng, _tsl=tsl):
                last = None
                for a in range(8):
                    last = eng.matmul(psA[:], w_sb[:, a, :], xn_bf[:, a, _tsl],
                                      start=(a == 0), stop=(a == 7))
                return last
            sp.mms(fz)
            sp.copy(xs_g[:, tc * 512:(tc + 1) * 512], psA[:], engine="scalar")
        sp.dma(z_o[g * 128:(g + 1) * 128, :], xs_g[:])

    for tc in range(4):
        tsl = slice(tc * 512, tc * 512 + 512)
        def fB(eng, _tsl=tsl):
            last = None
            for g in range(16):
                last = eng.matmul(psB[:, :], bwt_sb[:, g, :], xs_bf[:, g, _tsl],
                                  start=(g == 0), stop=(g == 15))
            return last
        sp.mms(fB)
        sp.copy(brow[:], psB[:, :], engine="scalar")
        sp.dma(B_o[:, tsl], brow[:])
        def fC(eng, _tsl=tsl):
            last = None
            for g in range(16):
                last = eng.matmul(psC[:, :], cwt_sb[:, g, :], xs_bf[:, g, _tsl],
                                  start=(g == 0), stop=(g == 15))
            return last
        sp.mms(fC)
        sp.copy(brow[:], psC[:, :], engine="scalar")
        sp.dma(C_o[:, tsl], brow[:])

    for g in range(16):
        sp.dma(dtw_g[:], dtwT[:, g * 128:(g + 1) * 128].rearrange("(f p) c -> p f c", p=128))
        sp.dma(xs_g[:], xc_o[g * 128:(g + 1) * 128, :])
        for tc in range(4):
            tsl = slice(tc * 512, tc * 512 + 512)
            def fdl(eng, _tsl=tsl):
                last = None
                for f in range(16):
                    last = eng.matmul(psA[:], dtw_g[:, f, :], xs_bf[:, f, _tsl],
                                      start=(f == 0), stop=(f == 15))
                return last
            sp.mms(fdl)
            sp.ts(t1b[:], psA[:], bdt[:, g:g + 1], ALU.add)
            sp.op("scalar", lambda e: e.activation(out=t2b[:], in_=t1b[:], func=AF.Abs))
            sp.op("scalar", lambda e: e.activation(out=t2b[:], in_=t2b[:], func=AF.Exp, scale=-1.0))
            def fln(eng):
                return eng.activation(out=t2b[:], in_=t2b[:], func=AF.Ln, bias=one_col[:])
            sp.op("scalar", fln)
            sp.ts(t3b[:], t1b[:], 0.0, ALU.max)
            sp.tt(t3b[:], t3b[:], t2b[:], ALU.add)
            sp.ts(t3b[:], t3b[:], 1e-4, ALU.add)
            sp.dma(dt_o[g * 128:(g + 1) * 128, tsl], t3b[:])
            sp.tt(t3b[:], t3b[:], xs_g[:, tsl], ALU.mult)
            sp.dma(u_o[g * 128:(g + 1) * 128, tsl], t3b[:])
    return sp


def run_l2(X, inp, Wp):
    sp = build_l2a()
    cw = inp["L2_attn_conv_w"].reshape(2048, 4)
    shared = {
        "winT": np.ascontiguousarray(Wp["L2_attn_in_proj_w_tb"]),
        "bwT": np.ascontiguousarray(Wp["L2_attn_B_w_tb"]),
        "cwT": np.ascontiguousarray(Wp["L2_attn_C_w_tb"]),
        "dtwT": np.ascontiguousarray(Wp["L2_attn_dt_w_tb"]),
        "nwL": np.ascontiguousarray(inp["L2_attn_norm_w"].reshape(8, 128).T),
        "cwL": np.ascontiguousarray(cw.reshape(16, 128, 4).transpose(1, 0, 2)),
        "cbL": np.ascontiguousarray(inp["L2_attn_conv_b"].reshape(16, 128).T),
        "bdtL": np.ascontiguousarray(inp["L2_attn_dt_b"].reshape(16, 128).T),
    }
    in_maps = []
    for c in range(NC):
        b, q = c // 4, c % 4
        xb = X[b * 8192:(b + 1) * 8192]
        t0 = 2048 * q
        xh = np.zeros((2051, 1024), dtype=np.float32)
        lo = max(0, t0 - 3)
        xh[3 - (t0 - lo):3] = xb[lo:t0]
        xh[3:] = xb[t0:t0 + 2048]
        m = dict(shared)
        m["xhT"] = np.ascontiguousarray(xh.T)
        in_maps.append(m)
    res = sp.run(in_maps, core_ids=range(NC))
    parts_list = [dict(r) for r in res.results]
    y = run_scan(parts_list, "dt_o", "u_o", "B_o", "C_o", inp["L2_attn_A_log"], 64)
    spc = build_l1c()
    shared_c = {
        "DpL": np.ascontiguousarray(inp["L2_attn_D_param"].reshape(16, 128).T),
        "woTb": np.ascontiguousarray(Wp["L2_attn_out_proj_w_tb"]),
    }
    in_maps = []
    for c in range(NC):
        m = dict(shared_c)
        m["xT"] = np.ascontiguousarray(X[c * TC:(c + 1) * TC].T)
        m["ysT"] = np.ascontiguousarray(y[:, c * TC:(c + 1) * TC])
        m["xsT"] = np.asarray(parts_list[c]["xc_o"])
        m["zT"] = np.asarray(parts_list[c]["z_o"])
        in_maps.append(m)
    res = spc.run(in_maps, core_ids=range(NC))
    out = np.empty_like(X)
    for c in range(NC):
        out[c * TC:(c + 1) * TC] = res.results[c]["yT"].T
    return out


def build_mlp2(sp=None, io=None):
    if sp is None:
        sp = SP()
    io = io or {}
    def din(n, s, d=F32):
        return io[n] if n in io else sp.din(n, s, d)
    def dout(n, s, d=F32):
        return io[n] if n in io else sp.dout(n, s, d)
    xT = din("xT", (1024, TC))
    Wub = din("Wub", (1024, 4096), BF16)
    Wdb = din("Wdb", (4096, 1024), BF16)
    bupL = None if "bup_row" in (io or {}) else din("bupL", (128, 32))
    bdnL = None if "bup_row" in (io or {}) else din("bdnL", (128, 8))
    yT = dout("yT", (1024, TC))

    ar = Arena(sp, 40960)
    x_sb = ar.f32(0, (8, TC))
    xn_bf = ar.bf16(16384, (8, TC))
    rows_off = 24576
    h_bf = ar.bf16(0, (16, TC))
    acc = ar.f32(24576, (8, TC))

    ones_col = sp.sbuf("onc", [128, 1], F32)
    ones_r128 = sp.sbuf("o128", [1, 128], F32)
    bup = sp.sbuf("bup", [128, 32], F32)
    bdn = sp.sbuf("bdn", [128, 8], F32)
    brow_t = sp.sbuf("brt", [1, 4096], F32)
    ident1 = make_ident(sp, "id1", 1, F32)
    psT1 = sp.psum("pT1", [128, 1])
    w_sb = sp.sbuf("wsb", [128, 8, 128], BF16)
    wdn = sp.sbuf("wdn", [128, 16, 128], BF16)
    sq = sp.sbuf("sqq", [128, 512], F32)
    tmp_f = sp.sbuf("tmf", [128, 512], F32)
    xload = sp.sbuf("xld", [128, TC], F32)
    psA = sp.psum_at(0, "psa", (128, TC))
    psO = sp.psum_at(4, "pso", (128, TC))
    psB0 = sp.psum("pb0", [128, 512])
    psB1 = sp.psum("pb1", [128, 512])
    psR = sp.psum("psr", [1, 512])

    sp.memset(ones_col[:], 1.0, engine="gpsimd")
    sp.memset(ones_r128[:], 1.0, engine="gpsimd")
    if "bup_row" in io:
        sp.dmas([(x_sb, xT.rearrange("(a p) t -> p a t", p=128)),
                 (brow_t[:], io["bup_row"])])
        for j in range(32):
            sp.transpose(psT1[:, :], brow_t[0:1, j * 128:(j + 1) * 128], ident1[:])
            sp.copy(bup[:, j:j + 1], psT1[:, :], engine="scalar")
        sp.dma(brow_t[0:1, 0:1024], io["bdn_row"])
        for j in range(8):
            sp.transpose(psT1[:, :], brow_t[0:1, j * 128:(j + 1) * 128], ident1[:])
            sp.copy(bdn[:, j:j + 1], psT1[:, :], engine="scalar")
    else:
        sp.dmas([(x_sb, xT.rearrange("(a p) t -> p a t", p=128)),
                 (bup[:], bupL), (bdn[:], bdnL)])
    sp.ts(bdn[:], bdn[:], 0.5, ALU.mult)

    mrow = ar.row(rows_off, TC)
    srow = ar.row(rows_off + TC, TC)
    sinv = ar.row(rows_off + 2 * TC, TC)
    msr = ar.row(rows_off + 3 * TC, TC)
    for tc in range(4):
        tsl = slice(tc * 512, tc * 512 + 512)
        def fmean(eng, _tsl=tsl):
            last = None
            for a in range(8):
                last = eng.matmul(psR[0:1, :], ones_col[:], x_sb[:, a, _tsl],
                                  start=(a == 0), stop=(a == 7))
            return last
        sp.mms(fmean)
        sp.act(mrow[:, tsl], psR[0:1, :], AF.Copy, scale=1.0 / 1024)
        for a in range(8):
            sp.tt(sq[:], x_sb[:, a, tsl], x_sb[:, a, tsl], ALU.mult)
            def fsq(eng, _a=a):
                return eng.matmul(psR[0:1, :], ones_col[:], sq[:],
                                  start=(_a == 0), stop=(_a == 7))
            sp.mms(fsq)
        sp.act(srow[:, tsl], psR[0:1, :], AF.Copy, scale=1.0 / 1024)
    sp.tt(msr[:], mrow[:], mrow[:], ALU.mult)
    sp.tt(srow[:], srow[:], msr[:], ALU.subtract)
    sp.ts(srow[:], srow[:], 1e-5, ALU.add)
    sp.act(srow[:], srow[:], AF.Sqrt)
    sp.recip(sinv[:], srow[:])
    sp.tt(msr[:], mrow[:], sinv[:], ALU.mult)
    for tc in range(4):
        tsl = slice(tc * 512, tc * 512 + 512)
        def fb(eng, _tsl=tsl):
            eng.matmul(psB0[:], ones_r128[:], sinv[:, _tsl], start=True, stop=True)
            return eng.matmul(psB1[:], ones_r128[:], msr[:, _tsl], start=True, stop=True)
        sp.mms(fb)
        for a in range(8):
            sp.tt(tmp_f[:], x_sb[:, a, tsl], psB0[:], ALU.mult)
            sp.tt(xn_bf[:, a, tsl], tmp_f[:], psB1[:], ALU.subtract)

    for half in range(2):
        for hf in range(16):
            j = half * 16 + hf
            sp.dma(w_sb[:], Wub[:, j * 128:(j + 1) * 128].rearrange("(a p) f -> p a f", p=128))
            def fa(eng):
                last = None
                for tc in range(4):
                    tsl = slice(tc * 512, tc * 512 + 512)
                    for a in range(8):
                        last = eng.matmul(psA[:, tsl], w_sb[:, a, :], xn_bf[:, a, tsl],
                                          start=(a == 0), stop=(a == 7))
                return last
            sp.mms(fa)
            def fgel(eng, _hf=hf, _j=j):
                return eng.activation(out=h_bf[:, _hf, :], in_=psA[:, :], func=AF.Gelu,
                                      bias=bup[:, _j:_j + 1])
            sp.op("scalar", fgel)
        for fo in range(8):
            sp.dma(wdn[:], Wdb[half * 2048:(half + 1) * 2048, fo * 128:(fo + 1) * 128]
                   .rearrange("(g p) f -> p g f", p=128))
            def fd(eng):
                last = None
                for tc in range(4):
                    tsl = slice(tc * 512, tc * 512 + 512)
                    for g in range(16):
                        last = eng.matmul(psO[:, tsl], wdn[:, g, :], h_bf[:, g, tsl],
                                          start=(g == 0), stop=(g == 15))
                return last
            sp.mms(fd)
            if half == 0:
                sp.ts(acc[:, fo, :], psO[:, :], 0.5, ALU.mult)
            else:
                def facc(eng, _fo=fo):
                    return eng.scalar_tensor_tensor(out=acc[:, _fo, :], in0=psO[:, :],
                                                    scalar=0.5, in1=acc[:, _fo, :],
                                                    op0=ALU.mult, op1=ALU.add)
                sp.op("vector", facc)
    for fo in range(8):
        sp.dma(xload[:], xT[fo * 128:(fo + 1) * 128, :])
        def ffin(eng, _fo=fo):
            return eng.scalar_tensor_tensor(out=xload[:], in0=acc[:, _fo, :],
                                            scalar=bdn[:, _fo:_fo + 1], in1=xload[:],
                                            op0=ALU.add, op1=ALU.add)
        sp.op("vector", ffin)
        sp.dma(yT[fo * 128:(fo + 1) * 128, :], xload[:])
    return sp


def run_mlp2(X, inp, Wp):
    sp = build_mlp2()
    shared = {
        "Wub": np.ascontiguousarray(Wp["W_exp_fb"]),
        "Wdb": np.ascontiguousarray(Wp["W_con_fb"]),
        "bupL": np.ascontiguousarray(np.asarray(Wp["bexp_f"]).reshape(32, 128).T),
        "bdnL": np.ascontiguousarray(np.asarray(Wp["bcon_f"]).reshape(8, 128).T),
    }
    in_maps = []
    for c in range(NC):
        m = dict(shared)
        m["xT"] = np.ascontiguousarray(X[c * TC:(c + 1) * TC].T)
        in_maps.append(m)
    res = sp.run(in_maps, core_ids=range(NC))
    out = np.empty_like(X)
    for c in range(NC):
        out[c * TC:(c + 1) * TC] = res.results[c]["yT"].T
    return out


def build_mlp3(sp=None, io=None):
    if sp is None:
        sp = SP()
    io = io or {}
    def din(n, s, d=F32):
        return io[n] if n in io else sp.din(n, s, d)
    def dout(n, s, d=F32):
        return io[n] if n in io else sp.dout(n, s, d)
    xT = din("xT", (1024, TC))
    Wub = din("Wub", (1024, 4096), BF16)
    Wdb = din("Wdb", (4096, 1024), BF16)
    yT = dout("yT", (1024, TC))

    ar = Arena(sp, 40960)
    x_sb = ar.f32(0, (8, TC))
    xn_bf = ar.bf16(16384, (8, TC))
    rows_off = 24576
    h_lo = ar.bf16(0, (16, TC))
    h_hi = ar.bf16(24576, (16, TC))

    def hv(j):
        return (h_lo if j < 16 else h_hi)[:, j % 16, :]

    ones_col = sp.sbuf("onc", [128, 1], F32)
    ones_r128 = sp.sbuf("o128", [1, 128], F32)
    w_sb = sp.sbuf("wsb", [128, 8, 128], BF16)
    wdn = sp.sbuf("wdn", [128, 32, 128], BF16)
    sq = sp.sbuf("sqq", [128, 512], F32)
    tmp_f = sp.sbuf("tmf", [128, 512], F32)
    tmp_fw = sp.sbuf("tfw", [128, TC], F32)
    tmp2w = sp.sbuf("t2w", [128, TC], F32)
    sqw = tmp_fw
    s1rep = sp.sbuf("s1r", [128, TC], F32)
    xload = sp.sbuf("xld", [128, TC], F32)
    psA = sp.psum_at(0, "psa", (128, TC))
    psO = sp.psum_at(0, "pso", (128, TC))
    psB0 = sp.psum("pb0", [128, 512])
    psB1 = sp.psum("pb1", [128, 512])
    psR2 = sp.psum_at(4, "ps2", (1, TC))
    psR = psR2[0:1, 0:512]

    sp.memset(ones_col[:], 1.0, engine="gpsimd")
    sp.memset(ones_r128[:], 1.0, engine="gpsimd")
    sp.dma(x_sb, xT.rearrange("(a p) t -> p a t", p=128))

    mrow = ar.row(rows_off, TC)
    srow = ar.row(rows_off + TC, TC)
    sinv = ar.row(rows_off + 2 * TC, TC)
    msr = ar.row(rows_off + 3 * TC, TC)
    for tc in range(4):
        tsl = slice(tc * 512, tc * 512 + 512)
        def fmean(eng, _tsl=tsl):
            last = None
            for a in range(8):
                last = eng.matmul(psR[0:1, :], ones_col[:], x_sb[:, a, _tsl],
                                  start=(a == 0), stop=(a == 7))
            return last
        sp.mms(fmean)
        sp.act(mrow[:, tsl], psR[0:1, :], AF.Copy, scale=1.0 / 1024)
        for a in range(8):
            sp.tt(sq[:], x_sb[:, a, tsl], x_sb[:, a, tsl], ALU.mult)
            def fsq(eng, _a=a):
                return eng.matmul(psR[0:1, :], ones_col[:], sq[:],
                                  start=(_a == 0), stop=(_a == 7))
            sp.mms(fsq)
        sp.act(srow[:, tsl], psR[0:1, :], AF.Copy, scale=1.0 / 1024)
    sp.tt(msr[:], mrow[:], mrow[:], ALU.mult)
    sp.tt(srow[:], srow[:], msr[:], ALU.subtract)
    sp.ts(srow[:], srow[:], 1e-5, ALU.add)
    sp.act(srow[:], srow[:], AF.Sqrt)
    sp.recip(sinv[:], srow[:])
    sp.tt(msr[:], mrow[:], sinv[:], ALU.mult)
    for tc in range(4):
        tsl = slice(tc * 512, tc * 512 + 512)
        def fb(eng, _tsl=tsl):
            eng.matmul(psB0[:], ones_r128[:], sinv[:, _tsl], start=True, stop=True)
            return eng.matmul(psB1[:], ones_r128[:], msr[:, _tsl], start=True, stop=True)
        sp.mms(fb)
        for a in range(8):
            sp.tt(tmp_f[:], x_sb[:, a, tsl], psB0[:], ALU.mult)
            sp.tt(x_sb[:, a, tsl], tmp_f[:], psB1[:], ALU.subtract)
    for tc in range(4):
        tsl = slice(tc * 512, tc * 512 + 512)
        for a in range(8):
            sp.tt(sq[:], x_sb[:, a, tsl], x_sb[:, a, tsl], ALU.mult)
            def fsq(eng, _a=a):
                return eng.matmul(psR[0:1, :], ones_col[:], sq[:],
                                  start=(_a == 0), stop=(_a == 7))
            sp.mms(fsq)
        sp.act(srow[:, tsl], psR[0:1, :], AF.Copy, scale=1.0 / 1024)
    sp.act(srow[:], srow[:], AF.Sqrt)
    sp.ts(srow[:], srow[:], 1e-8, ALU.add)
    sp.recip(sinv[:], srow[:])
    for tc in range(4):
        tsl = slice(tc * 512, tc * 512 + 512)
        def fb1(eng, _tsl=tsl):
            return eng.matmul(psB0[:], ones_r128[:], sinv[:, _tsl], start=True, stop=True)
        sp.mms(fb1)
        sp.copy(s1rep[:, tsl], psB0[:])
        for a in range(8):
            sp.copy(xn_bf[:, a, tsl], x_sb[:, a, tsl])

    for j in range(32):
        sp.dma(w_sb[:], Wub[:, j * 128:(j + 1) * 128].rearrange("(a p) f -> p a f", p=128))
        def fa(eng):
            last = None
            for tc in range(4):
                tsl = slice(tc * 512, tc * 512 + 512)
                for a in range(8):
                    last = eng.matmul(psA[:, tsl], w_sb[:, a, :], xn_bf[:, a, tsl],
                                      start=(a == 0), stop=(a == 7))
            return last
        sp.mms(fa)
        sp.tt(tmp_fw[:], psA[:, :], s1rep[:, :], ALU.mult)
        sp.op("scalar", lambda e: e.activation(out=tmp2w[:], in_=tmp_fw[:], func=AF.Gelu))
        sp.copy(hv(j), tmp2w[:])
        sp.tt(sqw[:], tmp2w[:], tmp2w[:], ALU.mult)
        def fh2(eng, _j=j):
            last = None
            for tc in range(4):
                last = eng.matmul(psR2[0:1, tc * 512:(tc + 1) * 512],
                                  ones_col[:], sqw[:, tc * 512:(tc + 1) * 512],
                                  start=(_j == 0), stop=(_j == 31))
            return last
        sp.mms(fh2)
    srow2 = ar.row(16384, TC)
    sinv2 = ar.row(16384 + TC, TC)
    s2rep = ar.f32(16384 + 2 * TC, (TC,))
    sp.act(srow2[:], psR2[0:1, :], AF.Copy, scale=1.0 / 4096)
    sp.act(srow2[:], srow2[:], AF.Sqrt)
    sp.ts(srow2[:], srow2[:], 1e-8, ALU.add)
    sp.recip(sinv2[:], srow2[:])
    for tc in range(4):
        tsl = slice(tc * 512, tc * 512 + 512)
        def fb2(eng, _tsl=tsl):
            return eng.matmul(psB0[:], ones_r128[:], sinv2[:, _tsl], start=True, stop=True)
        sp.mms(fb2)
        sp.copy(s2rep[:, tsl], psB0[:])
    for fo in range(8):
        sp.dma(wdn[:], Wdb[:, fo * 128:(fo + 1) * 128].rearrange("(g p) f -> p g f", p=128))
        sp.dma(xload[:], xT[fo * 128:(fo + 1) * 128, :])
        def fd(eng):
            last = None
            for tc in range(4):
                tsl = slice(tc * 512, tc * 512 + 512)
                for g in range(32):
                    last = eng.matmul(psO[:, tsl], wdn[:, g, :], hv(g)[:, tsl],
                                      start=(g == 0), stop=(g == 31))
            return last
        sp.mms(fd)
        sp.tt(tmp_fw[:], psO[:, :], s2rep[:, :], ALU.mult)
        def ffin(eng):
            return eng.scalar_tensor_tensor(out=xload[:], in0=tmp_fw[:],
                                            scalar=0.5, in1=xload[:],
                                            op0=ALU.mult, op1=ALU.add)
        sp.op("vector", ffin)
        sp.dma(yT[fo * 128:(fo + 1) * 128, :], xload[:])
    return sp


def run_mlp3(X, inp, Wp):
    sp = build_mlp3()
    shared = {
        "Wub": np.ascontiguousarray(Wp["L3_ffn_w_up_qtb"]),
        "Wdb": np.ascontiguousarray(Wp["L3_ffn_w_down_qtb"]),
    }
    in_maps = []
    for c in range(NC):
        m = dict(shared)
        m["xT"] = np.ascontiguousarray(X[c * TC:(c + 1) * TC].T)
        in_maps.append(m)
    res = sp.run(in_maps, core_ids=range(NC))
    out = np.empty_like(X)
    for c in range(NC):
        out[c * TC:(c + 1) * TC] = res.results[c]["yT"].T
    return out


def build_attn3(sp=None, io=None):
    if sp is None:
        sp = SP()
    io = io or {}
    def din(n, s, d=F32):
        return io[n] if n in io else sp.din(n, s, d)
    def dout(n, s, d=F32):
        return io[n] if n in io else sp.dout(n, s, d)
    xfT = None if "xf_chunk" in (io or {}) else din("xfT", (1024, L))
    xqT = din("xqT", (1024, TC))
    xrT = din("xrT", (1024, TC))
    Wqb = din("Wqb", (1024, 1024), BF16)
    Wkb = din("Wkb", (1024, 512), BF16)
    Wvb = din("Wvb", (1024, 512), BF16)
    memb = din("memb", (512, 512), BF16)
    mnL = din("mnL", (128, 4))
    mwC = din("mwC", (1, 1))
    yT = dout("yT", (1024, TC))

    KH = 4096
    ar = Arena(sp, 40960)
    k_sb = ar.bf16(0, (4, KH))
    vt = ar.bf16(8192, (8, 32, 65))
    q_sb = ar.bf16(16512, (8, TC))
    xch = ar.f32(24704, (8, 512))
    xnch = ar.bf16(28800, (8, 512))
    probsT = ar.bf16(24704, (TC,))
    qal = ar.bf16(29824, (TC,))
    pvb = ar.f32(30848, (TC,))[0:65, :]
    sig = ar.bf16(0, (4, TC))
    mem_sb = ar.bf16(8192, (4, 512))
    e1s = ar.f32(24704, (512,))
    mks = ar.f32(25216, (512,))
    LN_off = 32896
    sums_off = 34944

    ones_col = sp.sbuf("onc", [128, 1], F32)
    ones_r128 = sp.sbuf("o12", [1, 128], F32)
    ones_r64 = sp.sbuf("o64", [1, 64], F32)
    identf = make_ident(sp, "idf", 128, F32)
    wk_sb = sp.sbuf("wks", [128, 8, 512], BF16)
    wv_sb = sp.sbuf("wvs", [128, 8, 512], BF16)
    wq_h = sp.sbuf("wqh", [128, 8, 64], BF16)
    v_tile = sp.sbuf("vtl", [128, 512], F32)
    rrep = sp.sbuf("rrp", [64, 512], F32)
    xl64 = sp.sbuf("xl", [64, 512], F32)
    t64 = sp.sbuf("t64", [64, 512], F32)
    wrow = sp.sbuf("wrw", [1, 1], F32)
    wcol = sp.sbuf("wcl", [128, 1], F32)
    omw = sp.sbuf("omw", [128, 1], F32)
    mn_sb = sp.sbuf("mns", [128, 4], F32)
    mn_bf = sp.sbuf("mnb", [128, 4], BF16)
    wdram = sp.dtmp("wd", (1, 1))

    pv2 = sp.sbuf("pv2", [65, TC], F32)
    pv_dram = [sp.dtmp(f"pvd{h}", (65, TC)) for h in range(16)]
    sums_dram = sp.dtmp("smd", (1, 512))
    psS = sp.psum("psS", [128, 2048])
    psPV = sp.psum("psP", [128, 2048])

    sp.memset(ones_col[:], 1.0, engine="gpsimd")
    sp.memset(ones_r128[:], 1.0, engine="gpsimd")
    sp.memset(ones_r64[:], 1.0, engine="gpsimd")
    sp.memset(ar.t[:, 8192:16512].bitcast(BF16), 1.0, engine="gpsimd")
    sp.dmas([(wk_sb[:], Wkb.rearrange("(a p) f -> p a f", p=128)),
             (wv_sb[:], Wvb.rearrange("(a p) f -> p a f", p=128)),
             (mn_sb[:], mnL), (wrow[:], mwC)])
    sp.copy(mn_bf[:], mn_sb[:])
    sp.act(wrow[:], wrow[:], AF.Sigmoid)
    sp.dma(wdram, wrow[:])
    sp.dma(wcol[:], wdram.to_broadcast([128, 1]))
    sp.op("vector", lambda e: e.tensor_scalar(out=omw[:], in0=wcol[:], scalar1=-1.0,
                                              scalar2=1.0, op0=ALU.mult, op1=ALU.add))

    mrow = ar.row(LN_off, 512)
    srow = ar.row(LN_off + 512, 512)
    sinv = ar.row(LN_off + 1024, 512)
    msr = ar.row(LN_off + 1536, 512)
    sums = ar.row(sums_off, 512)
    rcp = ar.row(sums_off + 512, 512)

    def emit_ln_chunk(src_ap):
        def fmean(eng):
            last = None
            for a in range(8):
                last = eng.matmul(psS[0:1, 0:512], ones_col[:], xch[:, a, :],
                                  start=(a == 0), stop=(a == 7))
            return last
        sp.mms(fmean)
        sp.act(mrow[:], psS[0:1, 0:512], AF.Copy, scale=1.0 / 1024)
        for a in range(8):
            sp.tt(v_tile[:], xch[:, a, :], xch[:, a, :], ALU.mult)
            def fsq(eng, _a=a):
                return eng.matmul(psS[0:1, 512:1024], ones_col[:], v_tile[:],
                                  start=(_a == 0), stop=(_a == 7))
            sp.mms(fsq)
        sp.act(srow[:], psS[0:1, 512:1024], AF.Copy, scale=1.0 / 1024)
        sp.tt(msr[:], mrow[:], mrow[:], ALU.mult)
        sp.tt(srow[:], srow[:], msr[:], ALU.subtract)
        sp.ts(srow[:], srow[:], 1e-5, ALU.add)
        sp.act(srow[:], srow[:], AF.Sqrt)
        sp.recip(sinv[:], srow[:])
        sp.tt(msr[:], mrow[:], sinv[:], ALU.mult)
        def fb(eng):
            eng.matmul(psS[:, 512:1024], ones_r128[:], sinv[:], start=True, stop=True)
            return eng.matmul(psS[:, 1024:1536], ones_r128[:], msr[:], start=True, stop=True)
        sp.mms(fb)
        for a in range(8):
            sp.tt(v_tile[:], xch[:, a, :], psS[:, 512:1024], ALU.mult)
            sp.tt(xnch[:, a, :], v_tile[:], psS[:, 1024:1536], ALU.subtract)

    for tcc in range(4):
        tsl = slice(tcc * 512, tcc * 512 + 512)
        sp.dma(xch, xqT.rearrange("(a p) t -> p a t", p=128)[:, :, tsl])
        emit_ln_chunk(None)
        for h in range(16):
            ob = (h % 2) * 64
            sp.dma(wq_h[:], Wqb[:, h * 64:(h + 1) * 64].rearrange("(a p) f -> p a f", p=128))
            def fq(eng, _ob=ob):
                last = None
                for a in range(8):
                    last = eng.matmul(psS[_ob:_ob + 64, 1536:2048], wq_h[:, a, :],
                                      xnch[:, a, :], start=(a == 0), stop=(a == 7))
                return last
            sp.mms(fq)
            def fevq(eng, _h=h, _ob=ob, _tsl=tsl):
                return eng.copy(out=q_sb[_ob:_ob + 64, _h // 2, _tsl],
                                in_=psS[_ob:_ob + 64, 1536:2048])
            sp.op("scalar", fevq)

    for kh in range(2):
        for tcc in range(8):
            t0 = kh * KH + tcc * 512
            lsl = slice(tcc * 512, tcc * 512 + 512)
            if "xf_chunk" in io:
                sp.dma(xch, io["xf_chunk"](t0))
            else:
                sp.dma(xch, xfT.rearrange("(a p) t -> p a t", p=128)[:, :, t0:t0 + 512])
            emit_ln_chunk(None)
            for g in range(8):
                ob = (g % 2) * 64
                def fk(eng, _g=g, _ob=ob):
                    last = None
                    for a in range(8):
                        last = eng.matmul(psS[_ob:_ob + 64, 1536:2048],
                                          wk_sb[:, a, _g * 64:(_g + 1) * 64],
                                          xnch[:, a, :], start=(a == 0), stop=(a == 7))
                    return last
                sp.mms(fk)
                def fevk(eng, _g=g, _ob=ob, _lsl=lsl):
                    return eng.copy(out=k_sb[_ob:_ob + 64, _g // 2, _lsl],
                                    in_=psS[_ob:_ob + 64, 1536:2048])
                sp.op("scalar", fevk)
            for fo in range(4):
                def fv(eng, _fo=fo):
                    last = None
                    for a in range(8):
                        last = eng.matmul(psS[:, 1536:2048],
                                          wv_sb[:, a, _fo * 128:(_fo + 1) * 128],
                                          xnch[:, a, :], start=(a == 0), stop=(a == 7))
                    return last
                sp.mms(fv)
                sp.copy(v_tile[:], psS[:, 1536:2048], engine="scalar")
                for hh in range(2):
                    g = fo * 2 + hh
                    for kb in range(4):
                        kbg = tcc * 4 + kb
                        def ftr(eng, _hh=hh, _kb=kb):
                            return eng.transpose(psPV[:, 0:64],
                                                 v_tile[_hh * 64:(_hh + 1) * 64,
                                                        _kb * 128:(_kb + 1) * 128],
                                                 identf[_hh * 64:(_hh + 1) * 64,
                                                        _hh * 64:(_hh + 1) * 64])
                        sp.op("tensor", ftr)
                        def fevt(eng, _g=g, _kbg=kbg):
                            return eng.copy(out=vt[:, _g, _kbg, 0:64], in_=psPV[:, 0:64])
                        sp.op("scalar", fevt)
        for h in range(16):
            g = h // 2
            obq = (h % 2) * 64
            obk = (g % 2) * 64
            sp.dma(qal.rearrange("p t -> p t")[obk:obk + 64, :] if False else
                   ar.t[obk:obk + 64, 29824:29824 + 1024].bitcast(BF16),
                   ar.t[obq:obq + 64, 16512 + (h // 2) * 1024:16512 + (h // 2) * 1024 + 1024]
                   .bitcast(BF16), engine="sync")
            for kbg in range(32):
                ksl = slice(kbg * 128, kbg * 128 + 128)
                def fsT(eng, _g=g, _obk=obk, _ksl=ksl):
                    last = None
                    for qc in range(4):
                        last = eng.matmul(psS[:, qc * 512:(qc + 1) * 512],
                                          k_sb[_obk:_obk + 64, _g // 2, _ksl],
                                          ar.t[_obk:_obk + 64, 29824 + qc * 256:29824 + (qc + 1) * 256]
                                          .bitcast(BF16),
                                          start=True, stop=True)
                    return last
                sp.mms(fsT)
                sp.act(probsT, psS[:, :], AF.Exp, scale=0.125)
                def fpv(eng, _g=g, _kbg=kbg):
                    last = None
                    pr = probsT.rearrange("p (qc t) -> p qc t", qc=4)
                    for qc in range(4):
                        last = eng.matmul(psPV[0:65, qc * 512:(qc + 1) * 512],
                                          vt[:, _g, _kbg, :], pr[:, qc, :],
                                          start=(_kbg == 0), stop=(_kbg == 31))
                    return last
                sp.mms(fpv)
            sp.copy(pvb, psPV[0:65, :], engine="scalar")
            if kh == 0:
                sp.dma(pv_dram[h], pvb)
            else:
                sp.dma(pv2[:], pv_dram[h])
                sp.tt(pvb, pvb, pv2[:], ALU.add)
                for tc in range(4):
                    tsl = slice(tc * 512, tc * 512 + 512)
                    sp.dma(sums_dram, pvb[64:65, tsl])
                    sp.dma(sums, sums_dram)
                    sp.recip(rcp[:], sums[:])
                    def fbc(eng):
                        return eng.matmul(psS[0:64, 0:512], ones_r64[:], rcp[:],
                                          start=True, stop=True)
                    sp.mms(fbc)
                    sp.copy(rrep[:], psS[0:64, 0:512], engine="scalar")
                    sp.tt(t64[:], pvb[0:64, tsl], rrep[:], ALU.mult)
                    sp.ts(t64[:], t64[:], omw[0:64, :], ALU.mult)
                    sp.dma(xl64[:], xrT[h * 64:(h + 1) * 64, tsl])
                    sp.tt(xl64[:], xl64[:], t64[:], ALU.add)
                    sp.dma(yT[h * 64:(h + 1) * 64, tsl], xl64[:])

    sp.dma(mem_sb[:], memb.rearrange("(j p) f -> p j f", p=128))
    e1v = ar.f32(24704, (512,))
    mkv = ar.f32(25216, (512,))
    t128 = sp.sbuf("t128", [128, 512], F32)
    xl128 = sp.sbuf("xl2", [128, 512], F32)
    rrep128 = sp.sbuf("rr2", [128, 512], F32)
    for g2 in range(2):
        for jj in range(4):
            j = g2 * 4 + jj
            for tc in range(4):
                tsl = slice(tc * 512, tc * 512 + 512)
                sp.ts(mkv, q_sb[:, j, tsl], 0.0, ALU.min)
                sp.op("scalar", lambda e: e.activation(out=e1v, in_=mkv, func=AF.Exp))
                sp.ts(e1v, e1v, 1.0, ALU.subtract)
                sp.ts(mkv, q_sb[:, j, tsl], 0.0, ALU.max)
                sp.tt(sig[:, jj, tsl], mkv, e1v, ALU.add)
        for tc in range(4):
            tsl = slice(tc * 512, tc * 512 + 512)
            def fden(eng, _tsl=tsl):
                last = None
                for jj in range(4):
                    last = eng.matmul(psS[0:1, 0:512], mn_bf[:, jj:jj + 1],
                                      sig[:, jj, _tsl], start=(jj == 0), stop=(jj == 3))
                return last
            sp.mms(fden)
            sp.act(sums[:], psS[0:1, 0:512], AF.Copy)
            sp.recip(rcp[:], sums[:])
            def fbc2(eng):
                eng.matmul(psS[0:64, 512:1024], ones_r64[:], rcp[:], start=True, stop=True)
                return eng.matmul(psS[64:128, 512:1024], ones_r64[:], rcp[:], start=True, stop=True)
            sp.mms(fbc2)
            sp.copy(rrep128[:], psS[:, 512:1024], engine="scalar")
            for fo in range(4):
                def fnum(eng, _fo=fo, _tsl=tsl):
                    last = None
                    for jj in range(4):
                        last = eng.matmul(psPV[0:128, 0:512],
                                          mem_sb[:, jj, _fo * 128:(_fo + 1) * 128],
                                          sig[:, jj, _tsl], start=(jj == 0), stop=(jj == 3))
                    return last
                sp.mms(fnum)
                sp.tt(t128[:], psPV[0:128, 0:512], rrep128[:], ALU.mult)
                sp.ts(t128[:], t128[:], wcol[:], ALU.mult)
                r0 = g2 * 512 + fo * 128
                sp.dma(xl128[:], yT[r0:r0 + 128, tsl])
                sp.tt(xl128[:], xl128[:], t128[:], ALU.add)
                sp.dma(yT[r0:r0 + 128, tsl], xl128[:])
    return sp


def run_attn3(X, inp, Wp):
    sp = build_attn3()
    shared = {
        "Wqb": np.ascontiguousarray(Wp["L3_attn_Wq_b"]),
        "Wkb": np.ascontiguousarray(Wp["L3_attn_Wk_b"]),
        "Wvb": np.ascontiguousarray(Wp["L3_attn_Wv_b"]),
        "memb": np.ascontiguousarray(Wp["L3_attn_memory_b"]),
        "mnL": np.ascontiguousarray(inp["L3_attn_memory_norm"].reshape(4, 128).T),
        "mwC": np.asarray(inp["L3_attn_memory_weight"], np.float32).reshape(1, 1),
    }
    in_maps = []
    xfTs = [np.ascontiguousarray(X[b * 8192:(b + 1) * 8192].T) for b in range(2)]
    for c in range(NC):
        b, q = c // 4, c % 4
        m = dict(shared)
        m["xfT"] = xfTs[b]
        xo = np.ascontiguousarray(X[b * 8192 + 2048 * q: b * 8192 + 2048 * (q + 1)].T)
        m["xqT"] = xo
        m["xrT"] = xo
        in_maps.append(m)
    res = sp.run(in_maps, core_ids=range(NC))
    out = np.empty_like(X)
    for c in range(NC):
        out[c * TC:(c + 1) * TC] = res.results[c]["yT"].T
    return out


def kernel(**inp):
    inp = {k: np.asarray(v) for k, v in inp.items()}
    X = run_fused(inp)
    return X.reshape(2, 8192, 1024)


def kernel_multilaunch(**inp):
    inp = {k: np.asarray(v) for k, v in inp.items()}
    Wp = run_prep(inp)
    X = np.ascontiguousarray(np.asarray(inp["L0_attn_x"], np.float32).reshape(T, D))
    X = run_moe0(X, inp, Wp)
    X = run_attn0(X, inp, Wp)
    X = run_moe0(X, inp, Wp)
    X = run_moe1(X, inp, Wp)
    X = run_l1(X, inp, Wp)
    X = run_moe1(X, inp, Wp)
    X = run_mlp2(X, inp, Wp)
    X = run_l2(X, inp, Wp)
    X = run_mlp2(X, inp, Wp)
    X = run_mlp3(X, inp, Wp)
    X = run_attn3(X, inp, Wp)
    X = run_mlp3(X, inp, Wp)
    return X.reshape(2, 8192, 1024)


GROUPS8 = [list(range(8))]
GROUPSB = [[0, 1, 2, 3], [4, 5, 6, 7]]

_PREP_OUT_SHAPES = {
    "L0_attn_Wq1_b": ((1024, 1024), BF16), "L0_attn_Wq2_b": ((1024, 1024), BF16),
    "L0_attn_Wk1_b": ((1024, 128), BF16), "L0_attn_Wk2_b": ((1024, 128), BF16),
    "L0_attn_Wv_b": ((1024, 128), BF16), "L0_attn_Wout_b": ((1024, 1024), BF16),
    "L1_ffn_shared_gate_w_b": ((1024, 1024), BF16), "L1_ffn_shared_up_w_b": ((1024, 1024), BF16),
    "L1_ffn_shared_down_w_b": ((1024, 1024), BF16),
    "L3_attn_Wq_b": ((1024, 1024), BF16), "L3_attn_Wk_b": ((1024, 512), BF16),
    "L3_attn_Wv_b": ((1024, 512), BF16), "L3_attn_memory_b": ((512, 512), BF16),
    "L0_ffn_W1_b": ((8, 1024, 4096), BF16), "L0_ffn_W2_b": ((8, 2048, 1024), BF16),
    "L1_ffn_gate_w_b": ((16, 1024, 512), BF16), "L1_ffn_up_w_b": ((16, 1024, 512), BF16),
    "L1_ffn_down_w_b": ((16, 512, 1024), BF16),
    "L1_attn_w_in_tb": ((1024, 4096), BF16), "L1_attn_w_x_tb": ((2048, 96), BF16),
    "L1_attn_w_dt_tb": ((64, 2048), BF16), "L1_attn_w_out_tb": ((2048, 1024), BF16),
    "L2_attn_in_proj_w_tb": ((1024, 4096), BF16), "L2_attn_B_w_tb": ((2048, 64), BF16),
    "L2_attn_C_w_tb": ((2048, 64), BF16), "L2_attn_dt_w_tb": ((2048, 2048), BF16),
    "L2_attn_out_proj_w_tb": ((2048, 1024), BF16),
    "L3_ffn_w_up_qtb": ((1024, 4096), BF16), "L3_ffn_w_down_qtb": ((4096, 1024), BF16),
    "W_exp_fb": ((1024, 4096), BF16), "bexp_f": ((1, 4096), F32),
    "W_con_fb": ((4096, 1024), BF16), "bcon_f": ((1, 1024), F32),
    "lam": ((1, 1), F32),
}


def build_fused(n_sub=13):
    sp = SP(fused=True)
    nc = sp.nc
    xin = sp.din("xin", (1024, TC))
    out = sp.dout("out", (1024, TC))
    offs_in = sp.din("offs", (1, 32), mybir.dt.int32)
    hmask_in = sp.din("hmask", (128, 1))
    posrow = sp.din("posrow", (1, 1152))
    divrow = sp.din("divrow", (1, 32))
    alog1 = sp.din("AlogS1", (256, 16))
    alog2 = sp.din("AlogS2", (256, 64))
    raw = {}
    for nm, shp in [("WrL", (128, 8, 8)), ("brC", (8, 1)), ("b1L", (128, 32, 8)),
                    ("b2L", (128, 8, 8)), ("RwL", (128, 8, 16)),
                    ("gamL1", (128, 8)), ("betL1", (128, 8)), ("cwL1", (128, 16, 4)),
                    ("cbL1", (128, 16)), ("bdtL1", (128, 16)), ("DpL1", (128, 16)),
                    ("nwL2", (128, 8)), ("cwL2", (128, 16, 4)), ("cbL2", (128, 16)),
                    ("bdtL2", (128, 16)), ("DpL2", (128, 16)),
                    ("boutL", (128, 8)), ("mnL", (128, 4)), ("mwC", (1, 1))]:
        raw[nm] = sp.din(nm, shp)

    Wp = {nm: sp.dtmp(nm + "_i", shp, dt) for nm, (shp, dt) in _PREP_OUT_SHAPES.items()}
    Xa = sp.dtmp("Xa", (1024, TC))
    Xb = sp.dtmp("Xb", (1024, TC))
    XG = sp.dtmp("XG", (4096, TC))
    dG = sp.dtmp("dG", (T, TC))
    uG = sp.dtmp("uG", (T, TC))
    BG1 = sp.dtmp("BG1", (128, TC))
    CG1 = sp.dtmp("CG1", (128, TC))
    BG2 = sp.dtmp("BG2", (512, TC))
    CG2 = sp.dtmp("CG2", (512, TC))
    ys_own = sp.dtmp("ysown", (256, T))
    yG = sp.dtmp("yG", (2048, T))
    dS_loc = sp.dtmp("dSloc", (256, T))
    uS_loc = sp.dtmp("uSloc", (256, T))
    ys_loc = sp.dtmp("ysloc", (2048, TC))
    d_o = sp.dtmp("d_o", (2048, TC))
    u_o = sp.dtmp("u_o", (2048, TC))
    xs_o = sp.dtmp("xs_o", (2048, TC))
    z_o = sp.dtmp("z_o", (2048, TC))
    B_o1 = sp.dtmp("B_o1", (16, TC))
    C_o1 = sp.dtmp("C_o1", (16, TC))
    B_o2 = sp.dtmp("B_o2", (64, TC))
    C_o2 = sp.dtmp("C_o2", (64, TC))

    regs = {}

    def glue_regs(names_cols):
        t = sp.sbuf("OFF", [1, 32], mybir.dt.int32)
        sp.dma(t[:], offs_in)
        def f(eng, sem):
            for nm, col in names_cols:
                regs[nm] = eng.value_load(t[0:1, col:col + 1])
            return None
        sp.step("gpsimd", 0, f)

    def ag(ins, outs, groups):
        def f(eng, sem):
            return eng.collective_compute(
                "AllGather", ALU.bypass, replica_groups=groups,
                ins=[a.opt() for a in ins], outs=[a.opt() for a in outs])
        sp.step("gpsimd", 1, f)

    def mk_xw_load():
        def f(sp_, dst):
            glue_regs([("ra", 0), ("ca", 1), ("rb", 2), ("cb", 3)])
            def g(eng, sem):
                eng.dma_start(out=dst[:, :, 0:1024],
                              in_=XG[bass.ds(regs["ra"], 1024), bass.ds(regs["ca"], 1024)]
                              .rearrange("(a p) t -> p a t", p=128)).then_inc(sem, 16)
                eng.dma_start(out=dst[:, :, 1024:1152],
                              in_=XG[bass.ds(regs["rb"], 1024), bass.ds(regs["cb"], 128)]
                              .rearrange("(a p) t -> p a t", p=128)).then_inc(sem, 16)
                return None
            sp_.step("gpsimd", 32, g)
        return f

    def mk_xh_load(Xcur):
        def f(sp_, dst):
            glue_regs([("hr", 4)])
            hm = sp_.sbuf("HM", [128, 1], F32)
            sp_.dma(hm[:], hmask_in)
            def g(eng, sem):
                eng.dma_start(out=dst[:, :, 0:3],
                              in_=XG[bass.ds(regs["hr"], 1024), 2045:2048]
                              .rearrange("(a p) t -> p a t", p=128)).then_inc(sem, 16)
                eng.dma_start(out=dst[:, :, 3:2051],
                              in_=Xcur.rearrange("(a p) t -> p a t", p=128)).then_inc(sem, 16)
                return None
            sp_.step("gpsimd", 32, g)
            hv = dst[:, :, 0:3]
            sp_.ts(hv, hv, hm[:], ALU.mult)
        return f

    def reshard_du():
        glue_regs([("o256a", 5), ("o256b", 7)])
        buf = sp.sbuf("RSB", [128, 8, TC], F32)
        for dt, rg in ((0, "o256a"), (1, "o256b")):
            for srcg, dstl in ((dG, dS_loc), (uG, uS_loc)):
                def g(eng, sem, _rg=rg, _srcg=srcg):
                    return eng.dma_start(
                        out=buf,
                        in_=_srcg.rearrange("(r dd) t -> dd r t", dd=TC)[bass.ds(regs[_rg], 128), :, :])
                sp.step("gpsimd", 16, g)
                sp.dma(dstl[dt * 128:(dt + 1) * 128, :].rearrange("p (r t) -> p r t", r=8), buf)

    def reshard_ys():
        glue_regs([("o2048", 6)])
        buf = sp.sbuf("RSB2", [128, 8, TC], F32)
        for gh in range(2):
            def g(eng, sem, _gh=gh):
                return eng.dma_start(
                    out=buf,
                    in_=yG.rearrange("(g p) t -> p g t", p=128)[:, _gh * 8:(_gh + 1) * 8,
                                                               bass.ds(regs["o2048"], TC)])
            sp.step("gpsimd", 16, g)
            sp.dma(ys_loc.rearrange("(g p) t -> p g t", p=128)[:, gh * 8:(gh + 1) * 8, :], buf)

    def mk_load_bc(BGx, CGx, nst):
        def f(sp_, n, b, ch, Brep, Crep):
            pairs = []
            for ri, r in enumerate((b * 4 + ch * 2, b * 4 + ch * 2 + 1)):
                row = r * nst + n
                pairs.append((Brep[:, ri * 2048:(ri + 1) * 2048],
                              BGx[row:row + 1, :].to_broadcast([128, 2048])))
                pairs.append((Crep[:, ri * 2048:(ri + 1) * 2048],
                              CGx[row:row + 1, :].to_broadcast([128, 2048])))
            sp_.dmas(pairs)
        return f

    def mk_ys_load():
        def f(sp_, g_, dst):
            def g(eng, sem):
                return eng.dma_start(out=dst,
                                     in_=yG[g_ * 128:(g_ + 1) * 128, bass.ds(regs["o2048"], 2048)])
            sp_.step("gpsimd", 16, g)
        return f

    def xf_chunk(t0):
        cb = t0 // 2048
        c0 = t0 % 2048
        return XG[cb * 1024:(cb + 1) * 1024, c0:c0 + 512].rearrange("(a p) t -> p a t", p=128)

    sp.sub_begin()
    build_prep(sp=sp, io=Wp)

    cur, nxt = xin, Xa
    step_i = 0

    def bump(written):
        nonlocal cur, nxt, step_i
        cur = written
        nxt = Xb if written is Xa else Xa
        step_i += 1

    io_moe0 = dict(W1b=Wp["L0_ffn_W1_b"], W2b=Wp["L0_ffn_W2_b"], WrL=raw["WrL"],
                   brC=raw["brC"], b1L=raw["b1L"], b2L=raw["b2L"])
    io_moe1 = dict(Gb=Wp["L1_ffn_gate_w_b"], Ub=Wp["L1_ffn_up_w_b"], Db=Wp["L1_ffn_down_w_b"],
                   SGb=Wp["L1_ffn_shared_gate_w_b"], SUb=Wp["L1_ffn_shared_up_w_b"],
                   SDb=Wp["L1_ffn_shared_down_w_b"], RwL=raw["RwL"])
    io_mlp2 = dict(Wub=Wp["W_exp_fb"], Wdb=Wp["W_con_fb"],
                   bup_row=Wp["bexp_f"], bdn_row=Wp["bcon_f"])
    io_mlp3 = dict(Wub=Wp["L3_ffn_w_up_qtb"], Wdb=Wp["L3_ffn_w_down_qtb"])

    seq = []
    seq.append(("moe0a", lambda: build_moe0(sp=sp, io=dict(io_moe0, xT=cur, yT=nxt))))
    def attn0_step():
        ag([cur], [XG], GROUPSB)
        build_attn0(sp=sp, io=dict(
            xrT=cur, yT=nxt, xw_load=mk_xw_load(),
            Wq1b=Wp["L0_attn_Wq1_b"], Wq2b=Wp["L0_attn_Wq2_b"], Wk1b=Wp["L0_attn_Wk1_b"],
            Wk2b=Wp["L0_attn_Wk2_b"], Wvb=Wp["L0_attn_Wv_b"], Woutb=Wp["L0_attn_Wout_b"],
            boutL=raw["boutL"], lamC=Wp["lam"].to_broadcast([128, 1]),
            posrow=posrow, divrow=divrow))
    seq.append(("attn0", attn0_step))
    seq.append(("moe0b", lambda: build_moe0(sp=sp, io=dict(io_moe0, xT=cur, yT=nxt))))
    seq.append(("moe1a", lambda: build_moe1(sp=sp, io=dict(io_moe1, xT=cur, yT=nxt))))

    def l1_step():
        ag([cur], [XG], GROUPSB)
        build_l1a(sp=sp, io=dict(
            xh_load=mk_xh_load(cur), winT=Wp["L1_attn_w_in_tb"], wxT=Wp["L1_attn_w_x_tb"],
            wdtT=Wp["L1_attn_w_dt_tb"], gamL=raw["gamL1"], betL=raw["betL1"],
            cwL=raw["cwL1"], cbL=raw["cbL1"], bdtL=raw["bdtL1"],
            delta_o=d_o, u_o=u_o, xs_o=xs_o, z_o=z_o, B_o=B_o1, C_o=C_o1))
        ag([d_o], [dG], GROUPS8)
        ag([u_o], [uG], GROUPS8)
        ag([B_o1], [BG1], GROUPS8)
        ag([C_o1], [CG1], GROUPS8)
        sp.sub_begin()
        reshard_du()
        sp.sub_begin()
        build_scan(16, sp=sp, io=dict(dS=dS_loc, uS=uS_loc, load_bc=mk_load_bc(BG1, CG1, 16),
                                      AlogS=alog1, yS=ys_own))
        ag([ys_own], [yG], GROUPS8)
        sp.sub_begin()
        reshard_ys()
        sp.sub_begin()
        build_l1c(sp=sp, io=dict(ysT=ys_loc, xsT=xs_o, zT=z_o, xT=cur, yT=nxt,
                                 DpL=raw["DpL1"], woTb=Wp["L1_attn_w_out_tb"]))
    seq.append(("l1", l1_step))
    seq.append(("moe1b", lambda: build_moe1(sp=sp, io=dict(io_moe1, xT=cur, yT=nxt))))
    seq.append(("mlp2a", lambda: build_mlp2(sp=sp, io=dict(io_mlp2, xT=cur, yT=nxt))))

    def l2_step():
        ag([cur], [XG], GROUPSB)
        build_l2a(sp=sp, io=dict(
            xh_load=mk_xh_load(cur), winT=Wp["L2_attn_in_proj_w_tb"], bwT=Wp["L2_attn_B_w_tb"],
            cwT=Wp["L2_attn_C_w_tb"], dtwT=Wp["L2_attn_dt_w_tb"], nwL=raw["nwL2"],
            cwL=raw["cwL2"], cbL=raw["cbL2"], bdtL=raw["bdtL2"],
            dt_o=d_o, u_o=u_o, xc_o=xs_o, z_o=z_o, B_o=B_o2, C_o=C_o2))
        ag([d_o], [dG], GROUPS8)
        ag([u_o], [uG], GROUPS8)
        ag([B_o2], [BG2], GROUPS8)
        ag([C_o2], [CG2], GROUPS8)
        sp.sub_begin()
        reshard_du()
        sp.sub_begin()
        build_scan(64, sp=sp, io=dict(dS=dS_loc, uS=uS_loc, load_bc=mk_load_bc(BG2, CG2, 64),
                                      AlogS=alog2, yS=ys_own))
        ag([ys_own], [yG], GROUPS8)
        sp.sub_begin()
        reshard_ys()
        sp.sub_begin()
        build_l1c(sp=sp, io=dict(ysT=ys_loc, xsT=xs_o, zT=z_o, xT=cur, yT=nxt,
                                 DpL=raw["DpL2"], woTb=Wp["L2_attn_out_proj_w_tb"]))
    seq.append(("l2", l2_step))
    seq.append(("mlp2b", lambda: build_mlp2(sp=sp, io=dict(io_mlp2, xT=cur, yT=nxt))))
    seq.append(("mlp3a", lambda: build_mlp3(sp=sp, io=dict(io_mlp3, xT=cur, yT=nxt))))

    def attn3_step():
        ag([cur], [XG], GROUPSB)
        build_attn3(sp=sp, io=dict(
            xf_chunk=xf_chunk, xqT=cur, xrT=cur, yT=nxt,
            Wqb=Wp["L3_attn_Wq_b"], Wkb=Wp["L3_attn_Wk_b"], Wvb=Wp["L3_attn_Wv_b"],
            memb=Wp["L3_attn_memory_b"], mnL=raw["mnL"], mwC=raw["mwC"]))
    seq.append(("attn3", attn3_step))
    seq.append(("mlp3b", lambda: build_mlp3(sp=sp, io=dict(io_mlp3, xT=cur, yT=nxt))))

    for i, (nm, fn) in enumerate(seq):
        if i >= n_sub:
            break
        if i == n_sub - 1 or i == len(seq) - 1:
            nxt = out
        sp.sub_begin()
        fn()
        bump(nxt)
    return sp


def stage_fused(inp, c):
    b, q = c // 4, c % 4
    m = {}
    for name in ["L0_attn_Wq1", "L0_attn_Wq2", "L0_attn_Wk1", "L0_attn_Wk2", "L0_attn_Wv",
                 "L0_attn_Wout", "L1_ffn_shared_gate_w", "L1_ffn_shared_up_w",
                 "L1_ffn_shared_down_w", "L3_attn_Wq", "L3_attn_Wk", "L3_attn_Wv",
                 "L3_attn_memory", "L0_ffn_W1", "L0_ffn_W2", "L1_ffn_gate_w", "L1_ffn_up_w",
                 "L1_ffn_down_w", "L1_attn_w_in", "L1_attn_w_x", "L1_attn_w_dt",
                 "L1_attn_w_out", "L2_attn_in_proj_w", "L2_attn_B_w", "L2_attn_C_w",
                 "L2_attn_dt_w", "L2_attn_out_proj_w", "L3_ffn_w_up", "L3_ffn_w_down",
                 "L2_ffn_W_exp", "L2_ffn_W_con"]:
        m[name] = np.ascontiguousarray(inp[name])
    for k in range(4):
        m[f"upw{k}"] = np.ascontiguousarray(inp["L2_ffn_up_weights"][:, :, k // 2, k % 2])
        m[f"dnw{k}"] = np.ascontiguousarray(inp["L2_ffn_down_weights"][:, :, k // 2, k % 2])
    for name in ["L2_ffn_b_exp", "L2_ffn_up_bias", "L2_ffn_b_con", "L2_ffn_down_bias"]:
        m[name] = np.ascontiguousarray(inp[name].reshape(1, -1))
    for name in ["L0_attn_lq1", "L0_attn_lk1", "L0_attn_lq2", "L0_attn_lk2"]:
        m[name] = np.ascontiguousarray(inp[name].reshape(1, -1))
    m["WrL"] = np.ascontiguousarray(inp["L0_ffn_Wr"].reshape(8, 128, 8).transpose(1, 0, 2))
    m["brC"] = np.ascontiguousarray(inp["L0_ffn_br"].reshape(8, 1))
    m["b1L"] = np.ascontiguousarray(inp["L0_ffn_b1"].T.reshape(32, 128, 8).transpose(1, 0, 2))
    m["b2L"] = np.ascontiguousarray(inp["L0_ffn_b2"].T.reshape(8, 128, 8).transpose(1, 0, 2))
    m["RwL"] = np.ascontiguousarray(inp["L1_ffn_router_w"].reshape(8, 128, 16).transpose(1, 0, 2))
    m["gamL1"] = np.ascontiguousarray(inp["L1_attn_gamma"].reshape(8, 128).T)
    m["betL1"] = np.ascontiguousarray(inp["L1_attn_beta"].reshape(8, 128).T)
    m["cwL1"] = np.ascontiguousarray(inp["L1_attn_conv_w"].reshape(2048, 4).reshape(16, 128, 4).transpose(1, 0, 2))
    m["cbL1"] = np.ascontiguousarray(inp["L1_attn_conv_b"].reshape(16, 128).T)
    m["bdtL1"] = np.ascontiguousarray(inp["L1_attn_b_dt"].reshape(16, 128).T)
    m["DpL1"] = np.ascontiguousarray(inp["L1_attn_Dp"].reshape(16, 128).T)
    m["nwL2"] = np.ascontiguousarray(inp["L2_attn_norm_w"].reshape(8, 128).T)
    m["cwL2"] = np.ascontiguousarray(inp["L2_attn_conv_w"].reshape(2048, 4).reshape(16, 128, 4).transpose(1, 0, 2))
    m["cbL2"] = np.ascontiguousarray(inp["L2_attn_conv_b"].reshape(16, 128).T)
    m["bdtL2"] = np.ascontiguousarray(inp["L2_attn_dt_b"].reshape(16, 128).T)
    m["DpL2"] = np.ascontiguousarray(inp["L2_attn_D_param"].reshape(16, 128).T)
    m["boutL"] = np.ascontiguousarray(inp["L0_attn_bout"].reshape(8, 128).T)
    m["mnL"] = np.ascontiguousarray(inp["L3_attn_memory_norm"].reshape(4, 128).T)
    m["mwC"] = np.asarray(inp["L3_attn_memory_weight"], np.float32).reshape(1, 1)
    m["posrow"] = (1024.0 * q + np.arange(1152, dtype=np.float32)).reshape(1, 1152)
    m["divrow"] = np.exp(np.arange(0, 64, 2, dtype=np.float32) * (-np.log(10000.0) / 64)).reshape(1, 32)
    m["AlogS1"] = np.ascontiguousarray(inp["L1_attn_A_log"][256 * c:256 * (c + 1)])
    m["AlogS2"] = np.ascontiguousarray(inp["L2_attn_A_log"][256 * c:256 * (c + 1)])
    s1 = 1024 * q
    s2 = s1 + 1024
    offs = np.zeros((1, 32), np.int32)
    offs[0, 0] = (s1 // 2048) * 1024
    offs[0, 1] = s1 % 2048
    offs[0, 2] = (s2 // 2048) * 1024
    offs[0, 3] = s2 % 2048
    offs[0, 4] = (q - 1) * 1024 if q > 0 else 0
    offs[0, 5] = 256 * c
    offs[0, 6] = 2048 * c
    offs[0, 7] = 256 * c + 128
    for r in range(8):
        for dt in range(2):
            offs[0, 8 + r * 2 + dt] = 256 * c + r * 2048 + dt * 128
    m["offs"] = offs
    m["hmask"] = np.full((128, 1), 1.0 if q > 0 else 0.0, np.float32)
    X = np.asarray(inp["L0_attn_x"], np.float32).reshape(T, D)
    m["xin"] = np.ascontiguousarray(X[c * TC:(c + 1) * TC].T)
    return m


def run_fused(inp, n_sub=13):
    sp = build_fused(n_sub)
    in_maps = [stage_fused(inp, c) for c in range(NC)]
    res = sp.run(in_maps, core_ids=range(NC))
    out = np.empty((T, D), np.float32)
    for c in range(NC):
        out[c * TC:(c + 1) * TC] = res.results[c]["out"].T
    return out


# version 04 (final)
# speedup vs baseline: 1.0549x; 1.0202x over previous
import numpy as np
import ml_dtypes
import concourse.bass as bass
import concourse.mybir as mybir
from concourse.bass_utils import run_bass_kernel_spmd
from concourse.alu_op_type import AluOpType as ALU
from contextlib import ExitStack

F32 = mybir.dt.float32
BF16 = mybir.dt.bfloat16
AF = mybir.ActivationFunctionType
BF = ml_dtypes.bfloat16
ENGINES = ("sync", "scalar", "vector", "tensor", "gpsimd")

D = 1024
B = 2
L = 8192
T = B * L
NC = 8
TC = T // NC


SBUF_WORDS = 52992
PSUM_COLS = 4096


class SP:
    def __init__(self, fused=False):
        self.nc = bass.Bass("TRN2", target_bir_lowering=False, debug=False)
        self.steps = []
        self.stack = ExitStack()
        self._uid = 0
        self.fused = fused
        if fused:
            self._backing = self.stack.enter_context(
                self.nc.sbuf_tensor("GBUF", [128, SBUF_WORDS], F32))
            self._pbacking = self.stack.enter_context(
                self.nc.psum_tensor("GPSUM", [128, PSUM_COLS], F32))
            self._sb_base = 0
            self._ps_base = 0

    def sub_begin(self):
        if self.fused:
            self._sb_base = 0
            self._ps_base = 0

    def uid(self):
        self._uid += 1
        return self._uid

    def sbuf(self, name, shape, dtype):
        if self.fused:
            parts = shape[0]
            n = int(np.prod(shape[1:]))
            if dtype == BF16:
                words = (n + 1) // 2
            elif dtype in (F32, mybir.dt.int32):
                words = n
            else:
                raise ValueError(dtype)
            off = self._sb_base
            assert off + words <= SBUF_WORDS, (name, off, words)
            self._sb_base = off + words
            ap = self._backing[0:parts, off:off + words]
            if dtype != F32:
                ap = ap.bitcast(dtype)
                if dtype == BF16 and ap.shape[1] != n:
                    ap = ap[:, 0:n]
            if len(shape) > 1:
                dims = " ".join(f"d{i}" for i in range(1, len(shape)))
                ap = ap.rearrange(f"p ({dims}) -> p {dims}",
                                  **{f"d{i}": shape[i] for i in range(1, len(shape))})
            return ap
        return self.stack.enter_context(self.nc.sbuf_tensor(f"{name}{self.uid()}", shape, dtype))

    def psum_at(self, unit_off, name, shape, dtype=F32):
        assert self.fused and dtype == F32
        parts = shape[0]
        n = int(np.prod(shape[1:]))
        ap = self._pbacking[0:parts, unit_off * 512:unit_off * 512 + n]
        if len(shape) > 2:
            dims = " ".join(f"d{i}" for i in range(1, len(shape)))
            ap = ap.rearrange(f"p ({dims}) -> p {dims}",
                              **{f"d{i}": shape[i] for i in range(1, len(shape))})
        return ap

    def multi(self, engine, fns):
        def f(eng, sem):
            last = None
            for fn in fns:
                last = fn(eng)
            return last
        self.step(engine, 1, f)

    def psum(self, name, shape, dtype=F32):
        if self.fused:
            assert dtype == F32, name
            parts = shape[0]
            n = int(np.prod(shape[1:]))
            cols = ceil_div(n, 512) * 512
            off = self._ps_base
            assert off + cols <= PSUM_COLS, (name, off, cols)
            self._ps_base = off + cols
            ap = self._pbacking[0:parts, off:off + n]
            if len(shape) > 2:
                dims = " ".join(f"d{i}" for i in range(1, len(shape)))
                ap = ap.rearrange(f"p ({dims}) -> p {dims}",
                                  **{f"d{i}": shape[i] for i in range(1, len(shape))})
            return ap
        return self.stack.enter_context(self.nc.psum_tensor(f"{name}{self.uid()}", shape, dtype))

    def din(self, name, shape, dtype=F32):
        return self.nc.dram_tensor(name, list(shape), dtype, kind="ExternalInput").ap()

    def dout(self, name, shape, dtype=F32):
        return self.nc.dram_tensor(name, list(shape), dtype, kind="ExternalOutput").ap()

    def dtmp(self, name, shape, dtype=F32):
        return self.nc.dram_tensor(f"{name}{self.uid()}", list(shape), dtype, kind="Internal").ap()

    def step(self, engine, inc, fn):
        self.steps.append((engine, fn, inc))

    def op(self, engine, fn):
        self.step(engine, 1, lambda eng, sem: fn(eng))

    def ops(self, engine, fn, n):
        self.step(engine, 1, fn)

    def dma(self, out_ap, in_ap, engine="sync", **kw):
        def f(eng, sem):
            return eng.dma_start(out=out_ap, in_=in_ap, **kw)
        self.step(engine, 16, f)

    def dmas(self, pairs, engine="sync", **kw):
        def f(eng, sem):
            for o, i in pairs:
                eng.dma_start(out=o, in_=i, **kw).then_inc(sem, 16)
            return None
        self.step(engine, 16 * len(pairs), f)

    def mms(self, fn):
        self.step("tensor", 1, lambda eng, sem: fn(eng))

    def tt(self, out, a, b, op, engine="vector"):
        self.op(engine, lambda e: e.tensor_tensor(out=out, in0=a, in1=b, op=op))

    def ts(self, out, a, s1, op, engine="vector"):
        self.op(engine, lambda e: e.tensor_scalar(out=out, in0=a, scalar1=s1, scalar2=None, op0=op))

    def stt(self, out, in0, scalar, in1, op0, op1, engine="vector"):
        self.op(engine, lambda e: e.scalar_tensor_tensor(
            out=out, in0=in0, scalar=scalar, in1=in1, op0=op0, op1=op1))

    def act(self, out, in_, func, bias=0.0, scale=1.0, accum_out=None):
        self.op("scalar", lambda e: e.activation(
            out=out, in_=in_, func=func, bias=bias, scale=scale, accum_out=accum_out))

    def copy(self, out, in_, engine="vector"):
        if engine == "scalar":
            self.op(engine, lambda e: e.copy(out=out, in_=in_))
        else:
            self.op(engine, lambda e: e.tensor_copy(out=out, in_=in_))

    def memset(self, ap, v, engine="vector"):
        self.op(engine, lambda e: e.memset(ap, v))

    def scan(self, out, d0, d1, initial, op0=ALU.mult, op1=ALU.add):
        self.op("vector", lambda e: e.tensor_tensor_scan(
            out=out, data0=d0, data1=d1, initial=initial, op0=op0, op1=op1))

    def recip(self, out, in_):
        self.op("vector", lambda e: e.reciprocal(out=out, in_=in_))

    def transpose(self, out_psum, in_, ident):
        self.op("tensor", lambda e: e.transpose(out_psum, in_, ident))

    def reduce(self, out, in_, op):
        self.op("vector", lambda e: e.tensor_reduce(
            out=out, in_=in_, axis=mybir.AxisListType.X, op=op))

    def run(self, in_maps, core_ids, **kw):
        nc = self.nc
        waits = []
        k = 0
        for (en, fn, inc) in self.steps:
            waits.append(k)
            k += inc
        total = k
        with nc.semaphore("SS") as sem, nc.Block() as blk:
            for en_name in ENGINES:
                def mk(en_name):
                    def body(eng):
                        for (en, fn, inc), kb in zip(self.steps, waits):
                            if en != en_name:
                                continue
                            if kb:
                                eng.wait_ge(sem, kb)
                            inst = fn(eng, sem)
                            if inst is not None and inc:
                                inst.then_inc(sem, inc)
                    return body
                getattr(blk, en_name)(mk(en_name))

            def fin(eng):
                eng.wait_ge(sem, total)
            blk.gpsimd(fin)
        try:
            res = run_bass_kernel_spmd(nc, in_maps, core_ids=list(core_ids), **kw)
        finally:
            self.stack.close()
        return res


def ceil_div(a, b):
    return (a + b - 1) // b


def chunks(total, size):
    out = []
    o = 0
    while o < total:
        out.append((o, min(size, total - o)))
        o += size
    return out


def make_ident(sp, name, n, dtype):
    t = sp.sbuf(name, [n, n], dtype)
    sp.op("gpsimd", lambda e: e.memset(t[:], 0.0))
    sp.op("gpsimd", lambda e: e.affine_select(
        out=t[:], in_=t[:], compare_op=ALU.not_equal, fill=1.0,
        base=0, pattern=[[-1, n]], channel_multiplier=1))
    return t


def emit_linear(sp, out_psum_fn, w_tiles_fn, act_fn, nfi, N, start=True, stop=True):
    def f(eng):
        last = None
        for fi in range(nfi):
            last = eng.matmul(out_psum_fn(), w_tiles_fn(fi), act_fn(fi),
                              start=(start and fi == 0), stop=(stop and fi == nfi - 1))
        return last
    sp.mms(f)


def build_prep(sp=None, io=None):
    if sp is None:
        sp = SP()
    io = io or {}
    nc = sp.nc
    ins = {}
    outs = {}

    def inp(name, shape, dtype=F32):
        ins[name] = nc.dram_tensor(name, list(shape), dtype, kind="ExternalInput").ap()
        return ins[name]

    def out(name, shape, dtype=BF16):
        if name in io:
            outs[name] = io[name]
        else:
            outs[name] = nc.dram_tensor(name, list(shape), dtype, kind="ExternalOutput").ap()
        return outs[name]

    bi = sp.sbuf("ci", [128, 4096], F32)
    bo = sp.sbuf("co", [128, 4096], BF16)
    ident_f32 = make_ident(sp, "idf0", 128, F32)
    pstf0 = sp.psum("pstf0", [128, 128], F32)
    tbuf = sp.sbuf("tb", [128, 128], BF16)

    def cast_tensor(src_ap, dst_ap, R, C):
        assert R % 128 == 0 or R < 128
        for r0, rn in chunks(R, 128):
            for c0, cn in chunks(C, 4096):
                sp.dma(bi[:rn, :cn], src_ap[r0:r0 + rn, c0:c0 + cn])
                sp.copy(bo[:rn, :cn], bi[:rn, :cn])
                sp.dma(dst_ap[r0:r0 + rn, c0:c0 + cn], bo[:rn, :cn])

    casts = [
        ("L0_attn_Wq1", (1024, 1024)), ("L0_attn_Wq2", (1024, 1024)),
        ("L0_attn_Wk1", (1024, 128)), ("L0_attn_Wk2", (1024, 128)),
        ("L0_attn_Wv", (1024, 128)), ("L0_attn_Wout", (1024, 1024)),
        ("L1_ffn_shared_gate_w", (1024, 1024)), ("L1_ffn_shared_up_w", (1024, 1024)),
        ("L1_ffn_shared_down_w", (1024, 1024)),
        ("L3_attn_Wq", (1024, 1024)), ("L3_attn_Wk", (1024, 512)),
        ("L3_attn_Wv", (1024, 512)), ("L3_attn_memory", (512, 512)),
    ]
    for name, shp in casts:
        src = inp(name, shp)
        dst = out(name + "_b", shp)
        cast_tensor(src, dst, shp[0], shp[1])

    casts3 = [
        ("L0_ffn_W1", (8, 1024, 4096)), ("L0_ffn_W2", (8, 2048, 1024)),
        ("L1_ffn_gate_w", (16, 1024, 512)), ("L1_ffn_up_w", (16, 1024, 512)),
        ("L1_ffn_down_w", (16, 512, 1024)),
    ]
    for name, shp in casts3:
        src = inp(name, shp)
        dst = out(name + "_b", shp)
        e, r, c = shp
        cast_tensor(src.rearrange("e r c -> (e r) c"), dst.rearrange("e r c -> (e r) c"),
                    e * r, c)

    def transpose_tensor(name, R, C, src_ap=None, outname=None):
        src = src_ap if src_ap is not None else inp(name, (R, C))
        dst = out(outname or (name + "_tb"), (C, R))
        nR, nCc = ceil_div(R, 128), ceil_div(C, 128)
        for j in range(nCc):
            c0, cn = j * 128, min(128, C - j * 128)
            for i in range(nR):
                r0, rn = i * 128, min(128, R - i * 128)
                sp.dma(bi[:rn, :cn], src[r0:r0 + rn, c0:c0 + cn])
                sp.transpose(pstf0[:cn, :rn], bi[:rn, :cn], ident_f32[:rn, :rn])
                sp.copy(tbuf[:cn, :rn], pstf0[:cn, :rn], engine="scalar")
                sp.dma(dst[c0:c0 + cn, r0:r0 + rn], tbuf[:cn, :rn])

    transpose_tensor("L1_attn_w_in", 4096, 1024)
    transpose_tensor("L1_attn_w_x", 96, 2048)
    transpose_tensor("L1_attn_w_dt", 2048, 64)
    transpose_tensor("L1_attn_w_out", 1024, 2048)
    transpose_tensor("L2_attn_in_proj_w", 4096, 1024)
    transpose_tensor("L2_attn_B_w", 64, 2048)
    transpose_tensor("L2_attn_C_w", 64, 2048)
    transpose_tensor("L2_attn_dt_w", 2048, 2048)
    transpose_tensor("L2_attn_out_proj_w", 1024, 2048)

    ps1 = sp.psum("p1", [1, 8], F32)
    rowacc = sp.sbuf("racc", [128, 32], F32)
    onescol = sp.sbuf("ones", [128, 1], F32)
    sp.memset(onescol[:], 1.0)
    throw = sp.sbuf("thr", [1, 1], F32)
    thd = sp.dtmp("thd", [1, 1])
    thcol = sp.sbuf("thc", [128, 1], F32)
    absb = sp.sbuf("absb", [128, 4096], F32)
    sgnb = sp.sbuf("sgnb", [128, 4096], F32)

    sumcol = sp.sbuf("sumc", [128, 1], F32)

    def ternary_transpose(name, R, C):
        src = inp(name, (R, C))
        dst = out(name + "_qtb", (C, R))
        nR = R // 128
        for i in range(nR):
            sp.dma(bi[:, :C], src[i * 128:(i + 1) * 128, :])
            sp.op("scalar", lambda e, _i=i: e.activation(
                out=absb[:, :C], in_=bi[:, :C], func=AF.Abs, accum_out=rowacc[:, _i:_i + 1]))
        sp.reduce(sumcol[:], rowacc[:, 0:nR], ALU.add)
        def f(eng):
            return eng.matmul(ps1[0:1, 0:1], onescol[:], sumcol[:], start=True, stop=True)
        sp.mms(f)
        sp.act(throw[:], ps1[0:1, 0:1], AF.Copy, scale=0.5 / (R * C))
        sp.dma(thd, throw[:])
        sp.dma(thcol[:], thd.to_broadcast([128, 1]))
        for i in range(nR):
            sp.dma(bi[:, :C], src[i * 128:(i + 1) * 128, :])
            sp.op("scalar", lambda e: e.activation(out=absb[:, :C], in_=bi[:, :C], func=AF.Abs))
            sp.op("scalar", lambda e: e.sign(out=sgnb[:, :C], in_=bi[:, :C]))
            sp.ts(absb[:, :C], absb[:, :C], thcol[:], ALU.is_gt)
            sp.tt(bi[:, :C], sgnb[:, :C], absb[:, :C], ALU.mult)
            for j in range(C // 128):
                sp.transpose(pstf0[:], bi[:, j * 128:(j + 1) * 128], ident_f32[:])
                sp.copy(tbuf[:], pstf0[:], engine="scalar")
                sp.dma(dst[j * 128:(j + 1) * 128, i * 128:(i + 1) * 128], tbuf[:])

    ternary_transpose("L3_ffn_w_up", 4096, 1024)
    ternary_transpose("L3_ffn_w_down", 1024, 4096)

    upw = [inp(f"upw{k}", (12, 2048)) for k in range(4)]
    dnw = [inp(f"dnw{k}", (12, 2048)) for k in range(4)]
    W_exp = inp("L2_ffn_W_exp", (1024, 4096))
    b_exp = inp("L2_ffn_b_exp", (1, 4096))
    up_bias = inp("L2_ffn_up_bias", (1, 4096))
    W_con = inp("L2_ffn_W_con", (4096, 1024))
    b_con = inp("L2_ffn_b_con", (1, 1024))
    down_bias = inp("L2_ffn_down_bias", (1, 1024))

    wrow = sp.sbuf("wrow", [128, 4096], F32)
    wc = [sp.sbuf(f"wc{k}", [128, 2048], F32) for k in range(4)]
    tna = sp.sbuf("tna", [128, 2048], F32)
    tnb = sp.sbuf("tnb", [128, 2048], F32)
    t2 = sp.sbuf("t2b", [128, 2048], F32)
    ident_f = make_ident(sp, "idf", 128, F32)
    pstf = sp.psum("pstf", [128, 128], F32)
    tbf = sp.sbuf("tbf", [128, 128], F32)

    def fold_rowtile(weights4, nrow, reverse):
        stages = list(range(12))
        order = [0, 1, 2, 3] if not reverse else [0, 2, 1, 3]
        if reverse:
            stages = stages[::-1]
        for st in stages:
            s = 1 << st
            for k in range(4):
                sp.dma(wc[k][:nrow, :], weights4[order[k]][st:st + 1, :].to_broadcast([nrow, 2048]))
            y = wrow[:nrow, :].rearrange("p (nb two s) -> p nb two s", two=2, s=s)
            w4 = [wc[k][:nrow, :].rearrange("p (nb s) -> p nb s", s=s) for k in range(4)]
            a = y[:, :, 0, :]
            b_ = y[:, :, 1, :]
            nav = tna[:nrow, :].rearrange("p (nb s) -> p nb s", s=s)
            nbv = tnb[:nrow, :].rearrange("p (nb s) -> p nb s", s=s)
            tv = t2[:nrow, :].rearrange("p (nb s) -> p nb s", s=s)
            sp.tt(nav, a, w4[0], ALU.mult)
            sp.tt(tv, b_, w4[1], ALU.mult)
            sp.tt(nav, nav, tv, ALU.add)
            sp.tt(nbv, a, w4[2], ALU.mult)
            sp.tt(tv, b_, w4[3], ALU.mult)
            sp.tt(nbv, nbv, tv, ALU.add)
            sp.copy(a, nav)
            sp.copy(b_, nbv)

    W_exp_b = out("W_exp_fb", (1024, 4096))
    bexp_f = out("bexp_f", (1, 4096), F32)
    for rt in range(8):
        sp.dma(wrow[:], W_exp[rt * 128:(rt + 1) * 128, :])
        fold_rowtile(upw, 128, reverse=False)
        sp.copy(bo[:], wrow[:])
        sp.dma(W_exp_b[rt * 128:(rt + 1) * 128, :], bo[:])
    sp.dma(wrow[0:1, :], b_exp)
    fold_rowtile(upw, 1, reverse=False)
    sp.dma(bi[0:1, :4096], up_bias)
    sp.tt(wrow[0:1, :], wrow[0:1, :], bi[0:1, :4096], ALU.add)
    sp.dma(bexp_f, wrow[0:1, :])

    W_con_b = out("W_con_fb", (4096, 1024))
    for rt in range(8):
        for i in range(32):
            sp.dma(bi[:, :128], W_con[i * 128:(i + 1) * 128, rt * 128:(rt + 1) * 128])
            sp.transpose(pstf[:], bi[:, :128], ident_f[:])
            sp.copy(tbf[:], pstf[:], engine="scalar")
            sp.copy(wrow[:, i * 128:(i + 1) * 128], tbf[:])
        fold_rowtile(dnw, 128, reverse=True)
        for j in range(32):
            sp.transpose(pstf[:], wrow[:, j * 128:(j + 1) * 128], ident_f[:])
            sp.copy(tbuf[:], pstf[:], engine="scalar")
            sp.dma(W_con_b[j * 128:(j + 1) * 128, rt * 128:(rt + 1) * 128], tbuf[:])
    bcon_f = out("bcon_f", (1, 1024), F32)
    sp.dma(bi[0:1, :1024], b_con)
    sp.dma(tna[0:1, :1024], down_bias)
    sp.tt(bi[0:1, :1024], bi[0:1, :1024], tna[0:1, :1024], ALU.add)
    sp.dma(bcon_f, bi[0:1, :1024])

    lq1 = inp("L0_attn_lq1", (1, 64))
    lk1 = inp("L0_attn_lk1", (1, 64))
    lq2 = inp("L0_attn_lq2", (1, 64))
    lk2 = inp("L0_attn_lk2", (1, 64))
    lam_o = out("lam", (1, 1), F32)
    r1 = sp.sbuf("r1", [1, 64], F32)
    r2 = sp.sbuf("r2", [1, 64], F32)
    d1 = sp.sbuf("d1", [1, 1], F32)
    d2 = sp.sbuf("d2", [1, 1], F32)
    sp.dma(r1[:], lq1)
    sp.dma(r2[:], lk1)
    sp.tt(r1[:], r1[:], r2[:], ALU.mult)
    sp.op("scalar", lambda e: e.activation(out=r1[:], in_=r1[:], func=AF.Copy, accum_out=d1[:]))
    sp.dma(r1[:], lq2)
    sp.dma(r2[:], lk2)
    sp.tt(r1[:], r1[:], r2[:], ALU.mult)
    sp.op("scalar", lambda e: e.activation(out=r1[:], in_=r1[:], func=AF.Copy, accum_out=d2[:]))
    sp.act(d1[:], d1[:], AF.Exp)
    sp.act(d2[:], d2[:], AF.Exp)
    sp.tt(d1[:], d1[:], d2[:], ALU.subtract)
    sp.ts(d1[:], d1[:], 0.8, ALU.add)
    sp.ts(d1[:], d1[:], 0.9, ALU.min)
    sp.ts(d1[:], d1[:], 0.1, ALU.max)
    sp.dma(lam_o, d1[:])

    return sp, ins, outs


_PREP_KEYS3 = None


def run_prep(inp):
    sp, ins, outs = build_prep()
    feed = {}
    for name in ins:
        if name.startswith("upw") or name.startswith("dnw"):
            k = int(name[3])
            src = inp["L2_ffn_up_weights" if name[0] == "u" else "L2_ffn_down_weights"]
            feed[name] = np.ascontiguousarray(src[:, :, k // 2, k % 2])
        elif name in ("L2_ffn_b_exp", "L2_ffn_up_bias", "L2_ffn_b_con", "L2_ffn_down_bias"):
            feed[name] = np.ascontiguousarray(inp[name].reshape(1, -1))
        elif name.startswith("L0_attn_l"):
            feed[name] = np.ascontiguousarray(inp[name].reshape(1, -1))
        else:
            feed[name] = np.ascontiguousarray(inp[name])
    res = sp.run([feed], core_ids=[0])
    return dict(res.results[0])


class Arena:
    def __init__(self, sp, words=50176):
        if sp.fused:
            assert sp._sb_base == 0, "allocate Arena before smalls in fused mode"
            self.t = sp._backing[:, 0:words]
            sp._sb_base = words
        else:
            self.t = sp.sbuf("AR", [128, words], F32)
        self.words = words

    def f32(self, off, shape):
        n = int(np.prod(shape))
        assert off + n <= self.words, (off, shape)
        ap = self.t[:, off:off + n]
        if len(shape) > 1:
            dims = " ".join(f"d{i}" for i in range(len(shape)))
            ap = ap.rearrange(f"p ({dims}) -> p {dims}",
                              **{f"d{i}": shape[i] for i in range(len(shape))})
        return ap

    def row(self, off, n):
        assert off + n <= self.words
        return self.t[0:1, off:off + n]

    def bf16(self, off, shape):
        n = int(np.prod(shape))
        assert n % 2 == 0
        assert off + n // 2 <= self.words, (off, shape)
        ap = self.t[:, off:off + n // 2].bitcast(BF16)
        if len(shape) > 1:
            dims = " ".join(f"d{i}" for i in range(len(shape)))
            ap = ap.rearrange(f"p ({dims}) -> p {dims}",
                              **{f"d{i}": shape[i] for i in range(len(shape))})
        return ap


def emit_ln(sp, ar, x_sb, xn_f32, xn_bf, rows_off, ps_row, ps_b0, ps_b1, ones_col, ones_row,
            ntok=TC, nf=8, eps=1e-5, sq_off=None):
    NTC = ceil_div(ntok, 512)
    mrow = ar.f32(rows_off, (1, ntok))
    srow = ar.f32(rows_off + ntok, (1, ntok))
    sinv = ar.f32(rows_off + 2 * ntok, (1, ntok))
    msr = ar.f32(rows_off + 3 * ntok, (1, ntok))
    sq = ar.f32(sq_off, (nf, 512))
    for tc in range(NTC):
        t0 = tc * 512
        tn = min(512, ntok - t0)
        tsl = slice(t0, t0 + tn)
        for a in range(nf):
            sp.tt(sq[:, a, :tn].rearrange("p a t -> p (a t)") if False else sq[:, a, :tn],
                  x_sb[:, a, tsl], x_sb[:, a, tsl], ALU.mult)
        def fmean(eng, _tsl=tsl, _tn=tn):
            last = None
            for a in range(nf):
                last = eng.matmul(ps_row[0:1, :_tn], ones_col[:], x_sb[:, a, _tsl],
                                  start=(a == 0), stop=(a == nf - 1))
            return last
        sp.mms(fmean)
        sp.act(mrow[:, tsl], ps_row[0:1, :tn], AF.Copy, scale=1.0 / (nf * 128))
        def fsq(eng, _tn=tn):
            last = None
            for a in range(nf):
                last = eng.matmul(ps_row[0:1, :_tn], ones_col[:], sq[:, a, :_tn],
                                  start=(a == 0), stop=(a == nf - 1))
            return last
        sp.mms(fsq)
        sp.act(srow[:, tsl], ps_row[0:1, :tn], AF.Copy, scale=1.0 / (nf * 128))
    sp.tt(msr[:], mrow[:], mrow[:], ALU.mult)
    sp.tt(srow[:], srow[:], msr[:], ALU.subtract)
    sp.act(srow[:], srow[:], AF.Sqrt, bias=eps)
    sp.recip(sinv[:], srow[:])
    sp.tt(msr[:], mrow[:], sinv[:], ALU.mult)
    for tc in range(NTC):
        t0 = tc * 512
        tn = min(512, ntok - t0)
        tsl = slice(t0, t0 + tn)
        def fb(eng, _tsl=tsl, _tn=tn):
            eng.matmul(ps_b0[:, :_tn], ones_row[:], sinv[:, _tsl], start=True, stop=True)
            return eng.matmul(ps_b1[:, :_tn], ones_row[:], msr[:, _tsl], start=True, stop=True)
        sp.mms(fb)
        for a in range(nf):
            if xn_f32 is not None:
                sp.tt(xn_f32[:, a, tsl], x_sb[:, a, tsl], ps_b0[:, :tn], ALU.mult)
                sp.tt(xn_f32[:, a, tsl], xn_f32[:, a, tsl], ps_b1[:, :tn], ALU.subtract)
                sp.copy(xn_bf[:, a, tsl], xn_f32[:, a, tsl])
            else:
                tmp = sq[:, 0, :tn]
                sp.tt(tmp, x_sb[:, a, tsl], ps_b0[:, :tn], ALU.mult)
                sp.tt(xn_bf[:, a, tsl], tmp, ps_b1[:, :tn], ALU.subtract)


def emit_gating_top2(sp, ar, goff, lsb8, c_dram, nexp, renorm, scale_out, goff_mk=None, goff_r=None):
    l_dram = sp.dtmp("gl", (nexp, TC))
    sp.dma(l_dram, lsb8)
    if goff_mk is None:
        goff_mk = goff + nexp * 512
    if goff_r is None:
        goff_r = goff_mk + nexp * 512
    le = [ar.row(goff + e * 512, 512) for e in range(nexp)]
    mk = [ar.row(goff_mk + e * 512, 512) for e in range(nexp)]
    r0 = goff_r
    m1 = ar.row(r0, 512)
    m2 = ar.row(r0 + 512, 512)
    t1 = ar.row(r0 + 1024, 512)
    t2 = ar.row(r0 + 1536, 512)
    for tc in range(TC // 512):
        tsl = slice(tc * 512, tc * 512 + 512)
        sp.dmas([(le[e], l_dram[e:e + 1, tsl]) for e in range(nexp)])
        sp.tt(m1, le[0], le[1], ALU.max)
        for e in range(2, nexp):
            sp.tt(m1, m1, le[e], ALU.max)
        if renorm:
            for e in range(nexp):
                sp.tt(mk[e], le[e], m1, ALU.is_equal)
                sp.stt(mk[e], mk[e], -1e30, le[e], ALU.mult, ALU.add)
            sp.tt(m2, mk[0], mk[1], ALU.max)
            for e in range(2, nexp):
                sp.tt(m2, m2, mk[e], ALU.max)
            for e in range(nexp):
                sp.tt(mk[e], le[e], m2, ALU.is_ge)
            sp.tt(t1, m2, m1, ALU.subtract)
            sp.act(t1, t1, AF.Exp)
            sp.ts(t1, t1, 1.0, ALU.add)
            sp.recip(t1, t1)
            sp.ts(t1, t1, scale_out, ALU.mult)
            for e in range(nexp):
                sp.tt(le[e], le[e], m1, ALU.subtract)
                sp.act(le[e], le[e], AF.Exp)
                sp.tt(le[e], le[e], mk[e], ALU.mult)
                sp.tt(le[e], le[e], t1, ALU.mult)
        else:
            for e in range(nexp):
                sp.tt(le[e], le[e], m1, ALU.subtract)
                sp.act(le[e], le[e], AF.Exp)
            sp.tt(t1, le[0], le[1], ALU.add)
            for e in range(2, nexp):
                sp.tt(t1, t1, le[e], ALU.add)
            sp.recip(t1, t1)
            for e in range(nexp):
                sp.tt(le[e], le[e], t1, ALU.mult)
            sp.tt(m1, le[0], le[1], ALU.max)
            for e in range(2, nexp):
                sp.tt(m1, m1, le[e], ALU.max)
            for e in range(nexp):
                sp.tt(mk[e], le[e], m1, ALU.is_equal)
                sp.stt(mk[e], mk[e], -1e30, le[e], ALU.mult, ALU.add)
            sp.tt(m2, mk[0], mk[1], ALU.max)
            for e in range(2, nexp):
                sp.tt(m2, m2, mk[e], ALU.max)
            for e in range(nexp):
                sp.tt(mk[e], le[e], m2, ALU.is_ge)
                sp.tt(le[e], le[e], mk[e], ALU.mult)
                sp.ts(le[e], le[e], scale_out, ALU.mult)
        sp.dmas([(c_dram[e:e + 1, tsl], le[e]) for e in range(nexp)])


def build_moe0(sp=None, io=None):
    if sp is None:
        sp = SP(fused=True)
    io = io or {}
    def din(n, s, d=F32):
        return io[n] if n in io else sp.din(n, s, d)
    def dout(n, s, d=F32):
        return io[n] if n in io else sp.dout(n, s, d)
    xT = din("xT", (1024, TC))
    W1b = din("W1b", (8, 1024, 4096), BF16)
    W2b = din("W2b", (8, 2048, 1024), BF16)
    WrL = din("WrL", (128, 8, 8))
    brC = din("brC", (8, 1))
    b1L = din("b1L", (128, 32, 8))
    b2L = din("b2L", (128, 8, 8))
    yT = dout("yT", (1024, TC))
    c_dram = sp.dtmp("cdram", (8, TC))

    ar = Arena(sp, 40960)
    x_sb = ar.f32(0, (8, TC))
    xn_bf = ar.bf16(16384, (8, TC))
    sq = ar.f32(24576, (8, 512))
    rows_off = 28672
    goff = 25088
    act = ar.bf16(24576, (16, TC))
    acc = ar.f32(0, (8, TC))

    ones_col = sp.sbuf("onc", [128, 1], F32)
    ones_r8 = sp.sbuf("o8", [1, 8], F32)
    ones_r128 = sp.sbuf("o128", [1, 128], F32)
    wr_sb = sp.sbuf("wr", [128, 8, 8], F32)
    br_sb = sp.sbuf("brs", [8, 1], F32)
    b1_sb = sp.sbuf("b1s", [128, 32, 8], F32)
    b2_sb = sp.sbuf("b2s", [128, 8, 8], F32)
    ncol = sp.sbuf("ncl", [8, 1], F32)
    xr_sb = ar.t[0:8, 38912:40960]
    lsb8 = ar.t[0:8, 36864:38912]
    w1a = sp.sbuf("w1a", [128, 8, 128], BF16)
    w1g = sp.sbuf("w1g", [128, 8, 128], BF16)
    w2f = sp.sbuf("w2f", [128, 16, 128], BF16)
    c_rep = sp.sbuf("crp", [128, TC], F32)
    silu_a = sp.sbuf("sla", [128, TC], F32)
    tmp_w = sp.sbuf("tmw", [128, TC], F32)
    xload = tmp_w

    psSt = sp.psum_at(0, "pst", (1, 1024))
    psB0 = sp.psum_at(0, "pb0", (128, TC))
    psB1 = sp.psum_at(4, "pb1", (128, TC))
    ps_r8 = sp.psum_at(0, "pr8", (8, TC))
    ps8a = sp.psum_at(4, "p8a", (8, TC))
    ps8b = sp.psum_at(0, "p8b", (8, TC))
    psA = sp.psum_at(0, "psa", (128, TC))
    psG = sp.psum_at(4, "psg", (128, TC))
    psO = sp.psum_at(0, "pso", (128, TC))

    sp.multi("gpsimd", [lambda e: e.memset(ones_col[:], 1.0),
                        lambda e: e.memset(ones_r8[:], 1.0),
                        lambda e: e.memset(ones_r128[:], 1.0)])
    sp.dmas([(x_sb, xT.rearrange("(a p) t -> p a t", p=128)),
             (wr_sb[:], WrL), (br_sb[:], brC), (b1_sb[:], b1L), (b2_sb[:], b2L)])

    mrow = ar.row(rows_off, TC)
    srow = ar.row(rows_off + TC, TC)
    sinv = ar.row(rows_off + 2 * TC, TC)
    msr = ar.row(rows_off + 3 * TC, TC)
    NF = 8
    for tc in range(4):
        tsl = slice(tc * 512, tc * 512 + 512)
        sp.multi("vector", [
            (lambda e, _a=a, _tsl=tsl: e.tensor_tensor(out=sq[:, _a, :], in0=x_sb[:, _a, _tsl],
                                                       in1=x_sb[:, _a, _tsl], op=ALU.mult))
            for a in range(NF)])
        def fstat(eng, _tsl=tsl):
            last = None
            for a in range(NF):
                eng.matmul(psSt[0:1, 0:512], ones_col[:], x_sb[:, a, _tsl],
                           start=(a == 0), stop=(a == NF - 1))
            for a in range(NF):
                last = eng.matmul(psSt[0:1, 512:1024], ones_col[:], sq[:, a, :],
                                  start=(a == 0), stop=(a == NF - 1))
            return last
        sp.mms(fstat)
        sp.multi("scalar", [
            lambda e, _tsl=tsl: e.activation(out=mrow[:, _tsl], in_=psSt[0:1, 0:512],
                                             func=AF.Copy, scale=1.0 / 1024),
            lambda e, _tsl=tsl: e.activation(out=srow[:, _tsl], in_=psSt[0:1, 512:1024],
                                             func=AF.Copy, scale=1.0 / 1024)])
    sp.tt(msr[:], mrow[:], mrow[:], ALU.mult)
    sp.tt(srow[:], srow[:], msr[:], ALU.subtract)
    sp.ts(srow[:], srow[:], 1e-5, ALU.add)
    sp.act(srow[:], srow[:], AF.Sqrt)
    sp.recip(sinv[:], srow[:])
    sp.tt(msr[:], mrow[:], sinv[:], ALU.mult)
    def fb(eng):
        last = None
        for tc in range(4):
            tsl = slice(tc * 512, tc * 512 + 512)
            eng.matmul(psB0[:, tsl], ones_r128[:], sinv[:, tsl], start=True, stop=True)
            last = eng.matmul(psB1[:, tsl], ones_r128[:], msr[:, tsl], start=True, stop=True)
        return last
    sp.mms(fb)
    for a in range(NF):
        sp.tt(tmp_w[:], x_sb[:, a, :], psB0[:, :], ALU.mult)
        sp.tt(xn_bf[:, a, :], tmp_w[:], psB1[:, :], ALU.subtract)
    def fcs(eng):
        last = None
        for a in range(NF):
            last = eng.matmul(ps8b[0:8, 0:1], wr_sb[:, a, :], ones_col[:],
                              start=(a == 0), stop=(a == NF - 1))
        return last
    sp.mms(fcs)
    sp.act(ncol[:], ps8b[0:8, 0:1], AF.Copy, scale=-1.0)
    def fxr(eng):
        last = None
        for tc in range(4):
            tsl = slice(tc * 512, tc * 512 + 512)
            for a in range(NF):
                last = eng.matmul(ps_r8[:8, tsl], wr_sb[:, a, :], x_sb[:, a, tsl],
                                  start=(a == 0), stop=(a == NF - 1))
        return last
    sp.mms(fxr)
    sp.copy(xr_sb, ps_r8[:8, :], engine="scalar")
    def fb8(eng):
        last = None
        for tc in range(4):
            tsl = slice(tc * 512, tc * 512 + 512)
            eng.matmul(ps8a[:8, tsl], ones_r8[:], sinv[:, tsl], start=True, stop=True)
            last = eng.matmul(ps8b[:8, tsl], ones_r8[:], msr[:, tsl], start=True, stop=True)
        return last
    sp.mms(fb8)
    sp.tt(xr_sb, xr_sb, ps8a[:8, :], ALU.mult)
    sp.stt(lsb8, ps8b[:8, :], ncol[:], xr_sb, ALU.mult, ALU.add)
    sp.ts(lsb8, lsb8, br_sb[:], ALU.add)

    emit_gating_top2(sp, ar, goff, lsb8, c_dram, 8, renorm=True, scale_out=0.5)

    sp.memset(acc.rearrange("p a t -> p (a t)"), 0.0, engine="gpsimd")
    for e in range(8):
        sp.dmas([(c_rep[:, tc * 512:(tc + 1) * 512],
                  c_dram[e:e + 1, tc * 512:(tc + 1) * 512].to_broadcast([128, 512]))
                 for tc in range(4)])
        for h in range(16):
            sp.dmas([(w1a[:], W1b[e, :, h * 128:(h + 1) * 128].rearrange("(a p) f -> p a f", p=128)),
                     (w1g[:], W1b[e, :, 2048 + h * 128:2048 + (h + 1) * 128].rearrange("(a p) f -> p a f", p=128))])
            def fup(eng):
                last = None
                for tc in range(4):
                    tsl = slice(tc * 512, tc * 512 + 512)
                    for a in range(NF):
                        eng.matmul(psA[:, tsl], w1a[:, a, :], xn_bf[:, a, tsl],
                                   start=(a == 0), stop=(a == NF - 1))
                    for a in range(NF):
                        last = eng.matmul(psG[:, tsl], w1g[:, a, :], xn_bf[:, a, tsl],
                                          start=(a == 0), stop=(a == NF - 1))
                return last
            sp.mms(fup)
            def fsilu(eng, _h=h, _e=e):
                return eng.activation(out=silu_a[:], in_=psA[:, :], func=AF.Silu,
                                      bias=b1_sb[:, _h, _e:_e + 1])
            sp.op("scalar", fsilu)
            def fstt(eng, _h=h, _e=e):
                return eng.scalar_tensor_tensor(out=act[:, _h, :], in0=psG[:, :],
                                                scalar=b1_sb[:, 16 + _h, _e:_e + 1],
                                                in1=silu_a[:], op0=ALU.add, op1=ALU.mult)
            sp.op("vector", fstt)
        for fo in range(8):
            sp.dma(w2f[:], W2b[e, :, fo * 128:(fo + 1) * 128].rearrange("(hh p) f -> p hh f", p=128))
            def fdn(eng):
                last = None
                for tc in range(4):
                    tsl = slice(tc * 512, tc * 512 + 512)
                    for hh in range(16):
                        last = eng.matmul(psO[:, tsl], w2f[:, hh, :], act[:, hh, tsl],
                                          start=(hh == 0), stop=(hh == 15))
                return last
            sp.mms(fdn)
            def fev(eng, _fo=fo, _e=e):
                return eng.scalar_tensor_tensor(out=tmp_w[:], in0=psO[:, :],
                                                scalar=b2_sb[:, _fo, _e:_e + 1],
                                                in1=c_rep[:], op0=ALU.add, op1=ALU.mult)
            sp.op("vector", fev)
            sp.tt(acc[:, fo, :], acc[:, fo, :], tmp_w[:], ALU.add)

    for fo in range(8):
        sp.dma(xload[:], xT[fo * 128:(fo + 1) * 128, :])
        sp.tt(xload[:], xload[:], acc[:, fo, :], ALU.add)
        sp.dma(yT[fo * 128:(fo + 1) * 128, :], xload[:])
    return sp

def _stage_moe0(inp, Wp):
    Wr = inp["L0_ffn_Wr"]
    b1 = inp["L0_ffn_b1"]
    b2 = inp["L0_ffn_b2"]
    return {
        "W1b": np.ascontiguousarray(Wp["L0_ffn_W1_b"]),
        "W2b": np.ascontiguousarray(Wp["L0_ffn_W2_b"]),
        "WrL": np.ascontiguousarray(Wr.reshape(8, 128, 8).transpose(1, 0, 2)),
        "brC": np.ascontiguousarray(inp["L0_ffn_br"].reshape(8, 1)),
        "b1L": np.ascontiguousarray(b1.T.reshape(32, 128, 8).transpose(1, 0, 2)),
        "b2L": np.ascontiguousarray(b2.T.reshape(8, 128, 8).transpose(1, 0, 2)),
    }


def run_moe0(X, inp, Wp):
    sp = build_moe0()
    shared = _stage_moe0(inp, Wp)
    in_maps = []
    for c in range(NC):
        m = dict(shared)
        m["xT"] = np.ascontiguousarray(X[c * TC:(c + 1) * TC].T)
        in_maps.append(m)
    res = sp.run(in_maps, core_ids=range(NC))
    out = np.empty_like(X)
    for c in range(NC):
        out[c * TC:(c + 1) * TC] = res.results[c]["yT"].T
    return out


def build_attn0(sp=None, io=None):
    NW = 1152
    if sp is None:
        sp = SP()
    io = io or {}
    def din(n, s, d=F32):
        return io[n] if n in io else sp.din(n, s, d)
    def dout(n, s, d=F32):
        return io[n] if n in io else sp.dout(n, s, d)
    xwT = None if "xw_load" in (io or {}) else din("xwT", (1024, NW))
    xrT = din("xrT", (1024, TC))
    Wq1b = din("Wq1b", (1024, 1024), BF16)
    Wq2b = din("Wq2b", (1024, 1024), BF16)
    Wk1b = din("Wk1b", (1024, 128), BF16)
    Wk2b = din("Wk2b", (1024, 128), BF16)
    Wvb = din("Wvb", (1024, 128), BF16)
    Woutb = din("Woutb", (1024, 1024), BF16)
    boutL = din("boutL", (128, 8))
    lamC = din("lamC", (128, 1))
    posrow = din("posrow", (1, NW))
    divrow = din("divrow", (1, 32))
    yT = dout("yT", (1024, TC))

    ar = Arena(sp, 30720)
    x_sb = ar.f32(0, (8, NW))
    xn_bf = ar.bf16(9216, (8, NW))
    rows_off = 13824
    def v64(off, nh):
        return ar.t[0:64, off:off + nh * NW // 2].bitcast(BF16).rearrange(
            "p (h t) -> p h t", h=nh)
    qrot1_v = v64(0, 16)
    qrot2_v = v64(18432, 16)
    krot1_v = v64(27648, 2)
    krot2_v = v64(28800, 2)

    ones_col = sp.sbuf("onc", [128, 1], F32)
    ones_col_bf = sp.sbuf("onb", [128, 1], BF16)
    ones_r128 = sp.sbuf("o12", [1, 128], F32)
    ones_r64 = sp.sbuf("o64", [1, 64], F32)
    sq = sp.sbuf("sq", [128, 512], F32)
    tmp = sp.sbuf("tmp", [128, 512], F32)
    v_sb = sp.sbuf("vsb", [128, NW], F32)
    wq_sb = sp.sbuf("wqs", [128, 8, 128], BF16)
    wout_sb = sp.sbuf("wos", [128, 8, 1024], BF16)
    bout_sb = sp.sbuf("bos", [128, 8], F32)
    lam_sb = sp.sbuf("lam", [128, 1], F32)
    cos_sb = sp.sbuf("cos", [32, NW], F32)
    sin_sb = sp.sbuf("sin", [32, NW], F32)
    th_sb = sp.sbuf("ths", [32, NW], F32)
    msk_sb = sp.sbuf("msk", [32, NW], F32)
    vt_sb = sp.sbuf("vts", [128, 2, 2, 64], BF16)
    expT = sp.sbuf("exT", [128, 256], BF16)
    expT2 = sp.sbuf("ex2", [128, 256], BF16)
    rrow = sp.sbuf("rrw", [1, 256], F32)
    rrep = sp.sbuf("rrp", [128, 256], F32)
    a_b1 = sp.sbuf("ab1", [128, 256], F32)
    a_b2 = sp.sbuf("ab2", [128, 256], F32)
    aM = sp.sbuf("aM", [128, 8, 256], BF16)
    xr_load = sp.sbuf("xrl", [128, 256], F32)
    t32a = sp.sbuf("t3a", [32, 512], F32)
    t32b = sp.sbuf("t3b", [32, 512], F32)

    psE = sp.psum("psE", [32, 512])
    psO32 = sp.psum("psO", [32, 512])
    psS = sp.psum("psS", [128, 512])
    psW = sp.psum("psW", [128, 512])
    psPV = sp.psum("psP", [128, 256])
    psR = sp.psum("psR", [1, 256])
    psB = sp.psum("psB", [128, 256])
    psT = sp.psum("psT", [128, 64], F32)
    identb = make_ident(sp, "idb", 128, F32)

    sp.memset(ones_col[:], 1.0, engine="gpsimd")
    sp.memset(ones_col_bf[:], 1.0, engine="gpsimd")
    sp.memset(ones_r128[:], 1.0, engine="gpsimd")
    sp.memset(ones_r64[:], 1.0, engine="gpsimd")
    if "xw_load" in io:
        io["xw_load"](sp, x_sb)
    else:
        sp.dmas([(x_sb, xwT.rearrange("(a p) t -> p a t", p=128))])
    sp.dmas([(wout_sb[:], Woutb.rearrange("(a p) f -> p a f", p=128)),
             (bout_sb[:], boutL), (lam_sb[:], lamC)])
    sp.ts(lam_sb[:], lam_sb[:], -1.0, ALU.mult)

    mrow = ar.row(rows_off, NW)
    srow = ar.row(rows_off + NW, NW)
    sinv = ar.row(rows_off + 2 * NW, NW)
    msr = ar.row(rows_off + 3 * NW, NW)
    tch = chunks(NW, 512)
    for (t0, tn) in tch:
        tsl = slice(t0, t0 + tn)
        def fmean(eng, _tsl=tsl, _tn=tn):
            last = None
            for a in range(8):
                last = eng.matmul(psE[0:1, :_tn], ones_col[:], x_sb[:, a, _tsl],
                                  start=(a == 0), stop=(a == 7))
            return last
        sp.mms(fmean)
        sp.act(mrow[:, tsl], psE[0:1, :tn], AF.Copy, scale=1.0 / 1024)
        for a in range(8):
            sp.tt(sq[:, :tn], x_sb[:, a, tsl], x_sb[:, a, tsl], ALU.mult)
            def fsq(eng, _a=a, _tn=tn):
                return eng.matmul(psO32[0:1, :_tn], ones_col[:], sq[:, :_tn],
                                  start=(_a == 0), stop=(_a == 7))
            sp.mms(fsq)
        sp.act(srow[:, tsl], psO32[0:1, :tn], AF.Copy, scale=1.0 / 1024)
    sp.tt(msr[:], mrow[:], mrow[:], ALU.mult)
    sp.tt(srow[:], srow[:], msr[:], ALU.subtract)
    sp.ts(srow[:], srow[:], 1e-5, ALU.add)
    sp.act(srow[:], srow[:], AF.Sqrt)
    sp.recip(sinv[:], srow[:])
    sp.tt(msr[:], mrow[:], sinv[:], ALU.mult)
    for (t0, tn) in tch:
        tsl = slice(t0, t0 + tn)
        def fb(eng, _tsl=tsl, _tn=tn):
            eng.matmul(psW[:, :_tn], ones_r128[:], sinv[:, _tsl], start=True, stop=True)
            return eng.matmul(psS[:, :_tn], ones_r128[:], msr[:, _tsl], start=True, stop=True)
        sp.mms(fb)
        for a in range(8):
            sp.tt(tmp[:, :tn], x_sb[:, a, tsl], psW[:, :tn], ALU.mult)
            sp.tt(xn_bf[:, a, tsl], tmp[:, :tn], psS[:, :tn], ALU.subtract)

    div_sb = sp.sbuf("div", [1, 32], F32)
    pos_sb = sp.sbuf("pos", [1, NW], F32)
    sp.dmas([(div_sb[:], divrow), (pos_sb[:], posrow)])
    for (t0, tn) in tch:
        tsl = slice(t0, t0 + tn)
        def fth(eng, _tsl=tsl, _tn=tn):
            return eng.matmul(psE[0:32, :_tn], div_sb[:], pos_sb[0:1, _tsl],
                              start=True, stop=True)
        sp.mms(fth)
        sp.copy(th_sb[:, tsl], psE[0:32, :tn], engine="scalar")
    TWO_PI = float(2 * np.pi)
    i32t = sp.sbuf("i32", [32, NW], mybir.dt.int32)
    sp.ts(th_sb[:], th_sb[:], float(1.0 / TWO_PI), ALU.mult)
    sp.copy(i32t[:], th_sb[:])
    sp.copy(msk_sb[:], i32t[:])
    sp.stt(th_sb[:], msk_sb[:], -1.0, th_sb[:], ALU.mult, ALU.add)
    sp.ts(th_sb[:], th_sb[:], TWO_PI, ALU.mult)
    sp.ts(msk_sb[:], th_sb[:], float(np.pi), ALU.is_gt)
    sp.stt(th_sb[:], msk_sb[:], -TWO_PI, th_sb[:], ALU.mult, ALU.add)
    sp.ts(msk_sb[:], th_sb[:], float(-np.pi), ALU.is_lt)
    sp.stt(th_sb[:], msk_sb[:], TWO_PI, th_sb[:], ALU.mult, ALU.add)
    sp.act(sin_sb[:], th_sb[:], AF.Sin)
    sp.ts(th_sb[:], th_sb[:], float(np.pi / 2), ALU.add)
    sp.ts(msk_sb[:], th_sb[:], float(np.pi), ALU.is_gt)
    sp.stt(th_sb[:], msk_sb[:], -TWO_PI, th_sb[:], ALU.mult, ALU.add)
    sp.act(cos_sb[:], th_sb[:], AF.Sin)

    def project_rope(Wsrc, nfo, rot_dst):
        for fo in range(nfo):
            sp.dma(wq_sb[:], Wsrc[:, fo * 128:(fo + 1) * 128].rearrange("(a p) f -> p a f", p=128))
            wv = wq_sb[:].rearrange("p a (hh k par) -> p a hh k par", hh=2, par=2)
            for hh in range(2):
                h = fo * 2 + hh
                for (t0, tn) in tch:
                    tsl = slice(t0, t0 + tn)
                    def fe(eng, _hh=hh, _tsl=tsl, _tn=tn):
                        last = None
                        for a in range(8):
                            last = eng.matmul(psE[:, :_tn], wv[:, a, _hh, :, 0], xn_bf[:, a, _tsl],
                                              start=(a == 0), stop=(a == 7))
                        return last
                    sp.mms(fe)
                    def fodd(eng, _hh=hh, _tsl=tsl, _tn=tn):
                        last = None
                        for a in range(8):
                            last = eng.matmul(psO32[:, :_tn], wv[:, a, _hh, :, 1], xn_bf[:, a, _tsl],
                                              start=(a == 0), stop=(a == 7))
                        return last
                    sp.mms(fodd)
                    sp.tt(t32a[:, :tn], psE[0:32, :tn], cos_sb[:, tsl], ALU.mult)
                    sp.tt(t32b[:, :tn], psO32[0:32, :tn], sin_sb[:, tsl], ALU.mult)
                    sp.tt(rot_dst[0:32, h, tsl], t32a[:, :tn], t32b[:, :tn], ALU.subtract)
                    sp.tt(t32a[:, :tn], psE[0:32, :tn], sin_sb[:, tsl], ALU.mult)
                    sp.tt(t32b[:, :tn], psO32[0:32, :tn], cos_sb[:, tsl], ALU.mult)
                    sp.tt(rot_dst[32:64, h, tsl], t32a[:, :tn], t32b[:, :tn], ALU.add)

    project_rope(Wq1b, 8, qrot1_v)
    project_rope(Wq2b, 8, qrot2_v)
    project_rope(Wk1b, 1, krot1_v)
    project_rope(Wk2b, 1, krot2_v)
    sp.dma(wq_sb[:], Wvb.rearrange("(a p) f -> p a f", p=128))
    for (t0, tn) in tch:
        tsl = slice(t0, t0 + tn)
        def fv(eng, _tsl=tsl, _tn=tn):
            last = None
            for a in range(8):
                last = eng.matmul(psS[:, :_tn], wq_sb[:, a, :], xn_bf[:, a, _tsl],
                                  start=(a == 0), stop=(a == 7))
            return last
        sp.mms(fv)
        sp.copy(v_sb[:, tsl], psS[:, :tn], engine="scalar")

    for n in range(8):
        wsl = slice(n * 128, n * 128 + 256)
        for g in range(2):
            for half in range(2):
                ksl = slice(n * 128 + half * 128, n * 128 + half * 128 + 128)
                sp.transpose(psT[:, :], v_sb[g * 64:(g + 1) * 64, ksl],
                             identb[g * 64:(g + 1) * 64, g * 64:(g + 1) * 64])
                sp.copy(vt_sb[:, g, half, :], psT[:, :], engine="scalar")
        for h in range(16):
            g = h % 2
            for br, (qro, kro, adst) in enumerate([(qrot1_v, krot1_v, a_b1), (qrot2_v, krot2_v, a_b2)]):
                ob = (h % 2) * 64
                def fsT2(eng, _qro=qro, _kro=kro, _g=g, _h=h, _n=n):
                    last = None
                    for half in range(2):
                        ksl = slice(_n * 128 + half * 128, _n * 128 + half * 128 + 128)
                        last = eng.matmul(psS[:, half * 256:half * 256 + 256],
                                          _kro[0:64, _g, ksl],
                                          _qro[0:64, _h, _n * 128:_n * 128 + 256],
                                          start=True, stop=True)
                    return last
                sp.mms(fsT2)
                sp.act(expT[:, :], psS[:, 0:256], AF.Exp, scale=0.125)
                sp.act(expT2[:, :], psS[:, 256:512], AF.Exp, scale=0.125)
                def fpv(eng, _g=g, _ob=ob):
                    eng.matmul(psPV[_ob:_ob + 64, :], vt_sb[:, _g, 0, :], expT[:, :],
                               start=True, stop=False)
                    return eng.matmul(psPV[_ob:_ob + 64, :], vt_sb[:, _g, 1, :], expT2[:, :],
                                      start=False, stop=True)
                sp.mms(fpv)
                def fsum(eng):
                    eng.matmul(psR[:, :], ones_col_bf[:], expT[:, :], start=True, stop=False)
                    return eng.matmul(psR[:, :], ones_col_bf[:], expT2[:, :], start=False, stop=True)
                sp.mms(fsum)
                sp.recip(rrow[:], psR[0:1, :])
                def fbc(eng, _ob=ob):
                    return eng.matmul(psB[_ob:_ob + 64, :], ones_r64[:], rrow[:],
                                      start=True, stop=True)
                sp.mms(fbc)
                sp.copy(rrep[ob:ob + 64, :], psB[ob:ob + 64, :], engine="scalar")
                sp.tt(adst[ob:ob + 64, :], psPV[ob:ob + 64, :], rrep[ob:ob + 64, :], ALU.mult)
            ob = (h % 2) * 64
            sp.ts(a_b2[ob:ob + 64, :], a_b2[ob:ob + 64, :], lam_sb[ob:ob + 64, :], ALU.mult)
            sp.tt(aM[ob:ob + 64, h // 2, :], a_b1[ob:ob + 64, :], a_b2[ob:ob + 64, :], ALU.add)
        osl = slice(n * 256, n * 256 + 256)
        for fo in range(8):
            sp.dma(xr_load[:], xrT[fo * 128:(fo + 1) * 128, osl])
            def fw(eng, _fo=fo):
                last = None
                for a in range(8):
                    last = eng.matmul(psW[:, :256], wout_sb[:, a, _fo * 128:(_fo + 1) * 128],
                                      aM[:, a, :], start=(a == 0), stop=(a == 7))
                return last
            sp.mms(fw)
            sp.stt(xr_load[:], psW[:, :256], bout_sb[:, fo:fo + 1], xr_load[:],
                   ALU.add, ALU.add)
            sp.dma(yT[fo * 128:(fo + 1) * 128, osl], xr_load[:])
    return sp


def _stage_attn0(inp, Wp):
    lam = np.asarray(Wp["lam"], dtype=np.float32).reshape(1)
    div = np.exp(np.arange(0, 64, 2, dtype=np.float32) * (-np.log(10000.0) / 64))
    return {
        "Wq1b": np.ascontiguousarray(Wp["L0_attn_Wq1_b"]),
        "Wq2b": np.ascontiguousarray(Wp["L0_attn_Wq2_b"]),
        "Wk1b": np.ascontiguousarray(Wp["L0_attn_Wk1_b"]),
        "Wk2b": np.ascontiguousarray(Wp["L0_attn_Wk2_b"]),
        "Wvb": np.ascontiguousarray(Wp["L0_attn_Wv_b"]),
        "Woutb": np.ascontiguousarray(Wp["L0_attn_Wout_b"]),
        "boutL": np.ascontiguousarray(inp["L0_attn_bout"].reshape(8, 128).T),
        "lamC": np.broadcast_to(lam.reshape(1, 1), (128, 1)).copy(),
        "divrow": div.reshape(1, 32),
    }


def run_attn0(X, inp, Wp):
    sp = build_attn0()
    shared = _stage_attn0(inp, Wp)
    in_maps = []
    for c in range(NC):
        b, q = c // 4, c % 4
        m = dict(shared)
        xb = X[b * 8192:(b + 1) * 8192]
        m["xwT"] = np.ascontiguousarray(xb[1024 * q:1024 * q + 1152].T)
        m["xrT"] = np.ascontiguousarray(xb[2048 * q:2048 * q + 2048].T)
        m["posrow"] = (1024.0 * q + np.arange(1152, dtype=np.float32)).reshape(1, 1152)
        in_maps.append(m)
    res = sp.run(in_maps, core_ids=range(NC))
    out = np.empty_like(X)
    for c in range(NC):
        out[c * TC:(c + 1) * TC] = res.results[c]["yT"].T
    return out


def build_moe1(skip_experts=False, skip_shared=False, sp=None, io=None):
    if sp is None:
        sp = SP()
    io = io or {}
    def din(n, s, d=F32):
        return io[n] if n in io else sp.din(n, s, d)
    def dout(n, s, d=F32):
        return io[n] if n in io else sp.dout(n, s, d)
    xT = din("xT", (1024, TC))
    Gb = din("Gb", (16, 1024, 512), BF16)
    Ub = din("Ub", (16, 1024, 512), BF16)
    Db = din("Db", (16, 512, 1024), BF16)
    SGb = din("SGb", (1024, 1024), BF16)
    SUb = din("SUb", (1024, 1024), BF16)
    SDb = din("SDb", (1024, 1024), BF16)
    RwL = din("RwL", (128, 8, 16))
    yT = dout("yT", (1024, TC))
    c_dram = sp.dtmp("cdram", (16, TC))

    ar = Arena(sp, 40960)
    x_sb = ar.f32(0, (8, TC))
    xn_bf = ar.bf16(16384, (8, TC))
    goff = 0
    act_sh = ar.bf16(0, (8, TC))
    act_ex = ar.bf16(8192, (4, TC))
    acc = ar.f32(24576, (8, TC))
    rows_off = 12288
    rows_off = 24576

    ones_col = sp.sbuf("onc", [128, 1], F32)
    ones_r16 = sp.sbuf("o16", [1, 16], F32)
    ones_r128 = sp.sbuf("o128", [1, 128], F32)
    rw_sb = sp.sbuf("rw", [128, 8, 16], F32)
    ncol = sp.sbuf("ncl", [16, 1], F32)
    xr_sb = sp.sbuf("xrs", [16, 512], F32)
    lsb = sp.sbuf("l16", [16, TC], F32)
    wg = sp.sbuf("wg", [128, 8, 128], BF16)
    wu = sp.sbuf("wu", [128, 8, 128], BF16)
    wd = sp.sbuf("wd", [128, 8, 128], BF16)
    c_rep = sp.sbuf("crp", [128, TC], F32)
    tmp_f = sp.sbuf("tmf", [128, 512], F32)
    silu_a2 = sp.sbuf("sl2", [128, TC], F32)
    tmp_w2 = sp.sbuf("tw2", [128, TC], F32)
    sq = sp.sbuf("sqq", [128, 512], F32)
    xload = tmp_w2

    ps_r16 = sp.psum("psr", [16, 512])
    ps16a = sp.psum("p6a", [16, 512])
    ps16b = sp.psum("p6b", [16, 512])
    psB0 = sp.psum("pb0", [128, 512])
    psB1 = sp.psum("pb1", [128, 512])
    psA = sp.psum_at(0, "psa", (128, TC))
    psG = sp.psum_at(4, "psg", (128, TC))
    psO = sp.psum_at(0, "pso", (128, TC))

    sp.memset(ones_col[:], 1.0, engine="gpsimd")
    sp.memset(ones_r16[:], 1.0, engine="gpsimd")
    sp.memset(ones_r128[:], 1.0, engine="gpsimd")
    sp.dmas([(x_sb, xT.rearrange("(a p) t -> p a t", p=128)), (rw_sb[:], RwL)])

    mrow = ar.row(rows_off, TC)
    srow = ar.row(rows_off + TC, TC)
    sinv = ar.row(rows_off + 2 * TC, TC)
    msr = ar.row(rows_off + 3 * TC, TC)
    NF = 8
    for tc in range(4):
        tsl = slice(tc * 512, tc * 512 + 512)
        def fmean(eng, _tsl=tsl):
            last = None
            for a in range(NF):
                last = eng.matmul(ps16a[0:1, :], ones_col[:], x_sb[:, a, _tsl],
                                  start=(a == 0), stop=(a == NF - 1))
            return last
        sp.mms(fmean)
        sp.act(mrow[:, tsl], ps16a[0:1, :], AF.Copy, scale=1.0 / 1024)
        for a in range(NF):
            sp.tt(sq[:], x_sb[:, a, tsl], x_sb[:, a, tsl], ALU.mult)
            def fsq(eng, _a=a):
                return eng.matmul(ps16b[0:1, :], ones_col[:], sq[:],
                                  start=(_a == 0), stop=(_a == NF - 1))
            sp.mms(fsq)
        sp.act(srow[:, tsl], ps16b[0:1, :], AF.Copy, scale=1.0 / 1024)
    sp.tt(msr[:], mrow[:], mrow[:], ALU.mult)
    sp.tt(srow[:], srow[:], msr[:], ALU.subtract)
    sp.ts(srow[:], srow[:], 1e-5, ALU.add)
    sp.act(srow[:], srow[:], AF.Sqrt)
    sp.recip(sinv[:], srow[:])
    sp.tt(msr[:], mrow[:], sinv[:], ALU.mult)
    def fcs(eng):
        last = None
        for a in range(NF):
            last = eng.matmul(ps16a[0:16, 0:1], rw_sb[:, a, :], ones_col[:],
                              start=(a == 0), stop=(a == NF - 1))
        return last
    sp.mms(fcs)
    sp.act(ncol[:], ps16a[0:16, 0:1], AF.Copy, scale=-1.0)
    for tc in range(4):
        tsl = slice(tc * 512, tc * 512 + 512)
        def fb(eng, _tsl=tsl):
            eng.matmul(psB0[:], ones_r128[:], sinv[:, _tsl], start=True, stop=True)
            return eng.matmul(psB1[:], ones_r128[:], msr[:, _tsl], start=True, stop=True)
        sp.mms(fb)
        for a in range(NF):
            sp.tt(tmp_f[:], x_sb[:, a, tsl], psB0[:], ALU.mult)
            sp.tt(xn_bf[:, a, tsl], tmp_f[:], psB1[:], ALU.subtract)
        def fxr(eng, _tsl=tsl):
            last = None
            for a in range(NF):
                last = eng.matmul(ps_r16[:16, :], rw_sb[:, a, :], x_sb[:, a, _tsl],
                                  start=(a == 0), stop=(a == NF - 1))
            return last
        sp.mms(fxr)
        sp.copy(xr_sb[:], ps_r16[:16, :], engine="scalar")
        def fb16(eng, _tsl=tsl):
            eng.matmul(ps16a[:16, :], ones_r16[:], sinv[:, _tsl], start=True, stop=True)
            return eng.matmul(ps16b[:16, :], ones_r16[:], msr[:, _tsl], start=True, stop=True)
        sp.mms(fb16)
        sp.tt(xr_sb[:], xr_sb[:], ps16a[:16, :], ALU.mult)
        sp.stt(lsb[:, tsl], ps16b[:16, :], ncol[:], xr_sb[:], ALU.mult, ALU.add)

    emit_gating_top2(sp, ar, goff, lsb[:], c_dram, 16, renorm=False, scale_out=0.5,
                     goff_mk=24576, goff_r=8192)

    if skip_shared:
        sp.memset(acc.rearrange("p a t -> p (a t)"), 0.0, engine="gpsimd")
    for hf in range(0 if skip_shared else 8):
        sp.dmas([(wg[:], SGb[:, hf * 128:(hf + 1) * 128].rearrange("(a p) f -> p a f", p=128)),
                 (wu[:], SUb[:, hf * 128:(hf + 1) * 128].rearrange("(a p) f -> p a f", p=128))])
        def fup(eng):
            last = None
            for tc in range(4):
                tsl = slice(tc * 512, tc * 512 + 512)
                for a in range(NF):
                    eng.matmul(psA[:, tsl], wg[:, a, :], xn_bf[:, a, tsl],
                               start=(a == 0), stop=(a == NF - 1))
                for a in range(NF):
                    last = eng.matmul(psG[:, tsl], wu[:, a, :], xn_bf[:, a, tsl],
                                      start=(a == 0), stop=(a == NF - 1))
            return last
        sp.mms(fup)
        sp.act(silu_a2[:], psA[:, :], AF.Silu)
        def fmul(eng, _hf=hf):
            return eng.tensor_tensor(out=act_sh[:, _hf, :], in0=psG[:, :], in1=silu_a2[:],
                                     op=ALU.mult)
        sp.op("vector", fmul)
    for fo in range(8):
        sp.dma(wd[:], SDb[:, fo * 128:(fo + 1) * 128].rearrange("(a p) f -> p a f", p=128))
        def fd(eng):
            last = None
            for tc in range(4):
                tsl = slice(tc * 512, tc * 512 + 512)
                for a in range(NF):
                    last = eng.matmul(psO[:, tsl], wd[:, a, :], act_sh[:, a, tsl],
                                      start=(a == 0), stop=(a == NF - 1))
            return last
        sp.mms(fd)
        sp.ts(acc[:, fo, :], psO[:, :], 0.5, ALU.mult)

    for e in range(0 if skip_experts else 16):
        sp.dmas([(c_rep[:, tc * 512:(tc + 1) * 512],
                  c_dram[e:e + 1, tc * 512:(tc + 1) * 512].to_broadcast([128, 512]))
                 for tc in range(4)])
        for hf in range(4):
            sp.dmas([(wg[:], Gb[e, :, hf * 128:(hf + 1) * 128].rearrange("(a p) f -> p a f", p=128)),
                     (wu[:], Ub[e, :, hf * 128:(hf + 1) * 128].rearrange("(a p) f -> p a f", p=128))])
            def fup2(eng):
                last = None
                for tc in range(4):
                    tsl = slice(tc * 512, tc * 512 + 512)
                    for a in range(NF):
                        eng.matmul(psA[:, tsl], wg[:, a, :], xn_bf[:, a, tsl],
                                   start=(a == 0), stop=(a == NF - 1))
                    for a in range(NF):
                        last = eng.matmul(psG[:, tsl], wu[:, a, :], xn_bf[:, a, tsl],
                                          start=(a == 0), stop=(a == NF - 1))
                return last
            sp.mms(fup2)
            sp.act(silu_a2[:], psA[:, :], AF.Silu)
            def fmul2(eng, _hf=hf):
                return eng.tensor_tensor(out=act_ex[:, _hf, :], in0=psG[:, :], in1=silu_a2[:],
                                         op=ALU.mult)
            sp.op("vector", fmul2)
        for fo in range(8):
            sp.dma(wd[:, 0:4, :], Db[e, :, fo * 128:(fo + 1) * 128].rearrange("(a p) f -> p a f", p=128))
            def fd2(eng):
                last = None
                for tc in range(4):
                    tsl = slice(tc * 512, tc * 512 + 512)
                    for a in range(4):
                        last = eng.matmul(psO[:, tsl], wd[:, a, :], act_ex[:, a, tsl],
                                          start=(a == 0), stop=(a == 3))
                return last
            sp.mms(fd2)
            sp.tt(tmp_w2[:], psO[:, :], c_rep[:, :], ALU.mult)
            sp.tt(acc[:, fo, :], acc[:, fo, :], tmp_w2[:], ALU.add)

    for fo in range(8):
        sp.dma(xload[:], xT[fo * 128:(fo + 1) * 128, :])
        sp.tt(xload[:], xload[:], acc[:, fo, :], ALU.add)
        sp.dma(yT[fo * 128:(fo + 1) * 128, :], xload[:])
    return sp


def run_moe1(X, inp, Wp):
    sp = build_moe1()
    Rw = inp["L1_ffn_router_w"]
    shared = {
        "Gb": np.ascontiguousarray(Wp["L1_ffn_gate_w_b"]),
        "Ub": np.ascontiguousarray(Wp["L1_ffn_up_w_b"]),
        "Db": np.ascontiguousarray(Wp["L1_ffn_down_w_b"]),
        "SGb": np.ascontiguousarray(Wp["L1_ffn_shared_gate_w_b"]),
        "SUb": np.ascontiguousarray(Wp["L1_ffn_shared_up_w_b"]),
        "SDb": np.ascontiguousarray(Wp["L1_ffn_shared_down_w_b"]),
        "RwL": np.ascontiguousarray(Rw.reshape(8, 128, 16).transpose(1, 0, 2)),
    }
    in_maps = []
    for c in range(NC):
        m = dict(shared)
        m["xT"] = np.ascontiguousarray(X[c * TC:(c + 1) * TC].T)
        in_maps.append(m)
    res = sp.run(in_maps, core_ids=range(NC))
    out = np.empty_like(X)
    for c in range(NC):
        out[c * TC:(c + 1) * TC] = res.results[c]["yT"].T
    return out


def build_l1a(sp=None, io=None):
    NT = 2051
    if sp is None:
        sp = SP()
    io = io or {}
    def din(n, s, d=F32):
        return io[n] if n in io else sp.din(n, s, d)
    def dout(n, s, d=F32):
        return io[n] if n in io else sp.dout(n, s, d)
    xhT = None if "xh_load" in (io or {}) else din("xhT", (1024, NT))
    winT = din("winT", (1024, 4096), BF16)
    wxT = din("wxT", (2048, 96), BF16)
    wdtT = din("wdtT", (64, 2048), BF16)
    gamL = din("gamL", (128, 8))
    betL = din("betL", (128, 8))
    cwL = din("cwL", (128, 16, 4))
    cbL = din("cbL", (128, 16))
    bdtL = din("bdtL", (128, 16))
    delta_o = dout("delta_o", (2048, TC))
    u_o = dout("u_o", (2048, TC))
    xs_o = dout("xs_o", (2048, TC))
    z_o = dout("z_o", (2048, TC))
    B_o = dout("B_o", (16, TC))
    C_o = dout("C_o", (16, TC))

    ar = Arena(sp, 42048)
    x_sb = ar.f32(0, (8, NT))
    xn_bf = ar.bf16(16416, (8, NT))
    rows_off = 24624
    xs_bf = ar.bf16(24624, (16, TC))
    dtl_off = 41008

    ones_col = sp.sbuf("onc", [128, 1], F32)
    one_col = sp.sbuf("one", [128, 1], F32)
    ones_r128 = sp.sbuf("o128", [1, 128], F32)
    gam = sp.sbuf("gam", [128, 8], F32)
    bet = sp.sbuf("bet", [128, 8], F32)
    cw = sp.sbuf("cw", [128, 16, 4], F32)
    cb = sp.sbuf("cb", [128, 16], F32)
    bdt = sp.sbuf("bdt", [128, 16], F32)
    w_sb = sp.sbuf("wsb", [128, 8, 128], BF16)
    wx_sb = sp.sbuf("wxs", [128, 16, 96], BF16)
    wdt_sb = sp.sbuf("wds", [64, 2048], BF16)
    xp_g = sp.sbuf("xpg", [128, NT], BF16)
    accv = sp.sbuf("acv", [128, TC], F32)
    xs_g = sp.sbuf("xsg", [128, TC], F32)
    brow = sp.sbuf("brw", [16, 512], F32)
    crow = sp.sbuf("crw", [16, 512], F32)
    t1b = sp.sbuf("t1b", [128, 512], F32)
    t2b = sp.sbuf("t2b", [128, 512], F32)
    t3b = sp.sbuf("t3b", [128, 512], F32)

    psA = sp.psum("psa", [128, 512])
    psDt = sp.psum("pdt", [64, 512])
    psB = sp.psum("psb", [16, 512])
    psC = sp.psum("psc", [16, 512])
    psB0 = sp.psum("pb0", [128, 512])
    psB1 = sp.psum("pb1", [128, 512])

    dtlow = ar.t[0:64, dtl_off:dtl_off + 1024].bitcast(BF16)

    sp.memset(ones_col[:], 1.0, engine="gpsimd")
    sp.memset(one_col[:], 1.0, engine="gpsimd")
    sp.memset(ones_r128[:], 1.0, engine="gpsimd")
    if "xh_load" in io:
        io["xh_load"](sp, x_sb)
    else:
        sp.dmas([(x_sb, xhT.rearrange("(a p) t -> p a t", p=128))])
    sp.dmas([
             (gam[:], gamL), (bet[:], betL), (cw[:], cwL), (cb[:], cbL), (bdt[:], bdtL),
             (wx_sb[:], wxT.rearrange("(g p) r -> p g r", p=128)),
             (wdt_sb[:], wdtT)])

    mrow = ar.row(rows_off, NT)
    srow = ar.row(rows_off + NT, NT)
    sinv = ar.row(rows_off + 2 * NT, NT)
    msr = ar.row(rows_off + 3 * NT, NT)
    tch = chunks(NT, 512)
    def ln_stats():
        for (t0, tn) in tch:
            tsl = slice(t0, t0 + tn)
            def fmean(eng, _tsl=tsl, _tn=tn):
                last = None
                for a in range(8):
                    last = eng.matmul(psB[0:1, :_tn], ones_col[:], x_sb[:, a, _tsl],
                                      start=(a == 0), stop=(a == 7))
                return last
            sp.mms(fmean)
            sp.act(mrow[:, tsl], psB[0:1, :tn], AF.Copy, scale=1.0 / 1024)
            for a in range(8):
                sp.tt(t1b[:, :tn], x_sb[:, a, tsl], x_sb[:, a, tsl], ALU.mult)
                def fsq(eng, _a=a, _tn=tn):
                    return eng.matmul(psC[0:1, :_tn], ones_col[:], t1b[:, :_tn],
                                      start=(_a == 0), stop=(_a == 7))
                sp.mms(fsq)
            sp.act(srow[:, tsl], psC[0:1, :tn], AF.Copy, scale=1.0 / 1024)
        sp.tt(msr[:], mrow[:], mrow[:], ALU.mult)
        sp.tt(srow[:], srow[:], msr[:], ALU.subtract)
        sp.ts(srow[:], srow[:], 1e-5, ALU.add)
        sp.act(srow[:], srow[:], AF.Sqrt)
        sp.recip(sinv[:], srow[:])
        sp.tt(msr[:], mrow[:], sinv[:], ALU.mult)
    ln_stats()
    for (t0, tn) in tch:
        tsl = slice(t0, t0 + tn)
        def fb(eng, _tsl=tsl, _tn=tn):
            eng.matmul(psB0[:, :_tn], ones_r128[:], sinv[:, _tsl], start=True, stop=True)
            return eng.matmul(psB1[:, :_tn], ones_r128[:], msr[:, _tsl], start=True, stop=True)
        sp.mms(fb)
        for a in range(8):
            sp.tt(t1b[:, :tn], x_sb[:, a, tsl], psB0[:, :tn], ALU.mult)
            sp.tt(x_sb[:, a, tsl], t1b[:, :tn], psB1[:, :tn], ALU.subtract)
    ln_stats()
    for (t0, tn) in tch:
        tsl = slice(t0, t0 + tn)
        def fb(eng, _tsl=tsl, _tn=tn):
            eng.matmul(psB0[:, :_tn], ones_r128[:], sinv[:, _tsl], start=True, stop=True)
            return eng.matmul(psB1[:, :_tn], ones_r128[:], msr[:, _tsl], start=True, stop=True)
        sp.mms(fb)
        for a in range(8):
            sp.tt(t1b[:, :tn], x_sb[:, a, tsl], psB0[:, :tn], ALU.mult)
            sp.tt(t2b[:, :tn], t1b[:, :tn], psB1[:, :tn], ALU.subtract)
            def fts(eng, _a=a, _tsl=tsl, _tn=tn):
                return eng.tensor_scalar(out=xn_bf[:, _a, _tsl], in0=t2b[:, :_tn],
                                         scalar1=gam[:, _a:_a + 1], scalar2=bet[:, _a:_a + 1],
                                         op0=ALU.mult, op1=ALU.add)
            sp.op("vector", fts)

    for g in range(16):
        sp.dma(w_sb[:], winT[:, g * 128:(g + 1) * 128].rearrange("(a p) f -> p a f", p=128))
        for (t0, tn) in tch:
            tsl = slice(t0, t0 + tn)
            def fxp(eng, _tsl=tsl, _tn=tn):
                last = None
                for a in range(8):
                    last = eng.matmul(psA[:, :_tn], w_sb[:, a, :], xn_bf[:, a, _tsl],
                                      start=(a == 0), stop=(a == 7))
                return last
            sp.mms(fxp)
            sp.copy(xp_g[:, tsl], psA[:, :tn], engine="scalar")
        def fc0(eng, _g=g):
            return eng.tensor_scalar(out=accv[:], in0=xp_g[:, 0:TC], scalar1=cw[:, _g, 0:1],
                                     scalar2=None, op0=ALU.mult)
        sp.op("vector", fc0)
        for k in range(1, 4):
            def fck(eng, _g=g, _k=k):
                return eng.scalar_tensor_tensor(out=accv[:], in0=xp_g[:, _k:TC + _k],
                                                scalar=cw[:, _g, _k:_k + 1], in1=accv[:],
                                                op0=ALU.mult, op1=ALU.add)
            sp.op("vector", fck)
        def fsilu(eng, _g=g):
            return eng.activation(out=xs_g[:], in_=accv[:], func=AF.Silu, bias=cb[:, _g:_g + 1])
        sp.op("scalar", fsilu)
        sp.dma(xs_o[g * 128:(g + 1) * 128, :], xs_g[:])
        sp.copy(xs_bf[:, g, :], xs_g[:])
    for g in range(16):
        sp.dma(w_sb[:], winT[:, 2048 + g * 128:2048 + (g + 1) * 128].rearrange("(a p) f -> p a f", p=128))
        for tc in range(4):
            tsl = slice(3 + tc * 512, 3 + tc * 512 + 512)
            def fz(eng, _tsl=tsl):
                last = None
                for a in range(8):
                    last = eng.matmul(psA[:], w_sb[:, a, :], xn_bf[:, a, _tsl],
                                      start=(a == 0), stop=(a == 7))
                return last
            sp.mms(fz)
            sp.copy(xs_g[:, tc * 512:(tc + 1) * 512], psA[:], engine="scalar")
        sp.dma(z_o[g * 128:(g + 1) * 128, :], xs_g[:])

    for tc in range(4):
        tsl = slice(tc * 512, tc * 512 + 512)
        def fdt(eng, _tsl=tsl):
            last = None
            for g in range(16):
                last = eng.matmul(psDt[:, :], wx_sb[:, g, 0:64], xs_bf[:, g, _tsl],
                                  start=(g == 0), stop=(g == 15))
            return last
        sp.mms(fdt)
        sp.copy(dtlow[:, tsl], psDt[:, :], engine="scalar")
        def fB(eng, _tsl=tsl):
            last = None
            for g in range(16):
                last = eng.matmul(psB[:, :], wx_sb[:, g, 64:80], xs_bf[:, g, _tsl],
                                  start=(g == 0), stop=(g == 15))
            return last
        sp.mms(fB)
        sp.copy(brow[:], psB[:, :], engine="scalar")
        sp.dma(B_o[:, tsl], brow[:])
        def fC(eng, _tsl=tsl):
            last = None
            for g in range(16):
                last = eng.matmul(psC[:, :], wx_sb[:, g, 80:96], xs_bf[:, g, _tsl],
                                  start=(g == 0), stop=(g == 15))
            return last
        sp.mms(fC)
        sp.copy(crow[:], psC[:, :], engine="scalar")
        sp.dma(C_o[:, tsl], crow[:])

    for g in range(16):
        sp.dma(xs_g[:], xs_o[g * 128:(g + 1) * 128, :])
        for tc in range(4):
            tsl = slice(tc * 512, tc * 512 + 512)
            def fdl(eng, _g=g, _tsl=tsl):
                return eng.matmul(psA[:], wdt_sb[:, _g * 128:(_g + 1) * 128],
                                  dtlow[:, _tsl], start=True, stop=True)
            sp.mms(fdl)
            sp.ts(t1b[:], psA[:], bdt[:, g:g + 1], ALU.add)
            sp.op("scalar", lambda e: e.activation(out=t2b[:], in_=t1b[:], func=AF.Abs))
            sp.op("scalar", lambda e: e.activation(out=t2b[:], in_=t2b[:], func=AF.Exp, scale=-1.0))
            def fln(eng):
                return eng.activation(out=t2b[:], in_=t2b[:], func=AF.Ln, bias=one_col[:])
            sp.op("scalar", fln)
            sp.ts(t3b[:], t1b[:], 0.0, ALU.max)
            sp.tt(t3b[:], t3b[:], t2b[:], ALU.add)
            sp.dma(delta_o[g * 128:(g + 1) * 128, tsl], t3b[:])
            sp.tt(t3b[:], t3b[:], xs_g[:, tsl], ALU.mult)
            sp.dma(u_o[g * 128:(g + 1) * 128, tsl], t3b[:])
    return sp


def run_l1a(X, inp, Wp):
    sp = build_l1a()
    cw = inp["L1_attn_conv_w"].reshape(2048, 4)
    shared = {
        "winT": np.ascontiguousarray(Wp["L1_attn_w_in_tb"]),
        "wxT": np.ascontiguousarray(Wp["L1_attn_w_x_tb"]),
        "wdtT": np.ascontiguousarray(Wp["L1_attn_w_dt_tb"]),
        "gamL": np.ascontiguousarray(inp["L1_attn_gamma"].reshape(8, 128).T),
        "betL": np.ascontiguousarray(inp["L1_attn_beta"].reshape(8, 128).T),
        "cwL": np.ascontiguousarray(cw.reshape(16, 128, 4).transpose(1, 0, 2)),
        "cbL": np.ascontiguousarray(inp["L1_attn_conv_b"].reshape(16, 128).T),
        "bdtL": np.ascontiguousarray(inp["L1_attn_b_dt"].reshape(16, 128).T),
    }
    outs = []
    in_maps = []
    for c in range(NC):
        b, q = c // 4, c % 4
        xb = X[b * 8192:(b + 1) * 8192]
        t0 = 2048 * q
        xh = np.zeros((2051, 1024), dtype=np.float32)
        lo = max(0, t0 - 3)
        xh[3 - (t0 - lo):3] = xb[lo:t0]
        xh[3:] = xb[t0:t0 + 2048]
        m = dict(shared)
        m["xhT"] = np.ascontiguousarray(xh.T)
        in_maps.append(m)
    res = sp.run(in_maps, core_ids=range(NC))
    return [dict(r) for r in res.results]


def build_scan(nstate, nchunk=4096, sp=None, io=None):
    if sp is None:
        sp = SP()
    io = io or {}
    def din(n, s, d=F32):
        return io[n] if n in io else sp.din(n, s, d)
    def dout(n, s, d=F32):
        return io[n] if n in io else sp.dout(n, s, d)
    dS = None if "load_du" in (io or {}) else din("dS", (256, T))
    uS = None if "load_du" in (io or {}) else din("uS", (256, T))
    Bin = None if "load_bc" in (io or {}) else din("Bin", (nstate, T))
    Cin = None if "load_bc" in (io or {}) else din("Cin", (nstate, T))
    AlogS = din("AlogS", (256, nstate))
    yS = dout("yS", (256, T))

    A_sb = sp.sbuf("asb", [128, 2, nstate], F32)
    carry = sp.sbuf("car", [128, nstate], F32)
    dblk = sp.sbuf("dbk", [128, nchunk], F32)
    ublk = sp.sbuf("ubk", [128, nchunk], F32)
    accb = sp.sbuf("acb", [128, nchunk], F32)
    dA = sp.sbuf("dab", [128, nchunk], F32)
    ut = sp.sbuf("utb", [128, nchunk], F32)
    hb = sp.sbuf("hbb", [128, nchunk], F32)
    Brep = sp.sbuf("brp", [128, nchunk], F32)
    Crep = sp.sbuf("crp", [128, nchunk], F32)

    sp.dma(A_sb[:], AlogS.rearrange("(a p) n -> p a n", p=128))
    sp.op("scalar", lambda e: e.activation(out=A_sb[:].rearrange("p a n -> p (a n)"),
                                           in_=A_sb[:].rearrange("p a n -> p (a n)"),
                                           func=AF.Exp, scale=1.0))
    sp.ts(A_sb[:].rearrange("p a n -> p (a n)"), A_sb[:].rearrange("p a n -> p (a n)"),
          -1.0, ALU.mult)

    NCH = L // nchunk
    for dt in range(2):
        for b in range(B):
            sp.memset(carry[:], 0.0, engine="gpsimd")
            for ch in range(NCH):
                c0 = b * L + ch * nchunk
                csl = slice(c0, c0 + nchunk)
                if "load_du" in io:
                    io["load_du"](sp, dt, b, ch, dblk[:], ublk[:])
                else:
                    sp.dmas([(dblk[:], dS[dt * 128:(dt + 1) * 128, csl]),
                             (ublk[:], uS[dt * 128:(dt + 1) * 128, csl])])
                first = True
                for n in range(nstate):
                    if "load_bc" in io:
                        io["load_bc"](sp, n, b, ch, Brep[:], Crep[:])
                    else:
                        sp.dmas([(Brep[:], Bin[n:n + 1, csl].to_broadcast([128, nchunk])),
                                 (Crep[:], Cin[n:n + 1, csl].to_broadcast([128, nchunk]))])
                    def fdA(eng, _dt=dt, _n=n):
                        return eng.activation(out=dA[:], in_=dblk[:], func=AF.Exp,
                                              scale=A_sb[:, _dt, _n:_n + 1])
                    sp.op("scalar", fdA)
                    sp.tt(ut[:], ublk[:], Brep[:], ALU.mult)
                    sp.scan(hb[:], dA[:], ut[:], carry[:, n:n + 1])
                    sp.copy(carry[:, n:n + 1], hb[:, nchunk - 1:nchunk])
                    if first:
                        sp.tt(accb[:], hb[:], Crep[:], ALU.mult)
                        first = False
                    else:
                        sp.tt(hb[:], hb[:], Crep[:], ALU.mult)
                        sp.tt(accb[:], accb[:], hb[:], ALU.add)
                sp.dma(yS[dt * 128:(dt + 1) * 128, csl], accb[:])
    return sp


def run_scan(parts_list, keyd, keyu, keyB, keyC, Alog, nstate):
    sp = build_scan(nstate)
    dall = np.concatenate([np.asarray(p[keyd]) for p in parts_list], axis=1)
    uall = np.concatenate([np.asarray(p[keyu]) for p in parts_list], axis=1)
    Ball = np.concatenate([np.asarray(p[keyB]) for p in parts_list], axis=1)
    Call = np.concatenate([np.asarray(p[keyC]) for p in parts_list], axis=1)
    in_maps = []
    for c in range(NC):
        m = {
            "dS": np.ascontiguousarray(dall[256 * c:256 * (c + 1)]),
            "uS": np.ascontiguousarray(uall[256 * c:256 * (c + 1)]),
            "Bin": Ball, "Cin": Call,
            "AlogS": np.ascontiguousarray(Alog[256 * c:256 * (c + 1)]),
        }
        in_maps.append(m)
    res = sp.run(in_maps, core_ids=range(NC))
    return np.concatenate([res.results[c]["yS"] for c in range(NC)], axis=0)


def build_l1c(rms=False, sp=None, io=None):
    if sp is None:
        sp = SP()
    io = io or {}
    def din(n, s, d=F32):
        return io[n] if n in io else sp.din(n, s, d)
    def dout(n, s, d=F32):
        return io[n] if n in io else sp.dout(n, s, d)
    xT = din("xT", (1024, TC))
    ysT = None if "ys_load" in (io or {}) else din("ysT", (2048, TC))
    xsT = din("xsT", (2048, TC))
    zT = din("zT", (2048, TC))
    DpL = din("DpL", (128, 16))
    woTb = din("woTb", (2048, 1024), BF16)
    yT = dout("yT", (1024, TC))

    ybf = sp.sbuf("ybf", [128, 16, TC], BF16)
    x_sb = sp.sbuf("xsb", [128, 8, TC], F32)
    g1 = sp.sbuf("g1", [128, TC], F32)
    g2 = sp.sbuf("g2", [128, TC], F32)
    g3 = sp.sbuf("g3", [128, TC], F32)
    dp = sp.sbuf("dp", [128, 16], F32)
    wo = sp.sbuf("wo", [128, 16, 128], BF16)
    psO = sp.psum("pso", [128, 512])
    psR = sp.psum("psr", [1, 512])
    psB0 = sp.psum("pb0", [128, 512])
    psB1 = sp.psum("pb1", [128, 512])
    ones_col = sp.sbuf("onc", [128, 1], F32)
    ones_r128 = sp.sbuf("o128", [1, 128], F32)
    mrow = sp.sbuf("mr", [1, TC], F32)
    srow = sp.sbuf("sr", [1, TC], F32)
    sinv = sp.sbuf("si", [1, TC], F32)
    msr = sp.sbuf("ms", [1, TC], F32)

    sp.memset(ones_col[:], 1.0, engine="gpsimd")
    sp.memset(ones_r128[:], 1.0, engine="gpsimd")
    sp.dma(dp[:], DpL)
    sp.dma(x_sb[:], xT.rearrange("(a p) t -> p a t", p=128))
    for tc in range(4):
        tsl = slice(tc * 512, tc * 512 + 512)
        if not rms:
            def fmean(eng, _tsl=tsl):
                last = None
                for a in range(8):
                    last = eng.matmul(psR[0:1, :], ones_col[:], x_sb[:, a, _tsl],
                                      start=(a == 0), stop=(a == 7))
                return last
            sp.mms(fmean)
            sp.act(mrow[0:1, tsl], psR[0:1, :], AF.Copy, scale=1.0 / 1024)
        for a in range(8):
            sp.tt(g1[:, :512], x_sb[:, a, tsl], x_sb[:, a, tsl], ALU.mult)
            def fsq(eng, _a=a):
                return eng.matmul(psB1[0:1, :], ones_col[:], g1[:, :512],
                                  start=(_a == 0), stop=(_a == 7))
            sp.mms(fsq)
        sp.act(srow[0:1, tsl], psB1[0:1, :], AF.Copy, scale=1.0 / 1024)
    if not rms:
        sp.tt(msr[:], mrow[:], mrow[:], ALU.mult)
        sp.tt(srow[:], srow[:], msr[:], ALU.subtract)
        sp.ts(srow[:], srow[:], 1e-5, ALU.add)
    else:
        sp.ts(srow[:], srow[:], 1e-6, ALU.add)
    sp.act(srow[:], srow[:], AF.Sqrt)
    sp.recip(sinv[:], srow[:])
    if not rms:
        sp.tt(msr[:], mrow[:], sinv[:], ALU.mult)
    for tc in range(4):
        tsl = slice(tc * 512, tc * 512 + 512)
        def fb(eng, _tsl=tsl):
            eng.matmul(psB0[:], ones_r128[:], sinv[0:1, _tsl], start=True, stop=True)
            if not rms:
                return eng.matmul(psB1[:], ones_r128[:], msr[0:1, _tsl], start=True, stop=True)
            return None
        if rms:
            def fb(eng, _tsl=tsl):
                return eng.matmul(psB0[:], ones_r128[:], sinv[0:1, _tsl], start=True, stop=True)
        sp.mms(fb)
        for a in range(8):
            sp.tt(g1[:, :512], x_sb[:, a, tsl], psB0[:], ALU.mult)
            if not rms:
                sp.tt(g1[:, :512], g1[:, :512], psB1[:], ALU.subtract)
            sp.tt(x_sb[:, a, tsl], x_sb[:, a, tsl], g1[:, :512], ALU.add)
    for g in range(16):
        if "ys_load" in io:
            io["ys_load"](sp, g, g1[:])
            sp.dmas([(g2[:], xsT[g * 128:(g + 1) * 128, :]),
                     (g3[:], zT[g * 128:(g + 1) * 128, :])])
        else:
            sp.dmas([(g1[:], ysT[g * 128:(g + 1) * 128, :]),
                     (g2[:], xsT[g * 128:(g + 1) * 128, :]),
                     (g3[:], zT[g * 128:(g + 1) * 128, :])])
        def fd(eng, _g=g):
            return eng.scalar_tensor_tensor(out=g1[:], in0=g2[:], scalar=dp[:, _g:_g + 1],
                                            in1=g1[:], op0=ALU.mult, op1=ALU.add)
        sp.op("vector", fd)
        sp.op("scalar", lambda e: e.activation(out=g3[:], in_=g3[:], func=AF.Silu))
        sp.tt(ybf[:, g, :], g1[:], g3[:], ALU.mult)
    for fo in range(8):
        sp.dma(wo[:], woTb[:, fo * 128:(fo + 1) * 128].rearrange("(g p) f -> p g f", p=128))
        for tc in range(4):
            tsl = slice(tc * 512, tc * 512 + 512)
            def fw(eng, _tsl=tsl):
                last = None
                for g in range(16):
                    last = eng.matmul(psO[:], wo[:, g, :], ybf[:, g, _tsl],
                                      start=(g == 0), stop=(g == 15))
                return last
            sp.mms(fw)
            sp.tt(x_sb[:, fo, tsl], x_sb[:, fo, tsl], psO[:], ALU.add)
        sp.dma(yT[fo * 128:(fo + 1) * 128, :], x_sb[:, fo, :])
    return sp


def run_l1bc(X, inp, Wp, parts_list):
    y = run_scan(parts_list, "delta_o", "u_o", "B_o", "C_o", inp["L1_attn_A_log"], 16)
    sp = build_l1c()
    shared = {
        "DpL": np.ascontiguousarray(inp["L1_attn_Dp"].reshape(16, 128).T),
        "woTb": np.ascontiguousarray(Wp["L1_attn_w_out_tb"]),
    }
    in_maps = []
    for c in range(NC):
        m = dict(shared)
        m["xT"] = np.ascontiguousarray(X[c * TC:(c + 1) * TC].T)
        m["ysT"] = np.ascontiguousarray(y[:, c * TC:(c + 1) * TC])
        m["xsT"] = np.asarray(parts_list[c]["xs_o"])
        m["zT"] = np.asarray(parts_list[c]["z_o"])
        in_maps.append(m)
    res = sp.run(in_maps, core_ids=range(NC))
    out = np.empty_like(X)
    for c in range(NC):
        out[c * TC:(c + 1) * TC] = res.results[c]["yT"].T
    return out


def run_l1(X, inp, Wp):
    parts_list = run_l1a(X, inp, Wp)
    return run_l1bc(X, inp, Wp, parts_list)


def build_l2a(sp=None, io=None):
    NT = 2051
    if sp is None:
        sp = SP()
    io = io or {}
    def din(n, s, d=F32):
        return io[n] if n in io else sp.din(n, s, d)
    def dout(n, s, d=F32):
        return io[n] if n in io else sp.dout(n, s, d)
    xhT = None if "xh_load" in (io or {}) else din("xhT", (1024, NT))
    winT = din("winT", (1024, 4096), BF16)
    bwT = din("bwT", (2048, 64), BF16)
    cwT = din("cwT", (2048, 64), BF16)
    dtwT = din("dtwT", (2048, 2048), BF16)
    nwL = din("nwL", (128, 8))
    cwL = din("cwL", (128, 16, 4))
    cbL = din("cbL", (128, 16))
    bdtL = din("bdtL", (128, 16))
    dt_o = dout("dt_o", (2048, TC))
    u_o = dout("u_o", (2048, TC))
    xc_o = dout("xc_o", (2048, TC))
    z_o = dout("z_o", (2048, TC))
    B_o = dout("B_o", (64, TC))
    C_o = dout("C_o", (64, TC))

    ar = Arena(sp, 42048)
    x_sb = ar.f32(0, (8, NT))
    xn_bf = ar.bf16(16416, (8, NT))
    rows_off = 24624
    xs_bf = ar.bf16(24624, (16, TC))

    ones_col = sp.sbuf("onc", [128, 1], F32)
    one_col = sp.sbuf("one", [128, 1], F32)
    ones_r128 = sp.sbuf("o128", [1, 128], F32)
    nw = sp.sbuf("nw", [128, 8], F32)
    cw = sp.sbuf("cw", [128, 16, 4], F32)
    cb = sp.sbuf("cb", [128, 16], F32)
    bdt = sp.sbuf("bdt", [128, 16], F32)
    w_sb = sp.sbuf("wsb", [128, 8, 128], BF16)
    bwt_sb = sp.sbuf("bws", [128, 16, 64], BF16)
    cwt_sb = sp.sbuf("cws", [128, 16, 64], BF16)
    dtw_g = sp.sbuf("dtg", [128, 16, 128], BF16)
    xp_g = sp.sbuf("xpg", [128, NT], BF16)
    accv = sp.sbuf("acv", [128, TC], F32)
    xs_g = sp.sbuf("xsg", [128, TC], F32)
    brow = sp.sbuf("brw", [64, 512], F32)
    t1b = sp.sbuf("t1b", [128, 512], F32)
    t2b = sp.sbuf("t2b", [128, 512], F32)
    t3b = sp.sbuf("t3b", [128, 512], F32)

    psA = sp.psum("psa", [128, 512])
    psB = sp.psum("psb", [64, 512])
    psC = sp.psum("psc", [64, 512])
    psB0 = sp.psum("pb0", [128, 512])
    psB1 = sp.psum("pb1", [128, 512])

    sp.memset(ones_col[:], 1.0, engine="gpsimd")
    sp.memset(one_col[:], 1.0, engine="gpsimd")
    sp.memset(ones_r128[:], 1.0, engine="gpsimd")
    if "xh_load" in io:
        io["xh_load"](sp, x_sb)
    else:
        sp.dmas([(x_sb, xhT.rearrange("(a p) t -> p a t", p=128))])
    sp.dmas([
             (nw[:], nwL), (cw[:], cwL), (cb[:], cbL), (bdt[:], bdtL),
             (bwt_sb[:], bwT.rearrange("(g p) n -> p g n", p=128)),
             (cwt_sb[:], cwT.rearrange("(g p) n -> p g n", p=128))])

    mrow = ar.row(rows_off, NT)
    srow = ar.row(rows_off + NT, NT)
    sinv = ar.row(rows_off + 2 * NT, NT)
    msr = ar.row(rows_off + 3 * NT, NT)
    tch = chunks(NT, 512)

    def ln_stats(with_mean=True, eps=1e-5):
        for (t0, tn) in tch:
            tsl = slice(t0, t0 + tn)
            if with_mean:
                def fmean(eng, _tsl=tsl, _tn=tn):
                    last = None
                    for a in range(8):
                        last = eng.matmul(psB[0:1, :_tn], ones_col[:], x_sb[:, a, _tsl],
                                          start=(a == 0), stop=(a == 7))
                    return last
                sp.mms(fmean)
                sp.act(mrow[:, tsl], psB[0:1, :tn], AF.Copy, scale=1.0 / 1024)
            for a in range(8):
                sp.tt(t1b[:, :tn], x_sb[:, a, tsl], x_sb[:, a, tsl], ALU.mult)
                def fsq(eng, _a=a, _tn=tn):
                    return eng.matmul(psC[0:1, :_tn], ones_col[:], t1b[:, :_tn],
                                      start=(_a == 0), stop=(_a == 7))
                sp.mms(fsq)
            sp.act(srow[:, tsl], psC[0:1, :tn], AF.Copy, scale=1.0 / 1024)
        if with_mean:
            sp.tt(msr[:], mrow[:], mrow[:], ALU.mult)
            sp.tt(srow[:], srow[:], msr[:], ALU.subtract)
        sp.ts(srow[:], srow[:], eps, ALU.add)
        sp.act(srow[:], srow[:], AF.Sqrt)
        sp.recip(sinv[:], srow[:])
        if with_mean:
            sp.tt(msr[:], mrow[:], sinv[:], ALU.mult)

    ln_stats(True, 1e-5)
    for (t0, tn) in tch:
        tsl = slice(t0, t0 + tn)
        def fb(eng, _tsl=tsl, _tn=tn):
            eng.matmul(psB0[:, :_tn], ones_r128[:], sinv[:, _tsl], start=True, stop=True)
            return eng.matmul(psB1[:, :_tn], ones_r128[:], msr[:, _tsl], start=True, stop=True)
        sp.mms(fb)
        for a in range(8):
            sp.tt(t1b[:, :tn], x_sb[:, a, tsl], psB0[:, :tn], ALU.mult)
            sp.tt(x_sb[:, a, tsl], t1b[:, :tn], psB1[:, :tn], ALU.subtract)
    ln_stats(False, 1e-6)
    for (t0, tn) in tch:
        tsl = slice(t0, t0 + tn)
        def fb2(eng, _tsl=tsl, _tn=tn):
            return eng.matmul(psB0[:, :_tn], ones_r128[:], sinv[:, _tsl], start=True, stop=True)
        sp.mms(fb2)
        for a in range(8):
            def fts(eng, _a=a, _tsl=tsl, _tn=tn):
                return eng.scalar_tensor_tensor(out=xn_bf[:, _a, _tsl], in0=x_sb[:, _a, _tsl],
                                                scalar=nw[:, _a:_a + 1], in1=psB0[:, :_tn],
                                                op0=ALU.mult, op1=ALU.mult)
            sp.op("vector", fts)

    for g in range(16):
        sp.dma(w_sb[:], winT[:, g * 128:(g + 1) * 128].rearrange("(a p) f -> p a f", p=128))
        for (t0, tn) in tch:
            tsl = slice(t0, t0 + tn)
            def fxp(eng, _tsl=tsl, _tn=tn):
                last = None
                for a in range(8):
                    last = eng.matmul(psA[:, :_tn], w_sb[:, a, :], xn_bf[:, a, _tsl],
                                      start=(a == 0), stop=(a == 7))
                return last
            sp.mms(fxp)
            sp.copy(xp_g[:, tsl], psA[:, :tn], engine="scalar")
        def fc0(eng, _g=g):
            return eng.tensor_scalar(out=accv[:], in0=xp_g[:, 0:TC], scalar1=cw[:, _g, 0:1],
                                     scalar2=None, op0=ALU.mult)
        sp.op("vector", fc0)
        for k in range(1, 4):
            def fck(eng, _g=g, _k=k):
                return eng.scalar_tensor_tensor(out=accv[:], in0=xp_g[:, _k:TC + _k],
                                                scalar=cw[:, _g, _k:_k + 1], in1=accv[:],
                                                op0=ALU.mult, op1=ALU.add)
            sp.op("vector", fck)
        def fsilu(eng, _g=g):
            return eng.activation(out=xs_g[:], in_=accv[:], func=AF.Silu, bias=cb[:, _g:_g + 1])
        sp.op("scalar", fsilu)
        sp.dma(xc_o[g * 128:(g + 1) * 128, :], xs_g[:])
        sp.copy(xs_bf[:, g, :], xs_g[:])
    for g in range(16):
        sp.dma(w_sb[:], winT[:, 2048 + g * 128:2048 + (g + 1) * 128].rearrange("(a p) f -> p a f", p=128))
        for tc in range(4):
            tsl = slice(3 + tc * 512, 3 + tc * 512 + 512)
            def fz(eng, _tsl=tsl):
                last = None
                for a in range(8):
                    last = eng.matmul(psA[:], w_sb[:, a, :], xn_bf[:, a, _tsl],
                                      start=(a == 0), stop=(a == 7))
                return last
            sp.mms(fz)
            sp.copy(xs_g[:, tc * 512:(tc + 1) * 512], psA[:], engine="scalar")
        sp.dma(z_o[g * 128:(g + 1) * 128, :], xs_g[:])

    for tc in range(4):
        tsl = slice(tc * 512, tc * 512 + 512)
        def fB(eng, _tsl=tsl):
            last = None
            for g in range(16):
                last = eng.matmul(psB[:, :], bwt_sb[:, g, :], xs_bf[:, g, _tsl],
                                  start=(g == 0), stop=(g == 15))
            return last
        sp.mms(fB)
        sp.copy(brow[:], psB[:, :], engine="scalar")
        sp.dma(B_o[:, tsl], brow[:])
        def fC(eng, _tsl=tsl):
            last = None
            for g in range(16):
                last = eng.matmul(psC[:, :], cwt_sb[:, g, :], xs_bf[:, g, _tsl],
                                  start=(g == 0), stop=(g == 15))
            return last
        sp.mms(fC)
        sp.copy(brow[:], psC[:, :], engine="scalar")
        sp.dma(C_o[:, tsl], brow[:])

    for g in range(16):
        sp.dma(dtw_g[:], dtwT[:, g * 128:(g + 1) * 128].rearrange("(f p) c -> p f c", p=128))
        sp.dma(xs_g[:], xc_o[g * 128:(g + 1) * 128, :])
        for tc in range(4):
            tsl = slice(tc * 512, tc * 512 + 512)
            def fdl(eng, _tsl=tsl):
                last = None
                for f in range(16):
                    last = eng.matmul(psA[:], dtw_g[:, f, :], xs_bf[:, f, _tsl],
                                      start=(f == 0), stop=(f == 15))
                return last
            sp.mms(fdl)
            sp.ts(t1b[:], psA[:], bdt[:, g:g + 1], ALU.add)
            sp.op("scalar", lambda e: e.activation(out=t2b[:], in_=t1b[:], func=AF.Abs))
            sp.op("scalar", lambda e: e.activation(out=t2b[:], in_=t2b[:], func=AF.Exp, scale=-1.0))
            def fln(eng):
                return eng.activation(out=t2b[:], in_=t2b[:], func=AF.Ln, bias=one_col[:])
            sp.op("scalar", fln)
            sp.ts(t3b[:], t1b[:], 0.0, ALU.max)
            sp.tt(t3b[:], t3b[:], t2b[:], ALU.add)
            sp.ts(t3b[:], t3b[:], 1e-4, ALU.add)
            sp.dma(dt_o[g * 128:(g + 1) * 128, tsl], t3b[:])
            sp.tt(t3b[:], t3b[:], xs_g[:, tsl], ALU.mult)
            sp.dma(u_o[g * 128:(g + 1) * 128, tsl], t3b[:])
    return sp


def run_l2(X, inp, Wp):
    sp = build_l2a()
    cw = inp["L2_attn_conv_w"].reshape(2048, 4)
    shared = {
        "winT": np.ascontiguousarray(Wp["L2_attn_in_proj_w_tb"]),
        "bwT": np.ascontiguousarray(Wp["L2_attn_B_w_tb"]),
        "cwT": np.ascontiguousarray(Wp["L2_attn_C_w_tb"]),
        "dtwT": np.ascontiguousarray(Wp["L2_attn_dt_w_tb"]),
        "nwL": np.ascontiguousarray(inp["L2_attn_norm_w"].reshape(8, 128).T),
        "cwL": np.ascontiguousarray(cw.reshape(16, 128, 4).transpose(1, 0, 2)),
        "cbL": np.ascontiguousarray(inp["L2_attn_conv_b"].reshape(16, 128).T),
        "bdtL": np.ascontiguousarray(inp["L2_attn_dt_b"].reshape(16, 128).T),
    }
    in_maps = []
    for c in range(NC):
        b, q = c // 4, c % 4
        xb = X[b * 8192:(b + 1) * 8192]
        t0 = 2048 * q
        xh = np.zeros((2051, 1024), dtype=np.float32)
        lo = max(0, t0 - 3)
        xh[3 - (t0 - lo):3] = xb[lo:t0]
        xh[3:] = xb[t0:t0 + 2048]
        m = dict(shared)
        m["xhT"] = np.ascontiguousarray(xh.T)
        in_maps.append(m)
    res = sp.run(in_maps, core_ids=range(NC))
    parts_list = [dict(r) for r in res.results]
    y = run_scan(parts_list, "dt_o", "u_o", "B_o", "C_o", inp["L2_attn_A_log"], 64)
    spc = build_l1c()
    shared_c = {
        "DpL": np.ascontiguousarray(inp["L2_attn_D_param"].reshape(16, 128).T),
        "woTb": np.ascontiguousarray(Wp["L2_attn_out_proj_w_tb"]),
    }
    in_maps = []
    for c in range(NC):
        m = dict(shared_c)
        m["xT"] = np.ascontiguousarray(X[c * TC:(c + 1) * TC].T)
        m["ysT"] = np.ascontiguousarray(y[:, c * TC:(c + 1) * TC])
        m["xsT"] = np.asarray(parts_list[c]["xc_o"])
        m["zT"] = np.asarray(parts_list[c]["z_o"])
        in_maps.append(m)
    res = spc.run(in_maps, core_ids=range(NC))
    out = np.empty_like(X)
    for c in range(NC):
        out[c * TC:(c + 1) * TC] = res.results[c]["yT"].T
    return out


def build_mlp2(sp=None, io=None):
    if sp is None:
        sp = SP()
    io = io or {}
    def din(n, s, d=F32):
        return io[n] if n in io else sp.din(n, s, d)
    def dout(n, s, d=F32):
        return io[n] if n in io else sp.dout(n, s, d)
    xT = din("xT", (1024, TC))
    Wub = din("Wub", (1024, 4096), BF16)
    Wdb = din("Wdb", (4096, 1024), BF16)
    bupL = None if "bup_row" in (io or {}) else din("bupL", (128, 32))
    bdnL = None if "bup_row" in (io or {}) else din("bdnL", (128, 8))
    yT = dout("yT", (1024, TC))

    ar = Arena(sp, 40960)
    x_sb = ar.f32(0, (8, TC))
    xn_bf = ar.bf16(16384, (8, TC))
    rows_off = 24576
    h_bf = ar.bf16(0, (16, TC))
    acc = ar.f32(24576, (8, TC))

    ones_col = sp.sbuf("onc", [128, 1], F32)
    ones_r128 = sp.sbuf("o128", [1, 128], F32)
    bup = sp.sbuf("bup", [128, 32], F32)
    bdn = sp.sbuf("bdn", [128, 8], F32)
    brow_t = sp.sbuf("brt", [1, 4096], F32)
    ident1 = make_ident(sp, "id1", 1, F32)
    psT1 = sp.psum("pT1", [128, 1])
    w_sb = sp.sbuf("wsb", [128, 8, 128], BF16)
    wdn = sp.sbuf("wdn", [128, 16, 128], BF16)
    sq = sp.sbuf("sqq", [128, 512], F32)
    tmp_f = sp.sbuf("tmf", [128, 512], F32)
    xload = sp.sbuf("xld", [128, TC], F32)
    psA = sp.psum_at(0, "psa", (128, TC))
    psO = sp.psum_at(4, "pso", (128, TC))
    psB0 = sp.psum("pb0", [128, 512])
    psB1 = sp.psum("pb1", [128, 512])
    psR = sp.psum("psr", [1, 512])

    sp.memset(ones_col[:], 1.0, engine="gpsimd")
    sp.memset(ones_r128[:], 1.0, engine="gpsimd")
    if "bup_row" in io:
        sp.dmas([(x_sb, xT.rearrange("(a p) t -> p a t", p=128)),
                 (brow_t[:], io["bup_row"])])
        for j in range(32):
            sp.transpose(psT1[:, :], brow_t[0:1, j * 128:(j + 1) * 128], ident1[:])
            sp.copy(bup[:, j:j + 1], psT1[:, :], engine="scalar")
        sp.dma(brow_t[0:1, 0:1024], io["bdn_row"])
        for j in range(8):
            sp.transpose(psT1[:, :], brow_t[0:1, j * 128:(j + 1) * 128], ident1[:])
            sp.copy(bdn[:, j:j + 1], psT1[:, :], engine="scalar")
    else:
        sp.dmas([(x_sb, xT.rearrange("(a p) t -> p a t", p=128)),
                 (bup[:], bupL), (bdn[:], bdnL)])
    sp.ts(bdn[:], bdn[:], 0.5, ALU.mult)

    mrow = ar.row(rows_off, TC)
    srow = ar.row(rows_off + TC, TC)
    sinv = ar.row(rows_off + 2 * TC, TC)
    msr = ar.row(rows_off + 3 * TC, TC)
    for tc in range(4):
        tsl = slice(tc * 512, tc * 512 + 512)
        def fmean(eng, _tsl=tsl):
            last = None
            for a in range(8):
                last = eng.matmul(psR[0:1, :], ones_col[:], x_sb[:, a, _tsl],
                                  start=(a == 0), stop=(a == 7))
            return last
        sp.mms(fmean)
        sp.act(mrow[:, tsl], psR[0:1, :], AF.Copy, scale=1.0 / 1024)
        for a in range(8):
            sp.tt(sq[:], x_sb[:, a, tsl], x_sb[:, a, tsl], ALU.mult)
            def fsq(eng, _a=a):
                return eng.matmul(psR[0:1, :], ones_col[:], sq[:],
                                  start=(_a == 0), stop=(_a == 7))
            sp.mms(fsq)
        sp.act(srow[:, tsl], psR[0:1, :], AF.Copy, scale=1.0 / 1024)
    sp.tt(msr[:], mrow[:], mrow[:], ALU.mult)
    sp.tt(srow[:], srow[:], msr[:], ALU.subtract)
    sp.ts(srow[:], srow[:], 1e-5, ALU.add)
    sp.act(srow[:], srow[:], AF.Sqrt)
    sp.recip(sinv[:], srow[:])
    sp.tt(msr[:], mrow[:], sinv[:], ALU.mult)
    for tc in range(4):
        tsl = slice(tc * 512, tc * 512 + 512)
        def fb(eng, _tsl=tsl):
            eng.matmul(psB0[:], ones_r128[:], sinv[:, _tsl], start=True, stop=True)
            return eng.matmul(psB1[:], ones_r128[:], msr[:, _tsl], start=True, stop=True)
        sp.mms(fb)
        for a in range(8):
            sp.tt(tmp_f[:], x_sb[:, a, tsl], psB0[:], ALU.mult)
            sp.tt(xn_bf[:, a, tsl], tmp_f[:], psB1[:], ALU.subtract)

    for half in range(2):
        for hf in range(16):
            j = half * 16 + hf
            sp.dma(w_sb[:], Wub[:, j * 128:(j + 1) * 128].rearrange("(a p) f -> p a f", p=128))
            def fa(eng):
                last = None
                for tc in range(4):
                    tsl = slice(tc * 512, tc * 512 + 512)
                    for a in range(8):
                        last = eng.matmul(psA[:, tsl], w_sb[:, a, :], xn_bf[:, a, tsl],
                                          start=(a == 0), stop=(a == 7))
                return last
            sp.mms(fa)
            def fgel(eng, _hf=hf, _j=j):
                return eng.activation(out=h_bf[:, _hf, :], in_=psA[:, :], func=AF.Gelu,
                                      bias=bup[:, _j:_j + 1])
            sp.op("scalar", fgel)
        for fo in range(8):
            sp.dma(wdn[:], Wdb[half * 2048:(half + 1) * 2048, fo * 128:(fo + 1) * 128]
                   .rearrange("(g p) f -> p g f", p=128))
            def fd(eng):
                last = None
                for tc in range(4):
                    tsl = slice(tc * 512, tc * 512 + 512)
                    for g in range(16):
                        last = eng.matmul(psO[:, tsl], wdn[:, g, :], h_bf[:, g, tsl],
                                          start=(g == 0), stop=(g == 15))
                return last
            sp.mms(fd)
            if half == 0:
                sp.ts(acc[:, fo, :], psO[:, :], 0.5, ALU.mult)
            else:
                def facc(eng, _fo=fo):
                    return eng.scalar_tensor_tensor(out=acc[:, _fo, :], in0=psO[:, :],
                                                    scalar=0.5, in1=acc[:, _fo, :],
                                                    op0=ALU.mult, op1=ALU.add)
                sp.op("vector", facc)
    for fo in range(8):
        sp.dma(xload[:], xT[fo * 128:(fo + 1) * 128, :])
        def ffin(eng, _fo=fo):
            return eng.scalar_tensor_tensor(out=xload[:], in0=acc[:, _fo, :],
                                            scalar=bdn[:, _fo:_fo + 1], in1=xload[:],
                                            op0=ALU.add, op1=ALU.add)
        sp.op("vector", ffin)
        sp.dma(yT[fo * 128:(fo + 1) * 128, :], xload[:])
    return sp


def run_mlp2(X, inp, Wp):
    sp = build_mlp2()
    shared = {
        "Wub": np.ascontiguousarray(Wp["W_exp_fb"]),
        "Wdb": np.ascontiguousarray(Wp["W_con_fb"]),
        "bupL": np.ascontiguousarray(np.asarray(Wp["bexp_f"]).reshape(32, 128).T),
        "bdnL": np.ascontiguousarray(np.asarray(Wp["bcon_f"]).reshape(8, 128).T),
    }
    in_maps = []
    for c in range(NC):
        m = dict(shared)
        m["xT"] = np.ascontiguousarray(X[c * TC:(c + 1) * TC].T)
        in_maps.append(m)
    res = sp.run(in_maps, core_ids=range(NC))
    out = np.empty_like(X)
    for c in range(NC):
        out[c * TC:(c + 1) * TC] = res.results[c]["yT"].T
    return out


def build_mlp3(sp=None, io=None):
    if sp is None:
        sp = SP()
    io = io or {}
    def din(n, s, d=F32):
        return io[n] if n in io else sp.din(n, s, d)
    def dout(n, s, d=F32):
        return io[n] if n in io else sp.dout(n, s, d)
    xT = din("xT", (1024, TC))
    Wub = din("Wub", (1024, 4096), BF16)
    Wdb = din("Wdb", (4096, 1024), BF16)
    yT = dout("yT", (1024, TC))

    ar = Arena(sp, 40960)
    x_sb = ar.f32(0, (8, TC))
    xn_bf = ar.bf16(16384, (8, TC))
    rows_off = 24576
    h_lo = ar.bf16(0, (16, TC))
    h_hi = ar.bf16(24576, (16, TC))

    def hv(j):
        return (h_lo if j < 16 else h_hi)[:, j % 16, :]

    ones_col = sp.sbuf("onc", [128, 1], F32)
    ones_r128 = sp.sbuf("o128", [1, 128], F32)
    w_sb = sp.sbuf("wsb", [128, 8, 128], BF16)
    wdn = sp.sbuf("wdn", [128, 32, 128], BF16)
    sq = sp.sbuf("sqq", [128, 512], F32)
    tmp_f = sp.sbuf("tmf", [128, 512], F32)
    tmp_fw = sp.sbuf("tfw", [128, TC], F32)
    tmp2w = sp.sbuf("t2w", [128, TC], F32)
    sqw = tmp_fw
    s1rep = sp.sbuf("s1r", [128, TC], F32)
    xload = sp.sbuf("xld", [128, TC], F32)
    psA = sp.psum_at(0, "psa", (128, TC))
    psO = sp.psum_at(0, "pso", (128, TC))
    psB0 = sp.psum("pb0", [128, 512])
    psB1 = sp.psum("pb1", [128, 512])
    psR2 = sp.psum_at(4, "ps2", (1, TC))
    psR = psR2[0:1, 0:512]

    sp.memset(ones_col[:], 1.0, engine="gpsimd")
    sp.memset(ones_r128[:], 1.0, engine="gpsimd")
    sp.dma(x_sb, xT.rearrange("(a p) t -> p a t", p=128))

    mrow = ar.row(rows_off, TC)
    srow = ar.row(rows_off + TC, TC)
    sinv = ar.row(rows_off + 2 * TC, TC)
    msr = ar.row(rows_off + 3 * TC, TC)
    for tc in range(4):
        tsl = slice(tc * 512, tc * 512 + 512)
        def fmean(eng, _tsl=tsl):
            last = None
            for a in range(8):
                last = eng.matmul(psR[0:1, :], ones_col[:], x_sb[:, a, _tsl],
                                  start=(a == 0), stop=(a == 7))
            return last
        sp.mms(fmean)
        sp.act(mrow[:, tsl], psR[0:1, :], AF.Copy, scale=1.0 / 1024)
        for a in range(8):
            sp.tt(sq[:], x_sb[:, a, tsl], x_sb[:, a, tsl], ALU.mult)
            def fsq(eng, _a=a):
                return eng.matmul(psR[0:1, :], ones_col[:], sq[:],
                                  start=(_a == 0), stop=(_a == 7))
            sp.mms(fsq)
        sp.act(srow[:, tsl], psR[0:1, :], AF.Copy, scale=1.0 / 1024)
    sp.tt(msr[:], mrow[:], mrow[:], ALU.mult)
    sp.tt(srow[:], srow[:], msr[:], ALU.subtract)
    sp.ts(srow[:], srow[:], 1e-5, ALU.add)
    sp.act(srow[:], srow[:], AF.Sqrt)
    sp.recip(sinv[:], srow[:])
    sp.tt(msr[:], mrow[:], sinv[:], ALU.mult)
    for tc in range(4):
        tsl = slice(tc * 512, tc * 512 + 512)
        def fb(eng, _tsl=tsl):
            eng.matmul(psB0[:], ones_r128[:], sinv[:, _tsl], start=True, stop=True)
            return eng.matmul(psB1[:], ones_r128[:], msr[:, _tsl], start=True, stop=True)
        sp.mms(fb)
        for a in range(8):
            sp.tt(tmp_f[:], x_sb[:, a, tsl], psB0[:], ALU.mult)
            sp.tt(x_sb[:, a, tsl], tmp_f[:], psB1[:], ALU.subtract)
    for tc in range(4):
        tsl = slice(tc * 512, tc * 512 + 512)
        for a in range(8):
            sp.tt(sq[:], x_sb[:, a, tsl], x_sb[:, a, tsl], ALU.mult)
            def fsq(eng, _a=a):
                return eng.matmul(psR[0:1, :], ones_col[:], sq[:],
                                  start=(_a == 0), stop=(_a == 7))
            sp.mms(fsq)
        sp.act(srow[:, tsl], psR[0:1, :], AF.Copy, scale=1.0 / 1024)
    sp.act(srow[:], srow[:], AF.Sqrt)
    sp.ts(srow[:], srow[:], 1e-8, ALU.add)
    sp.recip(sinv[:], srow[:])
    for tc in range(4):
        tsl = slice(tc * 512, tc * 512 + 512)
        def fb1(eng, _tsl=tsl):
            return eng.matmul(psB0[:], ones_r128[:], sinv[:, _tsl], start=True, stop=True)
        sp.mms(fb1)
        sp.copy(s1rep[:, tsl], psB0[:])
        for a in range(8):
            sp.copy(xn_bf[:, a, tsl], x_sb[:, a, tsl])

    for j in range(32):
        sp.dma(w_sb[:], Wub[:, j * 128:(j + 1) * 128].rearrange("(a p) f -> p a f", p=128))
        def fa(eng):
            last = None
            for tc in range(4):
                tsl = slice(tc * 512, tc * 512 + 512)
                for a in range(8):
                    last = eng.matmul(psA[:, tsl], w_sb[:, a, :], xn_bf[:, a, tsl],
                                      start=(a == 0), stop=(a == 7))
            return last
        sp.mms(fa)
        sp.tt(tmp_fw[:], psA[:, :], s1rep[:, :], ALU.mult)
        sp.op("scalar", lambda e: e.activation(out=tmp2w[:], in_=tmp_fw[:], func=AF.Gelu))
        sp.copy(hv(j), tmp2w[:])
        sp.tt(sqw[:], tmp2w[:], tmp2w[:], ALU.mult)
        def fh2(eng, _j=j):
            last = None
            for tc in range(4):
                last = eng.matmul(psR2[0:1, tc * 512:(tc + 1) * 512],
                                  ones_col[:], sqw[:, tc * 512:(tc + 1) * 512],
                                  start=(_j == 0), stop=(_j == 31))
            return last
        sp.mms(fh2)
    srow2 = ar.row(16384, TC)
    sinv2 = ar.row(16384 + TC, TC)
    s2rep = ar.f32(16384 + 2 * TC, (TC,))
    sp.act(srow2[:], psR2[0:1, :], AF.Copy, scale=1.0 / 4096)
    sp.act(srow2[:], srow2[:], AF.Sqrt)
    sp.ts(srow2[:], srow2[:], 1e-8, ALU.add)
    sp.recip(sinv2[:], srow2[:])
    for tc in range(4):
        tsl = slice(tc * 512, tc * 512 + 512)
        def fb2(eng, _tsl=tsl):
            return eng.matmul(psB0[:], ones_r128[:], sinv2[:, _tsl], start=True, stop=True)
        sp.mms(fb2)
        sp.copy(s2rep[:, tsl], psB0[:])
    for fo in range(8):
        sp.dma(wdn[:], Wdb[:, fo * 128:(fo + 1) * 128].rearrange("(g p) f -> p g f", p=128))
        sp.dma(xload[:], xT[fo * 128:(fo + 1) * 128, :])
        def fd(eng):
            last = None
            for tc in range(4):
                tsl = slice(tc * 512, tc * 512 + 512)
                for g in range(32):
                    last = eng.matmul(psO[:, tsl], wdn[:, g, :], hv(g)[:, tsl],
                                      start=(g == 0), stop=(g == 31))
            return last
        sp.mms(fd)
        sp.tt(tmp_fw[:], psO[:, :], s2rep[:, :], ALU.mult)
        def ffin(eng):
            return eng.scalar_tensor_tensor(out=xload[:], in0=tmp_fw[:],
                                            scalar=0.5, in1=xload[:],
                                            op0=ALU.mult, op1=ALU.add)
        sp.op("vector", ffin)
        sp.dma(yT[fo * 128:(fo + 1) * 128, :], xload[:])
    return sp


def run_mlp3(X, inp, Wp):
    sp = build_mlp3()
    shared = {
        "Wub": np.ascontiguousarray(Wp["L3_ffn_w_up_qtb"]),
        "Wdb": np.ascontiguousarray(Wp["L3_ffn_w_down_qtb"]),
    }
    in_maps = []
    for c in range(NC):
        m = dict(shared)
        m["xT"] = np.ascontiguousarray(X[c * TC:(c + 1) * TC].T)
        in_maps.append(m)
    res = sp.run(in_maps, core_ids=range(NC))
    out = np.empty_like(X)
    for c in range(NC):
        out[c * TC:(c + 1) * TC] = res.results[c]["yT"].T
    return out


def build_attn3(sp=None, io=None):
    if sp is None:
        sp = SP()
    io = io or {}
    def din(n, s, d=F32):
        return io[n] if n in io else sp.din(n, s, d)
    def dout(n, s, d=F32):
        return io[n] if n in io else sp.dout(n, s, d)
    xfT = None if "xf_chunk" in (io or {}) else din("xfT", (1024, L))
    xqT = din("xqT", (1024, TC))
    xrT = din("xrT", (1024, TC))
    Wqb = din("Wqb", (1024, 1024), BF16)
    Wkb = din("Wkb", (1024, 512), BF16)
    Wvb = din("Wvb", (1024, 512), BF16)
    memb = din("memb", (512, 512), BF16)
    mnL = din("mnL", (128, 4))
    mwC = din("mwC", (1, 1))
    yT = dout("yT", (1024, TC))

    KH = 4096
    ar = Arena(sp, 40960)
    k_sb = ar.bf16(0, (4, KH))
    vt = ar.bf16(8192, (8, 32, 65))
    q_sb = ar.bf16(16512, (8, TC))
    xch = ar.f32(24704, (8, 512))
    xnch = ar.bf16(28800, (8, 512))
    probsT = ar.bf16(24704, (TC,))
    qal = ar.bf16(29824, (TC,))
    pvb = ar.f32(30848, (TC,))[0:65, :]
    sig = ar.bf16(0, (4, TC))
    mem_sb = ar.bf16(8192, (4, 512))
    e1s = ar.f32(24704, (512,))
    mks = ar.f32(25216, (512,))
    LN_off = 32896
    sums_off = 34944

    ones_col = sp.sbuf("onc", [128, 1], F32)
    ones_r128 = sp.sbuf("o12", [1, 128], F32)
    ones_r64 = sp.sbuf("o64", [1, 64], F32)
    identf = make_ident(sp, "idf", 128, F32)
    wk_sb = sp.sbuf("wks", [128, 8, 512], BF16)
    wv_sb = sp.sbuf("wvs", [128, 8, 512], BF16)
    wq_h = sp.sbuf("wqh", [128, 8, 64], BF16)
    v_tile = sp.sbuf("vtl", [128, 512], F32)
    rrep = sp.sbuf("rrp", [64, 512], F32)
    xl64 = sp.sbuf("xl", [64, 512], F32)
    t64 = sp.sbuf("t64", [64, 512], F32)
    wrow = sp.sbuf("wrw", [1, 1], F32)
    wcol = sp.sbuf("wcl", [128, 1], F32)
    omw = sp.sbuf("omw", [128, 1], F32)
    mn_sb = sp.sbuf("mns", [128, 4], F32)
    mn_bf = sp.sbuf("mnb", [128, 4], BF16)
    wdram = sp.dtmp("wd", (1, 1))

    pv2 = sp.sbuf("pv2", [65, TC], F32)
    pv_dram = [sp.dtmp(f"pvd{h}", (65, TC)) for h in range(16)]
    sums_dram = sp.dtmp("smd", (1, 512))
    psS = sp.psum("psS", [128, 2048])
    psPV = sp.psum("psP", [128, 2048])

    sp.memset(ones_col[:], 1.0, engine="gpsimd")
    sp.memset(ones_r128[:], 1.0, engine="gpsimd")
    sp.memset(ones_r64[:], 1.0, engine="gpsimd")
    sp.memset(ar.t[:, 8192:16512].bitcast(BF16), 1.0, engine="gpsimd")
    sp.dmas([(wk_sb[:], Wkb.rearrange("(a p) f -> p a f", p=128)),
             (wv_sb[:], Wvb.rearrange("(a p) f -> p a f", p=128)),
             (mn_sb[:], mnL), (wrow[:], mwC)])
    sp.copy(mn_bf[:], mn_sb[:])
    sp.act(wrow[:], wrow[:], AF.Sigmoid)
    sp.dma(wdram, wrow[:])
    sp.dma(wcol[:], wdram.to_broadcast([128, 1]))
    sp.op("vector", lambda e: e.tensor_scalar(out=omw[:], in0=wcol[:], scalar1=-1.0,
                                              scalar2=1.0, op0=ALU.mult, op1=ALU.add))

    mrow = ar.row(LN_off, 512)
    srow = ar.row(LN_off + 512, 512)
    sinv = ar.row(LN_off + 1024, 512)
    msr = ar.row(LN_off + 1536, 512)
    sums = ar.row(sums_off, 512)
    rcp = ar.row(sums_off + 512, 512)

    def emit_ln_chunk(src_ap):
        def fmean(eng):
            last = None
            for a in range(8):
                last = eng.matmul(psS[0:1, 0:512], ones_col[:], xch[:, a, :],
                                  start=(a == 0), stop=(a == 7))
            return last
        sp.mms(fmean)
        sp.act(mrow[:], psS[0:1, 0:512], AF.Copy, scale=1.0 / 1024)
        for a in range(8):
            sp.tt(v_tile[:], xch[:, a, :], xch[:, a, :], ALU.mult)
            def fsq(eng, _a=a):
                return eng.matmul(psS[0:1, 512:1024], ones_col[:], v_tile[:],
                                  start=(_a == 0), stop=(_a == 7))
            sp.mms(fsq)
        sp.act(srow[:], psS[0:1, 512:1024], AF.Copy, scale=1.0 / 1024)
        sp.tt(msr[:], mrow[:], mrow[:], ALU.mult)
        sp.tt(srow[:], srow[:], msr[:], ALU.subtract)
        sp.ts(srow[:], srow[:], 1e-5, ALU.add)
        sp.act(srow[:], srow[:], AF.Sqrt)
        sp.recip(sinv[:], srow[:])
        sp.tt(msr[:], mrow[:], sinv[:], ALU.mult)
        def fb(eng):
            eng.matmul(psS[:, 512:1024], ones_r128[:], sinv[:], start=True, stop=True)
            return eng.matmul(psS[:, 1024:1536], ones_r128[:], msr[:], start=True, stop=True)
        sp.mms(fb)
        for a in range(8):
            sp.tt(v_tile[:], xch[:, a, :], psS[:, 512:1024], ALU.mult)
            sp.tt(xnch[:, a, :], v_tile[:], psS[:, 1024:1536], ALU.subtract)

    for tcc in range(4):
        tsl = slice(tcc * 512, tcc * 512 + 512)
        sp.dma(xch, xqT.rearrange("(a p) t -> p a t", p=128)[:, :, tsl])
        emit_ln_chunk(None)
        for h in range(16):
            ob = (h % 2) * 64
            sp.dma(wq_h[:], Wqb[:, h * 64:(h + 1) * 64].rearrange("(a p) f -> p a f", p=128))
            def fq(eng, _ob=ob):
                last = None
                for a in range(8):
                    last = eng.matmul(psS[_ob:_ob + 64, 1536:2048], wq_h[:, a, :],
                                      xnch[:, a, :], start=(a == 0), stop=(a == 7))
                return last
            sp.mms(fq)
            def fevq(eng, _h=h, _ob=ob, _tsl=tsl):
                return eng.copy(out=q_sb[_ob:_ob + 64, _h // 2, _tsl],
                                in_=psS[_ob:_ob + 64, 1536:2048])
            sp.op("scalar", fevq)

    for kh in range(2):
        for tcc in range(8):
            t0 = kh * KH + tcc * 512
            lsl = slice(tcc * 512, tcc * 512 + 512)
            if "xf_chunk" in io:
                sp.dma(xch, io["xf_chunk"](t0))
            else:
                sp.dma(xch, xfT.rearrange("(a p) t -> p a t", p=128)[:, :, t0:t0 + 512])
            emit_ln_chunk(None)
            for g in range(8):
                ob = (g % 2) * 64
                def fk(eng, _g=g, _ob=ob):
                    last = None
                    for a in range(8):
                        last = eng.matmul(psS[_ob:_ob + 64, 1536:2048],
                                          wk_sb[:, a, _g * 64:(_g + 1) * 64],
                                          xnch[:, a, :], start=(a == 0), stop=(a == 7))
                    return last
                sp.mms(fk)
                def fevk(eng, _g=g, _ob=ob, _lsl=lsl):
                    return eng.copy(out=k_sb[_ob:_ob + 64, _g // 2, _lsl],
                                    in_=psS[_ob:_ob + 64, 1536:2048])
                sp.op("scalar", fevk)
            for fo in range(4):
                def fv(eng, _fo=fo):
                    last = None
                    for a in range(8):
                        last = eng.matmul(psS[:, 1536:2048],
                                          wv_sb[:, a, _fo * 128:(_fo + 1) * 128],
                                          xnch[:, a, :], start=(a == 0), stop=(a == 7))
                    return last
                sp.mms(fv)
                sp.copy(v_tile[:], psS[:, 1536:2048], engine="scalar")
                for hh in range(2):
                    g = fo * 2 + hh
                    for kb in range(4):
                        kbg = tcc * 4 + kb
                        def ftr(eng, _hh=hh, _kb=kb):
                            return eng.transpose(psPV[:, 0:64],
                                                 v_tile[_hh * 64:(_hh + 1) * 64,
                                                        _kb * 128:(_kb + 1) * 128],
                                                 identf[_hh * 64:(_hh + 1) * 64,
                                                        _hh * 64:(_hh + 1) * 64])
                        sp.op("tensor", ftr)
                        def fevt(eng, _g=g, _kbg=kbg):
                            return eng.copy(out=vt[:, _g, _kbg, 0:64], in_=psPV[:, 0:64])
                        sp.op("scalar", fevt)
        for h in range(16):
            g = h // 2
            obq = (h % 2) * 64
            obk = (g % 2) * 64
            sp.dma(qal.rearrange("p t -> p t")[obk:obk + 64, :] if False else
                   ar.t[obk:obk + 64, 29824:29824 + 1024].bitcast(BF16),
                   ar.t[obq:obq + 64, 16512 + (h // 2) * 1024:16512 + (h // 2) * 1024 + 1024]
                   .bitcast(BF16), engine="sync")
            for kbg in range(32):
                ksl = slice(kbg * 128, kbg * 128 + 128)
                def fsT(eng, _g=g, _obk=obk, _ksl=ksl):
                    last = None
                    for qc in range(4):
                        last = eng.matmul(psS[:, qc * 512:(qc + 1) * 512],
                                          k_sb[_obk:_obk + 64, _g // 2, _ksl],
                                          ar.t[_obk:_obk + 64, 29824 + qc * 256:29824 + (qc + 1) * 256]
                                          .bitcast(BF16),
                                          start=True, stop=True)
                    return last
                sp.mms(fsT)
                sp.act(probsT, psS[:, :], AF.Exp, scale=0.125)
                def fpv(eng, _g=g, _kbg=kbg):
                    last = None
                    pr = probsT.rearrange("p (qc t) -> p qc t", qc=4)
                    for qc in range(4):
                        last = eng.matmul(psPV[0:65, qc * 512:(qc + 1) * 512],
                                          vt[:, _g, _kbg, :], pr[:, qc, :],
                                          start=(_kbg == 0), stop=(_kbg == 31))
                    return last
                sp.mms(fpv)
            sp.copy(pvb, psPV[0:65, :], engine="scalar")
            if kh == 0:
                sp.dma(pv_dram[h], pvb)
            else:
                sp.dma(pv2[:], pv_dram[h])
                sp.tt(pvb, pvb, pv2[:], ALU.add)
                for tc in range(4):
                    tsl = slice(tc * 512, tc * 512 + 512)
                    sp.dma(sums_dram, pvb[64:65, tsl])
                    sp.dma(sums, sums_dram)
                    sp.recip(rcp[:], sums[:])
                    def fbc(eng):
                        return eng.matmul(psS[0:64, 0:512], ones_r64[:], rcp[:],
                                          start=True, stop=True)
                    sp.mms(fbc)
                    sp.copy(rrep[:], psS[0:64, 0:512], engine="scalar")
                    sp.tt(t64[:], pvb[0:64, tsl], rrep[:], ALU.mult)
                    sp.ts(t64[:], t64[:], omw[0:64, :], ALU.mult)
                    sp.dma(xl64[:], xrT[h * 64:(h + 1) * 64, tsl])
                    sp.tt(xl64[:], xl64[:], t64[:], ALU.add)
                    sp.dma(yT[h * 64:(h + 1) * 64, tsl], xl64[:])

    sp.dma(mem_sb[:], memb.rearrange("(j p) f -> p j f", p=128))
    e1v = ar.f32(24704, (512,))
    mkv = ar.f32(25216, (512,))
    t128 = sp.sbuf("t128", [128, 512], F32)
    xl128 = sp.sbuf("xl2", [128, 512], F32)
    rrep128 = sp.sbuf("rr2", [128, 512], F32)
    for g2 in range(2):
        for jj in range(4):
            j = g2 * 4 + jj
            for tc in range(4):
                tsl = slice(tc * 512, tc * 512 + 512)
                sp.ts(mkv, q_sb[:, j, tsl], 0.0, ALU.min)
                sp.op("scalar", lambda e: e.activation(out=e1v, in_=mkv, func=AF.Exp))
                sp.ts(e1v, e1v, 1.0, ALU.subtract)
                sp.ts(mkv, q_sb[:, j, tsl], 0.0, ALU.max)
                sp.tt(sig[:, jj, tsl], mkv, e1v, ALU.add)
        for tc in range(4):
            tsl = slice(tc * 512, tc * 512 + 512)
            def fden(eng, _tsl=tsl):
                last = None
                for jj in range(4):
                    last = eng.matmul(psS[0:1, 0:512], mn_bf[:, jj:jj + 1],
                                      sig[:, jj, _tsl], start=(jj == 0), stop=(jj == 3))
                return last
            sp.mms(fden)
            sp.act(sums[:], psS[0:1, 0:512], AF.Copy)
            sp.recip(rcp[:], sums[:])
            def fbc2(eng):
                eng.matmul(psS[0:64, 512:1024], ones_r64[:], rcp[:], start=True, stop=True)
                return eng.matmul(psS[64:128, 512:1024], ones_r64[:], rcp[:], start=True, stop=True)
            sp.mms(fbc2)
            sp.copy(rrep128[:], psS[:, 512:1024], engine="scalar")
            for fo in range(4):
                def fnum(eng, _fo=fo, _tsl=tsl):
                    last = None
                    for jj in range(4):
                        last = eng.matmul(psPV[0:128, 0:512],
                                          mem_sb[:, jj, _fo * 128:(_fo + 1) * 128],
                                          sig[:, jj, _tsl], start=(jj == 0), stop=(jj == 3))
                    return last
                sp.mms(fnum)
                sp.tt(t128[:], psPV[0:128, 0:512], rrep128[:], ALU.mult)
                sp.ts(t128[:], t128[:], wcol[:], ALU.mult)
                r0 = g2 * 512 + fo * 128
                sp.dma(xl128[:], yT[r0:r0 + 128, tsl])
                sp.tt(xl128[:], xl128[:], t128[:], ALU.add)
                sp.dma(yT[r0:r0 + 128, tsl], xl128[:])
    return sp


def run_attn3(X, inp, Wp):
    sp = build_attn3()
    shared = {
        "Wqb": np.ascontiguousarray(Wp["L3_attn_Wq_b"]),
        "Wkb": np.ascontiguousarray(Wp["L3_attn_Wk_b"]),
        "Wvb": np.ascontiguousarray(Wp["L3_attn_Wv_b"]),
        "memb": np.ascontiguousarray(Wp["L3_attn_memory_b"]),
        "mnL": np.ascontiguousarray(inp["L3_attn_memory_norm"].reshape(4, 128).T),
        "mwC": np.asarray(inp["L3_attn_memory_weight"], np.float32).reshape(1, 1),
    }
    in_maps = []
    xfTs = [np.ascontiguousarray(X[b * 8192:(b + 1) * 8192].T) for b in range(2)]
    for c in range(NC):
        b, q = c // 4, c % 4
        m = dict(shared)
        m["xfT"] = xfTs[b]
        xo = np.ascontiguousarray(X[b * 8192 + 2048 * q: b * 8192 + 2048 * (q + 1)].T)
        m["xqT"] = xo
        m["xrT"] = xo
        in_maps.append(m)
    res = sp.run(in_maps, core_ids=range(NC))
    out = np.empty_like(X)
    for c in range(NC):
        out[c * TC:(c + 1) * TC] = res.results[c]["yT"].T
    return out


def kernel(**inp):
    inp = {k: np.asarray(v) for k, v in inp.items()}
    X = run_fused(inp)
    return X.reshape(2, 8192, 1024)


def kernel_multilaunch(**inp):
    inp = {k: np.asarray(v) for k, v in inp.items()}
    Wp = run_prep(inp)
    X = np.ascontiguousarray(np.asarray(inp["L0_attn_x"], np.float32).reshape(T, D))
    X = run_moe0(X, inp, Wp)
    X = run_attn0(X, inp, Wp)
    X = run_moe0(X, inp, Wp)
    X = run_moe1(X, inp, Wp)
    X = run_l1(X, inp, Wp)
    X = run_moe1(X, inp, Wp)
    X = run_mlp2(X, inp, Wp)
    X = run_l2(X, inp, Wp)
    X = run_mlp2(X, inp, Wp)
    X = run_mlp3(X, inp, Wp)
    X = run_attn3(X, inp, Wp)
    X = run_mlp3(X, inp, Wp)
    return X.reshape(2, 8192, 1024)


GROUPS8 = [list(range(8))]
GROUPSB = [[0, 1, 2, 3], [4, 5, 6, 7]]

_PREP_OUT_SHAPES = {
    "L0_attn_Wq1_b": ((1024, 1024), BF16), "L0_attn_Wq2_b": ((1024, 1024), BF16),
    "L0_attn_Wk1_b": ((1024, 128), BF16), "L0_attn_Wk2_b": ((1024, 128), BF16),
    "L0_attn_Wv_b": ((1024, 128), BF16), "L0_attn_Wout_b": ((1024, 1024), BF16),
    "L1_ffn_shared_gate_w_b": ((1024, 1024), BF16), "L1_ffn_shared_up_w_b": ((1024, 1024), BF16),
    "L1_ffn_shared_down_w_b": ((1024, 1024), BF16),
    "L3_attn_Wq_b": ((1024, 1024), BF16), "L3_attn_Wk_b": ((1024, 512), BF16),
    "L3_attn_Wv_b": ((1024, 512), BF16), "L3_attn_memory_b": ((512, 512), BF16),
    "L0_ffn_W1_b": ((8, 1024, 4096), BF16), "L0_ffn_W2_b": ((8, 2048, 1024), BF16),
    "L1_ffn_gate_w_b": ((16, 1024, 512), BF16), "L1_ffn_up_w_b": ((16, 1024, 512), BF16),
    "L1_ffn_down_w_b": ((16, 512, 1024), BF16),
    "L1_attn_w_in_tb": ((1024, 4096), BF16), "L1_attn_w_x_tb": ((2048, 96), BF16),
    "L1_attn_w_dt_tb": ((64, 2048), BF16), "L1_attn_w_out_tb": ((2048, 1024), BF16),
    "L2_attn_in_proj_w_tb": ((1024, 4096), BF16), "L2_attn_B_w_tb": ((2048, 64), BF16),
    "L2_attn_C_w_tb": ((2048, 64), BF16), "L2_attn_dt_w_tb": ((2048, 2048), BF16),
    "L2_attn_out_proj_w_tb": ((2048, 1024), BF16),
    "L3_ffn_w_up_qtb": ((1024, 4096), BF16), "L3_ffn_w_down_qtb": ((4096, 1024), BF16),
    "W_exp_fb": ((1024, 4096), BF16), "bexp_f": ((1, 4096), F32),
    "W_con_fb": ((4096, 1024), BF16), "bcon_f": ((1, 1024), F32),
    "lam": ((1, 1), F32),
}


def build_fused(n_sub=13):
    sp = SP(fused=True)
    nc = sp.nc
    xin = sp.din("xin", (1024, TC))
    out = sp.dout("out", (1024, TC))
    offs_in = sp.din("offs", (1, 32), mybir.dt.int32)
    hmask_in = sp.din("hmask", (128, 1))
    posrow = sp.din("posrow", (1, 1152))
    divrow = sp.din("divrow", (1, 32))
    alog1 = sp.din("AlogS1", (256, 16))
    alog2 = sp.din("AlogS2", (256, 64))
    raw = {}
    for nm, shp in [("WrL", (128, 8, 8)), ("brC", (8, 1)), ("b1L", (128, 32, 8)),
                    ("b2L", (128, 8, 8)), ("RwL", (128, 8, 16)),
                    ("gamL1", (128, 8)), ("betL1", (128, 8)), ("cwL1", (128, 16, 4)),
                    ("cbL1", (128, 16)), ("bdtL1", (128, 16)), ("DpL1", (128, 16)),
                    ("nwL2", (128, 8)), ("cwL2", (128, 16, 4)), ("cbL2", (128, 16)),
                    ("bdtL2", (128, 16)), ("DpL2", (128, 16)),
                    ("boutL", (128, 8)), ("mnL", (128, 4)), ("mwC", (1, 1))]:
        raw[nm] = sp.din(nm, shp)

    Wp = {nm: sp.dtmp(nm + "_i", shp, dt) for nm, (shp, dt) in _PREP_OUT_SHAPES.items()}
    Xa = sp.dtmp("Xa", (1024, TC))
    Xb = sp.dtmp("Xb", (1024, TC))
    XG = sp.dtmp("XG", (4096, TC))
    dG = sp.nc.dram_tensor("dGs", [T, TC], F32, kind="Internal", addr_space="Shared").ap()
    uG = sp.nc.dram_tensor("uGs", [T, TC], F32, kind="Internal", addr_space="Shared").ap()
    BG1 = sp.nc.dram_tensor("BG1s", [128, TC], F32, kind="Internal", addr_space="Shared").ap()
    CG1 = sp.nc.dram_tensor("CG1s", [128, TC], F32, kind="Internal", addr_space="Shared").ap()
    BG2 = sp.nc.dram_tensor("BG2s", [512, TC], F32, kind="Internal", addr_space="Shared").ap()
    CG2 = sp.nc.dram_tensor("CG2s", [512, TC], F32, kind="Internal", addr_space="Shared").ap()
    ys_own = sp.dtmp("ysown", (256, T))
    yG = sp.nc.dram_tensor("yGs", [2048, T], F32, kind="Internal", addr_space="Shared").ap()
    dS_loc = sp.dtmp("dSloc", (256, T))
    uS_loc = sp.dtmp("uSloc", (256, T))
    ys_loc = sp.dtmp("ysloc", (2048, TC))
    d_o = sp.dtmp("d_o", (2048, TC))
    u_o = sp.dtmp("u_o", (2048, TC))
    xs_o = sp.dtmp("xs_o", (2048, TC))
    z_o = sp.dtmp("z_o", (2048, TC))
    B_o1 = sp.dtmp("B_o1", (16, TC))
    C_o1 = sp.dtmp("C_o1", (16, TC))
    B_o2 = sp.dtmp("B_o2", (64, TC))
    C_o2 = sp.dtmp("C_o2", (64, TC))

    regs = {}

    def glue_regs(names_cols):
        t = sp.sbuf("OFF", [1, 32], mybir.dt.int32)
        sp.dma(t[:], offs_in)
        def f(eng, sem):
            for nm, col in names_cols:
                regs[nm] = eng.value_load(t[0:1, col:col + 1])
            return None
        sp.step("gpsimd", 0, f)

    def ag(ins, outs, groups):
        def f(eng, sem):
            return eng.collective_compute(
                "AllGather", ALU.bypass, replica_groups=groups,
                ins=[a.opt() for a in ins], outs=[a.opt() for a in outs])
        sp.step("gpsimd", 1, f)

    def mk_xw_load():
        def f(sp_, dst):
            glue_regs([("ra", 0), ("ca", 1), ("rb", 2), ("cb", 3)])
            def g(eng, sem):
                eng.dma_start(out=dst[:, :, 0:1024],
                              in_=XG[bass.ds(regs["ra"], 1024), bass.ds(regs["ca"], 1024)]
                              .rearrange("(a p) t -> p a t", p=128)).then_inc(sem, 16)
                eng.dma_start(out=dst[:, :, 1024:1152],
                              in_=XG[bass.ds(regs["rb"], 1024), bass.ds(regs["cb"], 128)]
                              .rearrange("(a p) t -> p a t", p=128)).then_inc(sem, 16)
                return None
            sp_.step("gpsimd", 32, g)
        return f

    def mk_xh_load(Xcur):
        def f(sp_, dst):
            glue_regs([("hr", 4)])
            hm = sp_.sbuf("HM", [128, 1], F32)
            sp_.dma(hm[:], hmask_in)
            def g(eng, sem):
                eng.dma_start(out=dst[:, :, 0:3],
                              in_=XG[bass.ds(regs["hr"], 1024), 2045:2048]
                              .rearrange("(a p) t -> p a t", p=128)).then_inc(sem, 16)
                eng.dma_start(out=dst[:, :, 3:2051],
                              in_=Xcur.rearrange("(a p) t -> p a t", p=128)).then_inc(sem, 16)
                return None
            sp_.step("gpsimd", 32, g)
            hv = dst[:, :, 0:3]
            sp_.ts(hv, hv, hm[:], ALU.mult)
        return f

    def reshard_du():
        glue_regs([("o256a", 5), ("o256b", 7)])
        buf = sp.sbuf("RSB", [128, 8, TC], F32)
        for dt, rg in ((0, "o256a"), (1, "o256b")):
            for srcg, dstl in ((dG, dS_loc), (uG, uS_loc)):
                def g(eng, sem, _rg=rg, _srcg=srcg):
                    return eng.dma_start(
                        out=buf,
                        in_=_srcg.rearrange("(r dd) t -> dd r t", dd=TC)[bass.ds(regs[_rg], 128), :, :])
                sp.step("gpsimd", 16, g)
                sp.dma(dstl[dt * 128:(dt + 1) * 128, :].rearrange("p (r t) -> p r t", r=8), buf)

    def reshard_ys():
        glue_regs([("o2048", 6)])
        buf = sp.sbuf("RSB2", [128, 8, TC], F32)
        for gh in range(2):
            def g(eng, sem, _gh=gh):
                return eng.dma_start(
                    out=buf,
                    in_=yG.rearrange("(g p) t -> p g t", p=128)[:, _gh * 8:(_gh + 1) * 8,
                                                               bass.ds(regs["o2048"], TC)])
            sp.step("gpsimd", 16, g)
            sp.dma(ys_loc.rearrange("(g p) t -> p g t", p=128)[:, gh * 8:(gh + 1) * 8, :], buf)

    def mk_load_bc(BGx, CGx, nst):
        def f(sp_, n, b, ch, Brep, Crep):
            pairs = []
            for ri, r in enumerate((b * 4 + ch * 2, b * 4 + ch * 2 + 1)):
                row = r * nst + n
                pairs.append((Brep[:, ri * 2048:(ri + 1) * 2048],
                              BGx[row:row + 1, :].to_broadcast([128, 2048])))
                pairs.append((Crep[:, ri * 2048:(ri + 1) * 2048],
                              CGx[row:row + 1, :].to_broadcast([128, 2048])))
            sp_.dmas(pairs)
        return f

    def mk_ys_load():
        def f(sp_, g_, dst):
            def g(eng, sem):
                return eng.dma_start(out=dst,
                                     in_=yG[g_ * 128:(g_ + 1) * 128, bass.ds(regs["o2048"], 2048)])
            sp_.step("gpsimd", 16, g)
        return f

    def xf_chunk(t0):
        cb = t0 // 2048
        c0 = t0 % 2048
        return XG[cb * 1024:(cb + 1) * 1024, c0:c0 + 512].rearrange("(a p) t -> p a t", p=128)

    sp.sub_begin()
    build_prep(sp=sp, io=Wp)

    cur, nxt = xin, Xa
    step_i = 0

    def bump(written):
        nonlocal cur, nxt, step_i
        cur = written
        nxt = Xb if written is Xa else Xa
        step_i += 1

    io_moe0 = dict(W1b=Wp["L0_ffn_W1_b"], W2b=Wp["L0_ffn_W2_b"], WrL=raw["WrL"],
                   brC=raw["brC"], b1L=raw["b1L"], b2L=raw["b2L"])
    io_moe1 = dict(Gb=Wp["L1_ffn_gate_w_b"], Ub=Wp["L1_ffn_up_w_b"], Db=Wp["L1_ffn_down_w_b"],
                   SGb=Wp["L1_ffn_shared_gate_w_b"], SUb=Wp["L1_ffn_shared_up_w_b"],
                   SDb=Wp["L1_ffn_shared_down_w_b"], RwL=raw["RwL"])
    io_mlp2 = dict(Wub=Wp["W_exp_fb"], Wdb=Wp["W_con_fb"],
                   bup_row=Wp["bexp_f"], bdn_row=Wp["bcon_f"])
    io_mlp3 = dict(Wub=Wp["L3_ffn_w_up_qtb"], Wdb=Wp["L3_ffn_w_down_qtb"])

    seq = []
    seq.append(("moe0a", lambda: build_moe0(sp=sp, io=dict(io_moe0, xT=cur, yT=nxt))))
    def attn0_step():
        ag([cur], [XG], GROUPSB)
        build_attn0(sp=sp, io=dict(
            xrT=cur, yT=nxt, xw_load=mk_xw_load(),
            Wq1b=Wp["L0_attn_Wq1_b"], Wq2b=Wp["L0_attn_Wq2_b"], Wk1b=Wp["L0_attn_Wk1_b"],
            Wk2b=Wp["L0_attn_Wk2_b"], Wvb=Wp["L0_attn_Wv_b"], Woutb=Wp["L0_attn_Wout_b"],
            boutL=raw["boutL"], lamC=Wp["lam"].to_broadcast([128, 1]),
            posrow=posrow, divrow=divrow))
    seq.append(("attn0", attn0_step))
    seq.append(("moe0b", lambda: build_moe0(sp=sp, io=dict(io_moe0, xT=cur, yT=nxt))))
    seq.append(("moe1a", lambda: build_moe1(sp=sp, io=dict(io_moe1, xT=cur, yT=nxt))))

    def l1_step():
        ag([cur], [XG], GROUPSB)
        build_l1a(sp=sp, io=dict(
            xh_load=mk_xh_load(cur), winT=Wp["L1_attn_w_in_tb"], wxT=Wp["L1_attn_w_x_tb"],
            wdtT=Wp["L1_attn_w_dt_tb"], gamL=raw["gamL1"], betL=raw["betL1"],
            cwL=raw["cwL1"], cbL=raw["cbL1"], bdtL=raw["bdtL1"],
            delta_o=d_o, u_o=u_o, xs_o=xs_o, z_o=z_o, B_o=B_o1, C_o=C_o1))
        ag([d_o], [dG], GROUPS8)
        ag([u_o], [uG], GROUPS8)
        ag([B_o1], [BG1], GROUPS8)
        ag([C_o1], [CG1], GROUPS8)
        sp.sub_begin()
        reshard_du()
        sp.sub_begin()
        build_scan(16, sp=sp, io=dict(dS=dS_loc, uS=uS_loc, load_bc=mk_load_bc(BG1, CG1, 16),
                                      AlogS=alog1, yS=ys_own))
        ag([ys_own], [yG], GROUPS8)
        sp.sub_begin()
        reshard_ys()
        sp.sub_begin()
        build_l1c(sp=sp, io=dict(ysT=ys_loc, xsT=xs_o, zT=z_o, xT=cur, yT=nxt,
                                 DpL=raw["DpL1"], woTb=Wp["L1_attn_w_out_tb"]))
    seq.append(("l1", l1_step))
    seq.append(("moe1b", lambda: build_moe1(sp=sp, io=dict(io_moe1, xT=cur, yT=nxt))))
    seq.append(("mlp2a", lambda: build_mlp2(sp=sp, io=dict(io_mlp2, xT=cur, yT=nxt))))

    def l2_step():
        ag([cur], [XG], GROUPSB)
        build_l2a(sp=sp, io=dict(
            xh_load=mk_xh_load(cur), winT=Wp["L2_attn_in_proj_w_tb"], bwT=Wp["L2_attn_B_w_tb"],
            cwT=Wp["L2_attn_C_w_tb"], dtwT=Wp["L2_attn_dt_w_tb"], nwL=raw["nwL2"],
            cwL=raw["cwL2"], cbL=raw["cbL2"], bdtL=raw["bdtL2"],
            dt_o=d_o, u_o=u_o, xc_o=xs_o, z_o=z_o, B_o=B_o2, C_o=C_o2))
        ag([d_o], [dG], GROUPS8)
        ag([u_o], [uG], GROUPS8)
        ag([B_o2], [BG2], GROUPS8)
        ag([C_o2], [CG2], GROUPS8)
        sp.sub_begin()
        reshard_du()
        sp.sub_begin()
        build_scan(64, sp=sp, io=dict(dS=dS_loc, uS=uS_loc, load_bc=mk_load_bc(BG2, CG2, 64),
                                      AlogS=alog2, yS=ys_own))
        ag([ys_own], [yG], GROUPS8)
        sp.sub_begin()
        reshard_ys()
        sp.sub_begin()
        build_l1c(sp=sp, io=dict(ysT=ys_loc, xsT=xs_o, zT=z_o, xT=cur, yT=nxt,
                                 DpL=raw["DpL2"], woTb=Wp["L2_attn_out_proj_w_tb"]))
    seq.append(("l2", l2_step))
    seq.append(("mlp2b", lambda: build_mlp2(sp=sp, io=dict(io_mlp2, xT=cur, yT=nxt))))
    seq.append(("mlp3a", lambda: build_mlp3(sp=sp, io=dict(io_mlp3, xT=cur, yT=nxt))))

    def attn3_step():
        ag([cur], [XG], GROUPSB)
        build_attn3(sp=sp, io=dict(
            xf_chunk=xf_chunk, xqT=cur, xrT=cur, yT=nxt,
            Wqb=Wp["L3_attn_Wq_b"], Wkb=Wp["L3_attn_Wk_b"], Wvb=Wp["L3_attn_Wv_b"],
            memb=Wp["L3_attn_memory_b"], mnL=raw["mnL"], mwC=raw["mwC"]))
    seq.append(("attn3", attn3_step))
    seq.append(("mlp3b", lambda: build_mlp3(sp=sp, io=dict(io_mlp3, xT=cur, yT=nxt))))

    for i, (nm, fn) in enumerate(seq):
        if i >= n_sub:
            break
        if i == n_sub - 1 or i == len(seq) - 1:
            nxt = out
        sp.sub_begin()
        fn()
        bump(nxt)
    return sp


def stage_fused(inp, c):
    b, q = c // 4, c % 4
    m = {}
    for name in ["L0_attn_Wq1", "L0_attn_Wq2", "L0_attn_Wk1", "L0_attn_Wk2", "L0_attn_Wv",
                 "L0_attn_Wout", "L1_ffn_shared_gate_w", "L1_ffn_shared_up_w",
                 "L1_ffn_shared_down_w", "L3_attn_Wq", "L3_attn_Wk", "L3_attn_Wv",
                 "L3_attn_memory", "L0_ffn_W1", "L0_ffn_W2", "L1_ffn_gate_w", "L1_ffn_up_w",
                 "L1_ffn_down_w", "L1_attn_w_in", "L1_attn_w_x", "L1_attn_w_dt",
                 "L1_attn_w_out", "L2_attn_in_proj_w", "L2_attn_B_w", "L2_attn_C_w",
                 "L2_attn_dt_w", "L2_attn_out_proj_w", "L3_ffn_w_up", "L3_ffn_w_down",
                 "L2_ffn_W_exp", "L2_ffn_W_con"]:
        m[name] = np.ascontiguousarray(inp[name])
    for k in range(4):
        m[f"upw{k}"] = np.ascontiguousarray(inp["L2_ffn_up_weights"][:, :, k // 2, k % 2])
        m[f"dnw{k}"] = np.ascontiguousarray(inp["L2_ffn_down_weights"][:, :, k // 2, k % 2])
    for name in ["L2_ffn_b_exp", "L2_ffn_up_bias", "L2_ffn_b_con", "L2_ffn_down_bias"]:
        m[name] = np.ascontiguousarray(inp[name].reshape(1, -1))
    for name in ["L0_attn_lq1", "L0_attn_lk1", "L0_attn_lq2", "L0_attn_lk2"]:
        m[name] = np.ascontiguousarray(inp[name].reshape(1, -1))
    m["WrL"] = np.ascontiguousarray(inp["L0_ffn_Wr"].reshape(8, 128, 8).transpose(1, 0, 2))
    m["brC"] = np.ascontiguousarray(inp["L0_ffn_br"].reshape(8, 1))
    m["b1L"] = np.ascontiguousarray(inp["L0_ffn_b1"].T.reshape(32, 128, 8).transpose(1, 0, 2))
    m["b2L"] = np.ascontiguousarray(inp["L0_ffn_b2"].T.reshape(8, 128, 8).transpose(1, 0, 2))
    m["RwL"] = np.ascontiguousarray(inp["L1_ffn_router_w"].reshape(8, 128, 16).transpose(1, 0, 2))
    m["gamL1"] = np.ascontiguousarray(inp["L1_attn_gamma"].reshape(8, 128).T)
    m["betL1"] = np.ascontiguousarray(inp["L1_attn_beta"].reshape(8, 128).T)
    m["cwL1"] = np.ascontiguousarray(inp["L1_attn_conv_w"].reshape(2048, 4).reshape(16, 128, 4).transpose(1, 0, 2))
    m["cbL1"] = np.ascontiguousarray(inp["L1_attn_conv_b"].reshape(16, 128).T)
    m["bdtL1"] = np.ascontiguousarray(inp["L1_attn_b_dt"].reshape(16, 128).T)
    m["DpL1"] = np.ascontiguousarray(inp["L1_attn_Dp"].reshape(16, 128).T)
    m["nwL2"] = np.ascontiguousarray(inp["L2_attn_norm_w"].reshape(8, 128).T)
    m["cwL2"] = np.ascontiguousarray(inp["L2_attn_conv_w"].reshape(2048, 4).reshape(16, 128, 4).transpose(1, 0, 2))
    m["cbL2"] = np.ascontiguousarray(inp["L2_attn_conv_b"].reshape(16, 128).T)
    m["bdtL2"] = np.ascontiguousarray(inp["L2_attn_dt_b"].reshape(16, 128).T)
    m["DpL2"] = np.ascontiguousarray(inp["L2_attn_D_param"].reshape(16, 128).T)
    m["boutL"] = np.ascontiguousarray(inp["L0_attn_bout"].reshape(8, 128).T)
    m["mnL"] = np.ascontiguousarray(inp["L3_attn_memory_norm"].reshape(4, 128).T)
    m["mwC"] = np.asarray(inp["L3_attn_memory_weight"], np.float32).reshape(1, 1)
    m["posrow"] = (1024.0 * q + np.arange(1152, dtype=np.float32)).reshape(1, 1152)
    m["divrow"] = np.exp(np.arange(0, 64, 2, dtype=np.float32) * (-np.log(10000.0) / 64)).reshape(1, 32)
    m["AlogS1"] = np.ascontiguousarray(inp["L1_attn_A_log"][256 * c:256 * (c + 1)])
    m["AlogS2"] = np.ascontiguousarray(inp["L2_attn_A_log"][256 * c:256 * (c + 1)])
    s1 = 1024 * q
    s2 = s1 + 1024
    offs = np.zeros((1, 32), np.int32)
    offs[0, 0] = (s1 // 2048) * 1024
    offs[0, 1] = s1 % 2048
    offs[0, 2] = (s2 // 2048) * 1024
    offs[0, 3] = s2 % 2048
    offs[0, 4] = (q - 1) * 1024 if q > 0 else 0
    offs[0, 5] = 256 * c
    offs[0, 6] = 2048 * c
    offs[0, 7] = 256 * c + 128
    for r in range(8):
        for dt in range(2):
            offs[0, 8 + r * 2 + dt] = 256 * c + r * 2048 + dt * 128
    m["offs"] = offs
    m["hmask"] = np.full((128, 1), 1.0 if q > 0 else 0.0, np.float32)
    X = np.asarray(inp["L0_attn_x"], np.float32).reshape(T, D)
    m["xin"] = np.ascontiguousarray(X[c * TC:(c + 1) * TC].T)
    return m


def run_fused(inp, n_sub=13):
    sp = build_fused(n_sub)
    in_maps = [stage_fused(inp, c) for c in range(NC)]
    res = sp.run(in_maps, core_ids=range(NC))
    out = np.empty((T, D), np.float32)
    for c in range(NC):
        out[c * TC:(c + 1) * TC] = res.results[c]["out"].T
    return out
